# version 02
# baseline (speedup 1.0000x reference)
import numpy as np
from contextlib import ExitStack

import concourse.bass as bass
import concourse.mybir as mybir
from concourse.bass_utils import run_bass_kernel_spmd

F32 = mybir.dt.float32
BF16 = mybir.dt.bfloat16
I32 = mybir.dt.int32
AF = mybir.ActivationFunctionType
OP = mybir.AluOpType
AX = mybir.AxisListType

NCORE = 8
B_, L_, D, DS, DM = 4, 8192, 2048, 16, 16384
TOK = 4096
CH = 512
NCH = TOK // CH
QT = 1024
ND = D // 128
GELU_C = 0.7978845608028654
GELU_A2 = 0.21145925545
EPS = 1e-5

ENGS = ("sync", "scalar", "tensor", "vector", "gpsimd")
DEQ_NW = 3
DEQ_NJT = 16
B3B_MODE = 2
B3A_MODE = 2


class Ctr:
    def __init__(self, stack, nc, name):
        self.h = stack.enter_context(nc.semaphore(name))
        self.n = 0
        self.name = name


class Plan:
    def __init__(self):
        self.streams = {e: [] for e in ENGS}
        self.lastwait = {e: {} for e in ENGS}
        self.dsems = []

    def finish_dma(self, extra=()):
        for ctr in list(self.dsems) + list(extra):
            if ctr.n:
                self.put("sync", (lambda e, c=ctr, v=ctr.n: e.wait_ge(c.h, v) and None))

    def put(self, eng, fn, waits=(), inc=None, inck=1):
        lw = self.lastwait[eng]
        w2 = []
        for ctr, val in waits:
            if val is None or val <= 0:
                continue
            if lw.get(ctr.name, -1) >= val:
                continue
            lw[ctr.name] = val
            w2.append((ctr, val))
        if inc is not None:
            inc.n += inck
        self.streams[eng].append((fn, w2, inc, inck))
        return None if inc is None else inc.n

    def emit(self, nc):
        with nc.Block() as blk:
            def runner(items):
                def run(e):
                    for fn, waits, inc, inck in items:
                        for ctr, val in waits:
                            e.wait_ge(ctr.h, val)
                        inst = fn(e)
                        if inc is not None:
                            inst.then_inc(inc.h, inck)
                return run
            for eng in ENGS:
                if self.streams[eng]:
                    getattr(blk, eng)(runner(self.streams[eng]))


def build_program(upto=9, dbg=(), b1part=9):
    nc = bass.Bass("TRN2", target_bir_lowering=False)

    x_in = nc.dram_tensor("x_sh", [TOK, D], F32, kind="ExternalInput")
    wp_in = {w: nc.dram_tensor(f"{w}wp", [2048, 1024], I32, kind="ExternalInput")
             for w in ("g", "u", "d")}
    sc_in = {w: nc.dram_tensor(f"{w}sc", [2048, 64], F32, kind="ExternalInput")
             for w in ("g", "u", "d")}
    wdt_sh = nc.dram_tensor("wdt_sh", [256, 2048], F32, kind="ExternalInput")
    wout_sh = nc.dram_tensor("wout_sh", [256, 2048], F32, kind="ExternalInput")
    alog = nc.dram_tensor("alog", [D, DS], F32, kind="ExternalInput")
    xpw = nc.dram_tensor("xpw", [32, D], F32, kind="ExternalInput")
    xpb_in = nc.dram_tensor("xpb", [32], F32, kind="ExternalInput")
    dtb_in = nc.dram_tensor("dtb", [D], F32, kind="ExternalInput")
    outb_in = nc.dram_tensor("outb", [D], F32, kind="ExternalInput")
    sel8_in = nc.dram_tensor("sel8", [128, 8], F32, kind="ExternalInput")
    out_sh = nc.dram_tensor("out_sh", [TOK, D], F32, kind="ExternalOutput")

    xn_s = nc.dram_tensor("xn_s", [D, TOK], BF16)
    dt_s = nc.dram_tensor("dt_s", [D, TOK], F32)
    ssum_s = nc.dram_tensor("ssum_s", [D, TOK], F32)
    bc_s = nc.dram_tensor("bc_s", [32, TOK], BF16)
    y_s = nc.dram_tensor("y_s", [D, TOK], F32)
    inter_s = nc.dram_tensor("inter_s", [DM, TOK], BF16)
    g_s = nc.dram_tensor("g_s", [D, TOK], F32)
    c1 = nc.dram_tensor("c1", [4096, 256], BF16)
    g1 = nc.dram_tensor("g1", [8 * 4096, 256], BF16)
    cW = {w: nc.dram_tensor(f"c{w}", [2048, 2048], BF16) for w in ("g", "u", "d")}
    gW = {w: nc.dram_tensor(f"gw{w}", [8 * 2048, 2048], BF16) for w in ("g", "u", "d")}
    st_c = nc.dram_tensor("st_c", [128, 256], F32)
    st_g = nc.dram_tensor("st_g", [8 * 128, 256], F32)
    scr = dict(xn_s=xn_s, dt_s=dt_s, ssum_s=ssum_s, bc_s=bc_s, y_s=y_s,
               inter_s=inter_s, g_s=g_s, c1=c1, g1=g1, st_c=st_c, st_g=st_g)
    scr.update({f"c{w}": cW[w] for w in cW})
    scr.update({f"gw{w}": gW[w] for w in gW})
    dbg_outs = []
    for nm in dbg:
        h = scr[nm]
        dbg_outs.append((nc.dram_tensor(f"dbg_{nm}", list(h.shape), h.dtype,
                                        kind="ExternalOutput"), h))

    stack = ExitStack()
    sb = lambda nm, shape, dt: stack.enter_context(nc.sbuf_tensor(nm, shape, dt))

    aneg = sb("aneg", [128, ND * DS], F32)
    dtb_sb = sb("dtb_sb", [128, ND], F32)
    outb_sb = sb("outb_sb", [128, ND], F32)
    xpb_sb = sb("xpb_sb", [32, 1], F32)
    sel8_sb = sb("sel8_sb", [128, 8], F32)
    state = sb("state", [128, ND * DS], F32)
    sstate = sb("sstate", [128, ND], F32)
    hin = sb("hin", [128, ND * DS], F32)
    hin_st = sb("hin_st", [128, ND * DS], F32)
    idbf = sb("idbf", [128, 128], BF16)
    idf32 = sb("idf32", [128, 128], F32)
    zeros = sb("zeros", [128, CH], F32)
    wxpT = sb("wxpT", [128, ND * 128], BF16)
    csts = sb("csts", [128, 4], F32)

    qsp = Ctr(stack, nc, "qsp")
    qact = Ctr(stack, nc, "qact")

    def dsem(p, name):
        c = Ctr(stack, nc, name)
        p.dsems.append(c)
        return c

    cpe = Ctr(stack, nc, "cpe")
    cdv = Ctr(stack, nc, "cdv")
    cac = Ctr(stack, nc, "cac")
    cpl = Ctr(stack, nc, "cpl")
    ccc = Ctr(stack, nc, "ccc")

    p = Plan()
    s_c0 = dsem(p, "s_c0")
    for i in range(ND):
        p.put("sync", lambda e, i=i: e.dma_start(
            aneg[:, i * DS:(i + 1) * DS], alog[i * 128:(i + 1) * 128, :]), inc=s_c0, inck=16)
        p.put("sync", lambda e, i=i: e.dma_start(
            dtb_sb[:, i:i + 1], bass.AP(dtb_in, i * 128, [[1, 128], [1, 1]])), inc=s_c0, inck=16)
        p.put("sync", lambda e, i=i: e.dma_start(
            outb_sb[:, i:i + 1], bass.AP(outb_in, i * 128, [[1, 128], [1, 1]])), inc=s_c0, inck=16)
    p.put("sync", lambda e: e.dma_start(xpb_sb[:, :], bass.AP(xpb_in, 0, [[1, 32], [1, 1]])),
          inc=s_c0, inck=16)
    p.put("sync", lambda e: e.dma_start(sel8_sb[:, :], sel8_in[:, :]), inc=s_c0, inck=16)
    ld_done = s_c0.n
    p.put("gpsimd", lambda e: e.memset(csts[:, 0:1], -4.0))
    p.put("gpsimd", lambda e: e.memset(csts[:, 1:2], -6.0))
    p.put("gpsimd", lambda e: e.memset(csts[:, 2:3], EPS))
    p.put("gpsimd", lambda e: e.memset(csts[:, 3:4], 1.0))
    p.put("gpsimd", lambda e: e.memset(idbf[:], 0.0))
    p.put("gpsimd", lambda e: e.affine_select(out=idbf[:], in_=idbf[:], compare_op=OP.not_equal,
                                              fill=1.0, base=0, pattern=[[-1, 128]],
                                              channel_multiplier=1))
    p.put("gpsimd", lambda e: e.memset(idf32[:], 0.0))
    p.put("gpsimd", lambda e: e.affine_select(out=idf32[:], in_=idf32[:], compare_op=OP.not_equal,
                                              fill=1.0, base=0, pattern=[[-1, 128]],
                                              channel_multiplier=1))
    p.put("gpsimd", lambda e: e.memset(zeros[:], 0.0))
    p.put("gpsimd", lambda e: e.memset(state[:], 0.0))
    p.put("gpsimd", lambda e: e.memset(sstate[:], 0.0))
    p.put("gpsimd", lambda e: e.memset(hin[:], 0.0))
    p.put("gpsimd", lambda e: e.memset(wxpT[:], 0.0))
    a_exp = p.put("scalar", lambda e: e.activation(aneg[:], aneg[:], AF.Exp),
                  waits=[(s_c0, ld_done)], inc=cac)
    p.put("vector", lambda e: e.tensor_scalar_mul(aneg[:], aneg[:], -1.0),
          waits=[(cac, a_exp)], inc=cdv)
    p.finish_dma()
    p.emit(nc)

    wstack = ExitStack()
    wdtT = wstack.enter_context(nc.sbuf_tensor("wdtT", [128, ND * 2048], BF16))
    if upto < 1:
        return _finish_debug(nc, dbg_outs, stack, qsp, qact, ccc)
    p = Plan()
    LW = {}
    s_xpw = dsem(p, "s_xpw"); s_w4 = dsem(p, "s_w4"); s_stg1 = dsem(p, "s_stg1")
    s_wp = [dsem(p, "s_wp0"), dsem(p, "s_wp1")]
    s_sc = [dsem(p, "s_sc0"), dsem(p, "s_sc1")]
    s_sg = [dsem(p, "s_sg0"), dsem(p, "s_sg1")]
    s_wdt = dsem(p, "s_wdt")
    bstack = ExitStack()
    _a = lambda nm, shape, dt: bstack.enter_context(nc.sbuf_tensor(nm, shape, dt))
    w4 = _a("w4", [128, 2 * 2048], F32)
    wb4 = _a("wb4", [128, 2 * 2048], BF16)
    stg1 = _a("stg1", [128, 256], BF16)
    xpw_f = _a("xpw_f", [32, 2048], F32)
    xpw_b = _a("xpw_b", [32, 2048], BF16)
    wpt = _a("wpt", [128, 2 * 1024], I32)
    sct = _a("sct", [128, 2 * 64], F32)
    loi = _a("loi", [128, 1024], I32)
    hii = _a("hii", [128, 1024], I32)
    pf = _a("pf", [128, 2 * 1024], F32)
    ge8 = _a("ge8", [128, 1024], F32)
    cc_ = _a("cc_", [128, 1024], F32)
    r1_ = _a("r1_", [128, 1024], F32)
    r2_ = _a("r2_", [128, 1024], F32)
    t1_ = _a("t1_", [128, 1024], F32)
    mm_ = _a("mm_", [128, 1024], F32)
    sg_ = _a("sg_", [128, 1024], F32)
    vv_ = _a("vv_", [128, 1024], F32)
    deq = _a("deq", [128, 2 * 2048], BF16)
    stg2 = _a("stg2", [128, 2 * 2048], BF16)
    pstrA = bstack.enter_context(nc.psum_tensor("pstrA", [128, 128], BF16))
    pstrB = bstack.enter_context(nc.psum_tensor("pstrB", [128, 128], BF16))
    pstrP = [pstrA, pstrB]
    if True:
        v = p.put("sync", lambda e: e.dma_start(xpw_f[:, :], xpw[:, :]), inc=s_xpw, inck=16)
        cast = p.put("vector", lambda e: e.tensor_copy(xpw_b[:], xpw_f[:]),
                     waits=[(s_xpw, v)], inc=cdv)
        for i0 in range(0, ND, 2):
            for j in range(2):
                i = i0 + j
                p.put("tensor", lambda e, i=i, sl=j: e.transpose(
                    pstrP[sl][:, 0:32],
                    xpw_b[:, i * 128:(i + 1) * 128], idbf[0:32, 0:32]),
                    waits=[(cdv, cast), (cac, LW.get(("pstr", j), 0))], inc=None)
            dr = p.put("tensor", lambda e: e.drain(), inc=cpe)
            for j in range(2):
                i = i0 + j
                ev = p.put("scalar", lambda e, i=i, sl=j: e.copy(
                    wxpT[:, i * 128:i * 128 + 32], pstrP[sl][:, 0:32]),
                    waits=[(cpe, dr)], inc=cac)
                LW[("pstr", j)] = ev

        for wsrc, c1row in (((wdt_sh, 0), (wout_sh, 2048)) if b1part >= 1 else ()):
            lds = []
            for j in range(2):
                lds.append(p.put("sync", lambda e, j=j, wsrc=wsrc: e.dma_start(
                    w4[:, j * 2048:(j + 1) * 2048], wsrc[j * 128:(j + 1) * 128, :]),
                    waits=[(cpe, LW.get("w4_free", 0))], inc=s_w4, inck=16))
            cast = p.put("vector", lambda e: e.tensor_copy(wb4[:], w4[:]),
                         waits=[(s_w4, lds[-1]), (cpe, LW.get("w4_free", 0))], inc=cdv)
            for i in range(ND):
                for j in range(2):
                    p.put("tensor", lambda e, i=i, j=j: e.transpose(
                        pstrP[j][:, :],
                        wb4[:, j * 2048 + i * 128: j * 2048 + (i + 1) * 128], idbf[:]),
                        waits=[(cdv, cast), (cac, LW.get(("pstr", j), 0))], inc=None)
                dr = p.put("tensor", lambda e: e.drain(), inc=cpe)
                ev = None
                for j in range(2):
                    ev = p.put("scalar", lambda e, j=j: e.copy(
                        stg1[:, j * 128:(j + 1) * 128], pstrP[j][:, :]),
                        waits=[(cpe, dr), (s_stg1, LW.get("stg1_free", 0))], inc=cac)
                    LW[("pstr", j)] = ev
                std = p.put("scalar", lambda e, i=i, c1row=c1row: e.dma_start(
                    c1[c1row + i * 128: c1row + (i + 1) * 128, :], stg1[:]),
                    waits=[(cac, ev)], inc=s_stg1, inck=16)
                LW["stg1_free"] = std
            LW["w4_free"] = cpe.n
        if b1part >= 2:
            p.put("gpsimd", lambda e: e.collective_compute(
                "AllGather", OP.bypass, replica_groups=[list(range(8))],
                ins=[c1.ap()], outs=[g1.ap()]),
                waits=[(s_stg1, s_stg1.n)], inc=ccc)

        for sl in range(2):
            LW[("pstr", sl)] = cac.n
        _wl = ("g", "u", "d") if b1part >= 3 else ()
        _wl = _wl[:DEQ_NW] if b1part in (3, 4, 5) else _wl
        for w in _wl:
            for jt in range(DEQ_NJT if b1part in (3, 4, 5) else 16):
                par = jt % 2
                ldw = p.put("sync", lambda e, w=w, jt=jt, par=par: e.dma_start(
                    wpt[:, par * 1024:(par + 1) * 1024],
                    wp_in[w][jt * 128:(jt + 1) * 128, :]),
                    waits=[(cdv, LW.get(("wpt", par), 0))], inc=s_wp[par], inck=16)
                lds = p.put("sync", lambda e, w=w, jt=jt, par=par: e.dma_start(
                    sct[:, par * 64:(par + 1) * 64],
                    sc_in[w][jt * 128:(jt + 1) * 128, :]),
                    waits=[(cdv, LW.get(("sct", par), 0))], inc=s_sc[par], inck=16)
                lo = p.put("vector", lambda e, par=par: e.tensor_scalar(
                    loi[:], wpt[:, par * 1024:(par + 1) * 1024], 15, None, OP.bitwise_and),
                    waits=[(s_wp[par], ldw), (cdv, LW.get("loi_free", 0))], inc=cdv)
                hi = p.put("vector", lambda e, par=par: e.tensor_scalar(
                    hii[:], wpt[:, par * 1024:(par + 1) * 1024], 4, None,
                    OP.logical_shift_right), inc=cdv)
                LW[("wpt", par)] = hi
                p.put("vector", lambda e: e.tensor_copy(pf[:, 0:1024], loi[:]), inc=cdv)
                cast2 = p.put("vector", lambda e: e.tensor_copy(pf[:, 1024:2048], hii[:]),
                              inc=cdv)
                LW["loi_free"] = cast2
                last_scale = 0
                for pi in range(2):
                    g8 = p.put("vector", lambda e, pi=pi: e.tensor_scalar(
                        ge8[:], pf[:, pi * 1024:(pi + 1) * 1024], 8.0, None, OP.is_ge),
                        waits=[(cpl, LW.get("ge8_free", 0))], inc=cdv)
                    cc1 = p.put("vector", lambda e, pi=pi: e.scalar_tensor_tensor(
                        cc_[:], ge8[:], -8.0, pf[:, pi * 1024:(pi + 1) * 1024],
                        OP.mult, OP.add),
                        waits=[(cpl, LW.get("cc_free", 0)), (cac, LW.get("cc_free_a", 0))],
                        inc=cdv)
                    ra = p.put("scalar", lambda e: e.activation(
                        r1_[:], cc_[:], AF.Relu, bias=csts[:, 0:1]),
                        waits=[(cdv, cc1), (cpl, LW.get("r1_free", 0))], inc=cac)
                    rb = p.put("scalar", lambda e: e.activation(
                        r2_[:], cc_[:], AF.Relu, bias=csts[:, 1:2]),
                        waits=[(cdv, LW.get("r2_free", 0))], inc=cac)
                    tt1 = p.put("gpsimd", lambda e: e.tensor_add(t1_[:], cc_[:], r1_[:]),
                                waits=[(cac, ra), (cdv, LW.get("t1_free", 0))], inc=cpl)
                    LW["cc_free"] = tt1
                    LW["cc_free_a"] = rb
                    LW["r1_free"] = tt1
                    mmv = p.put("vector", lambda e: e.scalar_tensor_tensor(
                        mm_[:], t1_[:], 0.5, r2_[:], OP.mult, OP.add),
                        waits=[(cpl, tt1), (cac, rb), (cpl, LW.get("mm_free", 0))], inc=cdv)
                    LW["t1_free"] = mmv
                    LW["r2_free"] = mmv
                    sgv = p.put("gpsimd", lambda e: e.tensor_scalar(
                        sg_[:], ge8[:], -2.0, 1.0, OP.mult, OP.add),
                        waits=[(cpl, LW.get("sg_free", 0))], inc=cpl)
                    LW["ge8_free"] = sgv
                    vvv = p.put("gpsimd", lambda e: e.tensor_mul(vv_[:], mm_[:], sg_[:]),
                                waits=[(cdv, mmv), (cdv, LW.get("vv_free", 0))], inc=cpl)
                    LW["mm_free"] = vvv
                    LW["sg_free"] = vvv
                    last_scale = p.put("vector", lambda e, pi=pi, par=par: e.tensor_tensor(
                        bass.AP(deq, par * 2048 + pi, [[4096, 128], [2, 1024]]),
                        vv_[:],
                        bass.AP(sct, par * 64, [[128, 128], [1, 64], [0, 16]]),
                        OP.mult),
                        waits=[(cpl, vvv), (s_sc[par], lds), (cpe, LW.get(("deq", par), 0))],
                        inc=cdv)
                    LW["vv_free"] = last_scale
                LW[("sct", par)] = last_scale
                evs = 0
                for i0 in (range(0, ND, 2) if b1part >= 4 else ()):
                    for j in range(2):
                        i = i0 + j
                        p.put("tensor", lambda e, i=i, par=par, sl=j: e.transpose(
                            pstrP[sl][:, :],
                            bass.AP(deq, par * 2048 + i * 128, [[4096, 128], [1, 128]]),
                            idbf[:]),
                            waits=[(cdv, last_scale), (cac, LW.get(("pstr", j), 0))],
                            inc=None)
                    dr = p.put("tensor", lambda e: e.drain(), inc=cpe)
                    evs = 0
                    for j in range(2):
                        i = i0 + j
                        evs = p.put("scalar", lambda e, i=i, par=par, sl=j: e.copy(
                            stg2[:, par * 2048 + i * 128: par * 2048 + (i + 1) * 128],
                            pstrP[sl][:, :]),
                            waits=[(cpe, dr), (s_sg[par], LW.get(("stg2", par), 0))], inc=cac)
                        LW[("pstr", j)] = evs
                LW[("deq", par)] = cpe.n
                if b1part < 4:
                    LW[("deq", par)] = 0
                    continue
                std = 0
                if b1part < 5:
                    LW[("stg2", par)] = 0
                    continue
                for q4 in range(4):
                    std = p.put("scalar", lambda e, w=w, jt=jt, par=par, q4=q4: e.dma_start(
                        bass.AP(cW[w], jt * 128 + q4 * 4 * 128 * 2048,
                                [[2048, 128], [128 * 2048, 4], [1, 128]]),
                        bass.AP(stg2, par * 2048 + q4 * 4 * 128,
                                [[4096, 128], [128, 4], [1, 128]])),
                        waits=[(cac, evs)], inc=s_sg[par], inck=16)
                LW[("stg2", par)] = std
            if b1part >= 6:
                p.put("gpsimd", lambda e, w=w: e.collective_compute(
                    "AllGather", OP.bypass, replica_groups=[list(range(8))],
                    ins=[cW[w].ap()], outs=[gW[w].ap()]),
                    waits=[(s_sg[0], s_sg[0].n), (s_sg[1], s_sg[1].n)], inc=ccc)
        wdt_lds = 0
        for r in range(8):
            wdt_lds = p.put("sync", lambda e, r=r: e.dma_start(
                bass.AP(wdtT, r * 256, [[ND * 2048, 128], [2048, ND], [1, 256]]),
                bass.AP(g1, (r * 4096) * 256, [[256, 128], [128 * 256, ND], [1, 256]])),
                waits=[(ccc, 1)], inc=s_wdt, inck=16)
        WDT_READY = (s_wdt, wdt_lds)
        p.finish_dma()
        p.emit(nc)
    bstack.close()
    AG1_DONE, AGG_DONE, AGU_DONE, AGD_DONE, AGS_DONE = 1, 2, 3, 4, 6

    if upto < 2:
        return _finish_debug(nc, dbg_outs, stack, qsp, qact, ccc)
    p = Plan()
    LW = {}
    s_xt = [dsem(p, "s_xt0"), dsem(p, "s_xt1")]
    s_xnsp = dsem(p, "s_xnsp")
    s_dtv = [dsem(p, "s_dtv0"), dsem(p, "s_dtv1")]
    s_bcv = dsem(p, "s_bcv")
    bstack = ExitStack()
    _a = lambda nm, shape, dt: bstack.enter_context(nc.sbuf_tensor(nm, shape, dt))
    xtok = _a("xtok", [128, 2 * 2048], F32)
    sqb = _a("sqb", [128, 2048], F32)
    lnst = _a("lnst", [128, 8], F32)
    xnt = _a("xnt", [128, 2 * 2048], BF16)
    xnc = _a("xnc", [128, ND * 512], BF16)
    spb = _a("spb", [128, 4 * 512], F32)
    dtv = _a("dtv", [128, 2 * 512], F32)
    bcv = _a("bcv", [32, 512], BF16)
    ps_trA = bstack.enter_context(nc.psum_tensor("ps_trA", [128, 128], BF16))
    ps_trB = bstack.enter_context(nc.psum_tensor("ps_trB", [128, 128], BF16))
    ps_trP = [ps_trA, ps_trB]
    ps_dt = bstack.enter_context(nc.psum_tensor("ps_dt", [128, 4 * 512], F32))
    ps_xp = bstack.enter_context(nc.psum_tensor("ps_xp", [128, 512], F32))
    if True:
        s_wdtS, wdt_ready = WDT_READY

        for c in range(NCH):
            xn_evs = 0
            for tau in range(4):
                par = tau % 2
                xl = p.put("sync", lambda e, c=c, tau=tau, par=par: e.dma_start(
                    xtok[:, par * 2048:(par + 1) * 2048],
                    x_in[c * 512 + tau * 128: c * 512 + (tau + 1) * 128, :]),
                    waits=[(cdv, LW.get(("xtok", par), 0))], inc=s_xt[par], inck=16)
                sq = p.put("scalar", lambda e, par=par: e.activation(
                    sqb[:], xtok[:, par * 2048:(par + 1) * 2048], AF.Square),
                    waits=[(s_xt[par], xl), (cdv, LW.get("sqb_free", 0))], inc=cac)
                s0v = p.put("vector", lambda e, par=par: e.reduce_sum(
                    lnst[:, 0:1], xtok[:, par * 2048:(par + 1) * 2048], AX.X),
                    waits=[(s_xt[par], xl)], inc=cdv)
                s1 = p.put("vector", lambda e: e.reduce_sum(lnst[:, 1:2], sqb[:], AX.X),
                           waits=[(cac, sq)], inc=cdv)
                LW["sqb_free"] = s1
                mean = p.put("vector", lambda e: e.tensor_scalar_mul(
                    lnst[:, 2:3], lnst[:, 0:1], 1.0 / 2048),
                    waits=[(cdv, s0v)], inc=cdv)
                tch1 = p.put("vector", lambda e: e.tensor_copy(lnst[:, 7:8], lnst[:, 2:3]),
                             waits=[(cdv, mean)], inc=cdv)
                m2 = p.put("scalar", lambda e: e.activation(
                    lnst[:, 3:4], lnst[:, 2:3], AF.Square), waits=[(cdv, tch1)], inc=cac)
                t2 = p.put("scalar", lambda e: e.copy(lnst[:, 7:8], lnst[:, 3:4]),
                           waits=[(cac, m2)], inc=cac)
                vr = p.put("vector", lambda e: e.scalar_tensor_tensor(
                    lnst[:, 4:5], lnst[:, 1:2], 1.0 / 2048, lnst[:, 3:4],
                    OP.mult, OP.subtract), waits=[(cac, t2), (cdv, s1)], inc=cdv)
                tch3 = p.put("vector", lambda e: e.tensor_copy(lnst[:, 7:8], lnst[:, 4:5]),
                             waits=[(cdv, vr)], inc=cdv)
                lnv = p.put("scalar", lambda e: e.activation(
                    lnst[:, 5:6], lnst[:, 4:5], AF.Ln, bias=csts[:, 2:3]),
                    waits=[(cdv, tch3)], inc=cac)
                lt = p.put("scalar", lambda e: e.copy(lnst[:, 7:8], lnst[:, 5:6]),
                           waits=[(cac, lnv)], inc=cac)
                rs = p.put("scalar", lambda e: e.activation(
                    lnst[:, 6:7], lnst[:, 5:6], AF.Exp, scale=-0.5),
                    waits=[(cac, lt)], inc=cac)
                t4 = p.put("scalar", lambda e: e.copy(lnst[:, 7:8], lnst[:, 6:7]),
                           waits=[(cac, rs)], inc=cac)
                xn1 = p.put("vector", lambda e, par=par: e.tensor_scalar(
                    xnt[:, par * 2048:(par + 1) * 2048],
                    xtok[:, par * 2048:(par + 1) * 2048],
                    lnst[:, 2:3], lnst[:, 6:7], OP.subtract, OP.mult),
                    waits=[(cac, t4)], inc=cdv)
                LW[("xtok", par)] = xn1
                for i0 in range(0, ND, 2):
                    for j in range(2):
                        i = i0 + j
                        p.put("tensor", lambda e, i=i, sl=j, par=par: e.transpose(
                            ps_trP[sl][:, :],
                            xnt[:, par * 2048 + i * 128: par * 2048 + (i + 1) * 128],
                            idbf[:]),
                            waits=[(cdv, xn1), (cac, LW.get(("ps_tr", j), 0)),
                                   (s_xnsp, LW.get("xnc_freeq", 0))], inc=None)
                    dr = p.put("tensor", lambda e: e.drain(), inc=cpe)
                    for j in range(2):
                        i = i0 + j
                        xn_evs = p.put("scalar", lambda e, i=i, sl=j, tau=tau: e.copy(
                            xnc[:, i * 512 + tau * 128: i * 512 + (tau + 1) * 128],
                            ps_trP[sl][:, :]),
                            waits=[(cpe, dr)], inc=cac)
                        LW[("ps_tr", j)] = xn_evs
            xn_sp = 0
            for part in range(4):
                xn_sp = p.put("scalar", lambda e, c=c, part=part: e.dma_start(
                    bass.AP(xn_s, (part * 4 * 128) * TOK + c * 512,
                            [[TOK, 128], [128 * TOK, 4], [1, 512]]),
                    bass.AP(xnc, part * 4 * 512, [[ND * 512, 128], [512, 4], [1, 512]])),
                    waits=[(cac, xn_evs)], inc=s_xnsp, inck=16)
            LW["xnc_freeq"] = xn_sp

            last_mm = 0
            for k in range(ND):
                wl = [(cac, xn_evs)]
                if k == 0:
                    wl.append((cac, LW.get("ps_xp", 0)))
                p.put("tensor", lambda e, k=k: e.matmul(
                    ps_xp[:], wxpT[:, k * 128:(k + 1) * 128], xnc[:, k * 512:(k + 1) * 512],
                    start=(k == 0), stop=(k == ND - 1)), waits=wl, inc=None)
            last_mm = p.put("tensor", lambda e: e.drain(), inc=cpe)
            LW["xnc_freep"] = last_mm
            bce = p.put("scalar", lambda e: e.activation(
                bcv[:, :], ps_xp[0:32, :], AF.Identity, bias=xpb_sb[:, :]),
                waits=[(cpe, last_mm), (s_bcv, LW.get("bcv_free", 0))], inc=cac)
            LW["ps_xp"] = bce
            bcd = p.put("scalar", lambda e, c=c: e.dma_start(
                bass.AP(bc_s, c * 512, [[TOK, 32], [1, 512]]), bcv[:, :]),
                waits=[(cac, bce)], inc=s_bcv, inck=16)
            LW["bcv_free"] = bcd
            for et in range(ND):
                bank = et % 4
                last_mm = 0
                for k in range(ND):
                    wl = [(s_wdtS, wdt_ready), (cac, xn_evs)]
                    if k == 0:
                        wl.append((cdv, LW.get(("ps_dt", bank), 0)))
                        wl.append((cac, LW.get(("ps_dt_a", bank), 0)))
                    p.put("tensor", lambda e, et=et, k=k, bank=bank: e.matmul(
                        ps_dt[:, bank * 512:(bank + 1) * 512],
                        wdtT[:, k * 2048 + et * 128: k * 2048 + (et + 1) * 128],
                        xnc[:, k * 512:(k + 1) * 512],
                        start=(k == 0), stop=(k == ND - 1)),
                        waits=wl, inc=None)
                last_mm = p.put("tensor", lambda e: e.drain(), inc=cpe)
                sp = bank % 2
                a1 = p.put("scalar", lambda e, et=et, bank=bank, sp=sp: e.activation(
                    spb[:, sp * 1024:sp * 1024 + 512],
                    ps_dt[:, bank * 512:(bank + 1) * 512], AF.Abs,
                    bias=dtb_sb[:, et:et + 1]),
                    waits=[(cpe, last_mm), (cdv, LW.get(("spb", sp), 0))], inc=cac)
                p.put("scalar", lambda e, sp=sp: e.activation(
                    spb[:, sp * 1024 + 512:sp * 1024 + 1024],
                    spb[:, sp * 1024:sp * 1024 + 512], AF.Exp, scale=-1.0), inc=cac)
                p.put("scalar", lambda e, sp=sp: e.activation(
                    spb[:, sp * 1024:sp * 1024 + 512],
                    spb[:, sp * 1024 + 512:sp * 1024 + 1024], AF.Ln, bias=csts[:, 3:4]), inc=cac)
                r1v = p.put("scalar", lambda e, et=et, bank=bank, sp=sp: e.activation(
                    spb[:, sp * 1024 + 512:sp * 1024 + 1024],
                    ps_dt[:, bank * 512:(bank + 1) * 512], AF.Relu,
                    bias=dtb_sb[:, et:et + 1]), inc=cac)
                LW[("ps_dt_a", bank)] = r1v
                dtc = p.put("vector", lambda e, et=et, sp=sp: e.tensor_add(
                    dtv[:, (et % 2) * 512:(et % 2) * 512 + 512],
                    spb[:, sp * 1024:sp * 1024 + 512],
                    spb[:, sp * 1024 + 512:sp * 1024 + 1024]),
                    waits=[(cac, r1v), (s_dtv[et % 2], LW.get(("dtv", et % 2), 0))], inc=cdv)
                LW[("ps_dt", bank)] = dtc
                LW[("spb", sp)] = dtc
                dsp = p.put("scalar", lambda e, c=c, et=et: e.dma_start(
                    bass.AP(dt_s, (et * 128) * TOK + c * 512, [[TOK, 128], [1, 512]]),
                    dtv[:, (et % 2) * 512:(et % 2) * 512 + 512]),
                    waits=[(cdv, dtc)], inc=s_dtv[et % 2], inck=16)
                LW[("dtv", et % 2)] = dsp
        p.emit(nc)
    bstack.close()
    wstack.close()
    QACT_B2 = qact.n

    if upto < 3:
        return _finish_debug(nc, dbg_outs, stack, qsp, qact, ccc)
    p = Plan()
    LW = {}
    s_xnh = dsem(p, "s_xnh")
    s_gs = [dsem(p, "s_gs0"), dsem(p, "s_gs1")]
    s_us = [dsem(p, "s_us0"), dsem(p, "s_us1")]
    s_dtc = dsem(p, "s_dtc"); s_xn2 = dsem(p, "s_xn2")
    s_bcb = dsem(p, "s_bcb"); s_ccb = dsem(p, "s_ccb")
    s_sb = [dsem(p, "s_sb0"), dsem(p, "s_sb1")]
    s_yb = [dsem(p, "s_yb0"), dsem(p, "s_yb1")]
    s_insp = [dsem(p, "s_insp0"), dsem(p, "s_insp1")]
    s_stc = dsem(p, "s_stc")
    bstack = ExitStack()
    _a = lambda nm, shape, dt: bstack.enter_context(nc.sbuf_tensor(nm, shape, dt))
    xnh = _a("xnh", [128, ND * 1024], BF16)
    gsl = _a("gsl", [128, 2 * ND * 256], BF16)
    usl = _a("usl", [128, 2 * ND * 256], BF16)
    qb = _a("qb", [128, 512], F32)
    pb = _a("pb", [128, 512], F32)
    tb = _a("tb", [128, 512], F32)
    rb = _a("rb", [128, 512], F32)
    inters = _a("inters", [128, 2 * 1024], BF16)
    dtc_s = _a("dtc_s", [128, ND * 512], F32)
    xncs = _a("xncs", [128, ND * 512], BF16)
    dtxb = _a("dtxb", [128, 2 * 512], F32)
    Sb = _a("Sb", [128, 2 * 512], F32)
    dab = _a("dab", [128, 2 * 512], F32)
    dbxb = _a("dbxb", [128, 2 * 512], F32)
    strip = _a("strip", [128, 4 * 512], F32)
    Bcb = _a("Bcb", [128, DS * 512], BF16)
    Ccb = _a("Ccb", [128, DS * 512], BF16)
    ytmp = _a("ytmp", [128, 2 * 512], F32)
    ybuf = _a("ybuf", [128, 2 * 512], F32)
    ps8 = bstack.enter_context(nc.psum_tensor("ps8", [128, 8 * 512], F32))
    if True:
        def emit_xnh_load(qt):
            last = 0
            for i in range(ND):
                last = p.put("sync", lambda e, qt=qt, i=i: e.dma_start(
                    xnh[:, i * 1024:(i + 1) * 1024],
                    bass.AP(xn_s, (i * 128) * TOK + qt * 1024, [[TOK, 128], [1, 1024]])),
                    waits=[(cpe, LW.get("xnh_free", 0))],
                    inc=s_xnh, inck=16)
            return last

        mt_count = [0]

        def emit_ffn_unit(qt, u, xnh_ready):
            r, q8 = u >> 3, u & 7
            par = u % 2
            g_ld = p.put("sync", lambda e, r=r, q8=q8, par=par: e.dma_start(
                bass.AP(gsl, par * ND * 256, [[2 * ND * 256, 128], [256, ND], [1, 256]]),
                bass.AP(gW["g"], (r * 2048) * 2048 + q8 * 256,
                        [[2048, 128], [128 * 2048, ND], [1, 256]])),
                waits=[(ccc, AGG_DONE), (cpe, LW.get(("gsl", par), 0))],
                inc=s_gs[par], inck=16)
            u_ld = p.put("sync", lambda e, r=r, q8=q8, par=par: e.dma_start(
                bass.AP(usl, par * ND * 256, [[2 * ND * 256, 128], [256, ND], [1, 256]]),
                bass.AP(gW["u"], (r * 2048) * 2048 + q8 * 256,
                        [[2048, 128], [128 * 2048, ND], [1, 256]])),
                waits=[(ccc, AGU_DONE), (cpe, LW.get(("usl", par), 0))],
                inc=s_us[par], inck=16)
            for s in range(2):
                m128 = r * 16 + q8 * 2 + s
                mpar = mt_count[0] % 2
                mt_count[0] += 1
                gbank, ubank = mpar * 4, mpar * 4 + 2
                last_g = last_u = 0
                for k in range(ND):
                    for tch in range(2):
                        wl = [(s_gs[par], g_ld), (s_xnh, xnh_ready)]
                        if k == 0 and tch == 0:
                            wl.append((cdv, LW.get(("geluA", mpar), 0)))
                        last_g = p.put("tensor", lambda e, k=k, tch=tch, s=s, par=par,
                                       gbank=gbank: e.matmul(
                            ps8[:, (gbank + tch) * 512:(gbank + tch + 1) * 512],
                            gsl[:, par * ND * 256 + k * 256 + s * 128:
                                par * ND * 256 + k * 256 + s * 128 + 128],
                            xnh[:, k * 1024 + tch * 512: k * 1024 + (tch + 1) * 512],
                            start=(k == 0), stop=(k == ND - 1)),
                            waits=wl, inc=None)
                for k in range(ND):
                    for tch in range(2):
                        wl = [(s_us[par], u_ld)]
                        if k == 0 and tch == 0:
                            wl.append((cdv, LW.get(("geluB", mpar), 0)))
                        p.put("tensor", lambda e, k=k, tch=tch, s=s, par=par,
                              ubank=ubank: e.matmul(
                            ps8[:, (ubank + tch) * 512:(ubank + tch + 1) * 512],
                            usl[:, par * ND * 256 + k * 256 + s * 128:
                                par * ND * 256 + k * 256 + s * 128 + 128],
                            xnh[:, k * 1024 + tch * 512: k * 1024 + (tch + 1) * 512],
                            start=(k == 0), stop=(k == ND - 1)),
                            waits=wl, inc=None)
                last_u = p.put("tensor", lambda e: e.drain(), inc=cpe)
                for tch in range(2):
                    qv = p.put("scalar", lambda e, gbank=gbank, tch=tch: e.activation(
                        qb[:], ps8[:, (gbank + tch) * 512:(gbank + tch + 1) * 512],
                        AF.Square, scale=GELU_A2),
                        waits=[(cpe, last_u), (cdv, LW.get("qb_free", 0))], inc=cac)
                    pv = p.put("vector", lambda e, gbank=gbank, tch=tch: e.scalar_tensor_tensor(
                        pb[:], qb[:], 1.0,
                        ps8[:, (gbank + tch) * 512:(gbank + tch + 1) * 512],
                        OP.add, OP.mult),
                        waits=[(cac, qv), (cac, LW.get("pb_free", 0))], inc=cdv)
                    LW["qb_free"] = pv
                    tv = p.put("scalar", lambda e: e.activation(
                        tb[:], pb[:], AF.Tanh, scale=GELU_C),
                        waits=[(cdv, pv), (cdv, LW.get("tb_free", 0))], inc=cac)
                    LW["pb_free"] = tv
                    rv = p.put("vector", lambda e, gbank=gbank, tch=tch: e.scalar_tensor_tensor(
                        rb[:], tb[:], 1.0,
                        ps8[:, (gbank + tch) * 512:(gbank + tch + 1) * 512],
                        OP.add, OP.mult),
                        waits=[(cac, tv), (cdv, LW.get("rb_free", 0))], inc=cdv)
                    LW["tb_free"] = rv
                    iv = p.put("vector", lambda e, ubank=ubank, tch=tch, s=s: e.scalar_tensor_tensor(
                        inters[:, (s % 2) * 1024 + tch * 512:(s % 2) * 1024 + (tch + 1) * 512],
                        rb[:], 0.5,
                        ps8[:, (ubank + tch) * 512:(ubank + tch + 1) * 512],
                        OP.mult, OP.mult),
                        waits=[(s_insp[s % 2], LW.get(("inters", s % 2), 0))], inc=cdv)
                    LW["rb_free"] = iv
                LW[("geluA", mpar)] = cdv.n
                LW[("geluB", mpar)] = cdv.n
                isp = p.put("scalar", lambda e, m128=m128, qt=qt, s=s: e.dma_start(
                    bass.AP(inter_s, (m128 * 128) * TOK + qt * 1024, [[TOK, 128], [1, 1024]]),
                    inters[:, (s % 2) * 1024:(s % 2) * 1024 + 1024]),
                    waits=[(cdv, cdv.n)], inc=s_insp[s % 2], inck=16)
                LW[("inters", s % 2)] = isp
            LW[("gsl", par)] = cpe.n
            LW[("usl", par)] = cpe.n

        def emit_scan_chunk(c):
            for part in range(4):
                p.put("scalar", lambda e, c=c, part=part: e.dma_start(
                    bass.AP(dtc_s, part * 4 * 512, [[ND * 512, 128], [512, 4], [1, 512]]),
                    bass.AP(dt_s, (part * 4 * 128) * TOK + c * 512,
                            [[TOK, 128], [128 * TOK, 4], [1, 512]])),
                    waits=[(cdv, LW.get("dtc_free", 0)),
                           (cac, LW.get("dtc_free_a", 0))], inc=s_dtc, inck=16)
                p.put("scalar", lambda e, c=c, part=part: e.dma_start(
                    bass.AP(xncs, part * 4 * 512, [[ND * 512, 128], [512, 4], [1, 512]]),
                    bass.AP(xn_s, (part * 4 * 128) * TOK + c * 512,
                            [[TOK, 128], [128 * TOK, 4], [1, 512]])),
                    waits=[(cdv, LW.get("dtc_free", 0))], inc=s_xn2, inck=16)
            ld_done = (s_dtc.n, s_xn2.n)
            for n in range(DS):
                p.put("sync", lambda e, c=c, n=n: e.dma_start(
                    Bcb[:, n * 512:(n + 1) * 512],
                    bass.AP(bc_s, n * TOK + c * 512, [[0, 128], [1, 512]])),
                    waits=[(cpl, LW.get("Bcb_free", 0))], inc=s_bcb, inck=16)
                p.put("sync", lambda e, c=c, n=n: e.dma_start(
                    Ccb[:, n * 512:(n + 1) * 512],
                    bass.AP(bc_s, (16 + n) * TOK + c * 512, [[0, 128], [1, 512]])),
                    waits=[(cdv, LW.get("Ccb_free", 0))], inc=s_ccb, inck=16)
            bc_done = (s_bcb.n, s_ccb.n)
            for i in range(ND):
                ip = i % 2
                dtx = p.put("vector", lambda e, i=i, ip=ip: e.tensor_mul(
                    dtxb[:, ip * 512:ip * 512 + 512],
                    dtc_s[:, i * 512:(i + 1) * 512], xncs[:, i * 512:(i + 1) * 512]),
                    waits=[(s_dtc, ld_done[0]), (s_xn2, ld_done[1]),
                           (cpl, LW.get(("dtxb", ip), 0))], inc=cdv)
                ssc = p.put("vector", lambda e, i=i, ip=ip: e.tensor_tensor_scan(
                    Sb[:, ip * 512:ip * 512 + 512],
                    dtc_s[:, i * 512:(i + 1) * 512], zeros[:, 0:512],
                    sstate[:, i:i + 1], OP.add, OP.add),
                    waits=[(s_sb[ip], LW.get(("Sb", ip), 0))], inc=cdv)
                p.put("vector", lambda e, i=i, ip=ip: e.tensor_copy(
                    sstate[:, i:i + 1], Sb[:, ip * 512 + 511:ip * 512 + 512]),
                    waits=[(cdv, ssc)], inc=cdv)
                ssp = p.put("scalar", lambda e, c=c, i=i, ip=ip: e.dma_start(
                    bass.AP(ssum_s, (i * 128) * TOK + c * 512, [[TOK, 128], [1, 512]]),
                    Sb[:, ip * 512:ip * 512 + 512]),
                    waits=[(cdv, ssc)], inc=s_sb[ip], inck=16)
                LW[("Sb", ip)] = ssp
                for g in range(4):
                    for nn in range(4):
                        n = g * 4 + nn
                        ab = n % 2
                        da = p.put("scalar", lambda e, i=i, n=n, ab=ab: e.activation(
                            dab[:, ab * 512:(ab + 1) * 512],
                            dtc_s[:, i * 512:(i + 1) * 512], AF.Exp,
                            scale=aneg[:, i * DS + n:i * DS + n + 1]),
                            waits=[(s_dtc, ld_done[0]), (cdv, LW.get(("dab", ab), 0))],
                            inc=cac)
                        dbx = p.put("gpsimd", lambda e, i=i, n=n, ab=ab, ip=ip: e.tensor_mul(
                            dbxb[:, ab * 512:(ab + 1) * 512],
                            dtxb[:, ip * 512:ip * 512 + 512],
                            Bcb[:, n * 512:(n + 1) * 512]),
                            waits=[(cdv, dtx), (s_bcb, bc_done[0]),
                                   (cdv, LW.get(("dbxb", ab), 0))], inc=cpl)
                        sc = p.put("vector", lambda e, i=i, n=n, nn=nn, ab=ab: e.tensor_tensor_scan(
                            strip[:, nn * 512:(nn + 1) * 512],
                            dab[:, ab * 512:(ab + 1) * 512],
                            dbxb[:, ab * 512:(ab + 1) * 512],
                            state[:, i * DS + n:i * DS + n + 1], OP.mult, OP.add),
                            waits=[(cac, da), (cpl, dbx)], inc=cdv)
                        LW[("dab", ab)] = sc
                        LW[("dbxb", ab)] = sc
                        last_sc = sc
                    p.put("vector", lambda e, i=i, g=g: e.tensor_copy(
                        state[:, i * DS + g * 4: i * DS + g * 4 + 4],
                        bass.AP(strip, 511, [[2048, 128], [512, 4]])),
                        waits=[(cdv, last_sc)], inc=cdv)
                    for nn in range(4):
                        n = g * 4 + nn
                        yp = n % 2
                        yt = p.put("vector", lambda e, n=n, nn=nn, yp=yp: e.tensor_mul(
                            ytmp[:, yp * 512:(yp + 1) * 512],
                            strip[:, nn * 512:(nn + 1) * 512],
                            Ccb[:, n * 512:(n + 1) * 512]),
                            waits=[(s_ccb, bc_done[1]),
                                   (cpl, LW.get(("ytmp", yp), 0))], inc=cdv)
                        if n == 0:
                            ya = p.put("gpsimd", lambda e, ip=ip: e.tensor_copy(
                                ybuf[:, ip * 512:ip * 512 + 512], ytmp[:, 0:512]),
                                waits=[(cdv, yt), (s_yb[ip], LW.get(("ybuf", ip), 0))],
                                inc=cpl)
                        else:
                            ya = p.put("gpsimd", lambda e, n=n, ip=ip, yp=yp: e.tensor_add(
                                ybuf[:, ip * 512:ip * 512 + 512],
                                ybuf[:, ip * 512:ip * 512 + 512],
                                ytmp[:, yp * 512:(yp + 1) * 512]),
                                waits=[(cdv, yt)], inc=cpl)
                        LW[("ytmp", yp)] = ya
                ysp = p.put("scalar", lambda e, c=c, i=i, ip=ip: e.dma_start(
                    bass.AP(y_s, (i * 128) * TOK + c * 512, [[TOK, 128], [1, 512]]),
                    ybuf[:, ip * 512:ip * 512 + 512]),
                    waits=[(cpl, cpl.n)], inc=s_yb[ip], inck=16)
                LW[("ybuf", ip)] = ysp
                LW[("dtxb", ip)] = cpl.n
            LW["dtc_free"] = cdv.n
            LW["dtc_free_a"] = cac.n
            LW["Bcb_free"] = cpl.n
            LW["Ccb_free"] = cdv.n

        sc_next = 0
        if B3A_MODE != 0:
            for qt in range(4):
                xnh_rdy = emit_xnh_load(qt)
                for u in range(64):
                    emit_ffn_unit(qt, u, xnh_rdy)
                    if B3A_MODE == 2 and (u == 31 or u == 63):
                        if sc_next < NCH:
                            emit_scan_chunk(sc_next)
                            sc_next += 1
                LW["xnh_free"] = cpe.n
        while sc_next < NCH:
            emit_scan_chunk(sc_next)
            sc_next += 1
        stp = p.put("scalar", lambda e: e.dma_start(st_c.ap(), state[:]),
                    waits=[(cdv, cdv.n), (cpl, cpl.n)], inc=s_stc, inck=16)
        p.put("gpsimd", lambda e: e.collective_compute(
            "AllGather", OP.bypass, replica_groups=[list(range(8))],
            ins=[st_c.ap()], outs=[st_g.ap()]),
            waits=[(s_stc, stp)], inc=ccc)
        p.put("gpsimd", lambda e: e.collective_compute(
            "AllGather", OP.bypass, replica_groups=[list(range(8))],
            ins=[st_c.ap()], outs=[st_g.ap()]),
            inc=ccc)
        p.finish_dma()
        p.emit(nc)
    bstack.close()
    QACT_B3A = qact.n

    if upto < 4:
        return _finish_debug(nc, dbg_outs, stack, qsp, qact, ccc)
    p = Plan()
    LW = {}
    s_hin = dsem(p, "s_hin"); s_ybl = dsem(p, "s_ybl"); s_sc2 = dsem(p, "s_sc2")
    s_cc2 = dsem(p, "s_cc2"); s_ybw = dsem(p, "s_ybw"); s_int = dsem(p, "s_int")
    s_rb = [dsem(p, f"s_rb{i}") for i in range(4)]
    s_gw = dsem(p, "s_gw")
    bstack = ExitStack()
    _a = lambda nm, shape, dt: bstack.enter_context(nc.sbuf_tensor(nm, shape, dt))
    interc = _a("interc", [128, 128 * 512], BF16)
    rbk = _a("rbk", [128, 4 * 1024], BF16)
    tnb = _a("tnb", [128, 512], F32)
    gvb = _a("gvb", [128, 512], F32)
    ybig = _a("ybig", [128, ND * 512], F32)
    Sc2 = _a("Sc2", [128, 512], F32)
    Eb = _a("Eb", [128, 512], F32)
    ctmp = _a("ctmp", [128, 2 * 512], F32)
    Ccb2 = _a("Ccb2", [128, DS * 512], BF16)
    psD = bstack.enter_context(nc.psum_tensor("psD", [128, 8 * 512], F32))
    if True:
        hin_ready = 0
        for r in range(8):
            hl = p.put("sync", lambda e, r=r: e.dma_start(
                hin_st[:, 0:256],
                st_g[r * 128:(r + 1) * 128, :]),
                waits=[(ccc, AGS_DONE), (cdv, hin_ready)], inc=s_hin, inck=16)
            hin_ready = p.put("vector", lambda e, r=r: e.scalar_tensor_tensor(
                hin[:], hin_st[:, 0:256], sel8_sb[:, r:r + 1], hin[:], OP.mult, OP.add),
                waits=[(s_hin, hl)], inc=cdv)

        def emit_corr_chunk(c):
            y_ld = 0
            for part in range(4):
                y_ld = p.put("scalar", lambda e, c=c, part=part: e.dma_start(
                    bass.AP(ybig, part * 4 * 512, [[ND * 512, 128], [512, 4], [1, 512]]),
                    bass.AP(y_s, (part * 4 * 128) * TOK + c * 512,
                            [[TOK, 128], [128 * TOK, 4], [1, 512]])),
                    waits=[(cpl, LW.get("ybig_free", 0)),
                           (s_ybw, LW.get("ybw_done", 0))],
                    inc=s_ybl, inck=16)
            cld = 0
            for n in range(DS):
                cld = p.put("sync", lambda e, c=c, n=n: e.dma_start(
                    Ccb2[:, n * 512:(n + 1) * 512],
                    bass.AP(bc_s, (16 + n) * TOK + c * 512, [[0, 128], [1, 512]])),
                    waits=[(cdv, LW.get("Ccb2_free", 0))], inc=s_cc2, inck=16)
            for i in range(ND):
                ip = i % 2
                sld = p.put("scalar", lambda e, c=c, i=i: e.dma_start(
                    Sc2[:, 0:512],
                    bass.AP(ssum_s, (i * 128) * TOK + c * 512, [[TOK, 128], [1, 512]])),
                    waits=[(cac, LW.get(("Sc2", 0), 0))],
                    inc=s_sc2, inck=16)
                for n in range(DS):
                    eb = n % 2
                    ev = p.put("scalar", lambda e, i=i, n=n: e.activation(
                        Eb[:, 0:512],
                        Sc2[:, 0:512], AF.Exp,
                        scale=aneg[:, i * DS + n:i * DS + n + 1]),
                        waits=[(s_sc2, sld), (cdv, LW.get(("Eb", 0), 0))], inc=cac)
                    tv = p.put("vector", lambda e, i=i, n=n, eb=eb: e.scalar_tensor_tensor(
                        ctmp[:, eb * 512:(eb + 1) * 512],
                        Eb[:, 0:512],
                        hin[:, i * DS + n:i * DS + n + 1],
                        Ccb2[:, n * 512:(n + 1) * 512], OP.mult, OP.mult),
                        waits=[(cac, ev), (cdv, hin_ready), (s_cc2, cld),
                               (cpl, LW.get(("ctmp", eb), 0))], inc=cdv)
                    LW[("Eb", 0)] = tv
                    ya = p.put("gpsimd", lambda e, i=i, eb=eb: e.tensor_add(
                        ybig[:, i * 512:(i + 1) * 512], ybig[:, i * 512:(i + 1) * 512],
                        ctmp[:, eb * 512:(eb + 1) * 512]),
                        waits=[(cdv, tv), (s_ybl, y_ld)], inc=cpl)
                    LW[("ctmp", eb)] = ya
                LW[("Sc2", 0)] = cac.n
            LW["Ccb2_free"] = cdv.n
            for part in range(4):
                p.put("scalar", lambda e, c=c, part=part: e.dma_start(
                    bass.AP(y_s, (part * 4 * 128) * TOK + c * 512,
                            [[TOK, 128], [128 * TOK, 4], [1, 512]]),
                    bass.AP(ybig, part * 4 * 512, [[ND * 512, 128], [512, 4], [1, 512]])),
                    waits=[(cpl, cpl.n)], inc=s_ybw, inck=16)
            LW["ybig_free"] = cpl.n
            LW["ybw_done"] = s_ybw.n

        def emit_down_tchunk(c2):
            i_ld = 0
            for part in range(16):
                i_ld = p.put("scalar", lambda e, c2=c2, part=part: e.dma_start(
                    bass.AP(interc, part * 8 * 512, [[128 * 512, 128], [512, 8], [1, 512]]),
                    bass.AP(inter_s, (part * 8 * 128) * TOK + c2 * 512,
                            [[TOK, 128], [128 * TOK, 8], [1, 512]])),
                    waits=[(cpe, LW.get("interc_free", 0))],
                    inc=s_int, inck=16)
            for hh in range(2):
                last_mm = 0
                for nt in range(128):
                    rpar = nt % 4
                    rld = p.put("sync", lambda e, nt=nt, hh=hh, rpar=rpar: e.dma_start(
                        rbk[:, rpar * 1024:(rpar + 1) * 1024],
                        bass.AP(gW["d"], (nt * 128) * 2048 + hh * 1024,
                                [[2048, 128], [1, 1024]])),
                        waits=[(ccc, AGD_DONE), (cpe, LW.get(("rbk", rpar), 0))],
                        inc=s_rb[rpar], inck=16)
                    for h8 in range(8):
                        wl = [(s_rb[rpar], rld), (s_int, i_ld)]
                        if nt == 0 and h8 == 0:
                            wl.append((cdv, LW.get("psD_free", 0)))
                        last_mm = p.put("tensor", lambda e, nt=nt, h8=h8, rpar=rpar: e.matmul(
                            psD[:, h8 * 512:(h8 + 1) * 512],
                            rbk[:, rpar * 1024 + h8 * 128: rpar * 1024 + (h8 + 1) * 128],
                            interc[:, nt * 512:(nt + 1) * 512],
                            start=(nt == 0), stop=(nt == 127)),
                            waits=wl, inc=(cpe if h8 == 7 else None))
                    LW[("rbk", rpar)] = last_mm
                for h8 in range(8):
                    tnv = p.put("scalar", lambda e, h8=h8: e.activation(
                        tnb[:, 0:512], psD[:, h8 * 512:(h8 + 1) * 512],
                        AF.Tanh, scale=0.5),
                        waits=[(cpe, last_mm), (cdv, LW.get(("tnb", 0), 0))], inc=cac)
                    gv = p.put("vector", lambda e, h8=h8: e.scalar_tensor_tensor(
                        gvb[:, 0:512], tnb[:, 0:512],
                        1.0, psD[:, h8 * 512:(h8 + 1) * 512], OP.add, OP.mult),
                        waits=[(cac, tnv), (s_gw, LW.get(("gvb", 0), 0))], inc=cdv)
                    LW[("tnb", 0)] = gv
                    gsp = p.put("scalar", lambda e, c2=c2, hh=hh, h8=h8: e.dma_start(
                        bass.AP(g_s, ((hh * 8 + h8) * 128) * TOK + c2 * 512,
                                [[TOK, 128], [1, 512]]),
                        gvb[:, 0:512]),
                        waits=[(cdv, gv)], inc=s_gw, inck=16)
                    LW[("gvb", 0)] = gsp
                LW["psD_free"] = cdv.n
            LW["interc_free"] = cpe.n

        corr_next = 0
        for c2 in range(NCH):
            if B3B_MODE != 1:
                emit_down_tchunk(c2)
            while B3B_MODE != 0 and corr_next < NCH and corr_next < (c2 + 1) * 2:
                emit_corr_chunk(corr_next)
                corr_next += 1
        p.finish_dma()
        p.emit(nc)
    bstack.close()
    QACT_B3B = qact.n

    if upto < 5:
        return _finish_debug(nc, dbg_outs, stack, qsp, qact, ccc)
    p = Plan()
    LW = {}
    s_wo = dsem(p, "s_wo")
    s_ym = [dsem(p, "s_ym0"), dsem(p, "s_ym1")]
    s_gb = dsem(p, "s_gb")
    s_x4 = [dsem(p, "s_x40"), dsem(p, "s_x41")]
    s_out = dsem(p, "s_out")
    bstack = ExitStack()
    _a = lambda nm, shape, dt: bstack.enter_context(nc.sbuf_tensor(nm, shape, dt))
    woutT = _a("woutT", [128, ND * 2048], BF16)
    ysm = _a("ysm", [128, 2 * 512], F32)
    ybf = _a("ybf", [128, ND * 512], BF16)
    gbig = _a("gbig", [128, ND * 512], F32)
    sA = _a("sA", [128, 2 * 512], F32)
    ubig = _a("ubig", [128, ND * 512], F32)
    utok = _a("utok", [128, 2 * 2048], F32)
    xt4 = _a("xt4", [128, 2 * 2048], F32)
    ob = _a("ob", [128, 2048], F32)
    psS = bstack.enter_context(nc.psum_tensor("psS", [128, 4 * 512], F32))
    psTa = bstack.enter_context(nc.psum_tensor("psTa", [128, 128], F32))
    psTb = bstack.enter_context(nc.psum_tensor("psTb", [128, 128], F32))
    psTP = [psTa, psTb]
    if True:
        wo_ready = 0
        for r in range(8):
            wo_ready = p.put("sync", lambda e, r=r: e.dma_start(
                bass.AP(woutT, r * 256, [[ND * 2048, 128], [2048, ND], [1, 256]]),
                bass.AP(g1, (2048 + r * 4096) * 256, [[256, 128], [128 * 256, ND], [1, 256]])),
                inc=s_wo, inck=16)
        for c in range(NCH):
            g_ld4 = 0
            for part in range(4):
                g_ld4 = p.put("scalar", lambda e, c=c, part=part: e.dma_start(
                    bass.AP(gbig, part * 4 * 512, [[ND * 512, 128], [512, 4], [1, 512]]),
                    bass.AP(g_s, (part * 4 * 128) * TOK + c * 512,
                            [[TOK, 128], [128 * TOK, 4], [1, 512]])),
                    waits=[(cdv, LW.get("gbig_free", 0))],
                    inc=s_gb, inck=16)
            cast_done = 0
            for i in range(ND):
                ipp = i % 2
                yld1 = p.put("scalar", lambda e, c=c, i=i, ipp=ipp: e.dma_start(
                    ysm[:, ipp * 512:(ipp + 1) * 512],
                    bass.AP(y_s, (i * 128) * TOK + c * 512, [[TOK, 128], [1, 512]])),
                    waits=[(cac, LW.get(("ysm", ipp), 0))],
                    inc=s_ym[ipp], inck=16)
                cast_done = p.put("scalar", lambda e, i=i, ipp=ipp: e.activation(
                    ybf[:, i * 512:(i + 1) * 512], ysm[:, ipp * 512:(ipp + 1) * 512],
                    AF.Copy),
                    waits=[(s_ym[ipp], yld1), (cpe, LW.get("ybf_free", 0))], inc=cac)
                LW[("ysm", ipp)] = cast_done
            for mt in range(ND):
                bank = mt % 4
                last_mm = 0
                for k in range(ND):
                    wl = [(s_wo, wo_ready), (cac, cast_done)]
                    if k == 0:
                        wl.append((cdv, LW.get(("psS", bank), 0)))
                    p.put("tensor", lambda e, mt=mt, k=k, bank=bank: e.matmul(
                        psS[:, bank * 512:(bank + 1) * 512],
                        woutT[:, k * 2048 + mt * 128: k * 2048 + (mt + 1) * 128],
                        ybf[:, k * 512:(k + 1) * 512],
                        start=(k == 0), stop=(k == ND - 1)),
                        waits=wl, inc=None)
                last_mm = p.put("tensor", lambda e: e.drain(), inc=cpe)
                sev = p.put("scalar", lambda e, mt=mt, bank=bank: e.activation(
                    sA[:, (mt % 2) * 512:(mt % 2) * 512 + 512],
                    psS[:, bank * 512:(bank + 1) * 512], AF.Identity,
                    bias=outb_sb[:, mt:mt + 1]),
                    waits=[(cpe, last_mm), (cdv, LW.get(("sA", mt % 2), 0))], inc=cac)
                uv = p.put("vector", lambda e, mt=mt: e.scalar_tensor_tensor(
                    ubig[:, mt * 512:(mt + 1) * 512],
                    sA[:, (mt % 2) * 512:(mt % 2) * 512 + 512], 0.5,
                    gbig[:, mt * 512:(mt + 1) * 512], OP.mult, OP.mult),
                    waits=[(cac, sev), (s_gb, g_ld4), (cpe, LW.get("ubig_free", 0))],
                    inc=cdv)
                LW[("sA", mt % 2)] = uv
                LW[("psS", bank)] = uv
            u_done = cdv.n
            LW["gbig_free"] = u_done
            for tau in range(4):
                tp = tau % 2
                xld = p.put("sync", lambda e, c=c, tau=tau, tp=tp: e.dma_start(
                    xt4[:, tp * 2048:(tp + 1) * 2048],
                    x_in[c * 512 + tau * 128: c * 512 + (tau + 1) * 128, :]),
                    waits=[(cdv, LW.get(("xt4", tp), 0))], inc=s_x4[tp], inck=16)
                ev_last = 0
                for i0 in range(0, ND, 2):
                    for j in range(2):
                        i = i0 + j
                        p.put("tensor", lambda e, i=i, sl=j, tau=tau: e.transpose(
                            psTP[sl][:, :],
                            bass.AP(ubig, i * 512 + tau * 128, [[ND * 512, 128], [1, 128]]),
                            idf32[:]),
                            waits=[(cdv, u_done), (cac, LW.get(("psT", j), 0))], inc=None)
                    dr = p.put("tensor", lambda e: e.drain(), inc=cpe)
                    for j in range(2):
                        i = i0 + j
                        ev_last = p.put("scalar", lambda e, i=i, sl=j, tp=tp: e.copy(
                            utok[:, tp * 2048 + i * 128: tp * 2048 + (i + 1) * 128],
                            psTP[sl][:, :]),
                            waits=[(cpe, dr), (cdv, LW.get(("utok", tp), 0))], inc=cac)
                        LW[("psT", j)] = ev_last
                oadd = p.put("vector", lambda e, tp=tp: e.tensor_add(
                    ob[:, 0:2048], xt4[:, tp * 2048:(tp + 1) * 2048],
                    utok[:, tp * 2048:(tp + 1) * 2048]),
                    waits=[(cac, ev_last), (s_x4[tp], xld), (s_out, LW.get(("ob", 0), 0))],
                    inc=cdv)
                LW[("xt4", tp)] = oadd
                LW[("utok", tp)] = oadd
                ost = p.put("scalar", lambda e, c=c, tau=tau: e.dma_start(
                    out_sh[c * 512 + tau * 128: c * 512 + (tau + 1) * 128, :],
                    ob[:, 0:2048]),
                    waits=[(cdv, oadd)], inc=s_out, inck=16)
                LW[("ob", 0)] = ost
            LW["ybf_free"] = cpe.n
            LW["ubig_free"] = cpe.n
        p.finish_dma()
        p.emit(nc)
    bstack.close()

    stack.close()
    nc.finalize()
    return nc


def _finish_debug(nc, dbg_outs, stack, qsp, qact, ccc=None):
    p = Plan()
    if ccc is not None and ccc.n:
        p.put("sync", lambda e: e.wait_ge(ccc.h, ccc.n) and None)
    n = 0
    for od, ih in dbg_outs:
        rows = ih.shape[0]
        step = max(1, min(rows, (1 << 20) // max(1, int(ih.shape[1]))))
        r = 0
        while r < rows:
            rr = min(step, rows - r)
            n = p.put("sync", lambda e, od=od, ih=ih, r=r, rr=rr: e.dma_start(
                od[r:r + rr, :], ih[r:r + rr, :]),
                waits=[(qsp, qsp.n), (qact, qact.n)], inc=qsp, inck=16)
            r += rr
    p.put("sync", lambda e: e.wait_ge(qsp.h, qsp.n) and None)
    p.put("scalar", lambda e: e.wait_ge(qact.h, qact.n) and None)
    p.emit(nc)
    stack.close()
    nc.finalize()
    return nc


_CACHE = {}


def kernel(L0_attn_x, L0_attn_A_log, L0_attn_x_proj_w, L0_attn_x_proj_b,
           L0_attn_dt_proj_w, L0_attn_dt_proj_b, L0_attn_out_proj_w,
           L0_attn_out_proj_b, L0_ffn_gate_W_packed, L0_ffn_gate_scales,
           L0_ffn_up_W_packed, L0_ffn_up_scales, L0_ffn_down_W_packed,
           L0_ffn_down_scales):
    x = np.ascontiguousarray(np.asarray(L0_attn_x, dtype=np.float32))
    if "nc" not in _CACHE:
        _CACHE["nc"] = build_program()
    nc = _CACHE["nc"]

    gwp = np.asarray(L0_ffn_gate_W_packed, dtype=np.int32)
    gsc = np.asarray(L0_ffn_gate_scales, dtype=np.float32)
    uwp = np.asarray(L0_ffn_up_W_packed, dtype=np.int32)
    usc = np.asarray(L0_ffn_up_scales, dtype=np.float32)
    dwp = np.asarray(L0_ffn_down_W_packed, dtype=np.int32)
    dsc = np.asarray(L0_ffn_down_scales, dtype=np.float32)
    wdt = np.asarray(L0_attn_dt_proj_w, dtype=np.float32)
    wout = np.asarray(L0_attn_out_proj_w, dtype=np.float32)

    in_maps = []
    for c in range(NCORE):
        b, hf = c // 2, c % 2
        sel = np.zeros((128, 8), np.float32)
        if hf == 1:
            sel[:, c - 1] = 1.0
        in_maps.append({
            "x_sh": np.ascontiguousarray(x[b, hf * TOK:(hf + 1) * TOK, :]),
            "gwp": np.ascontiguousarray(gwp[c * 2048:(c + 1) * 2048, :]),
            "gsc": np.ascontiguousarray(gsc[c * 2048:(c + 1) * 2048, :]),
            "uwp": np.ascontiguousarray(uwp[c * 2048:(c + 1) * 2048, :]),
            "usc": np.ascontiguousarray(usc[c * 2048:(c + 1) * 2048, :]),
            "dwp": np.ascontiguousarray(dwp[:, c * 1024:(c + 1) * 1024]),
            "dsc": np.ascontiguousarray(dsc[:, c * 64:(c + 1) * 64]),
            "wdt_sh": np.ascontiguousarray(wdt[c * 256:(c + 1) * 256, :]),
            "wout_sh": np.ascontiguousarray(wout[c * 256:(c + 1) * 256, :]),
            "alog": np.ascontiguousarray(np.asarray(L0_attn_A_log, np.float32)),
            "xpw": np.ascontiguousarray(np.asarray(L0_attn_x_proj_w, np.float32)),
            "xpb": np.ascontiguousarray(np.asarray(L0_attn_x_proj_b, np.float32)),
            "dtb": np.ascontiguousarray(np.asarray(L0_attn_dt_proj_b, np.float32)),
            "outb": np.ascontiguousarray(np.asarray(L0_attn_out_proj_b, np.float32)),
            "sel8": sel,
        })
    res = run_bass_kernel_spmd(nc, in_maps, core_ids=list(range(NCORE)))
    out = np.empty((B_, L_, D), np.float32)
    for c in range(NCORE):
        b, hf = c // 2, c % 2
        out[b, hf * TOK:(hf + 1) * TOK, :] = res.results[c]["out_sh"]
    return out


# version 03
# speedup vs baseline: 1.0719x; 1.0719x over previous
import numpy as np
from contextlib import ExitStack

import concourse.bass as bass
import concourse.mybir as mybir
from concourse.bass_utils import run_bass_kernel_spmd

F32 = mybir.dt.float32
BF16 = mybir.dt.bfloat16
I32 = mybir.dt.int32
AF = mybir.ActivationFunctionType
OP = mybir.AluOpType
AX = mybir.AxisListType

NCORE = 8
B_, L_, D, DS, DM = 4, 8192, 2048, 16, 16384
TOK = 4096
CH = 512
NCH = TOK // CH
QT = 1024
ND = D // 128
GELU_C = 0.7978845608028654
GELU_A2 = 0.21145925545
EPS = 1e-5

ENGS = ("sync", "scalar", "tensor", "vector", "gpsimd")
DEQ_NW = 3
DEQ_NJT = 16
B3B_MODE = 2
B3A_MODE = 2


class Ctr:
    def __init__(self, stack, nc, name):
        self.h = stack.enter_context(nc.semaphore(name))
        self.n = 0
        self.name = name


class Plan:
    def __init__(self):
        self.streams = {e: [] for e in ENGS}
        self.lastwait = {e: {} for e in ENGS}
        self.dsems = []

    def finish_dma(self, extra=()):
        for ctr in list(self.dsems) + list(extra):
            if ctr.n:
                self.put("sync", (lambda e, c=ctr, v=ctr.n: e.wait_ge(c.h, v) and None))

    def put(self, eng, fn, waits=(), inc=None, inck=1):
        lw = self.lastwait[eng]
        w2 = []
        for ctr, val in waits:
            if val is None or val <= 0:
                continue
            if lw.get(ctr.name, -1) >= val:
                continue
            lw[ctr.name] = val
            w2.append((ctr, val))
        if inc is not None:
            inc.n += inck
        self.streams[eng].append((fn, w2, inc, inck))
        return None if inc is None else inc.n

    def emit(self, nc):
        with nc.Block() as blk:
            def runner(items):
                def run(e):
                    for fn, waits, inc, inck in items:
                        for ctr, val in waits:
                            e.wait_ge(ctr.h, val)
                        inst = fn(e)
                        if inc is not None:
                            inst.then_inc(inc.h, inck)
                return run
            for eng in ENGS:
                if self.streams[eng]:
                    getattr(blk, eng)(runner(self.streams[eng]))


def build_program(upto=9, dbg=(), b1part=9):
    nc = bass.Bass("TRN2", target_bir_lowering=False)

    x_in = nc.dram_tensor("x_sh", [TOK, D], F32, kind="ExternalInput")
    wp_in = {w: nc.dram_tensor(f"{w}wp", [2048, 1024], I32, kind="ExternalInput")
             for w in ("g", "u", "d")}
    sc_in = {w: nc.dram_tensor(f"{w}sc", [2048, 64], F32, kind="ExternalInput")
             for w in ("g", "u", "d")}
    wdt_sh = nc.dram_tensor("wdt_sh", [256, 2048], F32, kind="ExternalInput")
    wout_sh = nc.dram_tensor("wout_sh", [256, 2048], F32, kind="ExternalInput")
    alog = nc.dram_tensor("alog", [D, DS], F32, kind="ExternalInput")
    xpw = nc.dram_tensor("xpw", [32, D], F32, kind="ExternalInput")
    xpb_in = nc.dram_tensor("xpb", [32], F32, kind="ExternalInput")
    dtb_in = nc.dram_tensor("dtb", [D], F32, kind="ExternalInput")
    outb_in = nc.dram_tensor("outb", [D], F32, kind="ExternalInput")
    sel8_in = nc.dram_tensor("sel8", [128, 8], F32, kind="ExternalInput")
    out_sh = nc.dram_tensor("out_sh", [TOK, D], F32, kind="ExternalOutput")

    xn_s = nc.dram_tensor("xn_s", [D, TOK], BF16)
    dt_s = nc.dram_tensor("dt_s", [D, TOK], F32)
    ssum_s = nc.dram_tensor("ssum_s", [D, TOK], F32)
    bc_s = nc.dram_tensor("bc_s", [32, TOK], BF16)
    y_s = nc.dram_tensor("y_s", [D, TOK], F32)
    inter_s = nc.dram_tensor("inter_s", [DM, TOK], BF16)
    g_s = nc.dram_tensor("g_s", [D, TOK], F32)
    c1 = nc.dram_tensor("c1", [4096, 256], BF16)
    g1 = nc.dram_tensor("g1", [8 * 4096, 256], BF16)
    cW = {w: nc.dram_tensor(f"c{w}", [2048, 2048], BF16) for w in ("g", "u", "d")}
    gW = {w: nc.dram_tensor(f"gw{w}", [8 * 2048, 2048], BF16) for w in ("g", "u", "d")}
    st_c = nc.dram_tensor("st_c", [128, 256], F32)
    st_g = nc.dram_tensor("st_g", [8 * 128, 256], F32)
    scr = dict(xn_s=xn_s, dt_s=dt_s, ssum_s=ssum_s, bc_s=bc_s, y_s=y_s,
               inter_s=inter_s, g_s=g_s, c1=c1, g1=g1, st_c=st_c, st_g=st_g)
    scr.update({f"c{w}": cW[w] for w in cW})
    scr.update({f"gw{w}": gW[w] for w in gW})
    dbg_outs = []
    for nm in dbg:
        h = scr[nm]
        dbg_outs.append((nc.dram_tensor(f"dbg_{nm}", list(h.shape), h.dtype,
                                        kind="ExternalOutput"), h))

    stack = ExitStack()
    sb = lambda nm, shape, dt: stack.enter_context(nc.sbuf_tensor(nm, shape, dt))

    aneg = sb("aneg", [128, ND * DS], F32)
    dtb_sb = sb("dtb_sb", [128, ND], F32)
    outb_sb = sb("outb_sb", [128, ND], F32)
    xpb_sb = sb("xpb_sb", [32, 1], F32)
    sel8_sb = sb("sel8_sb", [128, 8], F32)
    state = sb("state", [128, ND * DS], F32)
    sstate = sb("sstate", [128, ND], F32)
    hin = sb("hin", [128, ND * DS], F32)
    hin_st = sb("hin_st", [128, ND * DS], F32)
    idbf = sb("idbf", [128, 128], BF16)
    idf32 = sb("idf32", [128, 128], F32)
    zeros = sb("zeros", [128, CH], F32)
    wxpT = sb("wxpT", [128, ND * 128], BF16)
    csts = sb("csts", [128, 4], F32)

    qsp = Ctr(stack, nc, "qsp")
    qact = Ctr(stack, nc, "qact")

    def dsem(p, name):
        c = Ctr(stack, nc, name)
        p.dsems.append(c)
        return c

    cpe = Ctr(stack, nc, "cpe")
    cdv = Ctr(stack, nc, "cdv")
    cac = Ctr(stack, nc, "cac")
    cpl = Ctr(stack, nc, "cpl")
    ccc = Ctr(stack, nc, "ccc")

    p = Plan()
    s_c0 = dsem(p, "s_c0")
    for i in range(ND):
        p.put("sync", lambda e, i=i: e.dma_start(
            aneg[:, i * DS:(i + 1) * DS], alog[i * 128:(i + 1) * 128, :]), inc=s_c0, inck=16)
        p.put("sync", lambda e, i=i: e.dma_start(
            dtb_sb[:, i:i + 1], bass.AP(dtb_in, i * 128, [[1, 128], [1, 1]])), inc=s_c0, inck=16)
        p.put("sync", lambda e, i=i: e.dma_start(
            outb_sb[:, i:i + 1], bass.AP(outb_in, i * 128, [[1, 128], [1, 1]])), inc=s_c0, inck=16)
    p.put("sync", lambda e: e.dma_start(xpb_sb[:, :], bass.AP(xpb_in, 0, [[1, 32], [1, 1]])),
          inc=s_c0, inck=16)
    p.put("sync", lambda e: e.dma_start(sel8_sb[:, :], sel8_in[:, :]), inc=s_c0, inck=16)
    ld_done = s_c0.n
    p.put("gpsimd", lambda e: e.memset(csts[:, 0:1], -4.0))
    p.put("gpsimd", lambda e: e.memset(csts[:, 1:2], -6.0))
    p.put("gpsimd", lambda e: e.memset(csts[:, 2:3], EPS))
    p.put("gpsimd", lambda e: e.memset(csts[:, 3:4], 1.0))
    p.put("gpsimd", lambda e: e.memset(idbf[:], 0.0))
    p.put("gpsimd", lambda e: e.affine_select(out=idbf[:], in_=idbf[:], compare_op=OP.not_equal,
                                              fill=1.0, base=0, pattern=[[-1, 128]],
                                              channel_multiplier=1))
    p.put("gpsimd", lambda e: e.memset(idf32[:], 0.0))
    p.put("gpsimd", lambda e: e.affine_select(out=idf32[:], in_=idf32[:], compare_op=OP.not_equal,
                                              fill=1.0, base=0, pattern=[[-1, 128]],
                                              channel_multiplier=1))
    p.put("gpsimd", lambda e: e.memset(zeros[:], 0.0))
    p.put("gpsimd", lambda e: e.memset(state[:], 0.0))
    p.put("gpsimd", lambda e: e.memset(sstate[:], 0.0))
    p.put("gpsimd", lambda e: e.memset(hin[:], 0.0))
    p.put("gpsimd", lambda e: e.memset(wxpT[:], 0.0))
    a_exp = p.put("scalar", lambda e: e.activation(aneg[:], aneg[:], AF.Exp),
                  waits=[(s_c0, ld_done)], inc=cac)
    p.put("vector", lambda e: e.tensor_scalar_mul(aneg[:], aneg[:], -1.0),
          waits=[(cac, a_exp)], inc=cdv)
    p.finish_dma()
    p.emit(nc)

    wstack = ExitStack()
    wdtT = wstack.enter_context(nc.sbuf_tensor("wdtT", [128, ND * 2048], BF16))
    if upto < 1:
        return _finish_debug(nc, dbg_outs, stack, qsp, qact, ccc)
    p = Plan()
    LW = {}
    s_xpw = dsem(p, "s_xpw"); s_w4 = dsem(p, "s_w4"); s_stg1 = dsem(p, "s_stg1")
    s_wp = [dsem(p, "s_wp0"), dsem(p, "s_wp1")]
    s_sc = [dsem(p, "s_sc0"), dsem(p, "s_sc1")]
    s_sg = [dsem(p, "s_sg0"), dsem(p, "s_sg1")]
    s_wdt = dsem(p, "s_wdt")
    bstack = ExitStack()
    _a = lambda nm, shape, dt: bstack.enter_context(nc.sbuf_tensor(nm, shape, dt))
    w4 = _a("w4", [128, 2 * 2048], F32)
    wb4 = _a("wb4", [128, 2 * 2048], BF16)
    stg1 = _a("stg1", [128, 256], BF16)
    xpw_f = _a("xpw_f", [32, 2048], F32)
    xpw_b = _a("xpw_b", [32, 2048], BF16)
    wpt = _a("wpt", [128, 2 * 1024], I32)
    sct = _a("sct", [128, 2 * 64], F32)
    loi = _a("loi", [128, 1024], I32)
    hii = _a("hii", [128, 1024], I32)
    pf = _a("pf", [128, 2 * 1024], F32)
    ge8 = _a("ge8", [128, 1024], F32)
    cc_ = _a("cc_", [128, 1024], F32)
    r1_ = _a("r1_", [128, 1024], F32)
    r2_ = _a("r2_", [128, 1024], F32)
    t1_ = _a("t1_", [128, 1024], F32)
    mm_ = _a("mm_", [128, 1024], F32)
    sg_ = _a("sg_", [128, 1024], F32)
    vv_ = _a("vv_", [128, 1024], F32)
    deq = _a("deq", [128, 2 * 2048], BF16)
    stg2 = _a("stg2", [128, 2 * 2048], BF16)
    pstrA = bstack.enter_context(nc.psum_tensor("pstrA", [128, 128], BF16))
    pstrB = bstack.enter_context(nc.psum_tensor("pstrB", [128, 128], BF16))
    pstrP = [pstrA, pstrB]
    if True:
        v = p.put("sync", lambda e: e.dma_start(xpw_f[:, :], xpw[:, :]), inc=s_xpw, inck=16)
        cast = p.put("vector", lambda e: e.tensor_copy(xpw_b[:], xpw_f[:]),
                     waits=[(s_xpw, v)], inc=cdv)
        for i0 in range(0, ND, 2):
            for j in range(2):
                i = i0 + j
                p.put("tensor", lambda e, i=i, sl=j: e.transpose(
                    pstrP[sl][:, 0:32],
                    xpw_b[:, i * 128:(i + 1) * 128], idbf[0:32, 0:32]),
                    waits=[(cdv, cast), (cac, LW.get(("pstr", j), 0))], inc=None)
            dr = p.put("tensor", lambda e: e.drain(), inc=cpe)
            for j in range(2):
                i = i0 + j
                ev = p.put("scalar", lambda e, i=i, sl=j: e.copy(
                    wxpT[:, i * 128:i * 128 + 32], pstrP[sl][:, 0:32]),
                    waits=[(cpe, dr)], inc=cac)
                LW[("pstr", j)] = ev

        for wsrc, c1row in (((wdt_sh, 0), (wout_sh, 2048)) if b1part >= 1 else ()):
            lds = []
            for j in range(2):
                lds.append(p.put("sync", lambda e, j=j, wsrc=wsrc: e.dma_start(
                    w4[:, j * 2048:(j + 1) * 2048], wsrc[j * 128:(j + 1) * 128, :]),
                    waits=[(cpe, LW.get("w4_free", 0))], inc=s_w4, inck=16))
            cast = p.put("vector", lambda e: e.tensor_copy(wb4[:], w4[:]),
                         waits=[(s_w4, lds[-1]), (cpe, LW.get("w4_free", 0))], inc=cdv)
            for i in range(ND):
                for j in range(2):
                    p.put("tensor", lambda e, i=i, j=j: e.transpose(
                        pstrP[j][:, :],
                        wb4[:, j * 2048 + i * 128: j * 2048 + (i + 1) * 128], idbf[:]),
                        waits=[(cdv, cast), (cac, LW.get(("pstr", j), 0))], inc=None)
                dr = p.put("tensor", lambda e: e.drain(), inc=cpe)
                ev = None
                for j in range(2):
                    ev = p.put("scalar", lambda e, j=j: e.copy(
                        stg1[:, j * 128:(j + 1) * 128], pstrP[j][:, :]),
                        waits=[(cpe, dr), (s_stg1, LW.get("stg1_free", 0))], inc=cac)
                    LW[("pstr", j)] = ev
                std = p.put("scalar", lambda e, i=i, c1row=c1row: e.dma_start(
                    c1[c1row + i * 128: c1row + (i + 1) * 128, :], stg1[:]),
                    waits=[(cac, ev)], inc=s_stg1, inck=16)
                LW["stg1_free"] = std
            LW["w4_free"] = cpe.n
        if b1part >= 2:
            p.put("gpsimd", lambda e: e.collective_compute(
                "AllGather", OP.bypass, replica_groups=[list(range(8))],
                ins=[c1.ap()], outs=[g1.ap()]),
                waits=[(s_stg1, s_stg1.n)], inc=ccc)

        for sl in range(2):
            LW[("pstr", sl)] = cac.n
        _wl = ("g", "u", "d") if b1part >= 3 else ()
        _wl = _wl[:DEQ_NW] if b1part in (3, 4, 5) else _wl
        for w in _wl:
            for jt in range(DEQ_NJT if b1part in (3, 4, 5) else 16):
                par = jt % 2
                ldw = p.put("sync", lambda e, w=w, jt=jt, par=par: e.dma_start(
                    wpt[:, par * 1024:(par + 1) * 1024],
                    wp_in[w][jt * 128:(jt + 1) * 128, :]),
                    waits=[(cdv, LW.get(("wpt", par), 0))], inc=s_wp[par], inck=16)
                lds = p.put("sync", lambda e, w=w, jt=jt, par=par: e.dma_start(
                    sct[:, par * 64:(par + 1) * 64],
                    sc_in[w][jt * 128:(jt + 1) * 128, :]),
                    waits=[(cdv, LW.get(("sct", par), 0))], inc=s_sc[par], inck=16)
                lo = p.put("vector", lambda e, par=par: e.tensor_scalar(
                    loi[:], wpt[:, par * 1024:(par + 1) * 1024], 15, None, OP.bitwise_and),
                    waits=[(s_wp[par], ldw), (cdv, LW.get("loi_free", 0))], inc=cdv)
                hi = p.put("vector", lambda e, par=par: e.tensor_scalar(
                    hii[:], wpt[:, par * 1024:(par + 1) * 1024], 4, None,
                    OP.logical_shift_right), inc=cdv)
                LW[("wpt", par)] = hi
                p.put("vector", lambda e: e.tensor_copy(pf[:, 0:1024], loi[:]), inc=cdv)
                cast2 = p.put("vector", lambda e: e.tensor_copy(pf[:, 1024:2048], hii[:]),
                              inc=cdv)
                LW["loi_free"] = cast2
                last_scale = 0
                for pi in range(2):
                    g8 = p.put("vector", lambda e, pi=pi: e.tensor_scalar(
                        ge8[:], pf[:, pi * 1024:(pi + 1) * 1024], 8.0, None, OP.is_ge),
                        waits=[(cpl, LW.get("ge8_free", 0))], inc=cdv)
                    cc1 = p.put("vector", lambda e, pi=pi: e.scalar_tensor_tensor(
                        cc_[:], ge8[:], -8.0, pf[:, pi * 1024:(pi + 1) * 1024],
                        OP.mult, OP.add),
                        waits=[(cpl, LW.get("cc_free", 0)), (cac, LW.get("cc_free_a", 0))],
                        inc=cdv)
                    ra = p.put("scalar", lambda e: e.activation(
                        r1_[:], cc_[:], AF.Relu, bias=csts[:, 0:1]),
                        waits=[(cdv, cc1), (cpl, LW.get("r1_free", 0))], inc=cac)
                    rb = p.put("scalar", lambda e: e.activation(
                        r2_[:], cc_[:], AF.Relu, bias=csts[:, 1:2]),
                        waits=[(cdv, LW.get("r2_free", 0))], inc=cac)
                    tt1 = p.put("gpsimd", lambda e: e.tensor_add(t1_[:], cc_[:], r1_[:]),
                                waits=[(cac, ra), (cdv, LW.get("t1_free", 0))], inc=cpl)
                    LW["cc_free"] = tt1
                    LW["cc_free_a"] = rb
                    LW["r1_free"] = tt1
                    mmv = p.put("vector", lambda e: e.scalar_tensor_tensor(
                        mm_[:], t1_[:], 0.5, r2_[:], OP.mult, OP.add),
                        waits=[(cpl, tt1), (cac, rb), (cpl, LW.get("mm_free", 0))], inc=cdv)
                    LW["t1_free"] = mmv
                    LW["r2_free"] = mmv
                    sgv = p.put("gpsimd", lambda e: e.tensor_scalar(
                        sg_[:], ge8[:], -2.0, 1.0, OP.mult, OP.add),
                        waits=[(cpl, LW.get("sg_free", 0))], inc=cpl)
                    LW["ge8_free"] = sgv
                    vvv = p.put("gpsimd", lambda e: e.tensor_mul(vv_[:], mm_[:], sg_[:]),
                                waits=[(cdv, mmv), (cdv, LW.get("vv_free", 0))], inc=cpl)
                    LW["mm_free"] = vvv
                    LW["sg_free"] = vvv
                    last_scale = p.put("vector", lambda e, pi=pi, par=par: e.tensor_tensor(
                        bass.AP(deq, par * 2048 + pi, [[4096, 128], [2, 1024]]),
                        vv_[:],
                        bass.AP(sct, par * 64, [[128, 128], [1, 64], [0, 16]]),
                        OP.mult),
                        waits=[(cpl, vvv), (s_sc[par], lds), (cpe, LW.get(("deq", par), 0))],
                        inc=cdv)
                    LW["vv_free"] = last_scale
                LW[("sct", par)] = last_scale
                evs = 0
                for i0 in (range(0, ND, 2) if b1part >= 4 else ()):
                    for j in range(2):
                        i = i0 + j
                        p.put("tensor", lambda e, i=i, par=par, sl=j: e.transpose(
                            pstrP[sl][:, :],
                            bass.AP(deq, par * 2048 + i * 128, [[4096, 128], [1, 128]]),
                            idbf[:]),
                            waits=[(cdv, last_scale), (cac, LW.get(("pstr", j), 0))],
                            inc=None)
                    dr = p.put("tensor", lambda e: e.drain(), inc=cpe)
                    evs = 0
                    for j in range(2):
                        i = i0 + j
                        evs = p.put("scalar", lambda e, i=i, par=par, sl=j: e.copy(
                            stg2[:, par * 2048 + i * 128: par * 2048 + (i + 1) * 128],
                            pstrP[sl][:, :]),
                            waits=[(cpe, dr), (s_sg[par], LW.get(("stg2", par), 0))], inc=cac)
                        LW[("pstr", j)] = evs
                LW[("deq", par)] = cpe.n
                if b1part < 4:
                    LW[("deq", par)] = 0
                    continue
                std = 0
                if b1part < 5:
                    LW[("stg2", par)] = 0
                    continue
                for q4 in range(4):
                    std = p.put("scalar", lambda e, w=w, jt=jt, par=par, q4=q4: e.dma_start(
                        bass.AP(cW[w], jt * 128 + q4 * 4 * 128 * 2048,
                                [[2048, 128], [128 * 2048, 4], [1, 128]]),
                        bass.AP(stg2, par * 2048 + q4 * 4 * 128,
                                [[4096, 128], [128, 4], [1, 128]])),
                        waits=[(cac, evs)], inc=s_sg[par], inck=16)
                LW[("stg2", par)] = std
            if b1part >= 6:
                p.put("gpsimd", lambda e, w=w: e.collective_compute(
                    "AllGather", OP.bypass, replica_groups=[list(range(8))],
                    ins=[cW[w].ap()], outs=[gW[w].ap()]),
                    waits=[(s_sg[0], s_sg[0].n), (s_sg[1], s_sg[1].n)], inc=ccc)
        wdt_lds = 0
        for r in range(8):
            wdt_lds = p.put("sync", lambda e, r=r: e.dma_start(
                bass.AP(wdtT, r * 256, [[ND * 2048, 128], [2048, ND], [1, 256]]),
                bass.AP(g1, (r * 4096) * 256, [[256, 128], [128 * 256, ND], [1, 256]])),
                waits=[(ccc, 1)], inc=s_wdt, inck=16)
        WDT_READY = (s_wdt, wdt_lds)
        p.finish_dma()
        p.emit(nc)
    bstack.close()
    AG1_DONE, AGG_DONE, AGU_DONE, AGD_DONE, AGS_DONE = 1, 2, 3, 4, 6

    if upto < 2:
        return _finish_debug(nc, dbg_outs, stack, qsp, qact, ccc)
    p = Plan()
    LW = {}
    s_xt = [dsem(p, "s_xt0"), dsem(p, "s_xt1")]
    s_xnsp = dsem(p, "s_xnsp")
    s_dtv = [dsem(p, "s_dtv0"), dsem(p, "s_dtv1")]
    s_bcv = dsem(p, "s_bcv")
    bstack = ExitStack()
    _a = lambda nm, shape, dt: bstack.enter_context(nc.sbuf_tensor(nm, shape, dt))
    xtok = _a("xtok", [128, 2 * 2048], F32)
    sqb = _a("sqb", [128, 2048], F32)
    lnst = _a("lnst", [128, 8], F32)
    xnt = _a("xnt", [128, 2 * 2048], BF16)
    xnc = _a("xnc", [128, ND * 512], BF16)
    spb = _a("spb", [128, 4 * 512], F32)
    dtv = _a("dtv", [128, 2 * 512], F32)
    bcv = _a("bcv", [32, 512], BF16)
    ps_trA = bstack.enter_context(nc.psum_tensor("ps_trA", [128, 128], BF16))
    ps_trB = bstack.enter_context(nc.psum_tensor("ps_trB", [128, 128], BF16))
    ps_trP = [ps_trA, ps_trB]
    ps_dt = bstack.enter_context(nc.psum_tensor("ps_dt", [128, 4 * 512], F32))
    ps_xp = bstack.enter_context(nc.psum_tensor("ps_xp", [128, 512], F32))
    if True:
        s_wdtS, wdt_ready = WDT_READY

        for c in range(NCH):
            xn_evs = 0
            for tau in range(4):
                par = tau % 2
                xl = p.put("sync", lambda e, c=c, tau=tau, par=par: e.dma_start(
                    xtok[:, par * 2048:(par + 1) * 2048],
                    x_in[c * 512 + tau * 128: c * 512 + (tau + 1) * 128, :]),
                    waits=[(cdv, LW.get(("xtok", par), 0))], inc=s_xt[par], inck=16)
                sq = p.put("scalar", lambda e, par=par: e.activation(
                    sqb[:], xtok[:, par * 2048:(par + 1) * 2048], AF.Square),
                    waits=[(s_xt[par], xl), (cdv, LW.get("sqb_free", 0))], inc=cac)
                s0v = p.put("vector", lambda e, par=par: e.reduce_sum(
                    lnst[:, 0:1], xtok[:, par * 2048:(par + 1) * 2048], AX.X),
                    waits=[(s_xt[par], xl)], inc=cdv)
                s1 = p.put("vector", lambda e: e.reduce_sum(lnst[:, 1:2], sqb[:], AX.X),
                           waits=[(cac, sq)], inc=cdv)
                LW["sqb_free"] = s1
                mean = p.put("vector", lambda e: e.tensor_scalar_mul(
                    lnst[:, 2:3], lnst[:, 0:1], 1.0 / 2048),
                    waits=[(cdv, s0v)], inc=cdv)
                tch1 = p.put("vector", lambda e: e.tensor_copy(lnst[:, 7:8], lnst[:, 2:3]),
                             waits=[(cdv, mean)], inc=cdv)
                m2 = p.put("scalar", lambda e: e.activation(
                    lnst[:, 3:4], lnst[:, 2:3], AF.Square), waits=[(cdv, tch1)], inc=cac)
                t2 = p.put("scalar", lambda e: e.copy(lnst[:, 7:8], lnst[:, 3:4]),
                           waits=[(cac, m2)], inc=cac)
                vr = p.put("vector", lambda e: e.scalar_tensor_tensor(
                    lnst[:, 4:5], lnst[:, 1:2], 1.0 / 2048, lnst[:, 3:4],
                    OP.mult, OP.subtract), waits=[(cac, t2), (cdv, s1)], inc=cdv)
                tch3 = p.put("vector", lambda e: e.tensor_copy(lnst[:, 7:8], lnst[:, 4:5]),
                             waits=[(cdv, vr)], inc=cdv)
                lnv = p.put("scalar", lambda e: e.activation(
                    lnst[:, 5:6], lnst[:, 4:5], AF.Ln, bias=csts[:, 2:3]),
                    waits=[(cdv, tch3)], inc=cac)
                lt = p.put("scalar", lambda e: e.copy(lnst[:, 7:8], lnst[:, 5:6]),
                           waits=[(cac, lnv)], inc=cac)
                rs = p.put("scalar", lambda e: e.activation(
                    lnst[:, 6:7], lnst[:, 5:6], AF.Exp, scale=-0.5),
                    waits=[(cac, lt)], inc=cac)
                t4 = p.put("scalar", lambda e: e.copy(lnst[:, 7:8], lnst[:, 6:7]),
                           waits=[(cac, rs)], inc=cac)
                xn1 = p.put("vector", lambda e, par=par: e.tensor_scalar(
                    xnt[:, par * 2048:(par + 1) * 2048],
                    xtok[:, par * 2048:(par + 1) * 2048],
                    lnst[:, 2:3], lnst[:, 6:7], OP.subtract, OP.mult),
                    waits=[(cac, t4)], inc=cdv)
                LW[("xtok", par)] = xn1
                for i0 in range(0, ND, 2):
                    for j in range(2):
                        i = i0 + j
                        p.put("tensor", lambda e, i=i, sl=j, par=par: e.transpose(
                            ps_trP[sl][:, :],
                            xnt[:, par * 2048 + i * 128: par * 2048 + (i + 1) * 128],
                            idbf[:]),
                            waits=[(cdv, xn1), (cac, LW.get(("ps_tr", j), 0)),
                                   (s_xnsp, LW.get("xnc_freeq", 0))], inc=None)
                    dr = p.put("tensor", lambda e: e.drain(), inc=cpe)
                    for j in range(2):
                        i = i0 + j
                        xn_evs = p.put("scalar", lambda e, i=i, sl=j, tau=tau: e.copy(
                            xnc[:, i * 512 + tau * 128: i * 512 + (tau + 1) * 128],
                            ps_trP[sl][:, :]),
                            waits=[(cpe, dr)], inc=cac)
                        LW[("ps_tr", j)] = xn_evs
            xn_sp = 0
            for part in range(4):
                xn_sp = p.put("scalar", lambda e, c=c, part=part: e.dma_start(
                    bass.AP(xn_s, (part * 4 * 128) * TOK + c * 512,
                            [[TOK, 128], [128 * TOK, 4], [1, 512]]),
                    bass.AP(xnc, part * 4 * 512, [[ND * 512, 128], [512, 4], [1, 512]])),
                    waits=[(cac, xn_evs)], inc=s_xnsp, inck=16)
            LW["xnc_freeq"] = xn_sp

            last_mm = 0
            for k in range(ND):
                wl = [(cac, xn_evs)]
                if k == 0:
                    wl.append((cac, LW.get("ps_xp", 0)))
                p.put("tensor", lambda e, k=k: e.matmul(
                    ps_xp[:], wxpT[:, k * 128:(k + 1) * 128], xnc[:, k * 512:(k + 1) * 512],
                    start=(k == 0), stop=(k == ND - 1)), waits=wl, inc=None)
            last_mm = p.put("tensor", lambda e: e.drain(), inc=cpe)
            LW["xnc_freep"] = last_mm
            bce = p.put("scalar", lambda e: e.activation(
                bcv[:, :], ps_xp[0:32, :], AF.Identity, bias=xpb_sb[:, :]),
                waits=[(cpe, last_mm), (s_bcv, LW.get("bcv_free", 0))], inc=cac)
            LW["ps_xp"] = bce
            bcd = p.put("scalar", lambda e, c=c: e.dma_start(
                bass.AP(bc_s, c * 512, [[TOK, 32], [1, 512]]), bcv[:, :]),
                waits=[(cac, bce)], inc=s_bcv, inck=16)
            LW["bcv_free"] = bcd
            for et in range(ND):
                bank = et % 4
                last_mm = 0
                for k in range(ND):
                    wl = [(s_wdtS, wdt_ready), (cac, xn_evs)]
                    if k == 0:
                        wl.append((cdv, LW.get(("ps_dt", bank), 0)))
                        wl.append((cac, LW.get(("ps_dt_a", bank), 0)))
                    p.put("tensor", lambda e, et=et, k=k, bank=bank: e.matmul(
                        ps_dt[:, bank * 512:(bank + 1) * 512],
                        wdtT[:, k * 2048 + et * 128: k * 2048 + (et + 1) * 128],
                        xnc[:, k * 512:(k + 1) * 512],
                        start=(k == 0), stop=(k == ND - 1)),
                        waits=wl, inc=None)
                last_mm = p.put("tensor", lambda e: e.drain(), inc=cpe)
                sp = bank % 2
                a1 = p.put("scalar", lambda e, et=et, bank=bank, sp=sp: e.activation(
                    spb[:, sp * 1024:sp * 1024 + 512],
                    ps_dt[:, bank * 512:(bank + 1) * 512], AF.Abs,
                    bias=dtb_sb[:, et:et + 1]),
                    waits=[(cpe, last_mm), (cdv, LW.get(("spb", sp), 0))], inc=cac)
                p.put("scalar", lambda e, sp=sp: e.activation(
                    spb[:, sp * 1024 + 512:sp * 1024 + 1024],
                    spb[:, sp * 1024:sp * 1024 + 512], AF.Exp, scale=-1.0), inc=cac)
                p.put("scalar", lambda e, sp=sp: e.activation(
                    spb[:, sp * 1024:sp * 1024 + 512],
                    spb[:, sp * 1024 + 512:sp * 1024 + 1024], AF.Ln, bias=csts[:, 3:4]), inc=cac)
                r1v = p.put("scalar", lambda e, et=et, bank=bank, sp=sp: e.activation(
                    spb[:, sp * 1024 + 512:sp * 1024 + 1024],
                    ps_dt[:, bank * 512:(bank + 1) * 512], AF.Relu,
                    bias=dtb_sb[:, et:et + 1]), inc=cac)
                LW[("ps_dt_a", bank)] = r1v
                dtc = p.put("vector", lambda e, et=et, sp=sp: e.tensor_add(
                    dtv[:, (et % 2) * 512:(et % 2) * 512 + 512],
                    spb[:, sp * 1024:sp * 1024 + 512],
                    spb[:, sp * 1024 + 512:sp * 1024 + 1024]),
                    waits=[(cac, r1v), (s_dtv[et % 2], LW.get(("dtv", et % 2), 0))], inc=cdv)
                LW[("ps_dt", bank)] = dtc
                LW[("spb", sp)] = dtc
                dsp = p.put("scalar", lambda e, c=c, et=et: e.dma_start(
                    bass.AP(dt_s, (et * 128) * TOK + c * 512, [[TOK, 128], [1, 512]]),
                    dtv[:, (et % 2) * 512:(et % 2) * 512 + 512]),
                    waits=[(cdv, dtc)], inc=s_dtv[et % 2], inck=16)
                LW[("dtv", et % 2)] = dsp
        p.emit(nc)
    bstack.close()
    wstack.close()
    QACT_B2 = qact.n

    if upto < 3:
        return _finish_debug(nc, dbg_outs, stack, qsp, qact, ccc)
    p = Plan()
    LW = {}
    s_xnh = dsem(p, "s_xnh")
    s_gs = [dsem(p, "s_gs0"), dsem(p, "s_gs1")]
    s_us = [dsem(p, "s_us0"), dsem(p, "s_us1")]
    s_dtp = [dsem(p, "s_dtp0"), dsem(p, "s_dtp1")]
    s_xnp = [dsem(p, "s_xnp0"), dsem(p, "s_xnp1")]
    s_bcb = dsem(p, "s_bcb"); s_ccb = dsem(p, "s_ccb")
    s_sb = [dsem(p, "s_sb0"), dsem(p, "s_sb1")]
    s_yb = [dsem(p, "s_yb0"), dsem(p, "s_yb1")]
    s_insp = [dsem(p, "s_insp0"), dsem(p, "s_insp1")]
    s_stc = dsem(p, "s_stc")
    bstack = ExitStack()
    _a = lambda nm, shape, dt: bstack.enter_context(nc.sbuf_tensor(nm, shape, dt))
    xnh = _a("xnh", [128, ND * 1024], BF16)
    gsl = _a("gsl", [128, 2 * ND * 256], BF16)
    usl = _a("usl", [128, 2 * ND * 256], BF16)
    qb = _a("qb", [128, 512], F32)
    pb = _a("pb", [128, 512], F32)
    tb = _a("tb", [128, 512], F32)
    rb = _a("rb", [128, 512], F32)
    inters = _a("inters", [128, 2 * 1024], BF16)
    dtc_s = _a("dtc_s", [128, ND * 512], F32)
    xncs = _a("xncs", [128, ND * 512], BF16)
    dtxb = _a("dtxb", [128, 2 * 512], F32)
    Sb = _a("Sb", [128, 2 * 512], F32)
    dab = _a("dab", [128, 2 * 512], F32)
    dbxb = _a("dbxb", [128, 2 * 512], F32)
    strip = _a("strip", [128, 4 * 512], F32)
    Bcb = _a("Bcb", [128, DS * 512], BF16)
    Ccb = _a("Ccb", [128, DS * 512], BF16)
    ytmp = _a("ytmp", [128, 2 * 512], F32)
    ybuf = _a("ybuf", [128, 2 * 512], F32)
    ps8 = bstack.enter_context(nc.psum_tensor("ps8", [128, 8 * 512], F32))
    if True:
        def emit_xnh_load(qt):
            last = 0
            for i in range(ND):
                last = p.put("sync", lambda e, qt=qt, i=i: e.dma_start(
                    xnh[:, i * 1024:(i + 1) * 1024],
                    bass.AP(xn_s, (i * 128) * TOK + qt * 1024, [[TOK, 128], [1, 1024]])),
                    waits=[(cpe, LW.get("xnh_free", 0))],
                    inc=s_xnh, inck=16)
            return last

        mt_count = [0]

        def emit_ffn_unit(qt, u, xnh_ready):
            r, q8 = u >> 3, u & 7
            par = u % 2
            g_ld = p.put("sync", lambda e, r=r, q8=q8, par=par: e.dma_start(
                bass.AP(gsl, par * ND * 256, [[2 * ND * 256, 128], [256, ND], [1, 256]]),
                bass.AP(gW["g"], (r * 2048) * 2048 + q8 * 256,
                        [[2048, 128], [128 * 2048, ND], [1, 256]])),
                waits=[(ccc, AGG_DONE), (cpe, LW.get(("gsl", par), 0))],
                inc=s_gs[par], inck=16)
            u_ld = p.put("sync", lambda e, r=r, q8=q8, par=par: e.dma_start(
                bass.AP(usl, par * ND * 256, [[2 * ND * 256, 128], [256, ND], [1, 256]]),
                bass.AP(gW["u"], (r * 2048) * 2048 + q8 * 256,
                        [[2048, 128], [128 * 2048, ND], [1, 256]])),
                waits=[(ccc, AGU_DONE), (cpe, LW.get(("usl", par), 0))],
                inc=s_us[par], inck=16)
            for s in range(2):
                m128 = r * 16 + q8 * 2 + s
                mpar = mt_count[0] % 2
                mt_count[0] += 1
                gbank, ubank = mpar * 4, mpar * 4 + 2
                last_g = last_u = 0
                for k in range(ND):
                    for tch in range(2):
                        wl = [(s_gs[par], g_ld), (s_xnh, xnh_ready)]
                        if k == 0 and tch == 0:
                            wl.append((cdv, LW.get(("geluA", mpar), 0)))
                        last_g = p.put("tensor", lambda e, k=k, tch=tch, s=s, par=par,
                                       gbank=gbank: e.matmul(
                            ps8[:, (gbank + tch) * 512:(gbank + tch + 1) * 512],
                            gsl[:, par * ND * 256 + k * 256 + s * 128:
                                par * ND * 256 + k * 256 + s * 128 + 128],
                            xnh[:, k * 1024 + tch * 512: k * 1024 + (tch + 1) * 512],
                            start=(k == 0), stop=(k == ND - 1)),
                            waits=wl, inc=None)
                for k in range(ND):
                    for tch in range(2):
                        wl = [(s_us[par], u_ld)]
                        if k == 0 and tch == 0:
                            wl.append((cdv, LW.get(("geluB", mpar), 0)))
                        p.put("tensor", lambda e, k=k, tch=tch, s=s, par=par,
                              ubank=ubank: e.matmul(
                            ps8[:, (ubank + tch) * 512:(ubank + tch + 1) * 512],
                            usl[:, par * ND * 256 + k * 256 + s * 128:
                                par * ND * 256 + k * 256 + s * 128 + 128],
                            xnh[:, k * 1024 + tch * 512: k * 1024 + (tch + 1) * 512],
                            start=(k == 0), stop=(k == ND - 1)),
                            waits=wl, inc=None)
                last_u = p.put("tensor", lambda e: e.drain(), inc=cpe)
                for tch in range(2):
                    qv = p.put("scalar", lambda e, gbank=gbank, tch=tch: e.activation(
                        qb[:], ps8[:, (gbank + tch) * 512:(gbank + tch + 1) * 512],
                        AF.Square, scale=GELU_A2),
                        waits=[(cpe, last_u), (cdv, LW.get("qb_free", 0))], inc=cac)
                    pv = p.put("vector", lambda e, gbank=gbank, tch=tch: e.scalar_tensor_tensor(
                        pb[:], qb[:], 1.0,
                        ps8[:, (gbank + tch) * 512:(gbank + tch + 1) * 512],
                        OP.add, OP.mult),
                        waits=[(cac, qv), (cac, LW.get("pb_free", 0))], inc=cdv)
                    LW["qb_free"] = pv
                    tv = p.put("scalar", lambda e: e.activation(
                        tb[:], pb[:], AF.Tanh, scale=GELU_C),
                        waits=[(cdv, pv), (cdv, LW.get("tb_free", 0))], inc=cac)
                    LW["pb_free"] = tv
                    rv = p.put("vector", lambda e, gbank=gbank, tch=tch: e.scalar_tensor_tensor(
                        rb[:], tb[:], 1.0,
                        ps8[:, (gbank + tch) * 512:(gbank + tch + 1) * 512],
                        OP.add, OP.mult),
                        waits=[(cac, tv), (cdv, LW.get("rb_free", 0))], inc=cdv)
                    LW["tb_free"] = rv
                    iv = p.put("vector", lambda e, ubank=ubank, tch=tch, s=s: e.scalar_tensor_tensor(
                        inters[:, (s % 2) * 1024 + tch * 512:(s % 2) * 1024 + (tch + 1) * 512],
                        rb[:], 0.5,
                        ps8[:, (ubank + tch) * 512:(ubank + tch + 1) * 512],
                        OP.mult, OP.mult),
                        waits=[(s_insp[s % 2], LW.get(("inters", s % 2), 0))], inc=cdv)
                    LW["rb_free"] = iv
                LW[("geluA", mpar)] = cdv.n
                LW[("geluB", mpar)] = cdv.n
                isp = p.put("scalar", lambda e, m128=m128, qt=qt, s=s: e.dma_start(
                    bass.AP(inter_s, (m128 * 128) * TOK + qt * 1024, [[TOK, 128], [1, 1024]]),
                    inters[:, (s % 2) * 1024:(s % 2) * 1024 + 1024]),
                    waits=[(cdv, cdv.n)], inc=s_insp[s % 2], inck=16)
                LW[("inters", s % 2)] = isp
            LW[("gsl", par)] = cpe.n
            LW[("usl", par)] = cpe.n

        def emit_scan_half_loads(c, h):
            for part in (2 * h, 2 * h + 1):
                p.put("scalar", lambda e, c=c, part=part: e.dma_start(
                    bass.AP(dtc_s, part * 4 * 512, [[ND * 512, 128], [512, 4], [1, 512]]),
                    bass.AP(dt_s, (part * 4 * 128) * TOK + c * 512,
                            [[TOK, 128], [128 * TOK, 4], [1, 512]])),
                    waits=[(cdv, LW.get(("dtcF", h), 0)),
                           (cac, LW.get(("dtcFa", h), 0))], inc=s_dtp[h], inck=16)
                p.put("scalar", lambda e, c=c, part=part: e.dma_start(
                    bass.AP(xncs, part * 4 * 512, [[ND * 512, 128], [512, 4], [1, 512]]),
                    bass.AP(xn_s, (part * 4 * 128) * TOK + c * 512,
                            [[TOK, 128], [128 * TOK, 4], [1, 512]])),
                    waits=[(cdv, LW.get(("dtcF", h), 0))], inc=s_xnp[h], inck=16)

        def emit_bc_loads(c):
            for n in range(DS):
                p.put("gpsimd", lambda e, c=c, n=n: e.dma_start(
                    Bcb[:, n * 512:(n + 1) * 512],
                    bass.AP(bc_s, n * TOK + c * 512, [[0, 128], [1, 512]])),
                    waits=[(cpl, LW.get("Bcb_free", 0))], inc=s_bcb, inck=16)
                p.put("gpsimd", lambda e, c=c, n=n: e.dma_start(
                    Ccb[:, n * 512:(n + 1) * 512],
                    bass.AP(bc_s, (16 + n) * TOK + c * 512, [[0, 128], [1, 512]])),
                    waits=[(cdv, LW.get("Ccb_free", 0))], inc=s_ccb, inck=16)

        def emit_scan_dtile(c, i):
            h = i // 8
            ld_done = (32 * (c + 1), 32 * (c + 1))
            bc_done = (256 * (c + 1), 256 * (c + 1))
            if True:
                ip = i % 2
                dtx = p.put("vector", lambda e, i=i, ip=ip: e.tensor_mul(
                    dtxb[:, ip * 512:ip * 512 + 512],
                    dtc_s[:, i * 512:(i + 1) * 512], xncs[:, i * 512:(i + 1) * 512]),
                    waits=[(s_dtp[h], ld_done[0]), (s_xnp[h], ld_done[1]),
                           (cpl, LW.get(("dtxb", ip), 0))], inc=cdv)
                ssc = p.put("vector", lambda e, i=i, ip=ip: e.tensor_tensor_scan(
                    Sb[:, ip * 512:ip * 512 + 512],
                    dtc_s[:, i * 512:(i + 1) * 512], zeros[:, 0:512],
                    sstate[:, i:i + 1], OP.add, OP.add),
                    waits=[(s_sb[ip], LW.get(("Sb", ip), 0))], inc=cdv)
                p.put("vector", lambda e, i=i, ip=ip: e.tensor_copy(
                    sstate[:, i:i + 1], Sb[:, ip * 512 + 511:ip * 512 + 512]),
                    waits=[(cdv, ssc)], inc=cdv)
                ssp = p.put("gpsimd", lambda e, c=c, i=i, ip=ip: e.dma_start(
                    bass.AP(ssum_s, (i * 128) * TOK + c * 512, [[TOK, 128], [1, 512]]),
                    Sb[:, ip * 512:ip * 512 + 512]),
                    waits=[(cdv, ssc)], inc=s_sb[ip], inck=16)
                LW[("Sb", ip)] = ssp
                if i == 7 or i == 15:
                    LW[("dtcF", h)] = cdv.n
                    LW[("dtcFa", h)] = cac.n
                for g in range(4):
                    for nn in range(4):
                        n = g * 4 + nn
                        ab = n % 2
                        da = p.put("scalar", lambda e, i=i, n=n, ab=ab: e.activation(
                            dab[:, ab * 512:(ab + 1) * 512],
                            dtc_s[:, i * 512:(i + 1) * 512], AF.Exp,
                            scale=aneg[:, i * DS + n:i * DS + n + 1]),
                            waits=[(s_dtp[h], ld_done[0]), (cdv, LW.get(("dab", ab), 0))],
                            inc=cac)
                        dbx = p.put("gpsimd", lambda e, i=i, n=n, ab=ab, ip=ip: e.tensor_mul(
                            dbxb[:, ab * 512:(ab + 1) * 512],
                            dtxb[:, ip * 512:ip * 512 + 512],
                            Bcb[:, n * 512:(n + 1) * 512]),
                            waits=[(cdv, dtx), (s_bcb, bc_done[0]),
                                   (cdv, LW.get(("dbxb", ab), 0))], inc=cpl)
                        sc = p.put("vector", lambda e, i=i, n=n, nn=nn, ab=ab: e.tensor_tensor_scan(
                            strip[:, nn * 512:(nn + 1) * 512],
                            dab[:, ab * 512:(ab + 1) * 512],
                            dbxb[:, ab * 512:(ab + 1) * 512],
                            state[:, i * DS + n:i * DS + n + 1], OP.mult, OP.add),
                            waits=[(cac, da), (cpl, dbx)], inc=cdv)
                        LW[("dab", ab)] = sc
                        LW[("dbxb", ab)] = sc
                        last_sc = sc
                    p.put("vector", lambda e, i=i, g=g: e.tensor_copy(
                        state[:, i * DS + g * 4: i * DS + g * 4 + 4],
                        bass.AP(strip, 511, [[2048, 128], [512, 4]])),
                        waits=[(cdv, last_sc)], inc=cdv)
                    for nn in range(4):
                        n = g * 4 + nn
                        yp = n % 2
                        yt = p.put("vector", lambda e, n=n, nn=nn, yp=yp: e.tensor_mul(
                            ytmp[:, yp * 512:(yp + 1) * 512],
                            strip[:, nn * 512:(nn + 1) * 512],
                            Ccb[:, n * 512:(n + 1) * 512]),
                            waits=[(s_ccb, bc_done[1]),
                                   (cpl, LW.get(("ytmp", yp), 0))], inc=cdv)
                        if n == 0:
                            ya = p.put("gpsimd", lambda e, ip=ip: e.tensor_copy(
                                ybuf[:, ip * 512:ip * 512 + 512], ytmp[:, 0:512]),
                                waits=[(cdv, yt), (s_yb[ip], LW.get(("ybuf", ip), 0))],
                                inc=cpl)
                        else:
                            ya = p.put("gpsimd", lambda e, n=n, ip=ip, yp=yp: e.tensor_add(
                                ybuf[:, ip * 512:ip * 512 + 512],
                                ybuf[:, ip * 512:ip * 512 + 512],
                                ytmp[:, yp * 512:(yp + 1) * 512]),
                                waits=[(cdv, yt)], inc=cpl)
                        LW[("ytmp", yp)] = ya
                ysp = p.put("gpsimd", lambda e, c=c, i=i, ip=ip: e.dma_start(
                    bass.AP(y_s, (i * 128) * TOK + c * 512, [[TOK, 128], [1, 512]]),
                    ybuf[:, ip * 512:ip * 512 + 512]),
                    inc=s_yb[ip], inck=16)
                LW[("ybuf", ip)] = ysp
                LW[("dtxb", ip)] = cpl.n
            if i == 15:
                LW["Bcb_free"] = cpl.n
                LW["Ccb_free"] = cdv.n

        sc_state = {"c": 0, "i": 0, "h0": False}

        def emit_scan_piece():
            c, i = sc_state["c"], sc_state["i"]
            if c >= NCH:
                return
            if not sc_state["h0"]:
                emit_scan_half_loads(0, 0)
                emit_scan_half_loads(0, 1)
                emit_bc_loads(0)
                sc_state["h0"] = True
            emit_scan_dtile(c, i)
            if i == 7 and c + 1 < NCH:
                emit_scan_half_loads(c + 1, 0)
            if i == 15:
                if c + 1 < NCH:
                    emit_scan_half_loads(c + 1, 1)
                    emit_bc_loads(c + 1)
                sc_state["c"], sc_state["i"] = c + 1, 0
            else:
                sc_state["i"] = i + 1

        if B3A_MODE != 0:
            if B3A_MODE == 2:
                emit_scan_half_loads(0, 0)
                emit_scan_half_loads(0, 1)
                emit_bc_loads(0)
                sc_state["h0"] = True
            for qt in range(4):
                xnh_rdy = emit_xnh_load(qt)
                for u in range(64):
                    emit_ffn_unit(qt, u, xnh_rdy)
                    if B3A_MODE == 2 and (u % 2 == 1):
                        emit_scan_piece()
                LW["xnh_free"] = cpe.n
        while sc_state["c"] < NCH:
            emit_scan_piece()
        stp = p.put("scalar", lambda e: e.dma_start(st_c.ap(), state[:]),
                    waits=[(cdv, cdv.n), (cpl, cpl.n)], inc=s_stc, inck=16)
        p.put("gpsimd", lambda e: e.collective_compute(
            "AllGather", OP.bypass, replica_groups=[list(range(8))],
            ins=[st_c.ap()], outs=[st_g.ap()]),
            waits=[(s_stc, stp)], inc=ccc)
        p.put("gpsimd", lambda e: e.collective_compute(
            "AllGather", OP.bypass, replica_groups=[list(range(8))],
            ins=[st_c.ap()], outs=[st_g.ap()]),
            inc=ccc)
        p.finish_dma()
        p.emit(nc)
    bstack.close()
    QACT_B3A = qact.n

    if upto < 4:
        return _finish_debug(nc, dbg_outs, stack, qsp, qact, ccc)
    p = Plan()
    LW = {}
    s_hin = dsem(p, "s_hin"); s_ybl = dsem(p, "s_ybl"); s_sc2 = dsem(p, "s_sc2")
    s_cc2 = dsem(p, "s_cc2"); s_ybw = dsem(p, "s_ybw"); s_int = dsem(p, "s_int")
    s_rb = [dsem(p, f"s_rb{i}") for i in range(4)]
    s_gw = dsem(p, "s_gw")
    bstack = ExitStack()
    _a = lambda nm, shape, dt: bstack.enter_context(nc.sbuf_tensor(nm, shape, dt))
    interc = _a("interc", [128, 128 * 512], BF16)
    rbk = _a("rbk", [128, 4 * 1024], BF16)
    tnb = _a("tnb", [128, 512], F32)
    gvb = _a("gvb", [128, 512], F32)
    ybig = _a("ybig", [128, ND * 512], F32)
    Sc2 = _a("Sc2", [128, 512], F32)
    Eb = _a("Eb", [128, 512], F32)
    ctmp = _a("ctmp", [128, 2 * 512], F32)
    Ccb2 = _a("Ccb2", [128, DS * 512], BF16)
    psD = bstack.enter_context(nc.psum_tensor("psD", [128, 8 * 512], F32))
    if True:
        hin_ready = 0
        for r in range(8):
            hl = p.put("sync", lambda e, r=r: e.dma_start(
                hin_st[:, 0:256],
                st_g[r * 128:(r + 1) * 128, :]),
                waits=[(ccc, AGS_DONE), (cdv, hin_ready)], inc=s_hin, inck=16)
            hin_ready = p.put("vector", lambda e, r=r: e.scalar_tensor_tensor(
                hin[:], hin_st[:, 0:256], sel8_sb[:, r:r + 1], hin[:], OP.mult, OP.add),
                waits=[(s_hin, hl)], inc=cdv)

        def emit_corr_chunk(c):
            y_ld = 0
            for part in range(4):
                y_ld = p.put("scalar", lambda e, c=c, part=part: e.dma_start(
                    bass.AP(ybig, part * 4 * 512, [[ND * 512, 128], [512, 4], [1, 512]]),
                    bass.AP(y_s, (part * 4 * 128) * TOK + c * 512,
                            [[TOK, 128], [128 * TOK, 4], [1, 512]])),
                    waits=[(cpl, LW.get("ybig_free", 0)),
                           (s_ybw, LW.get("ybw_done", 0))],
                    inc=s_ybl, inck=16)
            cld = 0
            for n in range(DS):
                cld = p.put("sync", lambda e, c=c, n=n: e.dma_start(
                    Ccb2[:, n * 512:(n + 1) * 512],
                    bass.AP(bc_s, (16 + n) * TOK + c * 512, [[0, 128], [1, 512]])),
                    waits=[(cdv, LW.get("Ccb2_free", 0))], inc=s_cc2, inck=16)
            for i in range(ND):
                ip = i % 2
                sld = p.put("scalar", lambda e, c=c, i=i: e.dma_start(
                    Sc2[:, 0:512],
                    bass.AP(ssum_s, (i * 128) * TOK + c * 512, [[TOK, 128], [1, 512]])),
                    waits=[(cac, LW.get(("Sc2", 0), 0))],
                    inc=s_sc2, inck=16)
                for n in range(DS):
                    eb = n % 2
                    ev = p.put("scalar", lambda e, i=i, n=n: e.activation(
                        Eb[:, 0:512],
                        Sc2[:, 0:512], AF.Exp,
                        scale=aneg[:, i * DS + n:i * DS + n + 1]),
                        waits=[(s_sc2, sld), (cdv, LW.get(("Eb", 0), 0))], inc=cac)
                    tv = p.put("vector", lambda e, i=i, n=n, eb=eb: e.scalar_tensor_tensor(
                        ctmp[:, eb * 512:(eb + 1) * 512],
                        Eb[:, 0:512],
                        hin[:, i * DS + n:i * DS + n + 1],
                        Ccb2[:, n * 512:(n + 1) * 512], OP.mult, OP.mult),
                        waits=[(cac, ev), (cdv, hin_ready), (s_cc2, cld),
                               (cpl, LW.get(("ctmp", eb), 0))], inc=cdv)
                    LW[("Eb", 0)] = tv
                    ya = p.put("gpsimd", lambda e, i=i, eb=eb: e.tensor_add(
                        ybig[:, i * 512:(i + 1) * 512], ybig[:, i * 512:(i + 1) * 512],
                        ctmp[:, eb * 512:(eb + 1) * 512]),
                        waits=[(cdv, tv), (s_ybl, y_ld)], inc=cpl)
                    LW[("ctmp", eb)] = ya
                LW[("Sc2", 0)] = cac.n
            LW["Ccb2_free"] = cdv.n
            for part in range(4):
                p.put("scalar", lambda e, c=c, part=part: e.dma_start(
                    bass.AP(y_s, (part * 4 * 128) * TOK + c * 512,
                            [[TOK, 128], [128 * TOK, 4], [1, 512]]),
                    bass.AP(ybig, part * 4 * 512, [[ND * 512, 128], [512, 4], [1, 512]])),
                    waits=[(cpl, cpl.n)], inc=s_ybw, inck=16)
            LW["ybig_free"] = cpl.n
            LW["ybw_done"] = s_ybw.n

        def emit_down_tchunk(c2):
            i_ld = 0
            for part in range(16):
                i_ld = p.put("scalar", lambda e, c2=c2, part=part: e.dma_start(
                    bass.AP(interc, part * 8 * 512, [[128 * 512, 128], [512, 8], [1, 512]]),
                    bass.AP(inter_s, (part * 8 * 128) * TOK + c2 * 512,
                            [[TOK, 128], [128 * TOK, 8], [1, 512]])),
                    waits=[(cpe, LW.get("interc_free", 0))],
                    inc=s_int, inck=16)
            for hh in range(2):
                last_mm = 0
                for nt in range(128):
                    rpar = nt % 4
                    rld = p.put("sync", lambda e, nt=nt, hh=hh, rpar=rpar: e.dma_start(
                        rbk[:, rpar * 1024:(rpar + 1) * 1024],
                        bass.AP(gW["d"], (nt * 128) * 2048 + hh * 1024,
                                [[2048, 128], [1, 1024]])),
                        waits=[(ccc, AGD_DONE), (cpe, LW.get(("rbk", rpar), 0))],
                        inc=s_rb[rpar], inck=16)
                    for h8 in range(8):
                        wl = [(s_rb[rpar], rld), (s_int, i_ld)]
                        if nt == 0 and h8 == 0:
                            wl.append((cdv, LW.get("psD_free", 0)))
                        last_mm = p.put("tensor", lambda e, nt=nt, h8=h8, rpar=rpar: e.matmul(
                            psD[:, h8 * 512:(h8 + 1) * 512],
                            rbk[:, rpar * 1024 + h8 * 128: rpar * 1024 + (h8 + 1) * 128],
                            interc[:, nt * 512:(nt + 1) * 512],
                            start=(nt == 0), stop=(nt == 127)),
                            waits=wl, inc=(cpe if h8 == 7 else None))
                    LW[("rbk", rpar)] = last_mm
                for h8 in range(8):
                    tnv = p.put("scalar", lambda e, h8=h8: e.activation(
                        tnb[:, 0:512], psD[:, h8 * 512:(h8 + 1) * 512],
                        AF.Tanh, scale=0.5),
                        waits=[(cpe, last_mm), (cdv, LW.get(("tnb", 0), 0))], inc=cac)
                    gv = p.put("vector", lambda e, h8=h8: e.scalar_tensor_tensor(
                        gvb[:, 0:512], tnb[:, 0:512],
                        1.0, psD[:, h8 * 512:(h8 + 1) * 512], OP.add, OP.mult),
                        waits=[(cac, tnv), (s_gw, LW.get(("gvb", 0), 0))], inc=cdv)
                    LW[("tnb", 0)] = gv
                    gsp = p.put("scalar", lambda e, c2=c2, hh=hh, h8=h8: e.dma_start(
                        bass.AP(g_s, ((hh * 8 + h8) * 128) * TOK + c2 * 512,
                                [[TOK, 128], [1, 512]]),
                        gvb[:, 0:512]),
                        waits=[(cdv, gv)], inc=s_gw, inck=16)
                    LW[("gvb", 0)] = gsp
                LW["psD_free"] = cdv.n
            LW["interc_free"] = cpe.n

        corr_next = 0
        for c2 in range(NCH):
            if B3B_MODE != 1:
                emit_down_tchunk(c2)
            while B3B_MODE != 0 and corr_next < NCH and corr_next < (c2 + 1) * 2:
                emit_corr_chunk(corr_next)
                corr_next += 1
        p.finish_dma()
        p.emit(nc)
    bstack.close()
    QACT_B3B = qact.n

    if upto < 5:
        return _finish_debug(nc, dbg_outs, stack, qsp, qact, ccc)
    p = Plan()
    LW = {}
    s_wo = dsem(p, "s_wo")
    s_ym = [dsem(p, "s_ym0"), dsem(p, "s_ym1")]
    s_gb = dsem(p, "s_gb")
    s_x4 = [dsem(p, "s_x40"), dsem(p, "s_x41")]
    s_out = dsem(p, "s_out")
    bstack = ExitStack()
    _a = lambda nm, shape, dt: bstack.enter_context(nc.sbuf_tensor(nm, shape, dt))
    woutT = _a("woutT", [128, ND * 2048], BF16)
    ysm = _a("ysm", [128, 2 * 512], F32)
    ybf = _a("ybf", [128, ND * 512], BF16)
    gbig = _a("gbig", [128, ND * 512], F32)
    sA = _a("sA", [128, 2 * 512], F32)
    ubig = _a("ubig", [128, ND * 512], F32)
    utok = _a("utok", [128, 2 * 2048], F32)
    xt4 = _a("xt4", [128, 2 * 2048], F32)
    ob = _a("ob", [128, 2048], F32)
    psS = bstack.enter_context(nc.psum_tensor("psS", [128, 4 * 512], F32))
    psTa = bstack.enter_context(nc.psum_tensor("psTa", [128, 128], F32))
    psTb = bstack.enter_context(nc.psum_tensor("psTb", [128, 128], F32))
    psTP = [psTa, psTb]
    if True:
        wo_ready = 0
        for r in range(8):
            wo_ready = p.put("sync", lambda e, r=r: e.dma_start(
                bass.AP(woutT, r * 256, [[ND * 2048, 128], [2048, ND], [1, 256]]),
                bass.AP(g1, (2048 + r * 4096) * 256, [[256, 128], [128 * 256, ND], [1, 256]])),
                inc=s_wo, inck=16)
        for c in range(NCH):
            g_ld4 = 0
            for part in range(4):
                g_ld4 = p.put("scalar", lambda e, c=c, part=part: e.dma_start(
                    bass.AP(gbig, part * 4 * 512, [[ND * 512, 128], [512, 4], [1, 512]]),
                    bass.AP(g_s, (part * 4 * 128) * TOK + c * 512,
                            [[TOK, 128], [128 * TOK, 4], [1, 512]])),
                    waits=[(cdv, LW.get("gbig_free", 0))],
                    inc=s_gb, inck=16)
            cast_done = 0
            for i in range(ND):
                ipp = i % 2
                yld1 = p.put("scalar", lambda e, c=c, i=i, ipp=ipp: e.dma_start(
                    ysm[:, ipp * 512:(ipp + 1) * 512],
                    bass.AP(y_s, (i * 128) * TOK + c * 512, [[TOK, 128], [1, 512]])),
                    waits=[(cac, LW.get(("ysm", ipp), 0))],
                    inc=s_ym[ipp], inck=16)
                cast_done = p.put("scalar", lambda e, i=i, ipp=ipp: e.activation(
                    ybf[:, i * 512:(i + 1) * 512], ysm[:, ipp * 512:(ipp + 1) * 512],
                    AF.Copy),
                    waits=[(s_ym[ipp], yld1), (cpe, LW.get("ybf_free", 0))], inc=cac)
                LW[("ysm", ipp)] = cast_done
            for mt in range(ND):
                bank = mt % 4
                last_mm = 0
                for k in range(ND):
                    wl = [(s_wo, wo_ready), (cac, cast_done)]
                    if k == 0:
                        wl.append((cdv, LW.get(("psS", bank), 0)))
                    p.put("tensor", lambda e, mt=mt, k=k, bank=bank: e.matmul(
                        psS[:, bank * 512:(bank + 1) * 512],
                        woutT[:, k * 2048 + mt * 128: k * 2048 + (mt + 1) * 128],
                        ybf[:, k * 512:(k + 1) * 512],
                        start=(k == 0), stop=(k == ND - 1)),
                        waits=wl, inc=None)
                last_mm = p.put("tensor", lambda e: e.drain(), inc=cpe)
                sev = p.put("scalar", lambda e, mt=mt, bank=bank: e.activation(
                    sA[:, (mt % 2) * 512:(mt % 2) * 512 + 512],
                    psS[:, bank * 512:(bank + 1) * 512], AF.Identity,
                    bias=outb_sb[:, mt:mt + 1]),
                    waits=[(cpe, last_mm), (cdv, LW.get(("sA", mt % 2), 0))], inc=cac)
                uv = p.put("vector", lambda e, mt=mt: e.scalar_tensor_tensor(
                    ubig[:, mt * 512:(mt + 1) * 512],
                    sA[:, (mt % 2) * 512:(mt % 2) * 512 + 512], 0.5,
                    gbig[:, mt * 512:(mt + 1) * 512], OP.mult, OP.mult),
                    waits=[(cac, sev), (s_gb, g_ld4), (cpe, LW.get("ubig_free", 0))],
                    inc=cdv)
                LW[("sA", mt % 2)] = uv
                LW[("psS", bank)] = uv
            u_done = cdv.n
            LW["gbig_free"] = u_done
            for tau in range(4):
                tp = tau % 2
                xld = p.put("sync", lambda e, c=c, tau=tau, tp=tp: e.dma_start(
                    xt4[:, tp * 2048:(tp + 1) * 2048],
                    x_in[c * 512 + tau * 128: c * 512 + (tau + 1) * 128, :]),
                    waits=[(cdv, LW.get(("xt4", tp), 0))], inc=s_x4[tp], inck=16)
                ev_last = 0
                for i0 in range(0, ND, 2):
                    for j in range(2):
                        i = i0 + j
                        p.put("tensor", lambda e, i=i, sl=j, tau=tau: e.transpose(
                            psTP[sl][:, :],
                            bass.AP(ubig, i * 512 + tau * 128, [[ND * 512, 128], [1, 128]]),
                            idf32[:]),
                            waits=[(cdv, u_done), (cac, LW.get(("psT", j), 0))], inc=None)
                    dr = p.put("tensor", lambda e: e.drain(), inc=cpe)
                    for j in range(2):
                        i = i0 + j
                        ev_last = p.put("scalar", lambda e, i=i, sl=j, tp=tp: e.copy(
                            utok[:, tp * 2048 + i * 128: tp * 2048 + (i + 1) * 128],
                            psTP[sl][:, :]),
                            waits=[(cpe, dr), (cdv, LW.get(("utok", tp), 0))], inc=cac)
                        LW[("psT", j)] = ev_last
                oadd = p.put("vector", lambda e, tp=tp: e.tensor_add(
                    ob[:, 0:2048], xt4[:, tp * 2048:(tp + 1) * 2048],
                    utok[:, tp * 2048:(tp + 1) * 2048]),
                    waits=[(cac, ev_last), (s_x4[tp], xld), (s_out, LW.get(("ob", 0), 0))],
                    inc=cdv)
                LW[("xt4", tp)] = oadd
                LW[("utok", tp)] = oadd
                ost = p.put("scalar", lambda e, c=c, tau=tau: e.dma_start(
                    out_sh[c * 512 + tau * 128: c * 512 + (tau + 1) * 128, :],
                    ob[:, 0:2048]),
                    waits=[(cdv, oadd)], inc=s_out, inck=16)
                LW[("ob", 0)] = ost
            LW["ybf_free"] = cpe.n
            LW["ubig_free"] = cpe.n
        p.finish_dma()
        p.emit(nc)
    bstack.close()

    stack.close()
    nc.finalize()
    return nc


def _finish_debug(nc, dbg_outs, stack, qsp, qact, ccc=None):
    p = Plan()
    if ccc is not None and ccc.n:
        p.put("sync", lambda e: e.wait_ge(ccc.h, ccc.n) and None)
    n = 0
    for od, ih in dbg_outs:
        rows = ih.shape[0]
        step = max(1, min(rows, (1 << 20) // max(1, int(ih.shape[1]))))
        r = 0
        while r < rows:
            rr = min(step, rows - r)
            n = p.put("sync", lambda e, od=od, ih=ih, r=r, rr=rr: e.dma_start(
                od[r:r + rr, :], ih[r:r + rr, :]),
                waits=[(qsp, qsp.n), (qact, qact.n)], inc=qsp, inck=16)
            r += rr
    p.put("sync", lambda e: e.wait_ge(qsp.h, qsp.n) and None)
    p.put("scalar", lambda e: e.wait_ge(qact.h, qact.n) and None)
    p.emit(nc)
    stack.close()
    nc.finalize()
    return nc


_CACHE = {}


def kernel(L0_attn_x, L0_attn_A_log, L0_attn_x_proj_w, L0_attn_x_proj_b,
           L0_attn_dt_proj_w, L0_attn_dt_proj_b, L0_attn_out_proj_w,
           L0_attn_out_proj_b, L0_ffn_gate_W_packed, L0_ffn_gate_scales,
           L0_ffn_up_W_packed, L0_ffn_up_scales, L0_ffn_down_W_packed,
           L0_ffn_down_scales):
    x = np.ascontiguousarray(np.asarray(L0_attn_x, dtype=np.float32))
    if "nc" not in _CACHE:
        _CACHE["nc"] = build_program()
    nc = _CACHE["nc"]

    gwp = np.asarray(L0_ffn_gate_W_packed, dtype=np.int32)
    gsc = np.asarray(L0_ffn_gate_scales, dtype=np.float32)
    uwp = np.asarray(L0_ffn_up_W_packed, dtype=np.int32)
    usc = np.asarray(L0_ffn_up_scales, dtype=np.float32)
    dwp = np.asarray(L0_ffn_down_W_packed, dtype=np.int32)
    dsc = np.asarray(L0_ffn_down_scales, dtype=np.float32)
    wdt = np.asarray(L0_attn_dt_proj_w, dtype=np.float32)
    wout = np.asarray(L0_attn_out_proj_w, dtype=np.float32)

    in_maps = []
    for c in range(NCORE):
        b, hf = c // 2, c % 2
        sel = np.zeros((128, 8), np.float32)
        if hf == 1:
            sel[:, c - 1] = 1.0
        in_maps.append({
            "x_sh": np.ascontiguousarray(x[b, hf * TOK:(hf + 1) * TOK, :]),
            "gwp": np.ascontiguousarray(gwp[c * 2048:(c + 1) * 2048, :]),
            "gsc": np.ascontiguousarray(gsc[c * 2048:(c + 1) * 2048, :]),
            "uwp": np.ascontiguousarray(uwp[c * 2048:(c + 1) * 2048, :]),
            "usc": np.ascontiguousarray(usc[c * 2048:(c + 1) * 2048, :]),
            "dwp": np.ascontiguousarray(dwp[:, c * 1024:(c + 1) * 1024]),
            "dsc": np.ascontiguousarray(dsc[:, c * 64:(c + 1) * 64]),
            "wdt_sh": np.ascontiguousarray(wdt[c * 256:(c + 1) * 256, :]),
            "wout_sh": np.ascontiguousarray(wout[c * 256:(c + 1) * 256, :]),
            "alog": np.ascontiguousarray(np.asarray(L0_attn_A_log, np.float32)),
            "xpw": np.ascontiguousarray(np.asarray(L0_attn_x_proj_w, np.float32)),
            "xpb": np.ascontiguousarray(np.asarray(L0_attn_x_proj_b, np.float32)),
            "dtb": np.ascontiguousarray(np.asarray(L0_attn_dt_proj_b, np.float32)),
            "outb": np.ascontiguousarray(np.asarray(L0_attn_out_proj_b, np.float32)),
            "sel8": sel,
        })
    res = run_bass_kernel_spmd(nc, in_maps, core_ids=list(range(NCORE)))
    out = np.empty((B_, L_, D), np.float32)
    for c in range(NCORE):
        b, hf = c // 2, c % 2
        out[b, hf * TOK:(hf + 1) * TOK, :] = res.results[c]["out_sh"]
    return out


# version 04
# speedup vs baseline: 1.1775x; 1.0985x over previous
import numpy as np
from contextlib import ExitStack

import concourse.bass as bass
import concourse.mybir as mybir
from concourse.bass_utils import run_bass_kernel_spmd

F32 = mybir.dt.float32
BF16 = mybir.dt.bfloat16
I32 = mybir.dt.int32
AF = mybir.ActivationFunctionType
OP = mybir.AluOpType
AX = mybir.AxisListType

NCORE = 8
B_, L_, D, DS, DM = 4, 8192, 2048, 16, 16384
TOK = 4096
CH = 512
NCH = TOK // CH
QT = 1024
ND = D // 128
GELU_C = 0.7978845608028654
GELU_A2 = 0.21145925545
EPS = 1e-5

ENGS = ("sync", "scalar", "tensor", "vector", "gpsimd")
DEQ_NW = 3
DEQ_NJT = 16
B3B_MODE = 2
B3A_MODE = 2


class Ctr:
    def __init__(self, stack, nc, name):
        self.h = stack.enter_context(nc.semaphore(name))
        self.n = 0
        self.name = name


class Plan:
    def __init__(self):
        self.streams = {e: [] for e in ENGS}
        self.lastwait = {e: {} for e in ENGS}
        self.dsems = []

    def finish_dma(self, extra=()):
        for ctr in list(self.dsems) + list(extra):
            if ctr.n:
                self.put("sync", (lambda e, c=ctr, v=ctr.n: e.wait_ge(c.h, v) and None))

    def put(self, eng, fn, waits=(), inc=None, inck=1):
        lw = self.lastwait[eng]
        w2 = []
        for ctr, val in waits:
            if val is None or val <= 0:
                continue
            if lw.get(ctr.name, -1) >= val:
                continue
            lw[ctr.name] = val
            w2.append((ctr, val))
        if inc is not None:
            inc.n += inck
        self.streams[eng].append((fn, w2, inc, inck))
        return None if inc is None else inc.n

    def emit(self, nc):
        with nc.Block() as blk:
            def runner(items):
                def run(e):
                    for fn, waits, inc, inck in items:
                        for ctr, val in waits:
                            e.wait_ge(ctr.h, val)
                        inst = fn(e)
                        if inc is not None:
                            inst.then_inc(inc.h, inck)
                return run
            for eng in ENGS:
                if self.streams[eng]:
                    getattr(blk, eng)(runner(self.streams[eng]))


def build_program(upto=9, dbg=(), b1part=9):
    nc = bass.Bass("TRN2", target_bir_lowering=False)

    x_in = nc.dram_tensor("x_sh", [TOK, D], F32, kind="ExternalInput")
    wp_in = {w: nc.dram_tensor(f"{w}wp", [2048, 1024], I32, kind="ExternalInput")
             for w in ("g", "u", "d")}
    sc_in = {w: nc.dram_tensor(f"{w}sc", [2048, 64], F32, kind="ExternalInput")
             for w in ("g", "u", "d")}
    wdt_sh = nc.dram_tensor("wdt_sh", [256, 2048], F32, kind="ExternalInput")
    wout_sh = nc.dram_tensor("wout_sh", [256, 2048], F32, kind="ExternalInput")
    alog = nc.dram_tensor("alog", [D, DS], F32, kind="ExternalInput")
    xpw = nc.dram_tensor("xpw", [32, D], F32, kind="ExternalInput")
    xpb_in = nc.dram_tensor("xpb", [32], F32, kind="ExternalInput")
    dtb_in = nc.dram_tensor("dtb", [D], F32, kind="ExternalInput")
    outb_in = nc.dram_tensor("outb", [D], F32, kind="ExternalInput")
    sel8_in = nc.dram_tensor("sel8", [128, 8], F32, kind="ExternalInput")
    out_sh = nc.dram_tensor("out_sh", [TOK, D], F32, kind="ExternalOutput")

    xn_s = nc.dram_tensor("xn_s", [D, TOK], BF16)
    dt_s = nc.dram_tensor("dt_s", [D, TOK], F32)
    ssum_s = nc.dram_tensor("ssum_s", [D, TOK], F32)
    bc_s = nc.dram_tensor("bc_s", [32, TOK], BF16)
    y_s = nc.dram_tensor("y_s", [D, TOK], F32)
    inter_s = nc.dram_tensor("inter_s", [DM, TOK], BF16)
    g_s = nc.dram_tensor("g_s", [D, TOK], F32)
    c1 = nc.dram_tensor("c1", [4096, 256], BF16)
    g1 = nc.dram_tensor("g1", [8 * 4096, 256], BF16)
    cW = {w: nc.dram_tensor(f"c{w}", [2048, 2048], BF16) for w in ("g", "u", "d")}
    gW = {w: nc.dram_tensor(f"gw{w}", [8 * 2048, 2048], BF16) for w in ("g", "u", "d")}
    st_c = nc.dram_tensor("st_c", [128, 256], F32)
    st_g = nc.dram_tensor("st_g", [8 * 128, 256], F32)
    scr = dict(xn_s=xn_s, dt_s=dt_s, ssum_s=ssum_s, bc_s=bc_s, y_s=y_s,
               inter_s=inter_s, g_s=g_s, c1=c1, g1=g1, st_c=st_c, st_g=st_g)
    scr.update({f"c{w}": cW[w] for w in cW})
    scr.update({f"gw{w}": gW[w] for w in gW})
    dbg_outs = []
    for nm in dbg:
        h = scr[nm]
        dbg_outs.append((nc.dram_tensor(f"dbg_{nm}", list(h.shape), h.dtype,
                                        kind="ExternalOutput"), h))

    stack = ExitStack()
    sb = lambda nm, shape, dt: stack.enter_context(nc.sbuf_tensor(nm, shape, dt))

    aneg = sb("aneg", [128, ND * DS], F32)
    dtb_sb = sb("dtb_sb", [128, ND], F32)
    outb_sb = sb("outb_sb", [128, ND], F32)
    xpb_sb = sb("xpb_sb", [32, 1], F32)
    sel8_sb = sb("sel8_sb", [128, 8], F32)
    state = sb("state", [128, ND * DS], F32)
    sstate = sb("sstate", [128, ND], F32)
    hin = sb("hin", [128, ND * DS], F32)
    hin_st = sb("hin_st", [128, ND * DS], F32)
    idbf = sb("idbf", [128, 128], BF16)
    idf32 = sb("idf32", [128, 128], F32)
    zeros = sb("zeros", [128, CH], F32)
    wxpT = sb("wxpT", [128, ND * 128], BF16)
    csts = sb("csts", [128, 4], F32)

    qsp = Ctr(stack, nc, "qsp")
    qact = Ctr(stack, nc, "qact")

    def dsem(p, name):
        c = Ctr(stack, nc, name)
        p.dsems.append(c)
        return c

    cpe = Ctr(stack, nc, "cpe")
    cdv = Ctr(stack, nc, "cdv")
    cac = Ctr(stack, nc, "cac")
    cpl = Ctr(stack, nc, "cpl")
    ccc = Ctr(stack, nc, "ccc")

    p = Plan()
    s_c0 = dsem(p, "s_c0")
    for i in range(ND):
        p.put("sync", lambda e, i=i: e.dma_start(
            aneg[:, i * DS:(i + 1) * DS], alog[i * 128:(i + 1) * 128, :]), inc=s_c0, inck=16)
        p.put("sync", lambda e, i=i: e.dma_start(
            dtb_sb[:, i:i + 1], bass.AP(dtb_in, i * 128, [[1, 128], [1, 1]])), inc=s_c0, inck=16)
        p.put("sync", lambda e, i=i: e.dma_start(
            outb_sb[:, i:i + 1], bass.AP(outb_in, i * 128, [[1, 128], [1, 1]])), inc=s_c0, inck=16)
    p.put("sync", lambda e: e.dma_start(xpb_sb[:, :], bass.AP(xpb_in, 0, [[1, 32], [1, 1]])),
          inc=s_c0, inck=16)
    p.put("sync", lambda e: e.dma_start(sel8_sb[:, :], sel8_in[:, :]), inc=s_c0, inck=16)
    ld_done = s_c0.n
    p.put("gpsimd", lambda e: e.memset(csts[:, 0:1], -4.0))
    p.put("gpsimd", lambda e: e.memset(csts[:, 1:2], -6.0))
    p.put("gpsimd", lambda e: e.memset(csts[:, 2:3], EPS))
    p.put("gpsimd", lambda e: e.memset(csts[:, 3:4], 1.0))
    p.put("gpsimd", lambda e: e.memset(idbf[:], 0.0))
    p.put("gpsimd", lambda e: e.affine_select(out=idbf[:], in_=idbf[:], compare_op=OP.not_equal,
                                              fill=1.0, base=0, pattern=[[-1, 128]],
                                              channel_multiplier=1))
    p.put("gpsimd", lambda e: e.memset(idf32[:], 0.0))
    p.put("gpsimd", lambda e: e.affine_select(out=idf32[:], in_=idf32[:], compare_op=OP.not_equal,
                                              fill=1.0, base=0, pattern=[[-1, 128]],
                                              channel_multiplier=1))
    p.put("gpsimd", lambda e: e.memset(zeros[:], 0.0))
    p.put("gpsimd", lambda e: e.memset(state[:], 0.0))
    p.put("gpsimd", lambda e: e.memset(sstate[:], 0.0))
    p.put("gpsimd", lambda e: e.memset(hin[:], 0.0))
    p.put("gpsimd", lambda e: e.memset(wxpT[:], 0.0))
    a_exp = p.put("scalar", lambda e: e.activation(aneg[:], aneg[:], AF.Exp),
                  waits=[(s_c0, ld_done)], inc=cac)
    p.put("vector", lambda e: e.tensor_scalar_mul(aneg[:], aneg[:], -1.0),
          waits=[(cac, a_exp)], inc=cdv)
    p.finish_dma()
    p.emit(nc)

    wstack = ExitStack()
    wdtT = wstack.enter_context(nc.sbuf_tensor("wdtT", [128, ND * 2048], BF16))
    if upto < 1:
        return _finish_debug(nc, dbg_outs, stack, qsp, qact, ccc)
    p = Plan()
    LW = {}
    s_xpw = dsem(p, "s_xpw"); s_w4 = dsem(p, "s_w4"); s_stg1 = dsem(p, "s_stg1")
    s_wp = [dsem(p, "s_wp0"), dsem(p, "s_wp1")]
    s_sc = [dsem(p, "s_sc0"), dsem(p, "s_sc1")]
    s_sg = [dsem(p, "s_sg0"), dsem(p, "s_sg1")]
    s_wdt = dsem(p, "s_wdt")
    bstack = ExitStack()
    _a = lambda nm, shape, dt: bstack.enter_context(nc.sbuf_tensor(nm, shape, dt))
    w4 = _a("w4", [128, 2 * 2048], F32)
    wb4 = _a("wb4", [128, 2 * 2048], BF16)
    stg1 = _a("stg1", [128, 256], BF16)
    xpw_f = _a("xpw_f", [32, 2048], F32)
    xpw_b = _a("xpw_b", [32, 2048], BF16)
    wpt = _a("wpt", [128, 2 * 1024], I32)
    sct = _a("sct", [128, 2 * 64], F32)
    loi = _a("loi", [128, 1024], I32)
    hii = _a("hii", [128, 1024], I32)
    pf = _a("pf", [128, 2 * 1024], F32)
    ge8 = _a("ge8", [128, 1024], F32)
    cc_ = _a("cc_", [128, 1024], F32)
    r1_ = _a("r1_", [128, 1024], F32)
    r2_ = _a("r2_", [128, 1024], F32)
    t1_ = _a("t1_", [128, 1024], F32)
    mm_ = _a("mm_", [128, 1024], F32)
    sg_ = _a("sg_", [128, 1024], F32)
    vv_ = _a("vv_", [128, 1024], F32)
    deq = _a("deq", [128, 2 * 2048], BF16)
    stg2 = _a("stg2", [128, 2 * 2048], BF16)
    pstrA = bstack.enter_context(nc.psum_tensor("pstrA", [128, 128], BF16))
    pstrB = bstack.enter_context(nc.psum_tensor("pstrB", [128, 128], BF16))
    pstrP = [pstrA, pstrB]
    if True:
        v = p.put("sync", lambda e: e.dma_start(xpw_f[:, :], xpw[:, :]), inc=s_xpw, inck=16)
        cast = p.put("vector", lambda e: e.tensor_copy(xpw_b[:], xpw_f[:]),
                     waits=[(s_xpw, v)], inc=cdv)
        for i0 in range(0, ND, 2):
            for j in range(2):
                i = i0 + j
                p.put("tensor", lambda e, i=i, sl=j: e.transpose(
                    pstrP[sl][:, 0:32],
                    xpw_b[:, i * 128:(i + 1) * 128], idbf[0:32, 0:32]),
                    waits=[(cdv, cast), (cac, LW.get(("pstr", j), 0))], inc=None)
            dr = p.put("tensor", lambda e: e.drain(), inc=cpe)
            for j in range(2):
                i = i0 + j
                ev = p.put("scalar", lambda e, i=i, sl=j: e.copy(
                    wxpT[:, i * 128:i * 128 + 32], pstrP[sl][:, 0:32]),
                    waits=[(cpe, dr)], inc=cac)
                LW[("pstr", j)] = ev

        for wsrc, c1row in (((wdt_sh, 0), (wout_sh, 2048)) if b1part >= 1 else ()):
            lds = []
            for j in range(2):
                lds.append(p.put("sync", lambda e, j=j, wsrc=wsrc: e.dma_start(
                    w4[:, j * 2048:(j + 1) * 2048], wsrc[j * 128:(j + 1) * 128, :]),
                    waits=[(cpe, LW.get("w4_free", 0))], inc=s_w4, inck=16))
            cast = p.put("vector", lambda e: e.tensor_copy(wb4[:], w4[:]),
                         waits=[(s_w4, lds[-1]), (cpe, LW.get("w4_free", 0))], inc=cdv)
            for i in range(ND):
                for j in range(2):
                    p.put("tensor", lambda e, i=i, j=j: e.transpose(
                        pstrP[j][:, :],
                        wb4[:, j * 2048 + i * 128: j * 2048 + (i + 1) * 128], idbf[:]),
                        waits=[(cdv, cast), (cac, LW.get(("pstr", j), 0))], inc=None)
                dr = p.put("tensor", lambda e: e.drain(), inc=cpe)
                ev = None
                for j in range(2):
                    ev = p.put("scalar", lambda e, j=j: e.copy(
                        stg1[:, j * 128:(j + 1) * 128], pstrP[j][:, :]),
                        waits=[(cpe, dr), (s_stg1, LW.get("stg1_free", 0))], inc=cac)
                    LW[("pstr", j)] = ev
                std = p.put("scalar", lambda e, i=i, c1row=c1row: e.dma_start(
                    c1[c1row + i * 128: c1row + (i + 1) * 128, :], stg1[:]),
                    waits=[(cac, ev)], inc=s_stg1, inck=16)
                LW["stg1_free"] = std
            LW["w4_free"] = cpe.n
        if b1part >= 2:
            p.put("gpsimd", lambda e: e.collective_compute(
                "AllGather", OP.bypass, replica_groups=[list(range(8))],
                ins=[c1.ap()], outs=[g1.ap()]),
                waits=[(s_stg1, s_stg1.n)], inc=ccc)

        for sl in range(2):
            LW[("pstr", sl)] = cac.n
        _wl = ("g", "u", "d") if b1part >= 3 else ()
        _wl = _wl[:DEQ_NW] if b1part in (3, 4, 5) else _wl
        for w in _wl:
            for jt in range(DEQ_NJT if b1part in (3, 4, 5) else 16):
                par = jt % 2
                ldw = p.put("sync", lambda e, w=w, jt=jt, par=par: e.dma_start(
                    wpt[:, par * 1024:(par + 1) * 1024],
                    wp_in[w][jt * 128:(jt + 1) * 128, :]),
                    waits=[(cdv, LW.get(("wpt", par), 0))], inc=s_wp[par], inck=16)
                lds = p.put("sync", lambda e, w=w, jt=jt, par=par: e.dma_start(
                    sct[:, par * 64:(par + 1) * 64],
                    sc_in[w][jt * 128:(jt + 1) * 128, :]),
                    waits=[(cdv, LW.get(("sct", par), 0))], inc=s_sc[par], inck=16)
                lo = p.put("vector", lambda e, par=par: e.tensor_scalar(
                    loi[:], wpt[:, par * 1024:(par + 1) * 1024], 15, None, OP.bitwise_and),
                    waits=[(s_wp[par], ldw), (cdv, LW.get("loi_free", 0))], inc=cdv)
                hi = p.put("vector", lambda e, par=par: e.tensor_scalar(
                    hii[:], wpt[:, par * 1024:(par + 1) * 1024], 4, None,
                    OP.logical_shift_right), inc=cdv)
                LW[("wpt", par)] = hi
                p.put("vector", lambda e: e.tensor_copy(pf[:, 0:1024], loi[:]), inc=cdv)
                cast2 = p.put("vector", lambda e: e.tensor_copy(pf[:, 1024:2048], hii[:]),
                              inc=cdv)
                LW["loi_free"] = cast2
                last_scale = 0
                for pi in range(2):
                    g8 = p.put("vector", lambda e, pi=pi: e.tensor_scalar(
                        ge8[:], pf[:, pi * 1024:(pi + 1) * 1024], 8.0, None, OP.is_ge),
                        waits=[(cpl, LW.get("ge8_free", 0))], inc=cdv)
                    cc1 = p.put("vector", lambda e, pi=pi: e.scalar_tensor_tensor(
                        cc_[:], ge8[:], -8.0, pf[:, pi * 1024:(pi + 1) * 1024],
                        OP.mult, OP.add),
                        waits=[(cpl, LW.get("cc_free", 0)), (cac, LW.get("cc_free_a", 0))],
                        inc=cdv)
                    ra = p.put("scalar", lambda e: e.activation(
                        r1_[:], cc_[:], AF.Relu, bias=csts[:, 0:1]),
                        waits=[(cdv, cc1), (cpl, LW.get("r1_free", 0))], inc=cac)
                    rb = p.put("scalar", lambda e: e.activation(
                        r2_[:], cc_[:], AF.Relu, bias=csts[:, 1:2]),
                        waits=[(cdv, LW.get("r2_free", 0))], inc=cac)
                    tt1 = p.put("gpsimd", lambda e: e.tensor_add(t1_[:], cc_[:], r1_[:]),
                                waits=[(cac, ra), (cdv, LW.get("t1_free", 0))], inc=cpl)
                    LW["cc_free"] = tt1
                    LW["cc_free_a"] = rb
                    LW["r1_free"] = tt1
                    mmv = p.put("vector", lambda e: e.scalar_tensor_tensor(
                        mm_[:], t1_[:], 0.5, r2_[:], OP.mult, OP.add),
                        waits=[(cpl, tt1), (cac, rb), (cpl, LW.get("mm_free", 0))], inc=cdv)
                    LW["t1_free"] = mmv
                    LW["r2_free"] = mmv
                    sgv = p.put("gpsimd", lambda e: e.tensor_scalar(
                        sg_[:], ge8[:], -2.0, 1.0, OP.mult, OP.add),
                        waits=[(cpl, LW.get("sg_free", 0))], inc=cpl)
                    LW["ge8_free"] = sgv
                    vvv = p.put("gpsimd", lambda e: e.tensor_mul(vv_[:], mm_[:], sg_[:]),
                                waits=[(cdv, mmv), (cdv, LW.get("vv_free", 0))], inc=cpl)
                    LW["mm_free"] = vvv
                    LW["sg_free"] = vvv
                    last_scale = p.put("vector", lambda e, pi=pi, par=par: e.tensor_tensor(
                        bass.AP(deq, par * 2048 + pi, [[4096, 128], [2, 1024]]),
                        vv_[:],
                        bass.AP(sct, par * 64, [[128, 128], [1, 64], [0, 16]]),
                        OP.mult),
                        waits=[(cpl, vvv), (s_sc[par], lds), (cpe, LW.get(("deq", par), 0))],
                        inc=cdv)
                    LW["vv_free"] = last_scale
                LW[("sct", par)] = last_scale
                evs = 0
                for i0 in (range(0, ND, 2) if b1part >= 4 else ()):
                    for j in range(2):
                        i = i0 + j
                        p.put("tensor", lambda e, i=i, par=par, sl=j: e.transpose(
                            pstrP[sl][:, :],
                            bass.AP(deq, par * 2048 + i * 128, [[4096, 128], [1, 128]]),
                            idbf[:]),
                            waits=[(cdv, last_scale), (cac, LW.get(("pstr", j), 0))],
                            inc=None)
                    dr = p.put("tensor", lambda e: e.drain(), inc=cpe)
                    evs = 0
                    for j in range(2):
                        i = i0 + j
                        evs = p.put("scalar", lambda e, i=i, par=par, sl=j: e.copy(
                            stg2[:, par * 2048 + i * 128: par * 2048 + (i + 1) * 128],
                            pstrP[sl][:, :]),
                            waits=[(cpe, dr), (s_sg[par], LW.get(("stg2", par), 0))], inc=cac)
                        LW[("pstr", j)] = evs
                LW[("deq", par)] = cpe.n
                if b1part < 4:
                    LW[("deq", par)] = 0
                    continue
                std = 0
                if b1part < 5:
                    LW[("stg2", par)] = 0
                    continue
                for q4 in range(4):
                    std = p.put("scalar", lambda e, w=w, jt=jt, par=par, q4=q4: e.dma_start(
                        bass.AP(cW[w], jt * 128 + q4 * 4 * 128 * 2048,
                                [[2048, 128], [128 * 2048, 4], [1, 128]]),
                        bass.AP(stg2, par * 2048 + q4 * 4 * 128,
                                [[4096, 128], [128, 4], [1, 128]])),
                        waits=[(cac, evs)], inc=s_sg[par], inck=16)
                LW[("stg2", par)] = std
            if b1part >= 6:
                p.put("gpsimd", lambda e, w=w: e.collective_compute(
                    "AllGather", OP.bypass, replica_groups=[list(range(8))],
                    ins=[cW[w].ap()], outs=[gW[w].ap()]),
                    waits=[(s_sg[0], s_sg[0].n), (s_sg[1], s_sg[1].n)], inc=ccc)
        wdt_lds = 0
        for r in range(8):
            wdt_lds = p.put("sync", lambda e, r=r: e.dma_start(
                bass.AP(wdtT, r * 256, [[ND * 2048, 128], [2048, ND], [1, 256]]),
                bass.AP(g1, (r * 4096) * 256, [[256, 128], [128 * 256, ND], [1, 256]])),
                waits=[(ccc, 1)], inc=s_wdt, inck=16)
        WDT_READY = (s_wdt, wdt_lds)
        p.finish_dma()
        p.emit(nc)
    bstack.close()
    AG1_DONE, AGG_DONE, AGU_DONE, AGD_DONE, AGS_DONE = 1, 2, 3, 4, 6

    if upto < 2:
        return _finish_debug(nc, dbg_outs, stack, qsp, qact, ccc)
    p = Plan()
    LW = {}
    s_xt = [dsem(p, "s_xt0"), dsem(p, "s_xt1")]
    s_xnsp = dsem(p, "s_xnsp")
    s_dtv = [dsem(p, "s_dtv0"), dsem(p, "s_dtv1")]
    s_bcv = dsem(p, "s_bcv")
    bstack = ExitStack()
    _a = lambda nm, shape, dt: bstack.enter_context(nc.sbuf_tensor(nm, shape, dt))
    xtok = _a("xtok", [128, 2 * 2048], F32)
    sqb = _a("sqb", [128, 2048], F32)
    lnst = _a("lnst", [128, 8], F32)
    xnt = _a("xnt", [128, 2 * 2048], BF16)
    xnc = _a("xnc", [128, ND * 512], BF16)
    spb = _a("spb", [128, 4 * 512], F32)
    dtv = _a("dtv", [128, 2 * 512], F32)
    bcv = _a("bcv", [32, 512], BF16)
    ps_trA = bstack.enter_context(nc.psum_tensor("ps_trA", [128, 128], BF16))
    ps_trB = bstack.enter_context(nc.psum_tensor("ps_trB", [128, 128], BF16))
    ps_trP = [ps_trA, ps_trB]
    ps_dt = bstack.enter_context(nc.psum_tensor("ps_dt", [128, 4 * 512], F32))
    ps_xp = bstack.enter_context(nc.psum_tensor("ps_xp", [128, 512], F32))
    if True:
        s_wdtS, wdt_ready = WDT_READY

        for c in range(NCH):
            xn_evs = 0
            for tau in range(4):
                par = tau % 2
                xl = p.put("sync", lambda e, c=c, tau=tau, par=par: e.dma_start(
                    xtok[:, par * 2048:(par + 1) * 2048],
                    x_in[c * 512 + tau * 128: c * 512 + (tau + 1) * 128, :]),
                    waits=[(cdv, LW.get(("xtok", par), 0))], inc=s_xt[par], inck=16)
                sq = p.put("scalar", lambda e, par=par: e.activation(
                    sqb[:], xtok[:, par * 2048:(par + 1) * 2048], AF.Square),
                    waits=[(s_xt[par], xl), (cdv, LW.get("sqb_free", 0))], inc=cac)
                s0v = p.put("vector", lambda e, par=par: e.reduce_sum(
                    lnst[:, 0:1], xtok[:, par * 2048:(par + 1) * 2048], AX.X),
                    waits=[(s_xt[par], xl)], inc=cdv)
                s1 = p.put("vector", lambda e: e.reduce_sum(lnst[:, 1:2], sqb[:], AX.X),
                           waits=[(cac, sq)], inc=cdv)
                LW["sqb_free"] = s1
                mean = p.put("vector", lambda e: e.tensor_scalar_mul(
                    lnst[:, 2:3], lnst[:, 0:1], 1.0 / 2048),
                    waits=[(cdv, s0v)], inc=cdv)
                tch1 = p.put("vector", lambda e: e.tensor_copy(lnst[:, 7:8], lnst[:, 2:3]),
                             waits=[(cdv, mean)], inc=cdv)
                m2 = p.put("scalar", lambda e: e.activation(
                    lnst[:, 3:4], lnst[:, 2:3], AF.Square), waits=[(cdv, tch1)], inc=cac)
                t2 = p.put("scalar", lambda e: e.copy(lnst[:, 7:8], lnst[:, 3:4]),
                           waits=[(cac, m2)], inc=cac)
                vr = p.put("vector", lambda e: e.scalar_tensor_tensor(
                    lnst[:, 4:5], lnst[:, 1:2], 1.0 / 2048, lnst[:, 3:4],
                    OP.mult, OP.subtract), waits=[(cac, t2), (cdv, s1)], inc=cdv)
                tch3 = p.put("vector", lambda e: e.tensor_copy(lnst[:, 7:8], lnst[:, 4:5]),
                             waits=[(cdv, vr)], inc=cdv)
                lnv = p.put("scalar", lambda e: e.activation(
                    lnst[:, 5:6], lnst[:, 4:5], AF.Ln, bias=csts[:, 2:3]),
                    waits=[(cdv, tch3)], inc=cac)
                lt = p.put("scalar", lambda e: e.copy(lnst[:, 7:8], lnst[:, 5:6]),
                           waits=[(cac, lnv)], inc=cac)
                rs = p.put("scalar", lambda e: e.activation(
                    lnst[:, 6:7], lnst[:, 5:6], AF.Exp, scale=-0.5),
                    waits=[(cac, lt)], inc=cac)
                t4 = p.put("scalar", lambda e: e.copy(lnst[:, 7:8], lnst[:, 6:7]),
                           waits=[(cac, rs)], inc=cac)
                xn1 = p.put("vector", lambda e, par=par: e.tensor_scalar(
                    xnt[:, par * 2048:(par + 1) * 2048],
                    xtok[:, par * 2048:(par + 1) * 2048],
                    lnst[:, 2:3], lnst[:, 6:7], OP.subtract, OP.mult),
                    waits=[(cac, t4)], inc=cdv)
                LW[("xtok", par)] = xn1
                for i0 in range(0, ND, 2):
                    for j in range(2):
                        i = i0 + j
                        p.put("tensor", lambda e, i=i, sl=j, par=par: e.transpose(
                            ps_trP[sl][:, :],
                            xnt[:, par * 2048 + i * 128: par * 2048 + (i + 1) * 128],
                            idbf[:]),
                            waits=[(cdv, xn1), (cac, LW.get(("ps_tr", j), 0)),
                                   (s_xnsp, LW.get("xnc_freeq", 0))], inc=None)
                    dr = p.put("tensor", lambda e: e.drain(), inc=cpe)
                    for j in range(2):
                        i = i0 + j
                        xn_evs = p.put("scalar", lambda e, i=i, sl=j, tau=tau: e.copy(
                            xnc[:, i * 512 + tau * 128: i * 512 + (tau + 1) * 128],
                            ps_trP[sl][:, :]),
                            waits=[(cpe, dr)], inc=cac)
                        LW[("ps_tr", j)] = xn_evs
            xn_sp = 0
            for part in range(4):
                xn_sp = p.put("scalar", lambda e, c=c, part=part: e.dma_start(
                    bass.AP(xn_s, (part * 4 * 128) * TOK + c * 512,
                            [[TOK, 128], [128 * TOK, 4], [1, 512]]),
                    bass.AP(xnc, part * 4 * 512, [[ND * 512, 128], [512, 4], [1, 512]])),
                    waits=[(cac, xn_evs)], inc=s_xnsp, inck=16)
            LW["xnc_freeq"] = xn_sp

            last_mm = 0
            for k in range(ND):
                wl = [(cac, xn_evs)]
                if k == 0:
                    wl.append((cac, LW.get("ps_xp", 0)))
                p.put("tensor", lambda e, k=k: e.matmul(
                    ps_xp[:], wxpT[:, k * 128:(k + 1) * 128], xnc[:, k * 512:(k + 1) * 512],
                    start=(k == 0), stop=(k == ND - 1)), waits=wl, inc=None)
            last_mm = p.put("tensor", lambda e: e.drain(), inc=cpe)
            LW["xnc_freep"] = last_mm
            bce = p.put("scalar", lambda e: e.activation(
                bcv[:, :], ps_xp[0:32, :], AF.Identity, bias=xpb_sb[:, :]),
                waits=[(cpe, last_mm), (s_bcv, LW.get("bcv_free", 0))], inc=cac)
            LW["ps_xp"] = bce
            bcd = p.put("scalar", lambda e, c=c: e.dma_start(
                bass.AP(bc_s, c * 512, [[TOK, 32], [1, 512]]), bcv[:, :]),
                waits=[(cac, bce)], inc=s_bcv, inck=16)
            LW["bcv_free"] = bcd
            for et in range(ND):
                bank = et % 4
                last_mm = 0
                for k in range(ND):
                    wl = [(s_wdtS, wdt_ready), (cac, xn_evs)]
                    if k == 0:
                        wl.append((cdv, LW.get(("ps_dt", bank), 0)))
                        wl.append((cac, LW.get(("ps_dt_a", bank), 0)))
                    p.put("tensor", lambda e, et=et, k=k, bank=bank: e.matmul(
                        ps_dt[:, bank * 512:(bank + 1) * 512],
                        wdtT[:, k * 2048 + et * 128: k * 2048 + (et + 1) * 128],
                        xnc[:, k * 512:(k + 1) * 512],
                        start=(k == 0), stop=(k == ND - 1)),
                        waits=wl, inc=None)
                last_mm = p.put("tensor", lambda e: e.drain(), inc=cpe)
                sp = bank % 2
                a1 = p.put("scalar", lambda e, et=et, bank=bank, sp=sp: e.activation(
                    spb[:, sp * 1024:sp * 1024 + 512],
                    ps_dt[:, bank * 512:(bank + 1) * 512], AF.Abs,
                    bias=dtb_sb[:, et:et + 1]),
                    waits=[(cpe, last_mm), (cdv, LW.get(("spb", sp), 0))], inc=cac)
                p.put("scalar", lambda e, sp=sp: e.activation(
                    spb[:, sp * 1024 + 512:sp * 1024 + 1024],
                    spb[:, sp * 1024:sp * 1024 + 512], AF.Exp, scale=-1.0), inc=cac)
                p.put("scalar", lambda e, sp=sp: e.activation(
                    spb[:, sp * 1024:sp * 1024 + 512],
                    spb[:, sp * 1024 + 512:sp * 1024 + 1024], AF.Ln, bias=csts[:, 3:4]), inc=cac)
                r1v = p.put("scalar", lambda e, et=et, bank=bank, sp=sp: e.activation(
                    spb[:, sp * 1024 + 512:sp * 1024 + 1024],
                    ps_dt[:, bank * 512:(bank + 1) * 512], AF.Relu,
                    bias=dtb_sb[:, et:et + 1]), inc=cac)
                LW[("ps_dt_a", bank)] = r1v
                dtc = p.put("vector", lambda e, et=et, sp=sp: e.tensor_add(
                    dtv[:, (et % 2) * 512:(et % 2) * 512 + 512],
                    spb[:, sp * 1024:sp * 1024 + 512],
                    spb[:, sp * 1024 + 512:sp * 1024 + 1024]),
                    waits=[(cac, r1v), (s_dtv[et % 2], LW.get(("dtv", et % 2), 0))], inc=cdv)
                LW[("ps_dt", bank)] = dtc
                LW[("spb", sp)] = dtc
                dsp = p.put("scalar", lambda e, c=c, et=et: e.dma_start(
                    bass.AP(dt_s, (et * 128) * TOK + c * 512, [[TOK, 128], [1, 512]]),
                    dtv[:, (et % 2) * 512:(et % 2) * 512 + 512]),
                    waits=[(cdv, dtc)], inc=s_dtv[et % 2], inck=16)
                LW[("dtv", et % 2)] = dsp
        p.emit(nc)
    bstack.close()
    wstack.close()
    QACT_B2 = qact.n

    if upto < 3:
        return _finish_debug(nc, dbg_outs, stack, qsp, qact, ccc)
    p = Plan()
    LW = {}
    s_xnh = dsem(p, "s_xnh")
    s_gs = [dsem(p, "s_gs0"), dsem(p, "s_gs1")]
    s_us = [dsem(p, "s_us0"), dsem(p, "s_us1")]
    s_dtp = [dsem(p, "s_dtp0"), dsem(p, "s_dtp1")]
    s_xnp = [dsem(p, "s_xnp0"), dsem(p, "s_xnp1")]
    s_bcb = dsem(p, "s_bcb"); s_ccb = dsem(p, "s_ccb")
    s_sb = [dsem(p, "s_sb0"), dsem(p, "s_sb1")]
    s_yb = [dsem(p, "s_yb0"), dsem(p, "s_yb1")]
    s_insp = [dsem(p, "s_insp0"), dsem(p, "s_insp1")]
    s_stc = dsem(p, "s_stc")
    bstack = ExitStack()
    _a = lambda nm, shape, dt: bstack.enter_context(nc.sbuf_tensor(nm, shape, dt))
    xnh = _a("xnh", [128, ND * 1024], BF16)
    gsl = _a("gsl", [128, 2 * ND * 256], BF16)
    usl = _a("usl", [128, 2 * ND * 256], BF16)
    qb = _a("qb", [128, 512], F32)
    pb = _a("pb", [128, 512], F32)
    tb = _a("tb", [128, 512], F32)
    rb = _a("rb", [128, 512], F32)
    inters = _a("inters", [128, 2 * 1024], BF16)
    dtc_s = _a("dtc_s", [128, ND * 512], F32)
    xncs = _a("xncs", [128, ND * 512], BF16)
    dtxb = _a("dtxb", [128, 2 * 512], F32)
    Sb = _a("Sb", [128, 2 * 512], F32)
    dab = _a("dab", [128, 2 * 512], F32)
    dbxb = _a("dbxb", [128, 2 * 512], F32)
    strip = _a("strip", [128, 4 * 512], F32)
    Bcb = _a("Bcb", [128, DS * 512], BF16)
    Ccb = _a("Ccb", [128, DS * 512], BF16)
    ytmp = _a("ytmp", [128, 2 * 512], F32)
    ybuf = _a("ybuf", [128, 2 * 512], F32)
    ps8 = bstack.enter_context(nc.psum_tensor("ps8", [128, 8 * 512], F32))
    if True:
        def emit_xnh_load(qt):
            last = 0
            for i in range(ND):
                last = p.put("sync", lambda e, qt=qt, i=i: e.dma_start(
                    xnh[:, i * 1024:(i + 1) * 1024],
                    bass.AP(xn_s, (i * 128) * TOK + qt * 1024, [[TOK, 128], [1, 1024]])),
                    waits=[(cpe, LW.get("xnh_free", 0))],
                    inc=s_xnh, inck=16)
            return last

        mt_count = [0]

        def emit_ffn_unit(qt, u, xnh_ready, filler=None):
            r, q8 = u >> 3, u & 7
            par = u % 2
            g_ld = p.put("sync", lambda e, r=r, q8=q8, par=par: e.dma_start(
                bass.AP(gsl, par * ND * 256, [[2 * ND * 256, 128], [256, ND], [1, 256]]),
                bass.AP(gW["g"], (r * 2048) * 2048 + q8 * 256,
                        [[2048, 128], [128 * 2048, ND], [1, 256]])),
                waits=[(ccc, AGG_DONE), (cpe, LW.get(("gsl", par), 0))],
                inc=s_gs[par], inck=16)
            u_ld = p.put("sync", lambda e, r=r, q8=q8, par=par: e.dma_start(
                bass.AP(usl, par * ND * 256, [[2 * ND * 256, 128], [256, ND], [1, 256]]),
                bass.AP(gW["u"], (r * 2048) * 2048 + q8 * 256,
                        [[2048, 128], [128 * 2048, ND], [1, 256]])),
                waits=[(ccc, AGU_DONE), (cpe, LW.get(("usl", par), 0))],
                inc=s_us[par], inck=16)
            for s in range(2):
                m128 = r * 16 + q8 * 2 + s
                mpar = mt_count[0] % 2
                mt_count[0] += 1
                gbank, ubank = mpar * 4, mpar * 4 + 2
                last_g = last_u = 0
                for k in range(ND):
                    for tch in range(2):
                        wl = [(s_gs[par], g_ld), (s_xnh, xnh_ready)]
                        if k == 0 and tch == 0:
                            wl.append((cdv, LW.get(("geluA", mpar), 0)))
                        last_g = p.put("tensor", lambda e, k=k, tch=tch, s=s, par=par,
                                       gbank=gbank: e.matmul(
                            ps8[:, (gbank + tch) * 512:(gbank + tch + 1) * 512],
                            gsl[:, par * ND * 256 + k * 256 + s * 128:
                                par * ND * 256 + k * 256 + s * 128 + 128],
                            xnh[:, k * 1024 + tch * 512: k * 1024 + (tch + 1) * 512],
                            start=(k == 0), stop=(k == ND - 1)),
                            waits=wl, inc=None)
                for k in range(ND):
                    for tch in range(2):
                        wl = [(s_us[par], u_ld)]
                        if k == 0 and tch == 0:
                            wl.append((cdv, LW.get(("geluB", mpar), 0)))
                        p.put("tensor", lambda e, k=k, tch=tch, s=s, par=par,
                              ubank=ubank: e.matmul(
                            ps8[:, (ubank + tch) * 512:(ubank + tch + 1) * 512],
                            usl[:, par * ND * 256 + k * 256 + s * 128:
                                par * ND * 256 + k * 256 + s * 128 + 128],
                            xnh[:, k * 1024 + tch * 512: k * 1024 + (tch + 1) * 512],
                            start=(k == 0), stop=(k == ND - 1)),
                            waits=wl, inc=None)
                last_u = p.put("tensor", lambda e: e.drain(), inc=cpe)
                for tch in range(2):
                    qv = p.put("scalar", lambda e, gbank=gbank, tch=tch: e.activation(
                        qb[:], ps8[:, (gbank + tch) * 512:(gbank + tch + 1) * 512],
                        AF.Square, scale=GELU_A2),
                        waits=[(cpe, last_u), (cdv, LW.get("qb_free", 0))], inc=cac)
                    pv = p.put("vector", lambda e, gbank=gbank, tch=tch: e.scalar_tensor_tensor(
                        pb[:], qb[:], 1.0,
                        ps8[:, (gbank + tch) * 512:(gbank + tch + 1) * 512],
                        OP.add, OP.mult),
                        waits=[(cac, qv), (cac, LW.get("pb_free", 0))], inc=cdv)
                    LW["qb_free"] = pv
                    tv = p.put("scalar", lambda e: e.activation(
                        tb[:], pb[:], AF.Tanh, scale=GELU_C),
                        waits=[(cdv, pv), (cdv, LW.get("tb_free", 0))], inc=cac)
                    LW["pb_free"] = tv
                    rv = p.put("vector", lambda e, gbank=gbank, tch=tch: e.scalar_tensor_tensor(
                        rb[:], tb[:], 1.0,
                        ps8[:, (gbank + tch) * 512:(gbank + tch + 1) * 512],
                        OP.add, OP.mult),
                        waits=[(cac, tv), (cdv, LW.get("rb_free", 0))], inc=cdv)
                    LW["tb_free"] = rv
                    iv = p.put("vector", lambda e, ubank=ubank, tch=tch, s=s: e.scalar_tensor_tensor(
                        inters[:, (s % 2) * 1024 + tch * 512:(s % 2) * 1024 + (tch + 1) * 512],
                        rb[:], 0.5,
                        ps8[:, (ubank + tch) * 512:(ubank + tch + 1) * 512],
                        OP.mult, OP.mult),
                        waits=[(s_insp[s % 2], LW.get(("inters", s % 2), 0))], inc=cdv)
                    LW["rb_free"] = iv
                LW[("geluA", mpar)] = cdv.n
                LW[("geluB", mpar)] = cdv.n
                isp = p.put("scalar", lambda e, m128=m128, qt=qt, s=s: e.dma_start(
                    bass.AP(inter_s, (m128 * 128) * TOK + qt * 1024, [[TOK, 128], [1, 1024]]),
                    inters[:, (s % 2) * 1024:(s % 2) * 1024 + 1024]),
                    waits=[(cdv, cdv.n)], inc=s_insp[s % 2], inck=16)
                LW[("inters", s % 2)] = isp
                if filler is not None:
                    filler()
            LW[("gsl", par)] = cpe.n
            LW[("usl", par)] = cpe.n

        def emit_scan_half_loads(c, h):
            for part in (2 * h, 2 * h + 1):
                p.put("scalar", lambda e, c=c, part=part: e.dma_start(
                    bass.AP(dtc_s, part * 4 * 512, [[ND * 512, 128], [512, 4], [1, 512]]),
                    bass.AP(dt_s, (part * 4 * 128) * TOK + c * 512,
                            [[TOK, 128], [128 * TOK, 4], [1, 512]])),
                    waits=[(cdv, LW.get(("dtcF", h), 0)),
                           (cac, LW.get(("dtcFa", h), 0))], inc=s_dtp[h], inck=16)
                p.put("scalar", lambda e, c=c, part=part: e.dma_start(
                    bass.AP(xncs, part * 4 * 512, [[ND * 512, 128], [512, 4], [1, 512]]),
                    bass.AP(xn_s, (part * 4 * 128) * TOK + c * 512,
                            [[TOK, 128], [128 * TOK, 4], [1, 512]])),
                    waits=[(cdv, LW.get(("dtcF", h), 0))], inc=s_xnp[h], inck=16)

        def emit_bc_loads(c):
            for n in range(DS):
                p.put("gpsimd", lambda e, c=c, n=n: e.dma_start(
                    Bcb[:, n * 512:(n + 1) * 512],
                    bass.AP(bc_s, n * TOK + c * 512, [[0, 128], [1, 512]])),
                    waits=[(cpl, LW.get("Bcb_free", 0))], inc=s_bcb, inck=16)
                p.put("gpsimd", lambda e, c=c, n=n: e.dma_start(
                    Ccb[:, n * 512:(n + 1) * 512],
                    bass.AP(bc_s, (16 + n) * TOK + c * 512, [[0, 128], [1, 512]])),
                    waits=[(cdv, LW.get("Ccb_free", 0))], inc=s_ccb, inck=16)

        def emit_scan_dtile_head(c, i):
            h = i // 8
            ld_done = (32 * (c + 1), 32 * (c + 1))
            ip = i % 2
            dtx = p.put("vector", lambda e, i=i, ip=ip: e.tensor_mul(
                dtxb[:, ip * 512:ip * 512 + 512],
                dtc_s[:, i * 512:(i + 1) * 512], xncs[:, i * 512:(i + 1) * 512]),
                waits=[(s_dtp[h], ld_done[0]), (s_xnp[h], ld_done[1]),
                       (cpl, LW.get(("dtxb", ip), 0))], inc=cdv)
            ssc = p.put("vector", lambda e, i=i, ip=ip: e.tensor_tensor_scan(
                Sb[:, ip * 512:ip * 512 + 512],
                dtc_s[:, i * 512:(i + 1) * 512], zeros[:, 0:512],
                sstate[:, i:i + 1], OP.add, OP.add),
                waits=[(s_sb[ip], LW.get(("Sb", ip), 0))], inc=cdv)
            p.put("vector", lambda e, i=i, ip=ip: e.tensor_copy(
                sstate[:, i:i + 1], Sb[:, ip * 512 + 511:ip * 512 + 512]),
                waits=[(cdv, ssc)], inc=cdv)
            ssp = p.put("gpsimd", lambda e, c=c, i=i, ip=ip: e.dma_start(
                bass.AP(ssum_s, (i * 128) * TOK + c * 512, [[TOK, 128], [1, 512]]),
                Sb[:, ip * 512:ip * 512 + 512]),
                waits=[(cdv, ssc)], inc=s_sb[ip], inck=16)
            LW[("Sb", ip)] = ssp
            if i == 7 or i == 15:
                LW[("dtcF", h)] = cdv.n
                LW[("dtcFa", h)] = cac.n
            sc_state["cur"] = {"dtx": dtx, "ip": ip}

        def emit_scan_dtile_g(c, i, g):
            h = i // 8
            ld_done = (32 * (c + 1), 32 * (c + 1))
            bc_done = (256 * (c + 1), 256 * (c + 1))
            dtx = sc_state["cur"]["dtx"]
            ip = sc_state["cur"]["ip"]
            for nn in range(4):
                n = g * 4 + nn
                ab = n % 2
                da = p.put("scalar", lambda e, i=i, n=n, ab=ab: e.activation(
                    dab[:, ab * 512:(ab + 1) * 512],
                    dtc_s[:, i * 512:(i + 1) * 512], AF.Exp,
                    scale=aneg[:, i * DS + n:i * DS + n + 1]),
                    waits=[(s_dtp[h], ld_done[0]), (cdv, LW.get(("dab", ab), 0))],
                    inc=cac)
                dbx = p.put("gpsimd", lambda e, i=i, n=n, ab=ab, ip=ip: e.tensor_mul(
                    dbxb[:, ab * 512:(ab + 1) * 512],
                    dtxb[:, ip * 512:ip * 512 + 512],
                    Bcb[:, n * 512:(n + 1) * 512]),
                    waits=[(cdv, dtx), (s_bcb, bc_done[0]),
                           (cdv, LW.get(("dbxb", ab), 0))], inc=cpl)
                sc = p.put("vector", lambda e, i=i, n=n, nn=nn, ab=ab: e.tensor_tensor_scan(
                    strip[:, nn * 512:(nn + 1) * 512],
                    dab[:, ab * 512:(ab + 1) * 512],
                    dbxb[:, ab * 512:(ab + 1) * 512],
                    state[:, i * DS + n:i * DS + n + 1], OP.mult, OP.add),
                    waits=[(cac, da), (cpl, dbx)], inc=cdv)
                LW[("dab", ab)] = sc
                LW[("dbxb", ab)] = sc
                last_sc = sc
            p.put("vector", lambda e, i=i, g=g: e.tensor_copy(
                state[:, i * DS + g * 4: i * DS + g * 4 + 4],
                bass.AP(strip, 511, [[2048, 128], [512, 4]])),
                waits=[(cdv, last_sc)], inc=cdv)
            for nn in range(4):
                n = g * 4 + nn
                yp = n % 2
                yt = p.put("vector", lambda e, n=n, nn=nn, yp=yp: e.tensor_mul(
                    ytmp[:, yp * 512:(yp + 1) * 512],
                    strip[:, nn * 512:(nn + 1) * 512],
                    Ccb[:, n * 512:(n + 1) * 512]),
                    waits=[(s_ccb, bc_done[1]),
                           (cpl, LW.get(("ytmp", yp), 0))], inc=cdv)
                if n == 0:
                    ya = p.put("gpsimd", lambda e, ip=ip: e.tensor_copy(
                        ybuf[:, ip * 512:ip * 512 + 512], ytmp[:, 0:512]),
                        waits=[(cdv, yt), (s_yb[ip], LW.get(("ybuf", ip), 0))],
                        inc=cpl)
                else:
                    ya = p.put("gpsimd", lambda e, n=n, ip=ip, yp=yp: e.tensor_add(
                        ybuf[:, ip * 512:ip * 512 + 512],
                        ybuf[:, ip * 512:ip * 512 + 512],
                        ytmp[:, yp * 512:(yp + 1) * 512]),
                        waits=[(cdv, yt)], inc=cpl)
                LW[("ytmp", yp)] = ya
            if g == 3:
                ysp = p.put("gpsimd", lambda e, c=c, i=i, ip=ip: e.dma_start(
                    bass.AP(y_s, (i * 128) * TOK + c * 512, [[TOK, 128], [1, 512]]),
                    ybuf[:, ip * 512:ip * 512 + 512]),
                    inc=s_yb[ip], inck=16)
                LW[("ybuf", ip)] = ysp
                LW[("dtxb", ip)] = cpl.n
                if i == 15:
                    LW["Bcb_free"] = cpl.n
                    LW["Ccb_free"] = cdv.n

        sc_state = {"c": 0, "i": 0, "ph": 0, "h0": False, "cur": {}}

        def emit_scan_heads0():
            emit_scan_half_loads(0, 0)
            emit_scan_half_loads(0, 1)
            emit_bc_loads(0)
            sc_state["h0"] = True

        def emit_scan_micro():
            c, i, ph = sc_state["c"], sc_state["i"], sc_state["ph"]
            if c >= NCH:
                return
            if not sc_state["h0"]:
                emit_scan_heads0()
            if ph == 0:
                emit_scan_dtile_head(c, i)
                sc_state["ph"] = 1
                return
            emit_scan_dtile_g(c, i, ph - 1)
            if ph < 4:
                sc_state["ph"] = ph + 1
                return
            sc_state["ph"] = 0
            if i == 7 and c + 1 < NCH:
                emit_scan_half_loads(c + 1, 0)
            if i == 15:
                if c + 1 < NCH:
                    emit_scan_half_loads(c + 1, 1)
                    emit_bc_loads(c + 1)
                sc_state["c"], sc_state["i"] = c + 1, 0
            else:
                sc_state["i"] = i + 1

        if B3A_MODE != 0:
            filler = emit_scan_micro if B3A_MODE == 2 else None
            if B3A_MODE == 2:
                emit_scan_heads0()
            for qt in range(4):
                xnh_rdy = emit_xnh_load(qt)
                for u in range(64):
                    emit_ffn_unit(qt, u, xnh_rdy, filler)
                    if B3A_MODE == 2:
                        emit_scan_micro()
                LW["xnh_free"] = cpe.n
        while sc_state["c"] < NCH:
            emit_scan_micro()
        stp = p.put("scalar", lambda e: e.dma_start(st_c.ap(), state[:]),
                    waits=[(cdv, cdv.n), (cpl, cpl.n)], inc=s_stc, inck=16)
        p.put("gpsimd", lambda e: e.collective_compute(
            "AllGather", OP.bypass, replica_groups=[list(range(8))],
            ins=[st_c.ap()], outs=[st_g.ap()]),
            waits=[(s_stc, stp)], inc=ccc)
        p.put("gpsimd", lambda e: e.collective_compute(
            "AllGather", OP.bypass, replica_groups=[list(range(8))],
            ins=[st_c.ap()], outs=[st_g.ap()]),
            inc=ccc)
        p.finish_dma()
        p.emit(nc)
    bstack.close()
    QACT_B3A = qact.n

    if upto < 4:
        return _finish_debug(nc, dbg_outs, stack, qsp, qact, ccc)
    p = Plan()
    LW = {}
    s_hin = dsem(p, "s_hin"); s_ybl = dsem(p, "s_ybl"); s_sc2 = dsem(p, "s_sc2")
    s_cc2 = dsem(p, "s_cc2"); s_ybw = dsem(p, "s_ybw"); s_int = dsem(p, "s_int")
    s_rb = [dsem(p, f"s_rb{i}") for i in range(4)]
    s_gw = dsem(p, "s_gw")
    bstack = ExitStack()
    _a = lambda nm, shape, dt: bstack.enter_context(nc.sbuf_tensor(nm, shape, dt))
    interc = _a("interc", [128, 128 * 512], BF16)
    rbk = _a("rbk", [128, 4 * 1024], BF16)
    tnb = _a("tnb", [128, 512], F32)
    gvb = _a("gvb", [128, 512], F32)
    ybig = _a("ybig", [128, ND * 512], F32)
    Sc2 = _a("Sc2", [128, 512], F32)
    Eb = _a("Eb", [128, 512], F32)
    ctmp = _a("ctmp", [128, 2 * 512], F32)
    Ccb2 = _a("Ccb2", [128, DS * 512], BF16)
    psD = bstack.enter_context(nc.psum_tensor("psD", [128, 8 * 512], F32))
    if True:
        hin_ready = 0
        for r in range(8):
            hl = p.put("sync", lambda e, r=r: e.dma_start(
                hin_st[:, 0:256],
                st_g[r * 128:(r + 1) * 128, :]),
                waits=[(ccc, AGS_DONE), (cdv, hin_ready)], inc=s_hin, inck=16)
            hin_ready = p.put("vector", lambda e, r=r: e.scalar_tensor_tensor(
                hin[:], hin_st[:, 0:256], sel8_sb[:, r:r + 1], hin[:], OP.mult, OP.add),
                waits=[(s_hin, hl)], inc=cdv)

        def emit_corr_chunk(c):
            y_ld = 0
            for part in range(4):
                y_ld = p.put("scalar", lambda e, c=c, part=part: e.dma_start(
                    bass.AP(ybig, part * 4 * 512, [[ND * 512, 128], [512, 4], [1, 512]]),
                    bass.AP(y_s, (part * 4 * 128) * TOK + c * 512,
                            [[TOK, 128], [128 * TOK, 4], [1, 512]])),
                    waits=[(cpl, LW.get("ybig_free", 0)),
                           (s_ybw, LW.get("ybw_done", 0))],
                    inc=s_ybl, inck=16)
            cld = 0
            for n in range(DS):
                cld = p.put("sync", lambda e, c=c, n=n: e.dma_start(
                    Ccb2[:, n * 512:(n + 1) * 512],
                    bass.AP(bc_s, (16 + n) * TOK + c * 512, [[0, 128], [1, 512]])),
                    waits=[(cdv, LW.get("Ccb2_free", 0))], inc=s_cc2, inck=16)
            for i in range(ND):
                ip = i % 2
                sld = p.put("scalar", lambda e, c=c, i=i: e.dma_start(
                    Sc2[:, 0:512],
                    bass.AP(ssum_s, (i * 128) * TOK + c * 512, [[TOK, 128], [1, 512]])),
                    waits=[(cac, LW.get(("Sc2", 0), 0))],
                    inc=s_sc2, inck=16)
                for n in range(DS):
                    eb = n % 2
                    ev = p.put("scalar", lambda e, i=i, n=n: e.activation(
                        Eb[:, 0:512],
                        Sc2[:, 0:512], AF.Exp,
                        scale=aneg[:, i * DS + n:i * DS + n + 1]),
                        waits=[(s_sc2, sld), (cdv, LW.get(("Eb", 0), 0))], inc=cac)
                    tv = p.put("vector", lambda e, i=i, n=n, eb=eb: e.scalar_tensor_tensor(
                        ctmp[:, eb * 512:(eb + 1) * 512],
                        Eb[:, 0:512],
                        hin[:, i * DS + n:i * DS + n + 1],
                        Ccb2[:, n * 512:(n + 1) * 512], OP.mult, OP.mult),
                        waits=[(cac, ev), (cdv, hin_ready), (s_cc2, cld),
                               (cpl, LW.get(("ctmp", eb), 0))], inc=cdv)
                    LW[("Eb", 0)] = tv
                    ya = p.put("gpsimd", lambda e, i=i, eb=eb: e.tensor_add(
                        ybig[:, i * 512:(i + 1) * 512], ybig[:, i * 512:(i + 1) * 512],
                        ctmp[:, eb * 512:(eb + 1) * 512]),
                        waits=[(cdv, tv), (s_ybl, y_ld)], inc=cpl)
                    LW[("ctmp", eb)] = ya
                LW[("Sc2", 0)] = cac.n
            LW["Ccb2_free"] = cdv.n
            for part in range(4):
                p.put("scalar", lambda e, c=c, part=part: e.dma_start(
                    bass.AP(y_s, (part * 4 * 128) * TOK + c * 512,
                            [[TOK, 128], [128 * TOK, 4], [1, 512]]),
                    bass.AP(ybig, part * 4 * 512, [[ND * 512, 128], [512, 4], [1, 512]])),
                    waits=[(cpl, cpl.n)], inc=s_ybw, inck=16)
            LW["ybig_free"] = cpl.n
            LW["ybw_done"] = s_ybw.n

        def emit_down_tchunk(c2):
            i_ld = 0
            for part in range(16):
                i_ld = p.put("scalar", lambda e, c2=c2, part=part: e.dma_start(
                    bass.AP(interc, part * 8 * 512, [[128 * 512, 128], [512, 8], [1, 512]]),
                    bass.AP(inter_s, (part * 8 * 128) * TOK + c2 * 512,
                            [[TOK, 128], [128 * TOK, 8], [1, 512]])),
                    waits=[(cpe, LW.get("interc_free", 0))],
                    inc=s_int, inck=16)
            for hh in range(2):
                last_mm = 0
                for nt in range(128):
                    rpar = nt % 4
                    rld = p.put("sync", lambda e, nt=nt, hh=hh, rpar=rpar: e.dma_start(
                        rbk[:, rpar * 1024:(rpar + 1) * 1024],
                        bass.AP(gW["d"], (nt * 128) * 2048 + hh * 1024,
                                [[2048, 128], [1, 1024]])),
                        waits=[(ccc, AGD_DONE), (cpe, LW.get(("rbk", rpar), 0))],
                        inc=s_rb[rpar], inck=16)
                    for h8 in range(8):
                        wl = [(s_rb[rpar], rld), (s_int, i_ld)]
                        if nt == 0 and h8 == 0:
                            wl.append((cdv, LW.get("psD_free", 0)))
                        last_mm = p.put("tensor", lambda e, nt=nt, h8=h8, rpar=rpar: e.matmul(
                            psD[:, h8 * 512:(h8 + 1) * 512],
                            rbk[:, rpar * 1024 + h8 * 128: rpar * 1024 + (h8 + 1) * 128],
                            interc[:, nt * 512:(nt + 1) * 512],
                            start=(nt == 0), stop=(nt == 127)),
                            waits=wl, inc=(cpe if h8 == 7 else None))
                    LW[("rbk", rpar)] = last_mm
                for h8 in range(8):
                    tnv = p.put("scalar", lambda e, h8=h8: e.activation(
                        tnb[:, 0:512], psD[:, h8 * 512:(h8 + 1) * 512],
                        AF.Tanh, scale=0.5),
                        waits=[(cpe, last_mm), (cdv, LW.get(("tnb", 0), 0))], inc=cac)
                    gv = p.put("vector", lambda e, h8=h8: e.scalar_tensor_tensor(
                        gvb[:, 0:512], tnb[:, 0:512],
                        1.0, psD[:, h8 * 512:(h8 + 1) * 512], OP.add, OP.mult),
                        waits=[(cac, tnv), (s_gw, LW.get(("gvb", 0), 0))], inc=cdv)
                    LW[("tnb", 0)] = gv
                    gsp = p.put("scalar", lambda e, c2=c2, hh=hh, h8=h8: e.dma_start(
                        bass.AP(g_s, ((hh * 8 + h8) * 128) * TOK + c2 * 512,
                                [[TOK, 128], [1, 512]]),
                        gvb[:, 0:512]),
                        waits=[(cdv, gv)], inc=s_gw, inck=16)
                    LW[("gvb", 0)] = gsp
                LW["psD_free"] = cdv.n
            LW["interc_free"] = cpe.n

        corr_next = 0
        for c2 in range(NCH):
            if B3B_MODE != 1:
                emit_down_tchunk(c2)
            while B3B_MODE != 0 and corr_next < NCH and corr_next < (c2 + 1) * 2:
                emit_corr_chunk(corr_next)
                corr_next += 1
        p.finish_dma()
        p.emit(nc)
    bstack.close()
    QACT_B3B = qact.n

    if upto < 5:
        return _finish_debug(nc, dbg_outs, stack, qsp, qact, ccc)
    p = Plan()
    LW = {}
    s_wo = dsem(p, "s_wo")
    s_ym = [dsem(p, "s_ym0"), dsem(p, "s_ym1")]
    s_gb = dsem(p, "s_gb")
    s_x4 = [dsem(p, "s_x40"), dsem(p, "s_x41")]
    s_out = dsem(p, "s_out")
    bstack = ExitStack()
    _a = lambda nm, shape, dt: bstack.enter_context(nc.sbuf_tensor(nm, shape, dt))
    woutT = _a("woutT", [128, ND * 2048], BF16)
    ysm = _a("ysm", [128, 2 * 512], F32)
    ybf = _a("ybf", [128, ND * 512], BF16)
    gbig = _a("gbig", [128, ND * 512], F32)
    sA = _a("sA", [128, 2 * 512], F32)
    ubig = _a("ubig", [128, ND * 512], F32)
    utok = _a("utok", [128, 2 * 2048], F32)
    xt4 = _a("xt4", [128, 2 * 2048], F32)
    ob = _a("ob", [128, 2048], F32)
    psS = bstack.enter_context(nc.psum_tensor("psS", [128, 4 * 512], F32))
    psTa = bstack.enter_context(nc.psum_tensor("psTa", [128, 128], F32))
    psTb = bstack.enter_context(nc.psum_tensor("psTb", [128, 128], F32))
    psTP = [psTa, psTb]
    if True:
        wo_ready = 0
        for r in range(8):
            wo_ready = p.put("sync", lambda e, r=r: e.dma_start(
                bass.AP(woutT, r * 256, [[ND * 2048, 128], [2048, ND], [1, 256]]),
                bass.AP(g1, (2048 + r * 4096) * 256, [[256, 128], [128 * 256, ND], [1, 256]])),
                inc=s_wo, inck=16)
        for c in range(NCH):
            g_ld4 = 0
            for part in range(4):
                g_ld4 = p.put("scalar", lambda e, c=c, part=part: e.dma_start(
                    bass.AP(gbig, part * 4 * 512, [[ND * 512, 128], [512, 4], [1, 512]]),
                    bass.AP(g_s, (part * 4 * 128) * TOK + c * 512,
                            [[TOK, 128], [128 * TOK, 4], [1, 512]])),
                    waits=[(cdv, LW.get("gbig_free", 0))],
                    inc=s_gb, inck=16)
            cast_done = 0
            for i in range(ND):
                ipp = i % 2
                yld1 = p.put("scalar", lambda e, c=c, i=i, ipp=ipp: e.dma_start(
                    ysm[:, ipp * 512:(ipp + 1) * 512],
                    bass.AP(y_s, (i * 128) * TOK + c * 512, [[TOK, 128], [1, 512]])),
                    waits=[(cac, LW.get(("ysm", ipp), 0))],
                    inc=s_ym[ipp], inck=16)
                cast_done = p.put("scalar", lambda e, i=i, ipp=ipp: e.activation(
                    ybf[:, i * 512:(i + 1) * 512], ysm[:, ipp * 512:(ipp + 1) * 512],
                    AF.Copy),
                    waits=[(s_ym[ipp], yld1), (cpe, LW.get("ybf_free", 0))], inc=cac)
                LW[("ysm", ipp)] = cast_done
            for mt in range(ND):
                bank = mt % 4
                last_mm = 0
                for k in range(ND):
                    wl = [(s_wo, wo_ready), (cac, cast_done)]
                    if k == 0:
                        wl.append((cdv, LW.get(("psS", bank), 0)))
                    p.put("tensor", lambda e, mt=mt, k=k, bank=bank: e.matmul(
                        psS[:, bank * 512:(bank + 1) * 512],
                        woutT[:, k * 2048 + mt * 128: k * 2048 + (mt + 1) * 128],
                        ybf[:, k * 512:(k + 1) * 512],
                        start=(k == 0), stop=(k == ND - 1)),
                        waits=wl, inc=None)
                last_mm = p.put("tensor", lambda e: e.drain(), inc=cpe)
                sev = p.put("scalar", lambda e, mt=mt, bank=bank: e.activation(
                    sA[:, (mt % 2) * 512:(mt % 2) * 512 + 512],
                    psS[:, bank * 512:(bank + 1) * 512], AF.Identity,
                    bias=outb_sb[:, mt:mt + 1]),
                    waits=[(cpe, last_mm), (cdv, LW.get(("sA", mt % 2), 0))], inc=cac)
                uv = p.put("vector", lambda e, mt=mt: e.scalar_tensor_tensor(
                    ubig[:, mt * 512:(mt + 1) * 512],
                    sA[:, (mt % 2) * 512:(mt % 2) * 512 + 512], 0.5,
                    gbig[:, mt * 512:(mt + 1) * 512], OP.mult, OP.mult),
                    waits=[(cac, sev), (s_gb, g_ld4), (cpe, LW.get("ubig_free", 0))],
                    inc=cdv)
                LW[("sA", mt % 2)] = uv
                LW[("psS", bank)] = uv
            u_done = cdv.n
            LW["gbig_free"] = u_done
            for tau in range(4):
                tp = tau % 2
                xld = p.put("sync", lambda e, c=c, tau=tau, tp=tp: e.dma_start(
                    xt4[:, tp * 2048:(tp + 1) * 2048],
                    x_in[c * 512 + tau * 128: c * 512 + (tau + 1) * 128, :]),
                    waits=[(cdv, LW.get(("xt4", tp), 0))], inc=s_x4[tp], inck=16)
                ev_last = 0
                for i0 in range(0, ND, 2):
                    for j in range(2):
                        i = i0 + j
                        p.put("tensor", lambda e, i=i, sl=j, tau=tau: e.transpose(
                            psTP[sl][:, :],
                            bass.AP(ubig, i * 512 + tau * 128, [[ND * 512, 128], [1, 128]]),
                            idf32[:]),
                            waits=[(cdv, u_done), (cac, LW.get(("psT", j), 0))], inc=None)
                    dr = p.put("tensor", lambda e: e.drain(), inc=cpe)
                    for j in range(2):
                        i = i0 + j
                        ev_last = p.put("scalar", lambda e, i=i, sl=j, tp=tp: e.copy(
                            utok[:, tp * 2048 + i * 128: tp * 2048 + (i + 1) * 128],
                            psTP[sl][:, :]),
                            waits=[(cpe, dr), (cdv, LW.get(("utok", tp), 0))], inc=cac)
                        LW[("psT", j)] = ev_last
                oadd = p.put("vector", lambda e, tp=tp: e.tensor_add(
                    ob[:, 0:2048], xt4[:, tp * 2048:(tp + 1) * 2048],
                    utok[:, tp * 2048:(tp + 1) * 2048]),
                    waits=[(cac, ev_last), (s_x4[tp], xld), (s_out, LW.get(("ob", 0), 0))],
                    inc=cdv)
                LW[("xt4", tp)] = oadd
                LW[("utok", tp)] = oadd
                ost = p.put("scalar", lambda e, c=c, tau=tau: e.dma_start(
                    out_sh[c * 512 + tau * 128: c * 512 + (tau + 1) * 128, :],
                    ob[:, 0:2048]),
                    waits=[(cdv, oadd)], inc=s_out, inck=16)
                LW[("ob", 0)] = ost
            LW["ybf_free"] = cpe.n
            LW["ubig_free"] = cpe.n
        p.finish_dma()
        p.emit(nc)
    bstack.close()

    stack.close()
    nc.finalize()
    return nc


def _finish_debug(nc, dbg_outs, stack, qsp, qact, ccc=None):
    p = Plan()
    if ccc is not None and ccc.n:
        p.put("sync", lambda e: e.wait_ge(ccc.h, ccc.n) and None)
    n = 0
    for od, ih in dbg_outs:
        rows = ih.shape[0]
        step = max(1, min(rows, (1 << 20) // max(1, int(ih.shape[1]))))
        r = 0
        while r < rows:
            rr = min(step, rows - r)
            n = p.put("sync", lambda e, od=od, ih=ih, r=r, rr=rr: e.dma_start(
                od[r:r + rr, :], ih[r:r + rr, :]),
                waits=[(qsp, qsp.n), (qact, qact.n)], inc=qsp, inck=16)
            r += rr
    p.put("sync", lambda e: e.wait_ge(qsp.h, qsp.n) and None)
    p.put("scalar", lambda e: e.wait_ge(qact.h, qact.n) and None)
    p.emit(nc)
    stack.close()
    nc.finalize()
    return nc


_CACHE = {}


def kernel(L0_attn_x, L0_attn_A_log, L0_attn_x_proj_w, L0_attn_x_proj_b,
           L0_attn_dt_proj_w, L0_attn_dt_proj_b, L0_attn_out_proj_w,
           L0_attn_out_proj_b, L0_ffn_gate_W_packed, L0_ffn_gate_scales,
           L0_ffn_up_W_packed, L0_ffn_up_scales, L0_ffn_down_W_packed,
           L0_ffn_down_scales):
    x = np.ascontiguousarray(np.asarray(L0_attn_x, dtype=np.float32))
    if "nc" not in _CACHE:
        _CACHE["nc"] = build_program()
    nc = _CACHE["nc"]

    gwp = np.asarray(L0_ffn_gate_W_packed, dtype=np.int32)
    gsc = np.asarray(L0_ffn_gate_scales, dtype=np.float32)
    uwp = np.asarray(L0_ffn_up_W_packed, dtype=np.int32)
    usc = np.asarray(L0_ffn_up_scales, dtype=np.float32)
    dwp = np.asarray(L0_ffn_down_W_packed, dtype=np.int32)
    dsc = np.asarray(L0_ffn_down_scales, dtype=np.float32)
    wdt = np.asarray(L0_attn_dt_proj_w, dtype=np.float32)
    wout = np.asarray(L0_attn_out_proj_w, dtype=np.float32)

    in_maps = []
    for c in range(NCORE):
        b, hf = c // 2, c % 2
        sel = np.zeros((128, 8), np.float32)
        if hf == 1:
            sel[:, c - 1] = 1.0
        in_maps.append({
            "x_sh": np.ascontiguousarray(x[b, hf * TOK:(hf + 1) * TOK, :]),
            "gwp": np.ascontiguousarray(gwp[c * 2048:(c + 1) * 2048, :]),
            "gsc": np.ascontiguousarray(gsc[c * 2048:(c + 1) * 2048, :]),
            "uwp": np.ascontiguousarray(uwp[c * 2048:(c + 1) * 2048, :]),
            "usc": np.ascontiguousarray(usc[c * 2048:(c + 1) * 2048, :]),
            "dwp": np.ascontiguousarray(dwp[:, c * 1024:(c + 1) * 1024]),
            "dsc": np.ascontiguousarray(dsc[:, c * 64:(c + 1) * 64]),
            "wdt_sh": np.ascontiguousarray(wdt[c * 256:(c + 1) * 256, :]),
            "wout_sh": np.ascontiguousarray(wout[c * 256:(c + 1) * 256, :]),
            "alog": np.ascontiguousarray(np.asarray(L0_attn_A_log, np.float32)),
            "xpw": np.ascontiguousarray(np.asarray(L0_attn_x_proj_w, np.float32)),
            "xpb": np.ascontiguousarray(np.asarray(L0_attn_x_proj_b, np.float32)),
            "dtb": np.ascontiguousarray(np.asarray(L0_attn_dt_proj_b, np.float32)),
            "outb": np.ascontiguousarray(np.asarray(L0_attn_out_proj_b, np.float32)),
            "sel8": sel,
        })
    res = run_bass_kernel_spmd(nc, in_maps, core_ids=list(range(NCORE)))
    out = np.empty((B_, L_, D), np.float32)
    for c in range(NCORE):
        b, hf = c // 2, c % 2
        out[b, hf * TOK:(hf + 1) * TOK, :] = res.results[c]["out_sh"]
    return out


# version 05
# speedup vs baseline: 1.3395x; 1.1376x over previous
import numpy as np
from contextlib import ExitStack

import concourse.bass as bass
import concourse.mybir as mybir
from concourse.bass_utils import run_bass_kernel_spmd

F32 = mybir.dt.float32
BF16 = mybir.dt.bfloat16
I32 = mybir.dt.int32
AF = mybir.ActivationFunctionType
OP = mybir.AluOpType
AX = mybir.AxisListType

NCORE = 8
B_, L_, D, DS, DM = 4, 8192, 2048, 16, 16384
TOK = 4096
CH = 512
NCH = TOK // CH
QT = 1024
ND = D // 128
GELU_C = 0.7978845608028654
GELU_A2 = 0.21145925545
EPS = 1e-5

ENGS = ("sync", "scalar", "tensor", "vector", "gpsimd")
DEQ_NW = 3
DEQ_NJT = 16
B3B_MODE = 2
B3A_MODE = 2


class Ctr:
    def __init__(self, stack, nc, name):
        self.h = stack.enter_context(nc.semaphore(name))
        self.n = 0
        self.name = name


class Plan:
    def __init__(self):
        self.streams = {e: [] for e in ENGS}
        self.lastwait = {e: {} for e in ENGS}
        self.dsems = []

    def finish_dma(self, extra=()):
        for ctr in list(self.dsems) + list(extra):
            if ctr.n:
                self.put("sync", (lambda e, c=ctr, v=ctr.n: e.wait_ge(c.h, v) and None))

    def put(self, eng, fn, waits=(), inc=None, inck=1):
        lw = self.lastwait[eng]
        w2 = []
        for ctr, val in waits:
            if val is None or val <= 0:
                continue
            if lw.get(ctr.name, -1) >= val:
                continue
            lw[ctr.name] = val
            w2.append((ctr, val))
        if inc is not None:
            inc.n += inck
        self.streams[eng].append((fn, w2, inc, inck))
        return None if inc is None else inc.n

    def emit(self, nc):
        with nc.Block() as blk:
            def runner(items):
                def run(e):
                    for fn, waits, inc, inck in items:
                        for ctr, val in waits:
                            e.wait_ge(ctr.h, val)
                        inst = fn(e)
                        if inc is not None:
                            inst.then_inc(inc.h, inck)
                return run
            for eng in ENGS:
                if self.streams[eng]:
                    getattr(blk, eng)(runner(self.streams[eng]))


def build_program(upto=9, dbg=(), b1part=9):
    nc = bass.Bass("TRN2", target_bir_lowering=False)

    x_in = nc.dram_tensor("x_sh", [TOK, D], F32, kind="ExternalInput")
    wp_in = {w: nc.dram_tensor(f"{w}wp", [2048, 1024], I32, kind="ExternalInput")
             for w in ("g", "u", "d")}
    sc_in = {w: nc.dram_tensor(f"{w}sc", [2048, 64], F32, kind="ExternalInput")
             for w in ("g", "u", "d")}
    wdt_sh = nc.dram_tensor("wdt_sh", [256, 2048], F32, kind="ExternalInput")
    wout_sh = nc.dram_tensor("wout_sh", [256, 2048], F32, kind="ExternalInput")
    alog = nc.dram_tensor("alog", [D, DS], F32, kind="ExternalInput")
    xpw = nc.dram_tensor("xpw", [32, D], F32, kind="ExternalInput")
    xpb_in = nc.dram_tensor("xpb", [32], F32, kind="ExternalInput")
    dtb_in = nc.dram_tensor("dtb", [D], F32, kind="ExternalInput")
    outb_in = nc.dram_tensor("outb", [D], F32, kind="ExternalInput")
    sel8_in = nc.dram_tensor("sel8", [128, 8], F32, kind="ExternalInput")
    out_sh = nc.dram_tensor("out_sh", [TOK, D], F32, kind="ExternalOutput")

    xn_s = nc.dram_tensor("xn_s", [D, TOK], BF16)
    dt_s = nc.dram_tensor("dt_s", [D, TOK], F32)
    ssum_s = nc.dram_tensor("ssum_s", [D, TOK], F32)
    bc_s = nc.dram_tensor("bc_s", [32, TOK], BF16)
    y_s = nc.dram_tensor("y_s", [D, TOK], F32)
    inter_s = nc.dram_tensor("inter_s", [DM, TOK], BF16)
    g_s = nc.dram_tensor("g_s", [D, TOK], F32)
    c1 = nc.dram_tensor("c1", [4096, 256], BF16)
    g1 = nc.dram_tensor("g1", [8 * 4096, 256], BF16)
    cW = {w: nc.dram_tensor(f"c{w}", [2048, 2048], BF16) for w in ("g", "u", "d")}
    gW = {w: nc.dram_tensor(f"gw{w}", [8 * 2048, 2048], BF16) for w in ("g", "u", "d")}
    st_c = nc.dram_tensor("st_c", [128, 256], F32)
    st_g = nc.dram_tensor("st_g", [8 * 128, 256], F32)
    scr = dict(xn_s=xn_s, dt_s=dt_s, ssum_s=ssum_s, bc_s=bc_s, y_s=y_s,
               inter_s=inter_s, g_s=g_s, c1=c1, g1=g1, st_c=st_c, st_g=st_g)
    scr.update({f"c{w}": cW[w] for w in cW})
    scr.update({f"gw{w}": gW[w] for w in gW})
    dbg_outs = []
    for nm in dbg:
        h = scr[nm]
        dbg_outs.append((nc.dram_tensor(f"dbg_{nm}", list(h.shape), h.dtype,
                                        kind="ExternalOutput"), h))

    stack = ExitStack()
    sb = lambda nm, shape, dt: stack.enter_context(nc.sbuf_tensor(nm, shape, dt))

    aneg = sb("aneg", [128, ND * DS], F32)
    dtb_sb = sb("dtb_sb", [128, ND], F32)
    outb_sb = sb("outb_sb", [128, ND], F32)
    xpb_sb = sb("xpb_sb", [32, 1], F32)
    sel8_sb = sb("sel8_sb", [128, 8], F32)
    state = sb("state", [128, ND * DS], F32)
    sstate = sb("sstate", [128, ND], F32)
    hin = sb("hin", [128, ND * DS], F32)
    hin_st = sb("hin_st", [128, ND * DS], F32)
    idbf = sb("idbf", [128, 128], BF16)
    idf32 = sb("idf32", [128, 128], F32)
    zeros = sb("zeros", [128, CH], F32)
    wxpT = sb("wxpT", [128, ND * 128], BF16)
    csts = sb("csts", [128, 4], F32)

    qsp = Ctr(stack, nc, "qsp")
    qact = Ctr(stack, nc, "qact")

    def dsem(p, name):
        c = Ctr(stack, nc, name)
        p.dsems.append(c)
        return c

    cpe = Ctr(stack, nc, "cpe")
    cdv = Ctr(stack, nc, "cdv")
    cac = Ctr(stack, nc, "cac")
    cpl = Ctr(stack, nc, "cpl")
    ccc = Ctr(stack, nc, "ccc")

    p = Plan()
    s_c0 = dsem(p, "s_c0")
    for i in range(ND):
        p.put("sync", lambda e, i=i: e.dma_start(
            aneg[:, i * DS:(i + 1) * DS], alog[i * 128:(i + 1) * 128, :]), inc=s_c0, inck=16)
        p.put("sync", lambda e, i=i: e.dma_start(
            dtb_sb[:, i:i + 1], bass.AP(dtb_in, i * 128, [[1, 128], [1, 1]])), inc=s_c0, inck=16)
        p.put("sync", lambda e, i=i: e.dma_start(
            outb_sb[:, i:i + 1], bass.AP(outb_in, i * 128, [[1, 128], [1, 1]])), inc=s_c0, inck=16)
    p.put("sync", lambda e: e.dma_start(xpb_sb[:, :], bass.AP(xpb_in, 0, [[1, 32], [1, 1]])),
          inc=s_c0, inck=16)
    p.put("sync", lambda e: e.dma_start(sel8_sb[:, :], sel8_in[:, :]), inc=s_c0, inck=16)
    ld_done = s_c0.n
    p.put("gpsimd", lambda e: e.memset(csts[:, 0:1], -4.0))
    p.put("gpsimd", lambda e: e.memset(csts[:, 1:2], -6.0))
    p.put("gpsimd", lambda e: e.memset(csts[:, 2:3], EPS))
    p.put("gpsimd", lambda e: e.memset(csts[:, 3:4], 1.0))
    p.put("gpsimd", lambda e: e.memset(idbf[:], 0.0))
    p.put("gpsimd", lambda e: e.affine_select(out=idbf[:], in_=idbf[:], compare_op=OP.not_equal,
                                              fill=1.0, base=0, pattern=[[-1, 128]],
                                              channel_multiplier=1))
    p.put("gpsimd", lambda e: e.memset(idf32[:], 0.0))
    p.put("gpsimd", lambda e: e.affine_select(out=idf32[:], in_=idf32[:], compare_op=OP.not_equal,
                                              fill=1.0, base=0, pattern=[[-1, 128]],
                                              channel_multiplier=1))
    p.put("gpsimd", lambda e: e.memset(zeros[:], 0.0))
    p.put("gpsimd", lambda e: e.memset(state[:], 0.0))
    p.put("gpsimd", lambda e: e.memset(sstate[:], 0.0))
    p.put("gpsimd", lambda e: e.memset(hin[:], 0.0))
    p.put("gpsimd", lambda e: e.memset(wxpT[:], 0.0))
    a_exp = p.put("scalar", lambda e: e.activation(aneg[:], aneg[:], AF.Exp),
                  waits=[(s_c0, ld_done)], inc=cac)
    p.put("vector", lambda e: e.tensor_scalar_mul(aneg[:], aneg[:], -1.0),
          waits=[(cac, a_exp)], inc=cdv)
    p.finish_dma()
    p.emit(nc)

    wstack = ExitStack()
    wdtT = wstack.enter_context(nc.sbuf_tensor("wdtT", [128, ND * 2048], BF16))
    if upto < 1:
        return _finish_debug(nc, dbg_outs, stack, qsp, qact, ccc)
    p = Plan()
    LW = {}
    s_xpw = dsem(p, "s_xpw"); s_w4 = dsem(p, "s_w4"); s_stg1 = dsem(p, "s_stg1")
    s_wp = [dsem(p, "s_wp0"), dsem(p, "s_wp1")]
    s_sc = [dsem(p, "s_sc0"), dsem(p, "s_sc1")]
    s_sg = [dsem(p, "s_sg0"), dsem(p, "s_sg1")]
    s_wdt = dsem(p, "s_wdt")
    bstack = ExitStack()
    _a = lambda nm, shape, dt: bstack.enter_context(nc.sbuf_tensor(nm, shape, dt))
    w4 = _a("w4", [128, 2 * 2048], F32)
    wb4 = _a("wb4", [128, 2 * 2048], BF16)
    stg1 = _a("stg1", [128, 256], BF16)
    xpw_f = _a("xpw_f", [32, 2048], F32)
    xpw_b = _a("xpw_b", [32, 2048], BF16)
    wpt = _a("wpt", [128, 2 * 1024], I32)
    sct = _a("sct", [128, 2 * 64], F32)
    loi = _a("loi", [128, 1024], I32)
    hii = _a("hii", [128, 1024], I32)
    pf = _a("pf", [128, 2 * 1024], F32)
    ge8 = _a("ge8", [128, 1024], F32)
    cc_ = _a("cc_", [128, 1024], F32)
    r1_ = _a("r1_", [128, 1024], F32)
    r2_ = _a("r2_", [128, 1024], F32)
    t1_ = _a("t1_", [128, 1024], F32)
    mm_ = _a("mm_", [128, 1024], F32)
    sg_ = _a("sg_", [128, 1024], F32)
    vv_ = _a("vv_", [128, 1024], F32)
    deq = _a("deq", [128, 2 * 2048], BF16)
    stg2 = _a("stg2", [128, 2 * 2048], BF16)
    pstrA = bstack.enter_context(nc.psum_tensor("pstrA", [128, 128], BF16))
    pstrB = bstack.enter_context(nc.psum_tensor("pstrB", [128, 128], BF16))
    pstrP = [pstrA, pstrB]
    if True:
        v = p.put("sync", lambda e: e.dma_start(xpw_f[:, :], xpw[:, :]), inc=s_xpw, inck=16)
        cast = p.put("vector", lambda e: e.tensor_copy(xpw_b[:], xpw_f[:]),
                     waits=[(s_xpw, v)], inc=cdv)
        for i0 in range(0, ND, 2):
            for j in range(2):
                i = i0 + j
                p.put("tensor", lambda e, i=i, sl=j: e.transpose(
                    pstrP[sl][:, 0:32],
                    xpw_b[:, i * 128:(i + 1) * 128], idbf[0:32, 0:32]),
                    waits=[(cdv, cast), (cac, LW.get(("pstr", j), 0))], inc=None)
            dr = p.put("tensor", lambda e: e.drain(), inc=cpe)
            for j in range(2):
                i = i0 + j
                ev = p.put("scalar", lambda e, i=i, sl=j: e.copy(
                    wxpT[:, i * 128:i * 128 + 32], pstrP[sl][:, 0:32]),
                    waits=[(cpe, dr)], inc=cac)
                LW[("pstr", j)] = ev

        for wsrc, c1row in (((wdt_sh, 0), (wout_sh, 2048)) if b1part >= 1 else ()):
            lds = []
            for j in range(2):
                lds.append(p.put("sync", lambda e, j=j, wsrc=wsrc: e.dma_start(
                    w4[:, j * 2048:(j + 1) * 2048], wsrc[j * 128:(j + 1) * 128, :]),
                    waits=[(cpe, LW.get("w4_free", 0))], inc=s_w4, inck=16))
            cast = p.put("vector", lambda e: e.tensor_copy(wb4[:], w4[:]),
                         waits=[(s_w4, lds[-1]), (cpe, LW.get("w4_free", 0))], inc=cdv)
            for i in range(ND):
                for j in range(2):
                    p.put("tensor", lambda e, i=i, j=j: e.transpose(
                        pstrP[j][:, :],
                        wb4[:, j * 2048 + i * 128: j * 2048 + (i + 1) * 128], idbf[:]),
                        waits=[(cdv, cast), (cac, LW.get(("pstr", j), 0))], inc=None)
                dr = p.put("tensor", lambda e: e.drain(), inc=cpe)
                ev = None
                for j in range(2):
                    ev = p.put("scalar", lambda e, j=j: e.copy(
                        stg1[:, j * 128:(j + 1) * 128], pstrP[j][:, :]),
                        waits=[(cpe, dr), (s_stg1, LW.get("stg1_free", 0))], inc=cac)
                    LW[("pstr", j)] = ev
                std = p.put("scalar", lambda e, i=i, c1row=c1row: e.dma_start(
                    c1[c1row + i * 128: c1row + (i + 1) * 128, :], stg1[:]),
                    waits=[(cac, ev)], inc=s_stg1, inck=16)
                LW["stg1_free"] = std
            LW["w4_free"] = cpe.n
        if b1part >= 2:
            p.put("gpsimd", lambda e: e.collective_compute(
                "AllGather", OP.bypass, replica_groups=[list(range(8))],
                ins=[c1.ap()], outs=[g1.ap()]),
                waits=[(s_stg1, s_stg1.n)], inc=ccc)

        for sl in range(2):
            LW[("pstr", sl)] = cac.n
        _wl = ("g", "u", "d") if b1part >= 3 else ()
        _wl = _wl[:DEQ_NW] if b1part in (3, 4, 5) else _wl
        for w in _wl:
            for jt in range(DEQ_NJT if b1part in (3, 4, 5) else 16):
                par = jt % 2
                ldw = p.put("sync", lambda e, w=w, jt=jt, par=par: e.dma_start(
                    wpt[:, par * 1024:(par + 1) * 1024],
                    wp_in[w][jt * 128:(jt + 1) * 128, :]),
                    waits=[(cdv, LW.get(("wpt", par), 0))], inc=s_wp[par], inck=16)
                lds = p.put("sync", lambda e, w=w, jt=jt, par=par: e.dma_start(
                    sct[:, par * 64:(par + 1) * 64],
                    sc_in[w][jt * 128:(jt + 1) * 128, :]),
                    waits=[(cdv, LW.get(("sct", par), 0))], inc=s_sc[par], inck=16)
                lo = p.put("vector", lambda e, par=par: e.tensor_scalar(
                    loi[:], wpt[:, par * 1024:(par + 1) * 1024], 15, None, OP.bitwise_and),
                    waits=[(s_wp[par], ldw), (cdv, LW.get("loi_free", 0))], inc=cdv)
                hi = p.put("vector", lambda e, par=par: e.tensor_scalar(
                    hii[:], wpt[:, par * 1024:(par + 1) * 1024], 4, None,
                    OP.logical_shift_right), inc=cdv)
                LW[("wpt", par)] = hi
                p.put("vector", lambda e: e.tensor_copy(pf[:, 0:1024], loi[:]), inc=cdv)
                cast2 = p.put("vector", lambda e: e.tensor_copy(pf[:, 1024:2048], hii[:]),
                              inc=cdv)
                LW["loi_free"] = cast2
                last_scale = 0
                for pi in range(2):
                    g8 = p.put("vector", lambda e, pi=pi: e.tensor_scalar(
                        ge8[:], pf[:, pi * 1024:(pi + 1) * 1024], 8.0, None, OP.is_ge),
                        waits=[(cpl, LW.get("ge8_free", 0))], inc=cdv)
                    cc1 = p.put("vector", lambda e, pi=pi: e.scalar_tensor_tensor(
                        cc_[:], ge8[:], -8.0, pf[:, pi * 1024:(pi + 1) * 1024],
                        OP.mult, OP.add),
                        waits=[(cpl, LW.get("cc_free", 0)), (cac, LW.get("cc_free_a", 0))],
                        inc=cdv)
                    ra = p.put("scalar", lambda e: e.activation(
                        r1_[:], cc_[:], AF.Relu, bias=csts[:, 0:1]),
                        waits=[(cdv, cc1), (cpl, LW.get("r1_free", 0))], inc=cac)
                    rb = p.put("scalar", lambda e: e.activation(
                        r2_[:], cc_[:], AF.Relu, bias=csts[:, 1:2]),
                        waits=[(cdv, LW.get("r2_free", 0))], inc=cac)
                    tt1 = p.put("gpsimd", lambda e: e.tensor_add(t1_[:], cc_[:], r1_[:]),
                                waits=[(cac, ra), (cdv, LW.get("t1_free", 0))], inc=cpl)
                    LW["cc_free"] = tt1
                    LW["cc_free_a"] = rb
                    LW["r1_free"] = tt1
                    mmv = p.put("vector", lambda e: e.scalar_tensor_tensor(
                        mm_[:], t1_[:], 0.5, r2_[:], OP.mult, OP.add),
                        waits=[(cpl, tt1), (cac, rb), (cpl, LW.get("mm_free", 0))], inc=cdv)
                    LW["t1_free"] = mmv
                    LW["r2_free"] = mmv
                    sgv = p.put("gpsimd", lambda e: e.tensor_scalar(
                        sg_[:], ge8[:], -2.0, 1.0, OP.mult, OP.add),
                        waits=[(cpl, LW.get("sg_free", 0))], inc=cpl)
                    LW["ge8_free"] = sgv
                    vvv = p.put("gpsimd", lambda e: e.tensor_mul(vv_[:], mm_[:], sg_[:]),
                                waits=[(cdv, mmv), (cdv, LW.get("vv_free", 0))], inc=cpl)
                    LW["mm_free"] = vvv
                    LW["sg_free"] = vvv
                    last_scale = p.put("vector", lambda e, pi=pi, par=par: e.tensor_tensor(
                        bass.AP(deq, par * 2048 + pi, [[4096, 128], [2, 1024]]),
                        vv_[:],
                        bass.AP(sct, par * 64, [[128, 128], [1, 64], [0, 16]]),
                        OP.mult),
                        waits=[(cpl, vvv), (s_sc[par], lds), (cpe, LW.get(("deq", par), 0))],
                        inc=cdv)
                    LW["vv_free"] = last_scale
                LW[("sct", par)] = last_scale
                evs = 0
                for i0 in (range(0, ND, 2) if b1part >= 4 else ()):
                    for j in range(2):
                        i = i0 + j
                        p.put("tensor", lambda e, i=i, par=par, sl=j: e.transpose(
                            pstrP[sl][:, :],
                            bass.AP(deq, par * 2048 + i * 128, [[4096, 128], [1, 128]]),
                            idbf[:]),
                            waits=[(cdv, last_scale), (cac, LW.get(("pstr", j), 0))],
                            inc=None)
                    dr = p.put("tensor", lambda e: e.drain(), inc=cpe)
                    evs = 0
                    for j in range(2):
                        i = i0 + j
                        evs = p.put("scalar", lambda e, i=i, par=par, sl=j: e.copy(
                            stg2[:, par * 2048 + i * 128: par * 2048 + (i + 1) * 128],
                            pstrP[sl][:, :]),
                            waits=[(cpe, dr), (s_sg[par], LW.get(("stg2", par), 0))], inc=cac)
                        LW[("pstr", j)] = evs
                LW[("deq", par)] = cpe.n
                if b1part < 4:
                    LW[("deq", par)] = 0
                    continue
                std = 0
                if b1part < 5:
                    LW[("stg2", par)] = 0
                    continue
                for q4 in range(4):
                    std = p.put("scalar", lambda e, w=w, jt=jt, par=par, q4=q4: e.dma_start(
                        bass.AP(cW[w], jt * 128 + q4 * 4 * 128 * 2048,
                                [[2048, 128], [128 * 2048, 4], [1, 128]]),
                        bass.AP(stg2, par * 2048 + q4 * 4 * 128,
                                [[4096, 128], [128, 4], [1, 128]])),
                        waits=[(cac, evs)], inc=s_sg[par], inck=16)
                LW[("stg2", par)] = std
            if b1part >= 6:
                p.put("gpsimd", lambda e, w=w: e.collective_compute(
                    "AllGather", OP.bypass, replica_groups=[list(range(8))],
                    ins=[cW[w].ap()], outs=[gW[w].ap()]),
                    waits=[(s_sg[0], s_sg[0].n), (s_sg[1], s_sg[1].n)], inc=ccc)
        wdt_lds = 0
        for r in range(8):
            wdt_lds = p.put("sync", lambda e, r=r: e.dma_start(
                bass.AP(wdtT, r * 256, [[ND * 2048, 128], [2048, ND], [1, 256]]),
                bass.AP(g1, (r * 4096) * 256, [[256, 128], [128 * 256, ND], [1, 256]])),
                waits=[(ccc, 1)], inc=s_wdt, inck=16)
        WDT_READY = (s_wdt, wdt_lds)
        p.finish_dma()
        p.emit(nc)
    bstack.close()
    AG1_DONE, AGG_DONE, AGU_DONE, AGD_DONE, AGS_DONE = 1, 2, 3, 4, 6

    if upto < 2:
        return _finish_debug(nc, dbg_outs, stack, qsp, qact, ccc)
    p = Plan()
    LW = {}
    s_xt = [dsem(p, "s_xt0"), dsem(p, "s_xt1")]
    s_xnsp = dsem(p, "s_xnsp")
    s_dtv = [dsem(p, "s_dtv0"), dsem(p, "s_dtv1")]
    s_bcv = dsem(p, "s_bcv")
    bstack = ExitStack()
    _a = lambda nm, shape, dt: bstack.enter_context(nc.sbuf_tensor(nm, shape, dt))
    xtok = _a("xtok", [128, 2 * 2048], F32)
    sqb = _a("sqb", [128, 2048], F32)
    lnst = _a("lnst", [128, 8], F32)
    xnt = _a("xnt", [128, 2 * 2048], BF16)
    xnc = _a("xnc", [128, ND * 512], BF16)
    spb = _a("spb", [128, 4 * 512], F32)
    dtv = _a("dtv", [128, 2 * 512], F32)
    bcv = _a("bcv", [32, 512], BF16)
    ps_trA = bstack.enter_context(nc.psum_tensor("ps_trA", [128, 128], BF16))
    ps_trB = bstack.enter_context(nc.psum_tensor("ps_trB", [128, 128], BF16))
    ps_trP = [ps_trA, ps_trB]
    ps_dt = bstack.enter_context(nc.psum_tensor("ps_dt", [128, 4 * 512], F32))
    ps_xp = bstack.enter_context(nc.psum_tensor("ps_xp", [128, 512], F32))
    if True:
        s_wdtS, wdt_ready = WDT_READY

        for c in range(NCH):
            xn_evs = 0
            for tau in range(4):
                par = tau % 2
                xl = p.put("sync", lambda e, c=c, tau=tau, par=par: e.dma_start(
                    xtok[:, par * 2048:(par + 1) * 2048],
                    x_in[c * 512 + tau * 128: c * 512 + (tau + 1) * 128, :]),
                    waits=[(cdv, LW.get(("xtok", par), 0))], inc=s_xt[par], inck=16)
                sq = p.put("scalar", lambda e, par=par: e.activation(
                    sqb[:], xtok[:, par * 2048:(par + 1) * 2048], AF.Square),
                    waits=[(s_xt[par], xl), (cdv, LW.get("sqb_free", 0))], inc=cac)
                s0v = p.put("vector", lambda e, par=par: e.reduce_sum(
                    lnst[:, 0:1], xtok[:, par * 2048:(par + 1) * 2048], AX.X),
                    waits=[(s_xt[par], xl)], inc=cdv)
                s1 = p.put("vector", lambda e: e.reduce_sum(lnst[:, 1:2], sqb[:], AX.X),
                           waits=[(cac, sq)], inc=cdv)
                LW["sqb_free"] = s1
                mean = p.put("vector", lambda e: e.tensor_scalar_mul(
                    lnst[:, 2:3], lnst[:, 0:1], 1.0 / 2048),
                    waits=[(cdv, s0v)], inc=cdv)
                tch1 = p.put("vector", lambda e: e.tensor_copy(lnst[:, 7:8], lnst[:, 2:3]),
                             waits=[(cdv, mean)], inc=cdv)
                m2 = p.put("scalar", lambda e: e.activation(
                    lnst[:, 3:4], lnst[:, 2:3], AF.Square), waits=[(cdv, tch1)], inc=cac)
                t2 = p.put("scalar", lambda e: e.copy(lnst[:, 7:8], lnst[:, 3:4]),
                           waits=[(cac, m2)], inc=cac)
                vr = p.put("vector", lambda e: e.scalar_tensor_tensor(
                    lnst[:, 4:5], lnst[:, 1:2], 1.0 / 2048, lnst[:, 3:4],
                    OP.mult, OP.subtract), waits=[(cac, t2), (cdv, s1)], inc=cdv)
                tch3 = p.put("vector", lambda e: e.tensor_copy(lnst[:, 7:8], lnst[:, 4:5]),
                             waits=[(cdv, vr)], inc=cdv)
                lnv = p.put("scalar", lambda e: e.activation(
                    lnst[:, 5:6], lnst[:, 4:5], AF.Ln, bias=csts[:, 2:3]),
                    waits=[(cdv, tch3)], inc=cac)
                lt = p.put("scalar", lambda e: e.copy(lnst[:, 7:8], lnst[:, 5:6]),
                           waits=[(cac, lnv)], inc=cac)
                rs = p.put("scalar", lambda e: e.activation(
                    lnst[:, 6:7], lnst[:, 5:6], AF.Exp, scale=-0.5),
                    waits=[(cac, lt)], inc=cac)
                t4 = p.put("scalar", lambda e: e.copy(lnst[:, 7:8], lnst[:, 6:7]),
                           waits=[(cac, rs)], inc=cac)
                xn1 = p.put("vector", lambda e, par=par: e.tensor_scalar(
                    xnt[:, par * 2048:(par + 1) * 2048],
                    xtok[:, par * 2048:(par + 1) * 2048],
                    lnst[:, 2:3], lnst[:, 6:7], OP.subtract, OP.mult),
                    waits=[(cac, t4)], inc=cdv)
                LW[("xtok", par)] = xn1
                for i0 in range(0, ND, 2):
                    for j in range(2):
                        i = i0 + j
                        p.put("tensor", lambda e, i=i, sl=j, par=par: e.transpose(
                            ps_trP[sl][:, :],
                            xnt[:, par * 2048 + i * 128: par * 2048 + (i + 1) * 128],
                            idbf[:]),
                            waits=[(cdv, xn1), (cac, LW.get(("ps_tr", j), 0)),
                                   (s_xnsp, LW.get("xnc_freeq", 0))], inc=None)
                    dr = p.put("tensor", lambda e: e.drain(), inc=cpe)
                    for j in range(2):
                        i = i0 + j
                        xn_evs = p.put("scalar", lambda e, i=i, sl=j, tau=tau: e.copy(
                            xnc[:, i * 512 + tau * 128: i * 512 + (tau + 1) * 128],
                            ps_trP[sl][:, :]),
                            waits=[(cpe, dr)], inc=cac)
                        LW[("ps_tr", j)] = xn_evs
            xn_sp = 0
            for part in range(4):
                xn_sp = p.put("scalar", lambda e, c=c, part=part: e.dma_start(
                    bass.AP(xn_s, (part * 4 * 128) * TOK + c * 512,
                            [[TOK, 128], [128 * TOK, 4], [1, 512]]),
                    bass.AP(xnc, part * 4 * 512, [[ND * 512, 128], [512, 4], [1, 512]])),
                    waits=[(cac, xn_evs)], inc=s_xnsp, inck=16)
            LW["xnc_freeq"] = xn_sp

            last_mm = 0
            for k in range(ND):
                wl = [(cac, xn_evs)]
                if k == 0:
                    wl.append((cac, LW.get("ps_xp", 0)))
                p.put("tensor", lambda e, k=k: e.matmul(
                    ps_xp[:], wxpT[:, k * 128:(k + 1) * 128], xnc[:, k * 512:(k + 1) * 512],
                    start=(k == 0), stop=(k == ND - 1)), waits=wl, inc=None)
            last_mm = p.put("tensor", lambda e: e.drain(), inc=cpe)
            LW["xnc_freep"] = last_mm
            bce = p.put("scalar", lambda e: e.activation(
                bcv[:, :], ps_xp[0:32, :], AF.Identity, bias=xpb_sb[:, :]),
                waits=[(cpe, last_mm), (s_bcv, LW.get("bcv_free", 0))], inc=cac)
            LW["ps_xp"] = bce
            bcd = p.put("scalar", lambda e, c=c: e.dma_start(
                bass.AP(bc_s, c * 512, [[TOK, 32], [1, 512]]), bcv[:, :]),
                waits=[(cac, bce)], inc=s_bcv, inck=16)
            LW["bcv_free"] = bcd
            for et in range(ND):
                bank = et % 4
                last_mm = 0
                for k in range(ND):
                    wl = [(s_wdtS, wdt_ready), (cac, xn_evs)]
                    if k == 0:
                        wl.append((cdv, LW.get(("ps_dt", bank), 0)))
                        wl.append((cac, LW.get(("ps_dt_a", bank), 0)))
                    p.put("tensor", lambda e, et=et, k=k, bank=bank: e.matmul(
                        ps_dt[:, bank * 512:(bank + 1) * 512],
                        wdtT[:, k * 2048 + et * 128: k * 2048 + (et + 1) * 128],
                        xnc[:, k * 512:(k + 1) * 512],
                        start=(k == 0), stop=(k == ND - 1)),
                        waits=wl, inc=None)
                last_mm = p.put("tensor", lambda e: e.drain(), inc=cpe)
                sp = bank % 2
                a1 = p.put("scalar", lambda e, et=et, bank=bank, sp=sp: e.activation(
                    spb[:, sp * 1024:sp * 1024 + 512],
                    ps_dt[:, bank * 512:(bank + 1) * 512], AF.Abs,
                    bias=dtb_sb[:, et:et + 1]),
                    waits=[(cpe, last_mm), (cdv, LW.get(("spb", sp), 0))], inc=cac)
                p.put("scalar", lambda e, sp=sp: e.activation(
                    spb[:, sp * 1024 + 512:sp * 1024 + 1024],
                    spb[:, sp * 1024:sp * 1024 + 512], AF.Exp, scale=-1.0), inc=cac)
                p.put("scalar", lambda e, sp=sp: e.activation(
                    spb[:, sp * 1024:sp * 1024 + 512],
                    spb[:, sp * 1024 + 512:sp * 1024 + 1024], AF.Ln, bias=csts[:, 3:4]), inc=cac)
                r1v = p.put("scalar", lambda e, et=et, bank=bank, sp=sp: e.activation(
                    spb[:, sp * 1024 + 512:sp * 1024 + 1024],
                    ps_dt[:, bank * 512:(bank + 1) * 512], AF.Relu,
                    bias=dtb_sb[:, et:et + 1]), inc=cac)
                LW[("ps_dt_a", bank)] = r1v
                dtc = p.put("vector", lambda e, et=et, sp=sp: e.tensor_add(
                    dtv[:, (et % 2) * 512:(et % 2) * 512 + 512],
                    spb[:, sp * 1024:sp * 1024 + 512],
                    spb[:, sp * 1024 + 512:sp * 1024 + 1024]),
                    waits=[(cac, r1v), (s_dtv[et % 2], LW.get(("dtv", et % 2), 0))], inc=cdv)
                LW[("ps_dt", bank)] = dtc
                LW[("spb", sp)] = dtc
                dsp = p.put("scalar", lambda e, c=c, et=et: e.dma_start(
                    bass.AP(dt_s, (et * 128) * TOK + c * 512, [[TOK, 128], [1, 512]]),
                    dtv[:, (et % 2) * 512:(et % 2) * 512 + 512]),
                    waits=[(cdv, dtc)], inc=s_dtv[et % 2], inck=16)
                LW[("dtv", et % 2)] = dsp
        p.emit(nc)
    bstack.close()
    wstack.close()
    QACT_B2 = qact.n

    if upto < 3:
        return _finish_debug(nc, dbg_outs, stack, qsp, qact, ccc)
    p = Plan()
    LW = {}
    s_xnh = dsem(p, "s_xnh")
    s_gs = [dsem(p, "s_gs0"), dsem(p, "s_gs1")]
    s_us = [dsem(p, "s_us0"), dsem(p, "s_us1")]
    s_dtp = [dsem(p, "s_dtp0"), dsem(p, "s_dtp1")]
    s_xnp = [dsem(p, "s_xnp0"), dsem(p, "s_xnp1")]
    s_bcb = dsem(p, "s_bcb"); s_ccb = dsem(p, "s_ccb")
    s_sb = [dsem(p, "s_sb0"), dsem(p, "s_sb1")]
    s_yb = [dsem(p, "s_yb0"), dsem(p, "s_yb1")]
    s_insp = [dsem(p, "s_insp0"), dsem(p, "s_insp1")]
    s_stc = dsem(p, "s_stc")
    bstack = ExitStack()
    _a = lambda nm, shape, dt: bstack.enter_context(nc.sbuf_tensor(nm, shape, dt))
    xnh = _a("xnh", [128, ND * 1024], BF16)
    gsl = _a("gsl", [128, 2 * ND * 256], BF16)
    usl = _a("usl", [128, 2 * ND * 256], BF16)
    qb = _a("qb", [128, 512], F32)
    pb = _a("pb", [128, 512], F32)
    tb = _a("tb", [128, 512], F32)
    rb = _a("rb", [128, 512], F32)
    inters = _a("inters", [128, 2 * 1024], BF16)
    dtc_s = _a("dtc_s", [128, ND * 512], F32)
    xncs = _a("xncs", [128, ND * 512], BF16)
    dtxb = _a("dtxb", [128, 2 * 512], F32)
    Sb = _a("Sb", [128, 2 * 512], F32)
    dab = _a("dab", [128, 2 * 512], F32)
    dbxb = _a("dbxb", [128, 2 * 512], F32)
    strip = _a("strip", [128, 4 * 512], F32)
    Bcb = _a("Bcb", [128, DS * 512], BF16)
    Ccb = _a("Ccb", [128, DS * 512], BF16)
    ytmp = _a("ytmp", [128, 2 * 512], F32)
    ybuf = _a("ybuf", [128, 2 * 512], F32)
    ps8 = bstack.enter_context(nc.psum_tensor("ps8", [128, 8 * 512], F32))
    if True:
        def emit_xnh_load(qt):
            last = 0
            for i in range(ND):
                last = p.put("sync", lambda e, qt=qt, i=i: e.dma_start(
                    xnh[:, i * 1024:(i + 1) * 1024],
                    bass.AP(xn_s, (i * 128) * TOK + qt * 1024, [[TOK, 128], [1, 1024]])),
                    waits=[(cpe, LW.get("xnh_free", 0))],
                    inc=s_xnh, inck=16)
            return last

        mt_count = [0]

        def emit_ffn_unit(qt, u, xnh_ready, filler=None):
            r, q8 = u >> 3, u & 7
            par = u % 2
            g_ld = p.put("sync", lambda e, r=r, q8=q8, par=par: e.dma_start(
                bass.AP(gsl, par * ND * 256, [[2 * ND * 256, 128], [256, ND], [1, 256]]),
                bass.AP(gW["g"], (r * 2048) * 2048 + q8 * 256,
                        [[2048, 128], [128 * 2048, ND], [1, 256]])),
                waits=[(ccc, AGG_DONE), (cpe, LW.get(("gsl", par), 0))],
                inc=s_gs[par], inck=16)
            u_ld = p.put("sync", lambda e, r=r, q8=q8, par=par: e.dma_start(
                bass.AP(usl, par * ND * 256, [[2 * ND * 256, 128], [256, ND], [1, 256]]),
                bass.AP(gW["u"], (r * 2048) * 2048 + q8 * 256,
                        [[2048, 128], [128 * 2048, ND], [1, 256]])),
                waits=[(ccc, AGU_DONE), (cpe, LW.get(("usl", par), 0))],
                inc=s_us[par], inck=16)
            for s in range(2):
                m128 = r * 16 + q8 * 2 + s
                mpar = mt_count[0] % 2
                mt_count[0] += 1
                gbank, ubank = mpar * 4, mpar * 4 + 2
                last_g = last_u = 0
                for k in range(ND):
                    for tch in range(2):
                        wl = [(s_gs[par], g_ld), (s_xnh, xnh_ready)]
                        if k == 0 and tch == 0:
                            wl.append((cdv, LW.get(("geluA", mpar), 0)))
                        last_g = p.put("tensor", lambda e, k=k, tch=tch, s=s, par=par,
                                       gbank=gbank: e.matmul(
                            ps8[:, (gbank + tch) * 512:(gbank + tch + 1) * 512],
                            gsl[:, par * ND * 256 + k * 256 + s * 128:
                                par * ND * 256 + k * 256 + s * 128 + 128],
                            xnh[:, k * 1024 + tch * 512: k * 1024 + (tch + 1) * 512],
                            start=(k == 0), stop=(k == ND - 1)),
                            waits=wl, inc=None)
                for k in range(ND):
                    for tch in range(2):
                        wl = [(s_us[par], u_ld)]
                        if k == 0 and tch == 0:
                            wl.append((cdv, LW.get(("geluB", mpar), 0)))
                        p.put("tensor", lambda e, k=k, tch=tch, s=s, par=par,
                              ubank=ubank: e.matmul(
                            ps8[:, (ubank + tch) * 512:(ubank + tch + 1) * 512],
                            usl[:, par * ND * 256 + k * 256 + s * 128:
                                par * ND * 256 + k * 256 + s * 128 + 128],
                            xnh[:, k * 1024 + tch * 512: k * 1024 + (tch + 1) * 512],
                            start=(k == 0), stop=(k == ND - 1)),
                            waits=wl, inc=None)
                last_u = p.put("tensor", lambda e: e.drain(), inc=cpe)
                for tch in range(2):
                    qv = p.put("scalar", lambda e, gbank=gbank, tch=tch: e.activation(
                        qb[:], ps8[:, (gbank + tch) * 512:(gbank + tch + 1) * 512],
                        AF.Square, scale=GELU_A2),
                        waits=[(cpe, last_u), (cdv, LW.get("qb_free", 0))], inc=cac)
                    pv = p.put("vector", lambda e, gbank=gbank, tch=tch: e.scalar_tensor_tensor(
                        pb[:], qb[:], 1.0,
                        ps8[:, (gbank + tch) * 512:(gbank + tch + 1) * 512],
                        OP.add, OP.mult),
                        waits=[(cac, qv), (cac, LW.get("pb_free", 0))], inc=cdv)
                    LW["qb_free"] = pv
                    tv = p.put("scalar", lambda e: e.activation(
                        tb[:], pb[:], AF.Tanh, scale=GELU_C),
                        waits=[(cdv, pv), (cdv, LW.get("tb_free", 0))], inc=cac)
                    LW["pb_free"] = tv
                    rv = p.put("vector", lambda e, gbank=gbank, tch=tch: e.scalar_tensor_tensor(
                        rb[:], tb[:], 1.0,
                        ps8[:, (gbank + tch) * 512:(gbank + tch + 1) * 512],
                        OP.add, OP.mult),
                        waits=[(cac, tv), (cdv, LW.get("rb_free", 0))], inc=cdv)
                    LW["tb_free"] = rv
                    iv = p.put("vector", lambda e, ubank=ubank, tch=tch, s=s: e.scalar_tensor_tensor(
                        inters[:, (s % 2) * 1024 + tch * 512:(s % 2) * 1024 + (tch + 1) * 512],
                        rb[:], 0.5,
                        ps8[:, (ubank + tch) * 512:(ubank + tch + 1) * 512],
                        OP.mult, OP.mult),
                        waits=[(s_insp[s % 2], LW.get(("inters", s % 2), 0))], inc=cdv)
                    LW["rb_free"] = iv
                LW[("geluA", mpar)] = cdv.n
                LW[("geluB", mpar)] = cdv.n
                isp = p.put("scalar", lambda e, m128=m128, qt=qt, s=s: e.dma_start(
                    bass.AP(inter_s, (m128 * 128) * TOK + qt * 1024, [[TOK, 128], [1, 1024]]),
                    inters[:, (s % 2) * 1024:(s % 2) * 1024 + 1024]),
                    waits=[(cdv, cdv.n)], inc=s_insp[s % 2], inck=16)
                LW[("inters", s % 2)] = isp
                if filler is not None:
                    filler()
            LW[("gsl", par)] = cpe.n
            LW[("usl", par)] = cpe.n

        def emit_scan_half_loads(c, h):
            for part in (2 * h, 2 * h + 1):
                p.put("scalar", lambda e, c=c, part=part: e.dma_start(
                    bass.AP(dtc_s, part * 4 * 512, [[ND * 512, 128], [512, 4], [1, 512]]),
                    bass.AP(dt_s, (part * 4 * 128) * TOK + c * 512,
                            [[TOK, 128], [128 * TOK, 4], [1, 512]])),
                    waits=[(cdv, LW.get(("dtcF", h), 0)),
                           (cac, LW.get(("dtcFa", h), 0))], inc=s_dtp[h], inck=16)
                p.put("scalar", lambda e, c=c, part=part: e.dma_start(
                    bass.AP(xncs, part * 4 * 512, [[ND * 512, 128], [512, 4], [1, 512]]),
                    bass.AP(xn_s, (part * 4 * 128) * TOK + c * 512,
                            [[TOK, 128], [128 * TOK, 4], [1, 512]])),
                    waits=[(cdv, LW.get(("dtcF", h), 0))], inc=s_xnp[h], inck=16)

        def emit_bc_loads(c):
            for n in range(DS):
                p.put("gpsimd", lambda e, c=c, n=n: e.dma_start(
                    Bcb[:, n * 512:(n + 1) * 512],
                    bass.AP(bc_s, n * TOK + c * 512, [[0, 128], [1, 512]])),
                    waits=[(cpl, LW.get("Bcb_free", 0))], inc=s_bcb, inck=16)
                p.put("gpsimd", lambda e, c=c, n=n: e.dma_start(
                    Ccb[:, n * 512:(n + 1) * 512],
                    bass.AP(bc_s, (16 + n) * TOK + c * 512, [[0, 128], [1, 512]])),
                    waits=[(cdv, LW.get("Ccb_free", 0))], inc=s_ccb, inck=16)

        def emit_scan_dtile_head(c, i):
            h = i // 8
            ld_done = (32 * (c + 1), 32 * (c + 1))
            ip = i % 2
            dtx = p.put("vector", lambda e, i=i, ip=ip: e.tensor_mul(
                dtxb[:, ip * 512:ip * 512 + 512],
                dtc_s[:, i * 512:(i + 1) * 512], xncs[:, i * 512:(i + 1) * 512]),
                waits=[(s_dtp[h], ld_done[0]), (s_xnp[h], ld_done[1]),
                       (cpl, LW.get(("dtxb", ip), 0))], inc=cdv)
            ssc = p.put("vector", lambda e, i=i, ip=ip: e.tensor_tensor_scan(
                Sb[:, ip * 512:ip * 512 + 512],
                dtc_s[:, i * 512:(i + 1) * 512], zeros[:, 0:512],
                sstate[:, i:i + 1], OP.add, OP.add),
                waits=[(s_sb[ip], LW.get(("Sb", ip), 0))], inc=cdv)
            p.put("vector", lambda e, i=i, ip=ip: e.tensor_copy(
                sstate[:, i:i + 1], Sb[:, ip * 512 + 511:ip * 512 + 512]),
                waits=[(cdv, ssc)], inc=cdv)
            ssp = p.put("gpsimd", lambda e, c=c, i=i, ip=ip: e.dma_start(
                bass.AP(ssum_s, (i * 128) * TOK + c * 512, [[TOK, 128], [1, 512]]),
                Sb[:, ip * 512:ip * 512 + 512]),
                waits=[(cdv, ssc)], inc=s_sb[ip], inck=16)
            LW[("Sb", ip)] = ssp
            if i == 7 or i == 15:
                LW[("dtcF", h)] = cdv.n
                LW[("dtcFa", h)] = cac.n
            sc_state["cur"] = {"dtx": dtx, "ip": ip}

        def emit_scan_dtile_g(c, i, g):
            h = i // 8
            ld_done = (32 * (c + 1), 32 * (c + 1))
            bc_done = (256 * (c + 1), 256 * (c + 1))
            dtx = sc_state["cur"]["dtx"]
            ip = sc_state["cur"]["ip"]
            for nn in range(4):
                n = g * 4 + nn
                ab = n % 2
                da = p.put("scalar", lambda e, i=i, n=n, ab=ab: e.activation(
                    dab[:, ab * 512:(ab + 1) * 512],
                    dtc_s[:, i * 512:(i + 1) * 512], AF.Exp,
                    scale=aneg[:, i * DS + n:i * DS + n + 1]),
                    waits=[(s_dtp[h], ld_done[0]), (cdv, LW.get(("dab", ab), 0))],
                    inc=cac)
                dbx = p.put("gpsimd", lambda e, i=i, n=n, ab=ab, ip=ip: e.tensor_mul(
                    dbxb[:, ab * 512:(ab + 1) * 512],
                    dtxb[:, ip * 512:ip * 512 + 512],
                    Bcb[:, n * 512:(n + 1) * 512]),
                    waits=[(cdv, dtx), (s_bcb, bc_done[0]),
                           (cdv, LW.get(("dbxb", ab), 0))], inc=cpl)
                sc = p.put("vector", lambda e, i=i, n=n, nn=nn, ab=ab: e.tensor_tensor_scan(
                    strip[:, nn * 512:(nn + 1) * 512],
                    dab[:, ab * 512:(ab + 1) * 512],
                    dbxb[:, ab * 512:(ab + 1) * 512],
                    state[:, i * DS + n:i * DS + n + 1], OP.mult, OP.add),
                    waits=[(cac, da), (cpl, dbx)], inc=cdv)
                LW[("dab", ab)] = sc
                LW[("dbxb", ab)] = sc
                last_sc = sc
            p.put("vector", lambda e, i=i, g=g: e.tensor_copy(
                state[:, i * DS + g * 4: i * DS + g * 4 + 4],
                bass.AP(strip, 511, [[2048, 128], [512, 4]])),
                waits=[(cdv, last_sc)], inc=cdv)
            for nn in range(4):
                n = g * 4 + nn
                yp = n % 2
                yt = p.put("vector", lambda e, n=n, nn=nn, yp=yp: e.tensor_mul(
                    ytmp[:, yp * 512:(yp + 1) * 512],
                    strip[:, nn * 512:(nn + 1) * 512],
                    Ccb[:, n * 512:(n + 1) * 512]),
                    waits=[(s_ccb, bc_done[1]),
                           (cpl, LW.get(("ytmp", yp), 0))], inc=cdv)
                if n == 0:
                    ya = p.put("gpsimd", lambda e, ip=ip: e.tensor_copy(
                        ybuf[:, ip * 512:ip * 512 + 512], ytmp[:, 0:512]),
                        waits=[(cdv, yt), (s_yb[ip], LW.get(("ybuf", ip), 0))],
                        inc=cpl)
                else:
                    ya = p.put("gpsimd", lambda e, n=n, ip=ip, yp=yp: e.tensor_add(
                        ybuf[:, ip * 512:ip * 512 + 512],
                        ybuf[:, ip * 512:ip * 512 + 512],
                        ytmp[:, yp * 512:(yp + 1) * 512]),
                        waits=[(cdv, yt)], inc=cpl)
                LW[("ytmp", yp)] = ya
            if g == 3:
                ysp = p.put("gpsimd", lambda e, c=c, i=i, ip=ip: e.dma_start(
                    bass.AP(y_s, (i * 128) * TOK + c * 512, [[TOK, 128], [1, 512]]),
                    ybuf[:, ip * 512:ip * 512 + 512]),
                    inc=s_yb[ip], inck=16)
                LW[("ybuf", ip)] = ysp
                LW[("dtxb", ip)] = cpl.n
                if i == 15:
                    LW["Bcb_free"] = cpl.n
                    LW["Ccb_free"] = cdv.n

        sc_state = {"c": 0, "i": 0, "ph": 0, "h0": False, "cur": {}}

        def emit_scan_heads0():
            emit_scan_half_loads(0, 0)
            emit_scan_half_loads(0, 1)
            emit_bc_loads(0)
            sc_state["h0"] = True

        def emit_scan_micro():
            c, i, ph = sc_state["c"], sc_state["i"], sc_state["ph"]
            if c >= NCH:
                return
            if not sc_state["h0"]:
                emit_scan_heads0()
            if ph == 0:
                emit_scan_dtile_head(c, i)
                sc_state["ph"] = 1
                return
            emit_scan_dtile_g(c, i, ph - 1)
            if ph < 4:
                sc_state["ph"] = ph + 1
                return
            sc_state["ph"] = 0
            if i == 7 and c + 1 < NCH:
                emit_scan_half_loads(c + 1, 0)
            if i == 15:
                if c + 1 < NCH:
                    emit_scan_half_loads(c + 1, 1)
                    emit_bc_loads(c + 1)
                sc_state["c"], sc_state["i"] = c + 1, 0
            else:
                sc_state["i"] = i + 1

        if B3A_MODE != 0:
            filler = emit_scan_micro if B3A_MODE == 2 else None
            if B3A_MODE == 2:
                emit_scan_heads0()
            for qt in range(4):
                xnh_rdy = emit_xnh_load(qt)
                for u in range(64):
                    emit_ffn_unit(qt, u, xnh_rdy, filler)
                    if B3A_MODE == 2:
                        emit_scan_micro()
                LW["xnh_free"] = cpe.n
        while sc_state["c"] < NCH:
            emit_scan_micro()
        stp = p.put("scalar", lambda e: e.dma_start(st_c.ap(), state[:]),
                    waits=[(cdv, cdv.n), (cpl, cpl.n)], inc=s_stc, inck=16)
        p.put("gpsimd", lambda e: e.collective_compute(
            "AllGather", OP.bypass, replica_groups=[list(range(8))],
            ins=[st_c.ap()], outs=[st_g.ap()]),
            waits=[(s_stc, stp)], inc=ccc)
        p.put("gpsimd", lambda e: e.collective_compute(
            "AllGather", OP.bypass, replica_groups=[list(range(8))],
            ins=[st_c.ap()], outs=[st_g.ap()]),
            inc=ccc)
        p.finish_dma()
        p.emit(nc)
    bstack.close()
    QACT_B3A = qact.n

    if upto < 4:
        return _finish_debug(nc, dbg_outs, stack, qsp, qact, ccc)
    p = Plan()
    LW = {}
    s_hin = dsem(p, "s_hin")
    s_yl = [dsem(p, "s_yl0"), dsem(p, "s_yl1")]
    s_sc2 = [dsem(p, "s_sc2a"), dsem(p, "s_sc2b")]
    s_cc2 = [dsem(p, "s_cc2a"), dsem(p, "s_cc2b")]
    s_yw = [dsem(p, "s_yw0"), dsem(p, "s_yw1")]
    s_int = [dsem(p, "s_int0"), dsem(p, "s_int1")]
    s_rb = [dsem(p, f"s_rb{i}") for i in range(4)]
    s_gw = dsem(p, "s_gw")
    bstack = ExitStack()
    _a = lambda nm, shape, dt: bstack.enter_context(nc.sbuf_tensor(nm, shape, dt))
    interc = _a("interc", [128, 128 * 512], BF16)
    Ccb2 = _a("Ccb2", [128, 2 * DS * 512], BF16)
    rbk = _a("rbk", [128, 4 * 1024], BF16)
    tnb = _a("tnb", [128, 512], F32)
    gvb = _a("gvb", [128, 512], F32)
    Sc2 = _a("Sc2", [128, 2 * 512], F32)
    Eb = _a("Eb", [128, 2 * 512], F32)
    ctmp = _a("ctmp", [128, 2 * 512], F32)
    yacc = _a("yacc", [128, 2 * 512], F32)
    yld = _a("yld", [128, 2 * 512], F32)
    psD = bstack.enter_context(nc.psum_tensor("psD", [128, 8 * 512], F32))
    if True:
        hin_ready = 0
        for r in range(8):
            hl = p.put("gpsimd", lambda e, r=r: e.dma_start(
                hin_st[:, 0:256],
                st_g[r * 128:(r + 1) * 128, :]),
                waits=[(ccc, AGS_DONE), (cdv, hin_ready)], inc=s_hin, inck=16)
            hin_ready = p.put("vector", lambda e, r=r: e.scalar_tensor_tensor(
                hin[:], hin_st[:, 0:256], sel8_sb[:, r:r + 1], hin[:], OP.mult, OP.add),
                waits=[(s_hin, hl)], inc=cdv)

        prev_add = {"eng": None, "val": 0}

        def emit_sc2_load(ct):
            c, i = divmod(ct, 16)
            if c >= NCH:
                return
            sl = ct % 2
            p.put("scalar", lambda e, c=c, i=i, sl=sl: e.dma_start(
                Sc2[:, sl * 512:(sl + 1) * 512],
                bass.AP(ssum_s, (i * 128) * TOK + c * 512, [[TOK, 128], [1, 512]])),
                inc=s_sc2[sl], inck=16)

        def emit_yld_load(ct):
            c, i = divmod(ct, 16)
            if c >= NCH:
                return
            sl = ct % 2
            p.put("scalar", lambda e, c=c, i=i, sl=sl: e.dma_start(
                yld[:, sl * 512:(sl + 1) * 512],
                bass.AP(y_s, (i * 128) * TOK + c * 512, [[TOK, 128], [1, 512]])),
                waits=[(cpl, LW.get(("yldF", sl), 0))], inc=s_yl[sl], inck=16)

        def emit_corr_heads(c):
            cp = c % 2
            for n in range(DS):
                p.put("gpsimd", lambda e, c=c, cp=cp, n=n: e.dma_start(
                    Ccb2[:, cp * DS * 512 + n * 512:cp * DS * 512 + (n + 1) * 512],
                    bass.AP(bc_s, (16 + n) * TOK + c * 512, [[0, 128], [1, 512]])),
                    waits=[(cdv, LW.get(("cc2F", cp), 0))], inc=s_cc2[cp], inck=16)

        cr_state = {"c": 0, "i": 0}

        def emit_corr_piece():
            c, i = cr_state["c"], cr_state["i"]
            if c >= NCH:
                return
            cp = c % 2
            ct = c * 16 + i
            sl = ct % 2
            ya_off = sl * 512
            for n in range(DS):
                eb = n % 2
                ev = p.put("scalar", lambda e, i=i, n=n, sl=sl: e.activation(
                    Eb[:, (n % 2) * 512:(n % 2) * 512 + 512],
                    Sc2[:, sl * 512:(sl + 1) * 512], AF.Exp,
                    scale=aneg[:, i * DS + n:i * DS + n + 1]),
                    waits=[(s_sc2[sl], 16 * (ct // 2 + 1)),
                           (cdv, LW.get(("Eb2", eb), 0))], inc=cac)
                wl = [(cac, ev), (s_cc2[cp], 256 * (c // 2 + 1)), (cdv, hin_ready)]
                if n == 0:
                    wl.append((s_yw[sl], 16 * (ct // 2)))
                    tv = p.put("vector", lambda e, i=i, n=n, cp=cp, ya_off=ya_off:
                               e.scalar_tensor_tensor(
                        yacc[:, ya_off:ya_off + 512],
                        Eb[:, 0:512],
                        hin[:, i * DS + n:i * DS + n + 1],
                        Ccb2[:, cp * DS * 512 + n * 512:cp * DS * 512 + (n + 1) * 512],
                        OP.mult, OP.mult),
                        waits=wl, inc=cdv)
                    LW[("Eb2", eb)] = tv
                    prev_add["eng"], prev_add["val"] = "vector", tv
                    continue
                wctmp = LW.get(("ctmp2", eb), None)
                if wctmp is not None:
                    wl.append(wctmp)
                tv = p.put("vector", lambda e, i=i, n=n, eb=eb, cp=cp: e.scalar_tensor_tensor(
                    ctmp[:, eb * 512:(eb + 1) * 512],
                    Eb[:, eb * 512:(eb + 1) * 512],
                    hin[:, i * DS + n:i * DS + n + 1],
                    Ccb2[:, cp * DS * 512 + n * 512:cp * DS * 512 + (n + 1) * 512],
                    OP.mult, OP.mult),
                    waits=wl, inc=cdv)
                LW[("Eb2", eb)] = tv
                if n % 2 == 1:
                    wl2 = [(cdv, tv)]
                    if prev_add["eng"] == "vector":
                        wl2.append((cdv, prev_add["val"]))
                    ya = p.put("gpsimd", lambda e, ya_off=ya_off, eb=eb: e.tensor_add(
                        yacc[:, ya_off:ya_off + 512], yacc[:, ya_off:ya_off + 512],
                        ctmp[:, eb * 512:(eb + 1) * 512]),
                        waits=wl2, inc=cpl)
                    prev_add["eng"], prev_add["val"] = "gpsimd", ya
                    LW[("ctmp2", eb)] = (cpl, ya)
                else:
                    wl2 = [(cdv, tv)]
                    if prev_add["eng"] == "gpsimd":
                        wl2.append((cpl, prev_add["val"]))
                    ya = p.put("vector", lambda e, ya_off=ya_off, eb=eb: e.tensor_add(
                        yacc[:, ya_off:ya_off + 512], yacc[:, ya_off:ya_off + 512],
                        ctmp[:, eb * 512:(eb + 1) * 512]),
                        waits=wl2, inc=cdv)
                    prev_add["eng"], prev_add["val"] = "vector", ya
                    LW[("ctmp2", eb)] = (cdv, ya)
            wl3 = [(s_yl[sl], 16 * (ct // 2 + 1))]
            if prev_add["eng"] == "vector":
                wl3.append((cdv, prev_add["val"]))
            fa = p.put("gpsimd", lambda e, ya_off=ya_off, sl=sl: e.tensor_add(
                yacc[:, ya_off:ya_off + 512], yacc[:, ya_off:ya_off + 512],
                yld[:, sl * 512:(sl + 1) * 512]),
                waits=wl3, inc=cpl)
            prev_add["eng"], prev_add["val"] = "gpsimd", fa
            LW[("yldF", sl)] = fa
            p.put("gpsimd", lambda e, c=c, i=i, ya_off=ya_off: e.dma_start(
                bass.AP(y_s, (i * 128) * TOK + c * 512, [[TOK, 128], [1, 512]]),
                yacc[:, ya_off:ya_off + 512]),
                inc=s_yw[sl], inck=16)
            emit_sc2_load(ct + 1)
            emit_yld_load(ct + 1)
            if i == 8 and c + 1 < NCH:
                emit_corr_heads(c + 1)
            if i == 15:
                LW[("cc2F", cp)] = cdv.n
                cr_state["c"], cr_state["i"] = c + 1, 0
            else:
                cr_state["i"] = i + 1

        def emit_interc_half(c2, h):
            if c2 >= NCH:
                return
            for part in range(8 * h, 8 * h + 8):
                p.put("sync", lambda e, c2=c2, part=part: e.dma_start(
                    bass.AP(interc, part * 8 * 512, [[128 * 512, 128], [512, 8], [1, 512]]),
                    bass.AP(inter_s, (part * 8 * 128) * TOK + c2 * 512,
                            [[TOK, 128], [128 * TOK, 8], [1, 512]])),
                    waits=[(cpe, LW.get(("intF", h), 0))],
                    inc=s_int[h], inck=16)

        def emit_down_tchunk(c2, filler):
            for hh in range(2):
                last_mm = 0
                for nt in range(128):
                    rpar = nt % 4
                    rld = p.put("sync", lambda e, nt=nt, hh=hh, rpar=rpar: e.dma_start(
                        rbk[:, rpar * 1024:(rpar + 1) * 1024],
                        bass.AP(gW["d"], (nt * 128) * 2048 + hh * 1024,
                                [[2048, 128], [1, 1024]])),
                        waits=[(ccc, AGD_DONE), (cpe, LW.get(("rbk", rpar), 0))],
                        inc=s_rb[rpar], inck=16)
                    for h8 in range(8):
                        wl = [(s_rb[rpar], rld), (s_int[nt // 64], 128 * (c2 + 1))]
                        if nt == 0 and h8 == 0:
                            wl.append((cdv, LW.get("psD_free", 0)))
                        last_mm = p.put("tensor", lambda e, nt=nt, h8=h8, rpar=rpar: e.matmul(
                            psD[:, h8 * 512:(h8 + 1) * 512],
                            rbk[:, rpar * 1024 + h8 * 128: rpar * 1024 + (h8 + 1) * 128],
                            interc[:, nt * 512:(nt + 1) * 512],
                            start=(nt == 0), stop=(nt == 127)),
                            waits=wl, inc=(cpe if h8 == 7 else None))
                    LW[("rbk", rpar)] = last_mm
                    if nt % 8 == 7 and filler is not None:
                        filler()
                    if hh == 1 and nt == 63:
                        LW[("intF", 0)] = cpe.n
                        emit_interc_half(c2 + 1, 0)
                for h8 in range(8):
                    tnv = p.put("scalar", lambda e, h8=h8: e.activation(
                        tnb[:, 0:512], psD[:, h8 * 512:(h8 + 1) * 512],
                        AF.Tanh, scale=0.5),
                        waits=[(cpe, last_mm), (cdv, LW.get(("tnb", 0), 0))], inc=cac)
                    gv = p.put("vector", lambda e, h8=h8: e.scalar_tensor_tensor(
                        gvb[:, 0:512], tnb[:, 0:512],
                        1.0, psD[:, h8 * 512:(h8 + 1) * 512], OP.add, OP.mult),
                        waits=[(cac, tnv), (s_gw, LW.get(("gvb", 0), 0))], inc=cdv)
                    LW[("tnb", 0)] = gv
                    gsp = p.put("scalar", lambda e, c2=c2, hh=hh, h8=h8: e.dma_start(
                        bass.AP(g_s, ((hh * 8 + h8) * 128) * TOK + c2 * 512,
                                [[TOK, 128], [1, 512]]),
                        gvb[:, 0:512]),
                        waits=[(cdv, gv)], inc=s_gw, inck=16)
                    LW[("gvb", 0)] = gsp
                LW["psD_free"] = cdv.n
            LW[("intF", 1)] = cpe.n
            emit_interc_half(c2 + 1, 1)

        filler = emit_corr_piece if B3B_MODE not in (0,) else None
        if B3B_MODE != 0:
            emit_corr_heads(0)
            emit_sc2_load(0)
            emit_yld_load(0)
        if B3B_MODE != 1:
            emit_interc_half(0, 0)
            emit_interc_half(0, 1)
            for c2 in range(NCH):
                emit_down_tchunk(c2, filler)
        while B3B_MODE != 0 and cr_state["c"] < NCH:
            emit_corr_piece()
        p.finish_dma()
        p.emit(nc)
    bstack.close()
    QACT_B3B = qact.n

    if upto < 5:
        return _finish_debug(nc, dbg_outs, stack, qsp, qact, ccc)
    p = Plan()
    LW = {}
    s_wo = dsem(p, "s_wo")
    s_ym = [dsem(p, "s_ym0"), dsem(p, "s_ym1")]
    s_gb = dsem(p, "s_gb")
    s_x4 = [dsem(p, "s_x40"), dsem(p, "s_x41")]
    s_out = dsem(p, "s_out")
    bstack = ExitStack()
    _a = lambda nm, shape, dt: bstack.enter_context(nc.sbuf_tensor(nm, shape, dt))
    woutT = _a("woutT", [128, ND * 2048], BF16)
    ysm = _a("ysm", [128, 2 * 512], F32)
    ybf = _a("ybf", [128, ND * 512], BF16)
    gbig = _a("gbig", [128, ND * 512], F32)
    sA = _a("sA", [128, 2 * 512], F32)
    ubig = _a("ubig", [128, ND * 512], F32)
    utok = _a("utok", [128, 2 * 2048], F32)
    xt4 = _a("xt4", [128, 2 * 2048], F32)
    ob = _a("ob", [128, 2048], F32)
    psS = bstack.enter_context(nc.psum_tensor("psS", [128, 4 * 512], F32))
    psTa = bstack.enter_context(nc.psum_tensor("psTa", [128, 128], F32))
    psTb = bstack.enter_context(nc.psum_tensor("psTb", [128, 128], F32))
    psTP = [psTa, psTb]
    if True:
        wo_ready = 0
        for r in range(8):
            wo_ready = p.put("sync", lambda e, r=r: e.dma_start(
                bass.AP(woutT, r * 256, [[ND * 2048, 128], [2048, ND], [1, 256]]),
                bass.AP(g1, (2048 + r * 4096) * 256, [[256, 128], [128 * 256, ND], [1, 256]])),
                inc=s_wo, inck=16)
        for c in range(NCH):
            g_ld4 = 0
            for part in range(4):
                g_ld4 = p.put("scalar", lambda e, c=c, part=part: e.dma_start(
                    bass.AP(gbig, part * 4 * 512, [[ND * 512, 128], [512, 4], [1, 512]]),
                    bass.AP(g_s, (part * 4 * 128) * TOK + c * 512,
                            [[TOK, 128], [128 * TOK, 4], [1, 512]])),
                    waits=[(cdv, LW.get("gbig_free", 0))],
                    inc=s_gb, inck=16)
            cast_done = 0
            for i in range(ND):
                ipp = i % 2
                yld1 = p.put("scalar", lambda e, c=c, i=i, ipp=ipp: e.dma_start(
                    ysm[:, ipp * 512:(ipp + 1) * 512],
                    bass.AP(y_s, (i * 128) * TOK + c * 512, [[TOK, 128], [1, 512]])),
                    waits=[(cac, LW.get(("ysm", ipp), 0))],
                    inc=s_ym[ipp], inck=16)
                cast_done = p.put("scalar", lambda e, i=i, ipp=ipp: e.activation(
                    ybf[:, i * 512:(i + 1) * 512], ysm[:, ipp * 512:(ipp + 1) * 512],
                    AF.Copy),
                    waits=[(s_ym[ipp], yld1), (cpe, LW.get("ybf_free", 0))], inc=cac)
                LW[("ysm", ipp)] = cast_done
            for mt in range(ND):
                bank = mt % 4
                last_mm = 0
                for k in range(ND):
                    wl = [(s_wo, wo_ready), (cac, cast_done)]
                    if k == 0:
                        wl.append((cdv, LW.get(("psS", bank), 0)))
                    p.put("tensor", lambda e, mt=mt, k=k, bank=bank: e.matmul(
                        psS[:, bank * 512:(bank + 1) * 512],
                        woutT[:, k * 2048 + mt * 128: k * 2048 + (mt + 1) * 128],
                        ybf[:, k * 512:(k + 1) * 512],
                        start=(k == 0), stop=(k == ND - 1)),
                        waits=wl, inc=None)
                last_mm = p.put("tensor", lambda e: e.drain(), inc=cpe)
                sev = p.put("scalar", lambda e, mt=mt, bank=bank: e.activation(
                    sA[:, (mt % 2) * 512:(mt % 2) * 512 + 512],
                    psS[:, bank * 512:(bank + 1) * 512], AF.Identity,
                    bias=outb_sb[:, mt:mt + 1]),
                    waits=[(cpe, last_mm), (cdv, LW.get(("sA", mt % 2), 0))], inc=cac)
                uv = p.put("vector", lambda e, mt=mt: e.scalar_tensor_tensor(
                    ubig[:, mt * 512:(mt + 1) * 512],
                    sA[:, (mt % 2) * 512:(mt % 2) * 512 + 512], 0.5,
                    gbig[:, mt * 512:(mt + 1) * 512], OP.mult, OP.mult),
                    waits=[(cac, sev), (s_gb, g_ld4), (cpe, LW.get("ubig_free", 0))],
                    inc=cdv)
                LW[("sA", mt % 2)] = uv
                LW[("psS", bank)] = uv
            u_done = cdv.n
            LW["gbig_free"] = u_done
            for tau in range(4):
                tp = tau % 2
                xld = p.put("sync", lambda e, c=c, tau=tau, tp=tp: e.dma_start(
                    xt4[:, tp * 2048:(tp + 1) * 2048],
                    x_in[c * 512 + tau * 128: c * 512 + (tau + 1) * 128, :]),
                    waits=[(cdv, LW.get(("xt4", tp), 0))], inc=s_x4[tp], inck=16)
                ev_last = 0
                for i0 in range(0, ND, 2):
                    for j in range(2):
                        i = i0 + j
                        p.put("tensor", lambda e, i=i, sl=j, tau=tau: e.transpose(
                            psTP[sl][:, :],
                            bass.AP(ubig, i * 512 + tau * 128, [[ND * 512, 128], [1, 128]]),
                            idf32[:]),
                            waits=[(cdv, u_done), (cac, LW.get(("psT", j), 0))], inc=None)
                    dr = p.put("tensor", lambda e: e.drain(), inc=cpe)
                    for j in range(2):
                        i = i0 + j
                        ev_last = p.put("scalar", lambda e, i=i, sl=j, tp=tp: e.copy(
                            utok[:, tp * 2048 + i * 128: tp * 2048 + (i + 1) * 128],
                            psTP[sl][:, :]),
                            waits=[(cpe, dr), (cdv, LW.get(("utok", tp), 0))], inc=cac)
                        LW[("psT", j)] = ev_last
                oadd = p.put("vector", lambda e, tp=tp: e.tensor_add(
                    ob[:, 0:2048], xt4[:, tp * 2048:(tp + 1) * 2048],
                    utok[:, tp * 2048:(tp + 1) * 2048]),
                    waits=[(cac, ev_last), (s_x4[tp], xld), (s_out, LW.get(("ob", 0), 0))],
                    inc=cdv)
                LW[("xt4", tp)] = oadd
                LW[("utok", tp)] = oadd
                ost = p.put("scalar", lambda e, c=c, tau=tau: e.dma_start(
                    out_sh[c * 512 + tau * 128: c * 512 + (tau + 1) * 128, :],
                    ob[:, 0:2048]),
                    waits=[(cdv, oadd)], inc=s_out, inck=16)
                LW[("ob", 0)] = ost
            LW["ybf_free"] = cpe.n
            LW["ubig_free"] = cpe.n
        p.finish_dma()
        p.emit(nc)
    bstack.close()

    stack.close()
    nc.finalize()
    return nc


def _finish_debug(nc, dbg_outs, stack, qsp, qact, ccc=None):
    p = Plan()
    if ccc is not None and ccc.n:
        p.put("sync", lambda e: e.wait_ge(ccc.h, ccc.n) and None)
    n = 0
    for od, ih in dbg_outs:
        rows = ih.shape[0]
        step = max(1, min(rows, (1 << 20) // max(1, int(ih.shape[1]))))
        r = 0
        while r < rows:
            rr = min(step, rows - r)
            n = p.put("sync", lambda e, od=od, ih=ih, r=r, rr=rr: e.dma_start(
                od[r:r + rr, :], ih[r:r + rr, :]),
                waits=[(qsp, qsp.n), (qact, qact.n)], inc=qsp, inck=16)
            r += rr
    p.put("sync", lambda e: e.wait_ge(qsp.h, qsp.n) and None)
    p.put("scalar", lambda e: e.wait_ge(qact.h, qact.n) and None)
    p.emit(nc)
    stack.close()
    nc.finalize()
    return nc


_CACHE = {}


def kernel(L0_attn_x, L0_attn_A_log, L0_attn_x_proj_w, L0_attn_x_proj_b,
           L0_attn_dt_proj_w, L0_attn_dt_proj_b, L0_attn_out_proj_w,
           L0_attn_out_proj_b, L0_ffn_gate_W_packed, L0_ffn_gate_scales,
           L0_ffn_up_W_packed, L0_ffn_up_scales, L0_ffn_down_W_packed,
           L0_ffn_down_scales):
    x = np.ascontiguousarray(np.asarray(L0_attn_x, dtype=np.float32))
    if "nc" not in _CACHE:
        _CACHE["nc"] = build_program()
    nc = _CACHE["nc"]

    gwp = np.asarray(L0_ffn_gate_W_packed, dtype=np.int32)
    gsc = np.asarray(L0_ffn_gate_scales, dtype=np.float32)
    uwp = np.asarray(L0_ffn_up_W_packed, dtype=np.int32)
    usc = np.asarray(L0_ffn_up_scales, dtype=np.float32)
    dwp = np.asarray(L0_ffn_down_W_packed, dtype=np.int32)
    dsc = np.asarray(L0_ffn_down_scales, dtype=np.float32)
    wdt = np.asarray(L0_attn_dt_proj_w, dtype=np.float32)
    wout = np.asarray(L0_attn_out_proj_w, dtype=np.float32)

    in_maps = []
    for c in range(NCORE):
        b, hf = c // 2, c % 2
        sel = np.zeros((128, 8), np.float32)
        if hf == 1:
            sel[:, c - 1] = 1.0
        in_maps.append({
            "x_sh": np.ascontiguousarray(x[b, hf * TOK:(hf + 1) * TOK, :]),
            "gwp": np.ascontiguousarray(gwp[c * 2048:(c + 1) * 2048, :]),
            "gsc": np.ascontiguousarray(gsc[c * 2048:(c + 1) * 2048, :]),
            "uwp": np.ascontiguousarray(uwp[c * 2048:(c + 1) * 2048, :]),
            "usc": np.ascontiguousarray(usc[c * 2048:(c + 1) * 2048, :]),
            "dwp": np.ascontiguousarray(dwp[:, c * 1024:(c + 1) * 1024]),
            "dsc": np.ascontiguousarray(dsc[:, c * 64:(c + 1) * 64]),
            "wdt_sh": np.ascontiguousarray(wdt[c * 256:(c + 1) * 256, :]),
            "wout_sh": np.ascontiguousarray(wout[c * 256:(c + 1) * 256, :]),
            "alog": np.ascontiguousarray(np.asarray(L0_attn_A_log, np.float32)),
            "xpw": np.ascontiguousarray(np.asarray(L0_attn_x_proj_w, np.float32)),
            "xpb": np.ascontiguousarray(np.asarray(L0_attn_x_proj_b, np.float32)),
            "dtb": np.ascontiguousarray(np.asarray(L0_attn_dt_proj_b, np.float32)),
            "outb": np.ascontiguousarray(np.asarray(L0_attn_out_proj_b, np.float32)),
            "sel8": sel,
        })
    res = run_bass_kernel_spmd(nc, in_maps, core_ids=list(range(NCORE)))
    out = np.empty((B_, L_, D), np.float32)
    for c in range(NCORE):
        b, hf = c // 2, c % 2
        out[b, hf * TOK:(hf + 1) * TOK, :] = res.results[c]["out_sh"]
    return out


# version 06
# speedup vs baseline: 1.4208x; 1.0607x over previous
import numpy as np
from contextlib import ExitStack

import concourse.bass as bass
import concourse.mybir as mybir
from concourse.bass_utils import run_bass_kernel_spmd

F32 = mybir.dt.float32
BF16 = mybir.dt.bfloat16
I32 = mybir.dt.int32
AF = mybir.ActivationFunctionType
OP = mybir.AluOpType
AX = mybir.AxisListType

NCORE = 8
B_, L_, D, DS, DM = 4, 8192, 2048, 16, 16384
TOK = 4096
CH = 512
NCH = TOK // CH
QT = 1024
ND = D // 128
GELU_C = 0.7978845608028654
GELU_A2 = 0.21145925545
EPS = 1e-5

ENGS = ("sync", "scalar", "tensor", "vector", "gpsimd")
DEQ_NW = 3
DEQ_NJT = 16
B3B_MODE = 2
B3A_MODE = 2


class Ctr:
    def __init__(self, stack, nc, name):
        self.h = stack.enter_context(nc.semaphore(name))
        self.n = 0
        self.name = name


class Plan:
    def __init__(self):
        self.streams = {e: [] for e in ENGS}
        self.lastwait = {e: {} for e in ENGS}
        self.dsems = []

    def finish_dma(self, extra=()):
        for ctr in list(self.dsems) + list(extra):
            if ctr.n:
                self.put("sync", (lambda e, c=ctr, v=ctr.n: e.wait_ge(c.h, v) and None))

    def put(self, eng, fn, waits=(), inc=None, inck=1):
        lw = self.lastwait[eng]
        w2 = []
        for ctr, val in waits:
            if val is None or val <= 0:
                continue
            if lw.get(ctr.name, -1) >= val:
                continue
            lw[ctr.name] = val
            w2.append((ctr, val))
        if inc is not None:
            inc.n += inck
        self.streams[eng].append((fn, w2, inc, inck))
        return None if inc is None else inc.n

    def emit(self, nc):
        with nc.Block() as blk:
            def runner(items):
                def run(e):
                    for fn, waits, inc, inck in items:
                        for ctr, val in waits:
                            e.wait_ge(ctr.h, val)
                        inst = fn(e)
                        if inc is not None:
                            inst.then_inc(inc.h, inck)
                return run
            for eng in ENGS:
                if self.streams[eng]:
                    getattr(blk, eng)(runner(self.streams[eng]))


def build_program(upto=9, dbg=(), b1part=9):
    nc = bass.Bass("TRN2", target_bir_lowering=False)

    x_in = nc.dram_tensor("x_sh", [TOK, D], F32, kind="ExternalInput")
    wp_in = {w: nc.dram_tensor(f"{w}wp", [2048, 1024], I32, kind="ExternalInput")
             for w in ("g", "u", "d")}
    sc_in = {w: nc.dram_tensor(f"{w}sc", [2048, 64], F32, kind="ExternalInput")
             for w in ("g", "u", "d")}
    wdt_sh = nc.dram_tensor("wdt_sh", [256, 2048], F32, kind="ExternalInput")
    wout_sh = nc.dram_tensor("wout_sh", [256, 2048], F32, kind="ExternalInput")
    alog = nc.dram_tensor("alog", [D, DS], F32, kind="ExternalInput")
    xpw = nc.dram_tensor("xpw", [32, D], F32, kind="ExternalInput")
    xpb_in = nc.dram_tensor("xpb", [32], F32, kind="ExternalInput")
    dtb_in = nc.dram_tensor("dtb", [D], F32, kind="ExternalInput")
    outb_in = nc.dram_tensor("outb", [D], F32, kind="ExternalInput")
    sel8_in = nc.dram_tensor("sel8", [128, 8], F32, kind="ExternalInput")
    out_sh = nc.dram_tensor("out_sh", [TOK, D], F32, kind="ExternalOutput")

    xn_s = nc.dram_tensor("xn_s", [D, TOK], BF16)
    dt_s = nc.dram_tensor("dt_s", [D, TOK], F32)
    ssum_s = nc.dram_tensor("ssum_s", [D, TOK], F32)
    bc_s = nc.dram_tensor("bc_s", [32, TOK], BF16)
    y_s = nc.dram_tensor("y_s", [D, TOK], F32)
    inter_s = nc.dram_tensor("inter_s", [DM, TOK], BF16)
    g_s = nc.dram_tensor("g_s", [D, TOK], F32)
    c1 = nc.dram_tensor("c1", [4096, 256], BF16)
    g1 = nc.dram_tensor("g1", [8 * 4096, 256], BF16)
    cW = {w: nc.dram_tensor(f"c{w}", [2048, 2048], BF16) for w in ("g", "u", "d")}
    gW = {w: nc.dram_tensor(f"gw{w}", [8 * 2048, 2048], BF16) for w in ("g", "u", "d")}
    st_c = nc.dram_tensor("st_c", [128, 256], F32)
    st_g = nc.dram_tensor("st_g", [8 * 128, 256], F32)
    scr = dict(xn_s=xn_s, dt_s=dt_s, ssum_s=ssum_s, bc_s=bc_s, y_s=y_s,
               inter_s=inter_s, g_s=g_s, c1=c1, g1=g1, st_c=st_c, st_g=st_g)
    scr.update({f"c{w}": cW[w] for w in cW})
    scr.update({f"gw{w}": gW[w] for w in gW})
    dbg_outs = []
    for nm in dbg:
        h = scr[nm]
        dbg_outs.append((nc.dram_tensor(f"dbg_{nm}", list(h.shape), h.dtype,
                                        kind="ExternalOutput"), h))

    stack = ExitStack()
    sb = lambda nm, shape, dt: stack.enter_context(nc.sbuf_tensor(nm, shape, dt))

    aneg = sb("aneg", [128, ND * DS], F32)
    dtb_sb = sb("dtb_sb", [128, ND], F32)
    outb_sb = sb("outb_sb", [128, ND], F32)
    xpb_sb = sb("xpb_sb", [32, 1], F32)
    sel8_sb = sb("sel8_sb", [128, 8], F32)
    state = sb("state", [128, ND * DS], F32)
    sstate = sb("sstate", [128, ND], F32)
    hin = sb("hin", [128, ND * DS], F32)
    hin_st = sb("hin_st", [128, ND * DS], F32)
    idbf = sb("idbf", [128, 128], BF16)
    idf32 = sb("idf32", [128, 128], F32)
    zeros = sb("zeros", [128, CH], F32)
    wxpT = sb("wxpT", [128, ND * 128], BF16)
    csts = sb("csts", [128, 4], F32)

    qsp = Ctr(stack, nc, "qsp")
    qact = Ctr(stack, nc, "qact")

    def dsem(p, name):
        c = Ctr(stack, nc, name)
        p.dsems.append(c)
        return c

    cpe = Ctr(stack, nc, "cpe")
    cdv = Ctr(stack, nc, "cdv")
    cac = Ctr(stack, nc, "cac")
    cpl = Ctr(stack, nc, "cpl")
    ccc = Ctr(stack, nc, "ccc")

    p = Plan()
    s_c0 = dsem(p, "s_c0")
    for i in range(ND):
        p.put("sync", lambda e, i=i: e.dma_start(
            aneg[:, i * DS:(i + 1) * DS], alog[i * 128:(i + 1) * 128, :]), inc=s_c0, inck=16)
        p.put("sync", lambda e, i=i: e.dma_start(
            dtb_sb[:, i:i + 1], bass.AP(dtb_in, i * 128, [[1, 128], [1, 1]])), inc=s_c0, inck=16)
        p.put("sync", lambda e, i=i: e.dma_start(
            outb_sb[:, i:i + 1], bass.AP(outb_in, i * 128, [[1, 128], [1, 1]])), inc=s_c0, inck=16)
    p.put("sync", lambda e: e.dma_start(xpb_sb[:, :], bass.AP(xpb_in, 0, [[1, 32], [1, 1]])),
          inc=s_c0, inck=16)
    p.put("sync", lambda e: e.dma_start(sel8_sb[:, :], sel8_in[:, :]), inc=s_c0, inck=16)
    ld_done = s_c0.n
    p.put("gpsimd", lambda e: e.memset(csts[:, 0:1], -4.0))
    p.put("gpsimd", lambda e: e.memset(csts[:, 1:2], -6.0))
    p.put("gpsimd", lambda e: e.memset(csts[:, 2:3], EPS))
    p.put("gpsimd", lambda e: e.memset(csts[:, 3:4], 1.0))
    p.put("gpsimd", lambda e: e.memset(idbf[:], 0.0))
    p.put("gpsimd", lambda e: e.affine_select(out=idbf[:], in_=idbf[:], compare_op=OP.not_equal,
                                              fill=1.0, base=0, pattern=[[-1, 128]],
                                              channel_multiplier=1))
    p.put("gpsimd", lambda e: e.memset(idf32[:], 0.0))
    p.put("gpsimd", lambda e: e.affine_select(out=idf32[:], in_=idf32[:], compare_op=OP.not_equal,
                                              fill=1.0, base=0, pattern=[[-1, 128]],
                                              channel_multiplier=1))
    p.put("gpsimd", lambda e: e.memset(zeros[:], 0.0))
    p.put("gpsimd", lambda e: e.memset(state[:], 0.0))
    p.put("gpsimd", lambda e: e.memset(sstate[:], 0.0))
    p.put("gpsimd", lambda e: e.memset(hin[:], 0.0))
    p.put("gpsimd", lambda e: e.memset(wxpT[:], 0.0))
    a_exp = p.put("scalar", lambda e: e.activation(aneg[:], aneg[:], AF.Exp),
                  waits=[(s_c0, ld_done)], inc=cac)
    p.put("vector", lambda e: e.tensor_scalar_mul(aneg[:], aneg[:], -1.0),
          waits=[(cac, a_exp)], inc=cdv)
    p.finish_dma()
    p.emit(nc)

    wstack = ExitStack()
    wdtT = wstack.enter_context(nc.sbuf_tensor("wdtT", [128, ND * 2048], BF16))
    if upto < 1:
        return _finish_debug(nc, dbg_outs, stack, qsp, qact, ccc)
    p = Plan()
    LW = {}
    s_xpw = dsem(p, "s_xpw"); s_w4 = dsem(p, "s_w4"); s_stg1 = dsem(p, "s_stg1")
    s_wp = [dsem(p, "s_wp0"), dsem(p, "s_wp1")]
    s_sc = [dsem(p, "s_sc0"), dsem(p, "s_sc1")]
    s_sg = [dsem(p, "s_sg0"), dsem(p, "s_sg1")]
    s_wdt = dsem(p, "s_wdt")
    bstack = ExitStack()
    _a = lambda nm, shape, dt: bstack.enter_context(nc.sbuf_tensor(nm, shape, dt))
    w4 = _a("w4", [128, 2 * 2048], F32)
    wb4 = _a("wb4", [128, 2 * 2048], BF16)
    stg1 = _a("stg1", [128, 256], BF16)
    xpw_f = _a("xpw_f", [32, 2048], F32)
    xpw_b = _a("xpw_b", [32, 2048], BF16)
    wpt = _a("wpt", [128, 2 * 1024], I32)
    sct = _a("sct", [128, 2 * 64], F32)
    loi = _a("loi", [128, 1024], I32)
    hii = _a("hii", [128, 1024], I32)
    pf = _a("pf", [128, 2 * 1024], F32)
    ge8 = _a("ge8", [128, 2048], F32)
    cc_ = _a("cc_", [128, 2048], F32)
    r1_ = _a("r1_", [128, 2048], F32)
    r2_ = _a("r2_", [128, 2048], F32)
    t1_ = _a("t1_", [128, 2048], F32)
    mm_ = cc_
    sg_ = ge8
    vv_ = r1_
    deq = _a("deq", [128, 2 * 2048], BF16)
    stg2 = _a("stg2", [128, 2 * 2048], BF16)
    pstrA = bstack.enter_context(nc.psum_tensor("pstrA", [128, 128], BF16))
    pstrB = bstack.enter_context(nc.psum_tensor("pstrB", [128, 128], BF16))
    pstrC = bstack.enter_context(nc.psum_tensor("pstrC", [128, 128], BF16))
    pstrD = bstack.enter_context(nc.psum_tensor("pstrD", [128, 128], BF16))
    pstrP = [pstrA, pstrB, pstrC, pstrD]
    if True:
        v = p.put("sync", lambda e: e.dma_start(xpw_f[:, :], xpw[:, :]), inc=s_xpw, inck=16)
        cast = p.put("vector", lambda e: e.tensor_copy(xpw_b[:], xpw_f[:]),
                     waits=[(s_xpw, v)], inc=cdv)
        for i0 in range(0, ND, 2):
            for j in range(2):
                i = i0 + j
                p.put("tensor", lambda e, i=i, sl=j: e.transpose(
                    pstrP[sl][:, 0:32],
                    xpw_b[:, i * 128:(i + 1) * 128], idbf[0:32, 0:32]),
                    waits=[(cdv, cast), (cac, LW.get(("pstr", j), 0))], inc=None)
            dr = p.put("tensor", lambda e: e.drain(), inc=cpe)
            for j in range(2):
                i = i0 + j
                ev = p.put("scalar", lambda e, i=i, sl=j: e.copy(
                    wxpT[:, i * 128:i * 128 + 32], pstrP[sl][:, 0:32]),
                    waits=[(cpe, dr)], inc=cac)
                LW[("pstr", j)] = ev

        for wsrc, c1row in (((wdt_sh, 0), (wout_sh, 2048)) if b1part >= 1 else ()):
            lds = []
            for j in range(2):
                lds.append(p.put("sync", lambda e, j=j, wsrc=wsrc: e.dma_start(
                    w4[:, j * 2048:(j + 1) * 2048], wsrc[j * 128:(j + 1) * 128, :]),
                    waits=[(cpe, LW.get("w4_free", 0))], inc=s_w4, inck=16))
            cast = p.put("vector", lambda e: e.tensor_copy(wb4[:], w4[:]),
                         waits=[(s_w4, lds[-1]), (cpe, LW.get("w4_free", 0))], inc=cdv)
            for i in range(ND):
                for j in range(2):
                    p.put("tensor", lambda e, i=i, j=j: e.transpose(
                        pstrP[j][:, :],
                        wb4[:, j * 2048 + i * 128: j * 2048 + (i + 1) * 128], idbf[:]),
                        waits=[(cdv, cast), (cac, LW.get(("pstr", j), 0))], inc=None)
                dr = p.put("tensor", lambda e: e.drain(), inc=cpe)
                ev = None
                for j in range(2):
                    ev = p.put("scalar", lambda e, j=j: e.copy(
                        stg1[:, j * 128:(j + 1) * 128], pstrP[j][:, :]),
                        waits=[(cpe, dr), (s_stg1, LW.get("stg1_free", 0))], inc=cac)
                    LW[("pstr", j)] = ev
                std = p.put("scalar", lambda e, i=i, c1row=c1row: e.dma_start(
                    c1[c1row + i * 128: c1row + (i + 1) * 128, :], stg1[:]),
                    waits=[(cac, ev)], inc=s_stg1, inck=16)
                LW["stg1_free"] = std
            LW["w4_free"] = cpe.n
        if b1part >= 2:
            p.put("gpsimd", lambda e: e.collective_compute(
                "AllGather", OP.bypass, replica_groups=[list(range(8))],
                ins=[c1.ap()], outs=[g1.ap()]),
                waits=[(s_stg1, s_stg1.n)], inc=ccc)

        for sl in range(4):
            LW[("pstr", sl)] = cac.n
        _wl = ("g", "u", "d") if b1part >= 3 else ()
        _wl = _wl[:DEQ_NW] if b1part in (3, 4, 5) else _wl
        for w in _wl:
            for jt in range(DEQ_NJT if b1part in (3, 4, 5) else 16):
                par = jt % 2
                ldw = p.put("sync", lambda e, w=w, jt=jt, par=par: e.dma_start(
                    wpt[:, par * 1024:(par + 1) * 1024],
                    wp_in[w][jt * 128:(jt + 1) * 128, :]),
                    waits=[(cdv, LW.get(("wpt", par), 0))], inc=s_wp[par], inck=16)
                lds = p.put("sync", lambda e, w=w, jt=jt, par=par: e.dma_start(
                    sct[:, par * 64:(par + 1) * 64],
                    sc_in[w][jt * 128:(jt + 1) * 128, :]),
                    waits=[(cdv, LW.get(("sct", par), 0))], inc=s_sc[par], inck=16)
                lo = p.put("vector", lambda e, par=par: e.tensor_scalar(
                    loi[:], wpt[:, par * 1024:(par + 1) * 1024], 15, None, OP.bitwise_and),
                    waits=[(s_wp[par], ldw), (cdv, LW.get("loi_free", 0))], inc=cdv)
                hi = p.put("vector", lambda e, par=par: e.tensor_scalar(
                    hii[:], wpt[:, par * 1024:(par + 1) * 1024], 4, None,
                    OP.logical_shift_right), inc=cdv)
                LW[("wpt", par)] = hi
                p.put("vector", lambda e: e.tensor_copy(pf[:, 0:1024], loi[:]), inc=cdv)
                cast2 = p.put("vector", lambda e: e.tensor_copy(pf[:, 1024:2048], hii[:]),
                              inc=cdv)
                LW["loi_free"] = cast2
                g8 = p.put("vector", lambda e: e.tensor_scalar(
                    ge8[:], pf[:, 0:2048], 8.0, None, OP.is_ge),
                    waits=[(cpl, LW.get("ge8_free", 0))], inc=cdv)
                cc1 = p.put("vector", lambda e: e.scalar_tensor_tensor(
                    cc_[:], ge8[:], -8.0, pf[:, 0:2048],
                    OP.mult, OP.add),
                    waits=[(cpl, LW.get("cc_free", 0)), (cac, LW.get("cc_free_a", 0))],
                    inc=cdv)
                ra = p.put("scalar", lambda e: e.activation(
                    r1_[:], cc_[:], AF.Relu, bias=csts[:, 0:1]),
                    waits=[(cdv, cc1), (cdv, LW.get("r1_free", 0))], inc=cac)
                rb = p.put("scalar", lambda e: e.activation(
                    r2_[:], cc_[:], AF.Relu, bias=csts[:, 1:2]),
                    waits=[(cdv, LW.get("r2_free", 0))], inc=cac)
                tt1 = p.put("gpsimd", lambda e: e.tensor_add(t1_[:], cc_[:], r1_[:]),
                            waits=[(cac, ra), (cdv, LW.get("t1_free", 0))], inc=cpl)
                LW["cc_free_a"] = rb
                mmv = p.put("vector", lambda e: e.scalar_tensor_tensor(
                    mm_[:], t1_[:], 0.5, r2_[:], OP.mult, OP.add),
                    waits=[(cpl, tt1), (cac, rb)], inc=cdv)
                LW["t1_free"] = mmv
                LW["r2_free"] = mmv
                sgv = p.put("gpsimd", lambda e: e.tensor_scalar(
                    sg_[:], ge8[:], -2.0, 1.0, OP.mult, OP.add),
                    waits=[(cdv, cc1)], inc=cpl)
                vvv = p.put("gpsimd", lambda e: e.tensor_mul(vv_[:], mm_[:], sg_[:]),
                            waits=[(cdv, mmv)], inc=cpl)
                LW["ge8_free"] = vvv
                LW["cc_free"] = vvv
                last_scale = 0
                for pi in range(2):
                    last_scale = p.put("vector", lambda e, pi=pi, par=par: e.tensor_tensor(
                        bass.AP(deq, par * 2048 + pi, [[4096, 128], [2, 1024]]),
                        vv_[:, pi * 1024:(pi + 1) * 1024],
                        bass.AP(sct, par * 64, [[128, 128], [1, 64], [0, 16]]),
                        OP.mult),
                        waits=[(cpl, vvv), (s_sc[par], lds), (cpe, LW.get(("deq", par), 0))],
                        inc=cdv)
                LW["r1_free"] = last_scale
                LW[("sct", par)] = last_scale
                evs = 0
                for i0 in (range(0, ND, 4) if b1part >= 4 else ()):
                    for j in range(4):
                        i = i0 + j
                        p.put("tensor", lambda e, i=i, par=par, sl=j: e.transpose(
                            pstrP[sl][:, :],
                            bass.AP(deq, par * 2048 + i * 128, [[4096, 128], [1, 128]]),
                            idbf[:]),
                            waits=[(cdv, last_scale), (cac, LW.get(("pstr", j), 0))],
                            inc=None)
                    dr = p.put("tensor", lambda e: e.drain(), inc=cpe)
                    evs = 0
                    for j in range(4):
                        i = i0 + j
                        evs = p.put("scalar", lambda e, i=i, par=par, sl=j: e.copy(
                            stg2[:, par * 2048 + i * 128: par * 2048 + (i + 1) * 128],
                            pstrP[sl][:, :]),
                            waits=[(cpe, dr), (s_sg[par], LW.get(("stg2", par), 0))], inc=cac)
                        LW[("pstr", j)] = evs
                LW[("deq", par)] = cpe.n
                if b1part < 4:
                    LW[("deq", par)] = 0
                    continue
                std = 0
                if b1part < 5:
                    LW[("stg2", par)] = 0
                    continue
                for q4 in range(4):
                    std = p.put("scalar", lambda e, w=w, jt=jt, par=par, q4=q4: e.dma_start(
                        bass.AP(cW[w], jt * 128 + q4 * 4 * 128 * 2048,
                                [[2048, 128], [128 * 2048, 4], [1, 128]]),
                        bass.AP(stg2, par * 2048 + q4 * 4 * 128,
                                [[4096, 128], [128, 4], [1, 128]])),
                        waits=[(cac, evs)], inc=s_sg[par], inck=16)
                LW[("stg2", par)] = std
            if b1part >= 6:
                p.put("gpsimd", lambda e, w=w: e.collective_compute(
                    "AllGather", OP.bypass, replica_groups=[list(range(8))],
                    ins=[cW[w].ap()], outs=[gW[w].ap()]),
                    waits=[(s_sg[0], s_sg[0].n), (s_sg[1], s_sg[1].n)], inc=ccc)
        wdt_lds = 0
        for r in range(8):
            wdt_lds = p.put("sync", lambda e, r=r: e.dma_start(
                bass.AP(wdtT, r * 256, [[ND * 2048, 128], [2048, ND], [1, 256]]),
                bass.AP(g1, (r * 4096) * 256, [[256, 128], [128 * 256, ND], [1, 256]])),
                waits=[(ccc, 1)], inc=s_wdt, inck=16)
        WDT_READY = (s_wdt, wdt_lds)
        p.finish_dma()
        p.emit(nc)
    bstack.close()
    AG1_DONE, AGG_DONE, AGU_DONE, AGD_DONE, AGS_DONE = 1, 2, 3, 4, 6

    if upto < 2:
        return _finish_debug(nc, dbg_outs, stack, qsp, qact, ccc)
    p = Plan()
    LW = {}
    s_xt = [dsem(p, "s_xt0"), dsem(p, "s_xt1")]
    s_xnsp = dsem(p, "s_xnsp")
    s_dtv = [dsem(p, "s_dtv0"), dsem(p, "s_dtv1")]
    s_bcv = dsem(p, "s_bcv")
    bstack = ExitStack()
    _a = lambda nm, shape, dt: bstack.enter_context(nc.sbuf_tensor(nm, shape, dt))
    xtok = _a("xtok", [128, 2 * 2048], F32)
    sqb = _a("sqb", [128, 2048], F32)
    lnst = _a("lnst", [128, 8], F32)
    xnt = _a("xnt", [128, 2 * 2048], BF16)
    xnc = _a("xnc", [128, ND * 512], BF16)
    spb = _a("spb", [128, 4 * 512], F32)
    dtv = _a("dtv", [128, 2 * 512], F32)
    bcv = _a("bcv", [32, 512], BF16)
    ps_trA = bstack.enter_context(nc.psum_tensor("ps_trA", [128, 128], BF16))
    ps_trB = bstack.enter_context(nc.psum_tensor("ps_trB", [128, 128], BF16))
    ps_trP = [ps_trA, ps_trB]
    ps_dt = bstack.enter_context(nc.psum_tensor("ps_dt", [128, 4 * 512], F32))
    ps_xp = bstack.enter_context(nc.psum_tensor("ps_xp", [128, 512], F32))
    if True:
        s_wdtS, wdt_ready = WDT_READY

        for c in range(NCH):
            xn_evs = 0
            for tau in range(4):
                par = tau % 2
                xl = p.put("sync", lambda e, c=c, tau=tau, par=par: e.dma_start(
                    xtok[:, par * 2048:(par + 1) * 2048],
                    x_in[c * 512 + tau * 128: c * 512 + (tau + 1) * 128, :]),
                    waits=[(cdv, LW.get(("xtok", par), 0))], inc=s_xt[par], inck=16)
                sq = p.put("scalar", lambda e, par=par: e.activation(
                    sqb[:], xtok[:, par * 2048:(par + 1) * 2048], AF.Square),
                    waits=[(s_xt[par], xl), (cdv, LW.get("sqb_free", 0))], inc=cac)
                s0v = p.put("vector", lambda e, par=par: e.reduce_sum(
                    lnst[:, 0:1], xtok[:, par * 2048:(par + 1) * 2048], AX.X),
                    waits=[(s_xt[par], xl)], inc=cdv)
                s1 = p.put("vector", lambda e: e.reduce_sum(lnst[:, 1:2], sqb[:], AX.X),
                           waits=[(cac, sq)], inc=cdv)
                LW["sqb_free"] = s1
                mean = p.put("vector", lambda e: e.tensor_scalar_mul(
                    lnst[:, 2:3], lnst[:, 0:1], 1.0 / 2048),
                    waits=[(cdv, s0v)], inc=cdv)
                tch1 = p.put("vector", lambda e: e.tensor_copy(lnst[:, 7:8], lnst[:, 2:3]),
                             waits=[(cdv, mean)], inc=cdv)
                m2 = p.put("scalar", lambda e: e.activation(
                    lnst[:, 3:4], lnst[:, 2:3], AF.Square), waits=[(cdv, tch1)], inc=cac)
                t2 = p.put("scalar", lambda e: e.copy(lnst[:, 7:8], lnst[:, 3:4]),
                           waits=[(cac, m2)], inc=cac)
                vr = p.put("vector", lambda e: e.scalar_tensor_tensor(
                    lnst[:, 4:5], lnst[:, 1:2], 1.0 / 2048, lnst[:, 3:4],
                    OP.mult, OP.subtract), waits=[(cac, t2), (cdv, s1)], inc=cdv)
                tch3 = p.put("vector", lambda e: e.tensor_copy(lnst[:, 7:8], lnst[:, 4:5]),
                             waits=[(cdv, vr)], inc=cdv)
                lnv = p.put("scalar", lambda e: e.activation(
                    lnst[:, 5:6], lnst[:, 4:5], AF.Ln, bias=csts[:, 2:3]),
                    waits=[(cdv, tch3)], inc=cac)
                lt = p.put("scalar", lambda e: e.copy(lnst[:, 7:8], lnst[:, 5:6]),
                           waits=[(cac, lnv)], inc=cac)
                rs = p.put("scalar", lambda e: e.activation(
                    lnst[:, 6:7], lnst[:, 5:6], AF.Exp, scale=-0.5),
                    waits=[(cac, lt)], inc=cac)
                t4 = p.put("scalar", lambda e: e.copy(lnst[:, 7:8], lnst[:, 6:7]),
                           waits=[(cac, rs)], inc=cac)
                xn1 = p.put("vector", lambda e, par=par: e.tensor_scalar(
                    xnt[:, par * 2048:(par + 1) * 2048],
                    xtok[:, par * 2048:(par + 1) * 2048],
                    lnst[:, 2:3], lnst[:, 6:7], OP.subtract, OP.mult),
                    waits=[(cac, t4)], inc=cdv)
                LW[("xtok", par)] = xn1
                for i0 in range(0, ND, 2):
                    for j in range(2):
                        i = i0 + j
                        p.put("tensor", lambda e, i=i, sl=j, par=par: e.transpose(
                            ps_trP[sl][:, :],
                            xnt[:, par * 2048 + i * 128: par * 2048 + (i + 1) * 128],
                            idbf[:]),
                            waits=[(cdv, xn1), (cac, LW.get(("ps_tr", j), 0)),
                                   (s_xnsp, LW.get("xnc_freeq", 0))], inc=None)
                    dr = p.put("tensor", lambda e: e.drain(), inc=cpe)
                    for j in range(2):
                        i = i0 + j
                        xn_evs = p.put("scalar", lambda e, i=i, sl=j, tau=tau: e.copy(
                            xnc[:, i * 512 + tau * 128: i * 512 + (tau + 1) * 128],
                            ps_trP[sl][:, :]),
                            waits=[(cpe, dr)], inc=cac)
                        LW[("ps_tr", j)] = xn_evs
            xn_sp = 0
            for part in range(4):
                xn_sp = p.put("scalar", lambda e, c=c, part=part: e.dma_start(
                    bass.AP(xn_s, (part * 4 * 128) * TOK + c * 512,
                            [[TOK, 128], [128 * TOK, 4], [1, 512]]),
                    bass.AP(xnc, part * 4 * 512, [[ND * 512, 128], [512, 4], [1, 512]])),
                    waits=[(cac, xn_evs)], inc=s_xnsp, inck=16)
            LW["xnc_freeq"] = xn_sp

            last_mm = 0
            for k in range(ND):
                wl = [(cac, xn_evs)]
                if k == 0:
                    wl.append((cac, LW.get("ps_xp", 0)))
                p.put("tensor", lambda e, k=k: e.matmul(
                    ps_xp[:], wxpT[:, k * 128:(k + 1) * 128], xnc[:, k * 512:(k + 1) * 512],
                    start=(k == 0), stop=(k == ND - 1)), waits=wl, inc=None)
            last_mm = p.put("tensor", lambda e: e.drain(), inc=cpe)
            LW["xnc_freep"] = last_mm
            bce = p.put("scalar", lambda e: e.activation(
                bcv[:, :], ps_xp[0:32, :], AF.Identity, bias=xpb_sb[:, :]),
                waits=[(cpe, last_mm), (s_bcv, LW.get("bcv_free", 0))], inc=cac)
            LW["ps_xp"] = bce
            bcd = p.put("scalar", lambda e, c=c: e.dma_start(
                bass.AP(bc_s, c * 512, [[TOK, 32], [1, 512]]), bcv[:, :]),
                waits=[(cac, bce)], inc=s_bcv, inck=16)
            LW["bcv_free"] = bcd
            for et in range(ND):
                bank = et % 4
                last_mm = 0
                for k in range(ND):
                    wl = [(s_wdtS, wdt_ready), (cac, xn_evs)]
                    if k == 0:
                        wl.append((cdv, LW.get(("ps_dt", bank), 0)))
                        wl.append((cac, LW.get(("ps_dt_a", bank), 0)))
                    p.put("tensor", lambda e, et=et, k=k, bank=bank: e.matmul(
                        ps_dt[:, bank * 512:(bank + 1) * 512],
                        wdtT[:, k * 2048 + et * 128: k * 2048 + (et + 1) * 128],
                        xnc[:, k * 512:(k + 1) * 512],
                        start=(k == 0), stop=(k == ND - 1)),
                        waits=wl, inc=None)
                last_mm = p.put("tensor", lambda e: e.drain(), inc=cpe)
                sp = bank % 2
                a1 = p.put("scalar", lambda e, et=et, bank=bank, sp=sp: e.activation(
                    spb[:, sp * 1024:sp * 1024 + 512],
                    ps_dt[:, bank * 512:(bank + 1) * 512], AF.Abs,
                    bias=dtb_sb[:, et:et + 1]),
                    waits=[(cpe, last_mm), (cdv, LW.get(("spb", sp), 0))], inc=cac)
                p.put("scalar", lambda e, sp=sp: e.activation(
                    spb[:, sp * 1024 + 512:sp * 1024 + 1024],
                    spb[:, sp * 1024:sp * 1024 + 512], AF.Exp, scale=-1.0), inc=cac)
                p.put("scalar", lambda e, sp=sp: e.activation(
                    spb[:, sp * 1024:sp * 1024 + 512],
                    spb[:, sp * 1024 + 512:sp * 1024 + 1024], AF.Ln, bias=csts[:, 3:4]), inc=cac)
                r1v = p.put("scalar", lambda e, et=et, bank=bank, sp=sp: e.activation(
                    spb[:, sp * 1024 + 512:sp * 1024 + 1024],
                    ps_dt[:, bank * 512:(bank + 1) * 512], AF.Relu,
                    bias=dtb_sb[:, et:et + 1]), inc=cac)
                LW[("ps_dt_a", bank)] = r1v
                dtc = p.put("vector", lambda e, et=et, sp=sp: e.tensor_add(
                    dtv[:, (et % 2) * 512:(et % 2) * 512 + 512],
                    spb[:, sp * 1024:sp * 1024 + 512],
                    spb[:, sp * 1024 + 512:sp * 1024 + 1024]),
                    waits=[(cac, r1v), (s_dtv[et % 2], LW.get(("dtv", et % 2), 0))], inc=cdv)
                LW[("ps_dt", bank)] = dtc
                LW[("spb", sp)] = dtc
                dsp = p.put("scalar", lambda e, c=c, et=et: e.dma_start(
                    bass.AP(dt_s, (et * 128) * TOK + c * 512, [[TOK, 128], [1, 512]]),
                    dtv[:, (et % 2) * 512:(et % 2) * 512 + 512]),
                    waits=[(cdv, dtc)], inc=s_dtv[et % 2], inck=16)
                LW[("dtv", et % 2)] = dsp
        p.emit(nc)
    bstack.close()
    wstack.close()
    QACT_B2 = qact.n

    if upto < 3:
        return _finish_debug(nc, dbg_outs, stack, qsp, qact, ccc)
    p = Plan()
    LW = {}
    s_xnh = dsem(p, "s_xnh")
    s_gs = [dsem(p, "s_gs0"), dsem(p, "s_gs1")]
    s_us = [dsem(p, "s_us0"), dsem(p, "s_us1")]
    s_dtp = [dsem(p, "s_dtp0"), dsem(p, "s_dtp1")]
    s_xnp = [dsem(p, "s_xnp0"), dsem(p, "s_xnp1")]
    s_bcb = dsem(p, "s_bcb"); s_ccb = dsem(p, "s_ccb")
    s_sb = [dsem(p, "s_sb0"), dsem(p, "s_sb1")]
    s_yb = [dsem(p, "s_yb0"), dsem(p, "s_yb1")]
    s_insp = [dsem(p, "s_insp0"), dsem(p, "s_insp1")]
    s_stc = dsem(p, "s_stc")
    bstack = ExitStack()
    _a = lambda nm, shape, dt: bstack.enter_context(nc.sbuf_tensor(nm, shape, dt))
    xnh = _a("xnh", [128, ND * 1024], BF16)
    gsl = _a("gsl", [128, 2 * ND * 256], BF16)
    usl = _a("usl", [128, 2 * ND * 256], BF16)
    qb = _a("qb", [128, 1024], F32)
    pb = _a("pb", [128, 1024], F32)
    tb = qb
    rb = pb
    inters = _a("inters", [128, 2 * 1024], BF16)
    dtc_s = _a("dtc_s", [128, ND * 512], F32)
    xncs = _a("xncs", [128, ND * 512], BF16)
    dtxb = _a("dtxb", [128, 2 * 512], F32)
    Sb = _a("Sb", [128, 2 * 512], F32)
    dab = _a("dab", [128, 2 * 512], F32)
    dbxb = _a("dbxb", [128, 2 * 512], F32)
    strip = _a("strip", [128, 4 * 512], F32)
    Bcb = _a("Bcb", [128, DS * 512], BF16)
    Ccb = _a("Ccb", [128, DS * 512], BF16)
    ytmp = _a("ytmp", [128, 2 * 1024], F32)
    ybuf = _a("ybuf", [128, 2 * 512], F32)
    ps8 = bstack.enter_context(nc.psum_tensor("ps8", [128, 8 * 512], F32))
    if True:
        def emit_xnh_load(qt):
            last = 0
            for i in range(ND):
                last = p.put("sync", lambda e, qt=qt, i=i: e.dma_start(
                    xnh[:, i * 1024:(i + 1) * 1024],
                    bass.AP(xn_s, (i * 128) * TOK + qt * 1024, [[TOK, 128], [1, 1024]])),
                    waits=[(cpe, LW.get("xnh_free", 0))],
                    inc=s_xnh, inck=16)
            return last

        mt_count = [0]

        def emit_ffn_unit(qt, u, xnh_ready, filler=None):
            r, q8 = u >> 3, u & 7
            par = u % 2
            g_ld = p.put("sync", lambda e, r=r, q8=q8, par=par: e.dma_start(
                bass.AP(gsl, par * ND * 256, [[2 * ND * 256, 128], [256, ND], [1, 256]]),
                bass.AP(gW["g"], (r * 2048) * 2048 + q8 * 256,
                        [[2048, 128], [128 * 2048, ND], [1, 256]])),
                waits=[(ccc, AGG_DONE), (cpe, LW.get(("gsl", par), 0))],
                inc=s_gs[par], inck=16)
            u_ld = p.put("sync", lambda e, r=r, q8=q8, par=par: e.dma_start(
                bass.AP(usl, par * ND * 256, [[2 * ND * 256, 128], [256, ND], [1, 256]]),
                bass.AP(gW["u"], (r * 2048) * 2048 + q8 * 256,
                        [[2048, 128], [128 * 2048, ND], [1, 256]])),
                waits=[(ccc, AGU_DONE), (cpe, LW.get(("usl", par), 0))],
                inc=s_us[par], inck=16)
            for s in range(2):
                m128 = r * 16 + q8 * 2 + s
                mpar = mt_count[0] % 2
                mt_count[0] += 1
                gbank, ubank = mpar * 4, mpar * 4 + 2
                last_g = last_u = 0
                for k in range(ND):
                    for tch in range(2):
                        wl = [(s_gs[par], g_ld), (s_xnh, xnh_ready)]
                        if k == 0 and tch == 0:
                            wl.append((cdv, LW.get(("geluA", mpar), 0)))
                        last_g = p.put("tensor", lambda e, k=k, tch=tch, s=s, par=par,
                                       gbank=gbank: e.matmul(
                            ps8[:, (gbank + tch) * 512:(gbank + tch + 1) * 512],
                            gsl[:, par * ND * 256 + k * 256 + s * 128:
                                par * ND * 256 + k * 256 + s * 128 + 128],
                            xnh[:, k * 1024 + tch * 512: k * 1024 + (tch + 1) * 512],
                            start=(k == 0), stop=(k == ND - 1)),
                            waits=wl, inc=None)
                for k in range(ND):
                    for tch in range(2):
                        wl = [(s_us[par], u_ld)]
                        if k == 0 and tch == 0:
                            wl.append((cdv, LW.get(("geluB", mpar), 0)))
                        p.put("tensor", lambda e, k=k, tch=tch, s=s, par=par,
                              ubank=ubank: e.matmul(
                            ps8[:, (ubank + tch) * 512:(ubank + tch + 1) * 512],
                            usl[:, par * ND * 256 + k * 256 + s * 128:
                                par * ND * 256 + k * 256 + s * 128 + 128],
                            xnh[:, k * 1024 + tch * 512: k * 1024 + (tch + 1) * 512],
                            start=(k == 0), stop=(k == ND - 1)),
                            waits=wl, inc=None)
                last_u = p.put("tensor", lambda e: e.drain(), inc=cpe)
                if True:
                    qv = p.put("scalar", lambda e, gbank=gbank: e.activation(
                        qb[:], ps8[:, gbank * 512:(gbank + 2) * 512],
                        AF.Square, scale=GELU_A2),
                        waits=[(cpe, last_u), (cdv, LW.get("qb_free", 0))], inc=cac)
                    pv = p.put("vector", lambda e, gbank=gbank: e.scalar_tensor_tensor(
                        pb[:], qb[:], 1.0,
                        ps8[:, gbank * 512:(gbank + 2) * 512],
                        OP.add, OP.mult),
                        waits=[(cac, qv)], inc=cdv)
                    tv = p.put("scalar", lambda e: e.activation(
                        tb[:], pb[:], AF.Tanh, scale=GELU_C),
                        waits=[(cdv, pv)], inc=cac)
                    rv = p.put("vector", lambda e, gbank=gbank: e.scalar_tensor_tensor(
                        rb[:], tb[:], 1.0,
                        ps8[:, gbank * 512:(gbank + 2) * 512],
                        OP.add, OP.mult),
                        waits=[(cac, tv)], inc=cdv)
                    LW["qb_free"] = rv
                    iv = p.put("vector", lambda e, ubank=ubank, s=s: e.scalar_tensor_tensor(
                        inters[:, (s % 2) * 1024:(s % 2) * 1024 + 1024],
                        rb[:], 0.5,
                        ps8[:, ubank * 512:(ubank + 2) * 512],
                        OP.mult, OP.mult),
                        waits=[(s_insp[s % 2], LW.get(("inters", s % 2), 0))], inc=cdv)
                LW[("geluA", mpar)] = cdv.n
                LW[("geluB", mpar)] = cdv.n
                isp = p.put("scalar", lambda e, m128=m128, qt=qt, s=s: e.dma_start(
                    bass.AP(inter_s, (m128 * 128) * TOK + qt * 1024, [[TOK, 128], [1, 1024]]),
                    inters[:, (s % 2) * 1024:(s % 2) * 1024 + 1024]),
                    waits=[(cdv, cdv.n)], inc=s_insp[s % 2], inck=16)
                LW[("inters", s % 2)] = isp
                if filler is not None:
                    filler()
            LW[("gsl", par)] = cpe.n
            LW[("usl", par)] = cpe.n

        def emit_scan_half_loads(c, h):
            for part in (2 * h, 2 * h + 1):
                p.put("scalar", lambda e, c=c, part=part: e.dma_start(
                    bass.AP(dtc_s, part * 4 * 512, [[ND * 512, 128], [512, 4], [1, 512]]),
                    bass.AP(dt_s, (part * 4 * 128) * TOK + c * 512,
                            [[TOK, 128], [128 * TOK, 4], [1, 512]])),
                    waits=[(cdv, LW.get(("dtcF", h), 0)),
                           (cac, LW.get(("dtcFa", h), 0))], inc=s_dtp[h], inck=16)
                p.put("scalar", lambda e, c=c, part=part: e.dma_start(
                    bass.AP(xncs, part * 4 * 512, [[ND * 512, 128], [512, 4], [1, 512]]),
                    bass.AP(xn_s, (part * 4 * 128) * TOK + c * 512,
                            [[TOK, 128], [128 * TOK, 4], [1, 512]])),
                    waits=[(cdv, LW.get(("dtcF", h), 0))], inc=s_xnp[h], inck=16)

        def emit_bc_loads(c):
            for n in range(DS):
                p.put("gpsimd", lambda e, c=c, n=n: e.dma_start(
                    Bcb[:, n * 512:(n + 1) * 512],
                    bass.AP(bc_s, n * TOK + c * 512, [[0, 128], [1, 512]])),
                    waits=[(cpl, LW.get("Bcb_free", 0))], inc=s_bcb, inck=16)
                p.put("gpsimd", lambda e, c=c, n=n: e.dma_start(
                    Ccb[:, n * 512:(n + 1) * 512],
                    bass.AP(bc_s, (16 + n) * TOK + c * 512, [[0, 128], [1, 512]])),
                    waits=[(cdv, LW.get("Ccb_free", 0))], inc=s_ccb, inck=16)

        def emit_scan_dtile_head(c, i):
            h = i // 8
            ld_done = (32 * (c + 1), 32 * (c + 1))
            ip = i % 2
            dtx = p.put("vector", lambda e, i=i, ip=ip: e.tensor_mul(
                dtxb[:, ip * 512:ip * 512 + 512],
                dtc_s[:, i * 512:(i + 1) * 512], xncs[:, i * 512:(i + 1) * 512]),
                waits=[(s_dtp[h], ld_done[0]), (s_xnp[h], ld_done[1]),
                       (cpl, LW.get(("dtxb", ip), 0))], inc=cdv)
            ssc = p.put("vector", lambda e, i=i, ip=ip: e.tensor_tensor_scan(
                Sb[:, ip * 512:ip * 512 + 512],
                dtc_s[:, i * 512:(i + 1) * 512], zeros[:, 0:512],
                sstate[:, i:i + 1], OP.add, OP.add),
                waits=[(s_sb[ip], LW.get(("Sb", ip), 0))], inc=cdv)
            p.put("vector", lambda e, i=i, ip=ip: e.tensor_copy(
                sstate[:, i:i + 1], Sb[:, ip * 512 + 511:ip * 512 + 512]),
                waits=[(cdv, ssc)], inc=cdv)
            ssp = p.put("gpsimd", lambda e, c=c, i=i, ip=ip: e.dma_start(
                bass.AP(ssum_s, (i * 128) * TOK + c * 512, [[TOK, 128], [1, 512]]),
                Sb[:, ip * 512:ip * 512 + 512]),
                waits=[(cdv, ssc)], inc=s_sb[ip], inck=16)
            LW[("Sb", ip)] = ssp
            if i == 7 or i == 15:
                LW[("dtcF", h)] = cdv.n
                LW[("dtcFa", h)] = cac.n
            sc_state["cur"] = {"dtx": dtx, "ip": ip}

        def emit_scan_dtile_g(c, i, g):
            h = i // 8
            ld_done = (32 * (c + 1), 32 * (c + 1))
            bc_done = (256 * (c + 1), 256 * (c + 1))
            dtx = sc_state["cur"]["dtx"]
            ip = sc_state["cur"]["ip"]
            for nn in range(4):
                n = g * 4 + nn
                ab = n % 2
                da = p.put("scalar", lambda e, i=i, n=n, ab=ab: e.activation(
                    dab[:, ab * 512:(ab + 1) * 512],
                    dtc_s[:, i * 512:(i + 1) * 512], AF.Exp,
                    scale=aneg[:, i * DS + n:i * DS + n + 1]),
                    waits=[(s_dtp[h], ld_done[0]), (cdv, LW.get(("dab", ab), 0))],
                    inc=cac)
                dbx = p.put("gpsimd", lambda e, i=i, n=n, ab=ab, ip=ip: e.tensor_mul(
                    dbxb[:, ab * 512:(ab + 1) * 512],
                    dtxb[:, ip * 512:ip * 512 + 512],
                    Bcb[:, n * 512:(n + 1) * 512]),
                    waits=[(cdv, dtx), (s_bcb, bc_done[0]),
                           (cdv, LW.get(("dbxb", ab), 0))], inc=cpl)
                sc = p.put("vector", lambda e, i=i, n=n, nn=nn, ab=ab: e.tensor_tensor_scan(
                    strip[:, nn * 512:(nn + 1) * 512],
                    dab[:, ab * 512:(ab + 1) * 512],
                    dbxb[:, ab * 512:(ab + 1) * 512],
                    state[:, i * DS + n:i * DS + n + 1], OP.mult, OP.add),
                    waits=[(cac, da), (cpl, dbx)], inc=cdv)
                LW[("dab", ab)] = sc
                LW[("dbxb", ab)] = sc
                last_sc = sc
            p.put("vector", lambda e, i=i, g=g: e.tensor_copy(
                state[:, i * DS + g * 4: i * DS + g * 4 + 4],
                bass.AP(strip, 511, [[2048, 128], [512, 4]])),
                waits=[(cdv, last_sc)], inc=cdv)
            for np_ in range(2):
                nn0 = 2 * np_
                n0 = g * 4 + nn0
                yp = np_ % 2
                yt = p.put("vector", lambda e, n0=n0, nn0=nn0, yp=yp: e.tensor_mul(
                    ytmp[:, yp * 1024:(yp + 1) * 1024],
                    strip[:, nn0 * 512:(nn0 + 2) * 512],
                    Ccb[:, n0 * 512:(n0 + 2) * 512]),
                    waits=[(s_ccb, bc_done[1]),
                           (cpl, LW.get(("ytmp", yp), 0))], inc=cdv)
                for hh_ in range(2):
                    n = n0 + hh_
                    if n == 0:
                        ya = p.put("gpsimd", lambda e, ip=ip: e.tensor_copy(
                            ybuf[:, ip * 512:ip * 512 + 512], ytmp[:, 0:512]),
                            waits=[(cdv, yt), (s_yb[ip], LW.get(("ybuf", ip), 0))],
                            inc=cpl)
                    else:
                        ya = p.put("gpsimd", lambda e, ip=ip, yp=yp, hh_=hh_: e.tensor_add(
                            ybuf[:, ip * 512:ip * 512 + 512],
                            ybuf[:, ip * 512:ip * 512 + 512],
                            ytmp[:, yp * 1024 + hh_ * 512:yp * 1024 + (hh_ + 1) * 512]),
                            waits=[(cdv, yt)], inc=cpl)
                LW[("ytmp", yp)] = ya
            if g == 3:
                ysp = p.put("gpsimd", lambda e, c=c, i=i, ip=ip: e.dma_start(
                    bass.AP(y_s, (i * 128) * TOK + c * 512, [[TOK, 128], [1, 512]]),
                    ybuf[:, ip * 512:ip * 512 + 512]),
                    inc=s_yb[ip], inck=16)
                LW[("ybuf", ip)] = ysp
                LW[("dtxb", ip)] = cpl.n
                if i == 15:
                    LW["Bcb_free"] = cpl.n
                    LW["Ccb_free"] = cdv.n

        sc_state = {"c": 0, "i": 0, "ph": 0, "h0": False, "cur": {}}

        def emit_scan_heads0():
            emit_scan_half_loads(0, 0)
            emit_scan_half_loads(0, 1)
            emit_bc_loads(0)
            sc_state["h0"] = True

        def emit_scan_micro():
            c, i, ph = sc_state["c"], sc_state["i"], sc_state["ph"]
            if c >= NCH:
                return
            if not sc_state["h0"]:
                emit_scan_heads0()
            if ph == 0:
                emit_scan_dtile_head(c, i)
                sc_state["ph"] = 1
                return
            emit_scan_dtile_g(c, i, ph - 1)
            if ph < 4:
                sc_state["ph"] = ph + 1
                return
            sc_state["ph"] = 0
            if i == 7 and c + 1 < NCH:
                emit_scan_half_loads(c + 1, 0)
            if i == 15:
                if c + 1 < NCH:
                    emit_scan_half_loads(c + 1, 1)
                    emit_bc_loads(c + 1)
                sc_state["c"], sc_state["i"] = c + 1, 0
            else:
                sc_state["i"] = i + 1

        if B3A_MODE != 0:
            filler = emit_scan_micro if B3A_MODE == 2 else None
            if B3A_MODE == 2:
                emit_scan_heads0()
            for qt in range(4):
                xnh_rdy = emit_xnh_load(qt)
                for u in range(64):
                    emit_ffn_unit(qt, u, xnh_rdy, filler)
                    if B3A_MODE == 2:
                        emit_scan_micro()
                LW["xnh_free"] = cpe.n
        while sc_state["c"] < NCH:
            emit_scan_micro()
        stp = p.put("scalar", lambda e: e.dma_start(st_c.ap(), state[:]),
                    waits=[(cdv, cdv.n), (cpl, cpl.n)], inc=s_stc, inck=16)
        p.put("gpsimd", lambda e: e.collective_compute(
            "AllGather", OP.bypass, replica_groups=[list(range(8))],
            ins=[st_c.ap()], outs=[st_g.ap()]),
            waits=[(s_stc, stp)], inc=ccc)
        p.put("gpsimd", lambda e: e.collective_compute(
            "AllGather", OP.bypass, replica_groups=[list(range(8))],
            ins=[st_c.ap()], outs=[st_g.ap()]),
            inc=ccc)
        p.finish_dma()
        p.emit(nc)
    bstack.close()
    QACT_B3A = qact.n

    if upto < 4:
        return _finish_debug(nc, dbg_outs, stack, qsp, qact, ccc)
    p = Plan()
    LW = {}
    s_hin = dsem(p, "s_hin")
    s_yl = [dsem(p, "s_yl0"), dsem(p, "s_yl1")]
    s_sc2 = [dsem(p, "s_sc2a"), dsem(p, "s_sc2b")]
    s_cc2 = [dsem(p, "s_cc2a"), dsem(p, "s_cc2b")]
    s_yw = [dsem(p, "s_yw0"), dsem(p, "s_yw1")]
    s_int = [dsem(p, "s_int0"), dsem(p, "s_int1")]
    s_rb = [dsem(p, f"s_rb{i}") for i in range(4)]
    s_gw = dsem(p, "s_gw")
    bstack = ExitStack()
    _a = lambda nm, shape, dt: bstack.enter_context(nc.sbuf_tensor(nm, shape, dt))
    interc = _a("interc", [128, 128 * 512], BF16)
    Ccb2 = _a("Ccb2", [128, 2 * DS * 512], BF16)
    rbk = _a("rbk", [128, 4 * 1024], BF16)
    tnb = _a("tnb", [128, 512], F32)
    gvb = _a("gvb", [128, 512], F32)
    Sc2 = _a("Sc2", [128, 2 * 512], F32)
    Eb = _a("Eb", [128, 2 * 512], F32)
    ctmp = _a("ctmp", [128, 2 * 512], F32)
    yacc = _a("yacc", [128, 2 * 512], F32)
    yld = _a("yld", [128, 2 * 512], F32)
    psD = bstack.enter_context(nc.psum_tensor("psD", [128, 8 * 512], F32))
    if True:
        hin_ready = 0
        for r in range(8):
            hl = p.put("gpsimd", lambda e, r=r: e.dma_start(
                hin_st[:, 0:256],
                st_g[r * 128:(r + 1) * 128, :]),
                waits=[(ccc, AGS_DONE), (cdv, hin_ready)], inc=s_hin, inck=16)
            hin_ready = p.put("vector", lambda e, r=r: e.scalar_tensor_tensor(
                hin[:], hin_st[:, 0:256], sel8_sb[:, r:r + 1], hin[:], OP.mult, OP.add),
                waits=[(s_hin, hl)], inc=cdv)

        prev_add = {"eng": None, "val": 0}

        def emit_sc2_load(ct):
            c, i = divmod(ct, 16)
            if c >= NCH:
                return
            sl = ct % 2
            p.put("scalar", lambda e, c=c, i=i, sl=sl: e.dma_start(
                Sc2[:, sl * 512:(sl + 1) * 512],
                bass.AP(ssum_s, (i * 128) * TOK + c * 512, [[TOK, 128], [1, 512]])),
                inc=s_sc2[sl], inck=16)

        def emit_yld_load(ct):
            c, i = divmod(ct, 16)
            if c >= NCH:
                return
            sl = ct % 2
            p.put("scalar", lambda e, c=c, i=i, sl=sl: e.dma_start(
                yld[:, sl * 512:(sl + 1) * 512],
                bass.AP(y_s, (i * 128) * TOK + c * 512, [[TOK, 128], [1, 512]])),
                waits=[(cpl, LW.get(("yldF", sl), 0))], inc=s_yl[sl], inck=16)

        def emit_corr_heads(c):
            cp = c % 2
            for n in range(DS):
                p.put("gpsimd", lambda e, c=c, cp=cp, n=n: e.dma_start(
                    Ccb2[:, cp * DS * 512 + n * 512:cp * DS * 512 + (n + 1) * 512],
                    bass.AP(bc_s, (16 + n) * TOK + c * 512, [[0, 128], [1, 512]])),
                    waits=[(cdv, LW.get(("cc2F", cp), 0))], inc=s_cc2[cp], inck=16)

        cr_state = {"c": 0, "i": 0}

        def emit_corr_piece():
            c, i = cr_state["c"], cr_state["i"]
            if c >= NCH:
                return
            cp = c % 2
            ct = c * 16 + i
            sl = ct % 2
            ya_off = sl * 512
            for n in range(DS):
                eb = n % 2
                ev = p.put("scalar", lambda e, i=i, n=n, sl=sl: e.activation(
                    Eb[:, (n % 2) * 512:(n % 2) * 512 + 512],
                    Sc2[:, sl * 512:(sl + 1) * 512], AF.Exp,
                    scale=aneg[:, i * DS + n:i * DS + n + 1]),
                    waits=[(s_sc2[sl], 16 * (ct // 2 + 1)),
                           (cdv, LW.get(("Eb2", eb), 0))], inc=cac)
                wl = [(cac, ev), (s_cc2[cp], 256 * (c // 2 + 1)), (cdv, hin_ready)]
                if n == 0:
                    wl.append((s_yw[sl], 16 * (ct // 2)))
                    tv = p.put("vector", lambda e, i=i, n=n, cp=cp, ya_off=ya_off:
                               e.scalar_tensor_tensor(
                        yacc[:, ya_off:ya_off + 512],
                        Eb[:, 0:512],
                        hin[:, i * DS + n:i * DS + n + 1],
                        Ccb2[:, cp * DS * 512 + n * 512:cp * DS * 512 + (n + 1) * 512],
                        OP.mult, OP.mult),
                        waits=wl, inc=cdv)
                    LW[("Eb2", eb)] = tv
                    prev_add["eng"], prev_add["val"] = "vector", tv
                    continue
                wctmp = LW.get(("ctmp2", eb), None)
                if wctmp is not None:
                    wl.append(wctmp)
                tv = p.put("vector", lambda e, i=i, n=n, eb=eb, cp=cp: e.scalar_tensor_tensor(
                    ctmp[:, eb * 512:(eb + 1) * 512],
                    Eb[:, eb * 512:(eb + 1) * 512],
                    hin[:, i * DS + n:i * DS + n + 1],
                    Ccb2[:, cp * DS * 512 + n * 512:cp * DS * 512 + (n + 1) * 512],
                    OP.mult, OP.mult),
                    waits=wl, inc=cdv)
                LW[("Eb2", eb)] = tv
                if n % 2 == 1:
                    wl2 = [(cdv, tv)]
                    if prev_add["eng"] == "vector":
                        wl2.append((cdv, prev_add["val"]))
                    ya = p.put("gpsimd", lambda e, ya_off=ya_off, eb=eb: e.tensor_add(
                        yacc[:, ya_off:ya_off + 512], yacc[:, ya_off:ya_off + 512],
                        ctmp[:, eb * 512:(eb + 1) * 512]),
                        waits=wl2, inc=cpl)
                    prev_add["eng"], prev_add["val"] = "gpsimd", ya
                    LW[("ctmp2", eb)] = (cpl, ya)
                else:
                    wl2 = [(cdv, tv)]
                    if prev_add["eng"] == "gpsimd":
                        wl2.append((cpl, prev_add["val"]))
                    ya = p.put("vector", lambda e, ya_off=ya_off, eb=eb: e.tensor_add(
                        yacc[:, ya_off:ya_off + 512], yacc[:, ya_off:ya_off + 512],
                        ctmp[:, eb * 512:(eb + 1) * 512]),
                        waits=wl2, inc=cdv)
                    prev_add["eng"], prev_add["val"] = "vector", ya
                    LW[("ctmp2", eb)] = (cdv, ya)
            wl3 = [(s_yl[sl], 16 * (ct // 2 + 1))]
            if prev_add["eng"] == "vector":
                wl3.append((cdv, prev_add["val"]))
            fa = p.put("gpsimd", lambda e, ya_off=ya_off, sl=sl: e.tensor_add(
                yacc[:, ya_off:ya_off + 512], yacc[:, ya_off:ya_off + 512],
                yld[:, sl * 512:(sl + 1) * 512]),
                waits=wl3, inc=cpl)
            prev_add["eng"], prev_add["val"] = "gpsimd", fa
            LW[("yldF", sl)] = fa
            p.put("gpsimd", lambda e, c=c, i=i, ya_off=ya_off: e.dma_start(
                bass.AP(y_s, (i * 128) * TOK + c * 512, [[TOK, 128], [1, 512]]),
                yacc[:, ya_off:ya_off + 512]),
                inc=s_yw[sl], inck=16)
            emit_sc2_load(ct + 1)
            emit_yld_load(ct + 1)
            if i == 8 and c + 1 < NCH:
                emit_corr_heads(c + 1)
            if i == 15:
                LW[("cc2F", cp)] = cdv.n
                cr_state["c"], cr_state["i"] = c + 1, 0
            else:
                cr_state["i"] = i + 1

        def emit_interc_half(c2, h):
            if c2 >= NCH:
                return
            for part in range(8 * h, 8 * h + 8):
                p.put("sync", lambda e, c2=c2, part=part: e.dma_start(
                    bass.AP(interc, part * 8 * 512, [[128 * 512, 128], [512, 8], [1, 512]]),
                    bass.AP(inter_s, (part * 8 * 128) * TOK + c2 * 512,
                            [[TOK, 128], [128 * TOK, 8], [1, 512]])),
                    waits=[(cpe, LW.get(("intF", h), 0))],
                    inc=s_int[h], inck=16)

        def emit_down_tchunk(c2, filler):
            for hh in range(2):
                last_mm = 0
                for nt in range(128):
                    rpar = nt % 4
                    rld = p.put("sync", lambda e, nt=nt, hh=hh, rpar=rpar: e.dma_start(
                        rbk[:, rpar * 1024:(rpar + 1) * 1024],
                        bass.AP(gW["d"], (nt * 128) * 2048 + hh * 1024,
                                [[2048, 128], [1, 1024]])),
                        waits=[(ccc, AGD_DONE), (cpe, LW.get(("rbk", rpar), 0))],
                        inc=s_rb[rpar], inck=16)
                    for h8 in range(8):
                        wl = [(s_rb[rpar], rld), (s_int[nt // 64], 128 * (c2 + 1))]
                        if nt == 0 and h8 == 0:
                            wl.append((cdv, LW.get("psD_free", 0)))
                        last_mm = p.put("tensor", lambda e, nt=nt, h8=h8, rpar=rpar: e.matmul(
                            psD[:, h8 * 512:(h8 + 1) * 512],
                            rbk[:, rpar * 1024 + h8 * 128: rpar * 1024 + (h8 + 1) * 128],
                            interc[:, nt * 512:(nt + 1) * 512],
                            start=(nt == 0), stop=(nt == 127)),
                            waits=wl, inc=(cpe if h8 == 7 else None))
                    LW[("rbk", rpar)] = last_mm
                    if nt % 8 == 7 and filler is not None:
                        filler()
                    if hh == 1 and nt == 63:
                        LW[("intF", 0)] = cpe.n
                        emit_interc_half(c2 + 1, 0)
                for h8 in range(8):
                    tnv = p.put("scalar", lambda e, h8=h8: e.activation(
                        tnb[:, 0:512], psD[:, h8 * 512:(h8 + 1) * 512],
                        AF.Tanh, scale=0.5),
                        waits=[(cpe, last_mm), (cdv, LW.get(("tnb", 0), 0))], inc=cac)
                    gv = p.put("vector", lambda e, h8=h8: e.scalar_tensor_tensor(
                        gvb[:, 0:512], tnb[:, 0:512],
                        1.0, psD[:, h8 * 512:(h8 + 1) * 512], OP.add, OP.mult),
                        waits=[(cac, tnv), (s_gw, LW.get(("gvb", 0), 0))], inc=cdv)
                    LW[("tnb", 0)] = gv
                    gsp = p.put("scalar", lambda e, c2=c2, hh=hh, h8=h8: e.dma_start(
                        bass.AP(g_s, ((hh * 8 + h8) * 128) * TOK + c2 * 512,
                                [[TOK, 128], [1, 512]]),
                        gvb[:, 0:512]),
                        waits=[(cdv, gv)], inc=s_gw, inck=16)
                    LW[("gvb", 0)] = gsp
                LW["psD_free"] = cdv.n
            LW[("intF", 1)] = cpe.n
            emit_interc_half(c2 + 1, 1)

        filler = emit_corr_piece if B3B_MODE not in (0,) else None
        if B3B_MODE != 0:
            emit_corr_heads(0)
            emit_sc2_load(0)
            emit_yld_load(0)
        if B3B_MODE != 1:
            emit_interc_half(0, 0)
            emit_interc_half(0, 1)
            for c2 in range(NCH):
                emit_down_tchunk(c2, filler)
        while B3B_MODE != 0 and cr_state["c"] < NCH:
            emit_corr_piece()
        p.finish_dma()
        p.emit(nc)
    bstack.close()
    QACT_B3B = qact.n

    if upto < 5:
        return _finish_debug(nc, dbg_outs, stack, qsp, qact, ccc)
    p = Plan()
    LW = {}
    s_wo = dsem(p, "s_wo")
    s_ym = [dsem(p, "s_ym0"), dsem(p, "s_ym1")]
    s_gb = dsem(p, "s_gb")
    s_x4 = [dsem(p, "s_x40"), dsem(p, "s_x41")]
    s_out = dsem(p, "s_out")
    bstack = ExitStack()
    _a = lambda nm, shape, dt: bstack.enter_context(nc.sbuf_tensor(nm, shape, dt))
    woutT = _a("woutT", [128, ND * 2048], BF16)
    ysm = _a("ysm", [128, 2 * 512], F32)
    ybf = _a("ybf", [128, ND * 512], BF16)
    gbig = _a("gbig", [128, ND * 512], F32)
    sA = _a("sA", [128, 2 * 512], F32)
    ubig = _a("ubig", [128, ND * 512], F32)
    utok = _a("utok", [128, 2 * 2048], F32)
    xt4 = _a("xt4", [128, 2 * 2048], F32)
    ob = _a("ob", [128, 2048], F32)
    psS = bstack.enter_context(nc.psum_tensor("psS", [128, 4 * 512], F32))
    psTa = bstack.enter_context(nc.psum_tensor("psTa", [128, 128], F32))
    psTb = bstack.enter_context(nc.psum_tensor("psTb", [128, 128], F32))
    psTP = [psTa, psTb]
    if True:
        wo_ready = 0
        for r in range(8):
            wo_ready = p.put("sync", lambda e, r=r: e.dma_start(
                bass.AP(woutT, r * 256, [[ND * 2048, 128], [2048, ND], [1, 256]]),
                bass.AP(g1, (2048 + r * 4096) * 256, [[256, 128], [128 * 256, ND], [1, 256]])),
                inc=s_wo, inck=16)
        for c in range(NCH):
            g_ld4 = 0
            for part in range(4):
                g_ld4 = p.put("scalar", lambda e, c=c, part=part: e.dma_start(
                    bass.AP(gbig, part * 4 * 512, [[ND * 512, 128], [512, 4], [1, 512]]),
                    bass.AP(g_s, (part * 4 * 128) * TOK + c * 512,
                            [[TOK, 128], [128 * TOK, 4], [1, 512]])),
                    waits=[(cdv, LW.get("gbig_free", 0))],
                    inc=s_gb, inck=16)
            cast_done = 0
            for i in range(ND):
                ipp = i % 2
                yld1 = p.put("scalar", lambda e, c=c, i=i, ipp=ipp: e.dma_start(
                    ysm[:, ipp * 512:(ipp + 1) * 512],
                    bass.AP(y_s, (i * 128) * TOK + c * 512, [[TOK, 128], [1, 512]])),
                    waits=[(cac, LW.get(("ysm", ipp), 0))],
                    inc=s_ym[ipp], inck=16)
                cast_done = p.put("scalar", lambda e, i=i, ipp=ipp: e.activation(
                    ybf[:, i * 512:(i + 1) * 512], ysm[:, ipp * 512:(ipp + 1) * 512],
                    AF.Copy),
                    waits=[(s_ym[ipp], yld1), (cpe, LW.get("ybf_free", 0))], inc=cac)
                LW[("ysm", ipp)] = cast_done
            for mt in range(ND):
                bank = mt % 4
                last_mm = 0
                for k in range(ND):
                    wl = [(s_wo, wo_ready), (cac, cast_done)]
                    if k == 0:
                        wl.append((cdv, LW.get(("psS", bank), 0)))
                    p.put("tensor", lambda e, mt=mt, k=k, bank=bank: e.matmul(
                        psS[:, bank * 512:(bank + 1) * 512],
                        woutT[:, k * 2048 + mt * 128: k * 2048 + (mt + 1) * 128],
                        ybf[:, k * 512:(k + 1) * 512],
                        start=(k == 0), stop=(k == ND - 1)),
                        waits=wl, inc=None)
                last_mm = p.put("tensor", lambda e: e.drain(), inc=cpe)
                sev = p.put("scalar", lambda e, mt=mt, bank=bank: e.activation(
                    sA[:, (mt % 2) * 512:(mt % 2) * 512 + 512],
                    psS[:, bank * 512:(bank + 1) * 512], AF.Identity,
                    bias=outb_sb[:, mt:mt + 1]),
                    waits=[(cpe, last_mm), (cdv, LW.get(("sA", mt % 2), 0))], inc=cac)
                uv = p.put("vector", lambda e, mt=mt: e.scalar_tensor_tensor(
                    ubig[:, mt * 512:(mt + 1) * 512],
                    sA[:, (mt % 2) * 512:(mt % 2) * 512 + 512], 0.5,
                    gbig[:, mt * 512:(mt + 1) * 512], OP.mult, OP.mult),
                    waits=[(cac, sev), (s_gb, g_ld4), (cpe, LW.get("ubig_free", 0))],
                    inc=cdv)
                LW[("sA", mt % 2)] = uv
                LW[("psS", bank)] = uv
            u_done = cdv.n
            LW["gbig_free"] = u_done
            for tau in range(4):
                tp = tau % 2
                xld = p.put("sync", lambda e, c=c, tau=tau, tp=tp: e.dma_start(
                    xt4[:, tp * 2048:(tp + 1) * 2048],
                    x_in[c * 512 + tau * 128: c * 512 + (tau + 1) * 128, :]),
                    waits=[(cdv, LW.get(("xt4", tp), 0))], inc=s_x4[tp], inck=16)
                ev_last = 0
                for i0 in range(0, ND, 2):
                    for j in range(2):
                        i = i0 + j
                        p.put("tensor", lambda e, i=i, sl=j, tau=tau: e.transpose(
                            psTP[sl][:, :],
                            bass.AP(ubig, i * 512 + tau * 128, [[ND * 512, 128], [1, 128]]),
                            idf32[:]),
                            waits=[(cdv, u_done), (cac, LW.get(("psT", j), 0))], inc=None)
                    dr = p.put("tensor", lambda e: e.drain(), inc=cpe)
                    for j in range(2):
                        i = i0 + j
                        ev_last = p.put("scalar", lambda e, i=i, sl=j, tp=tp: e.copy(
                            utok[:, tp * 2048 + i * 128: tp * 2048 + (i + 1) * 128],
                            psTP[sl][:, :]),
                            waits=[(cpe, dr), (cdv, LW.get(("utok", tp), 0))], inc=cac)
                        LW[("psT", j)] = ev_last
                oadd = p.put("vector", lambda e, tp=tp: e.tensor_add(
                    ob[:, 0:2048], xt4[:, tp * 2048:(tp + 1) * 2048],
                    utok[:, tp * 2048:(tp + 1) * 2048]),
                    waits=[(cac, ev_last), (s_x4[tp], xld), (s_out, LW.get(("ob", 0), 0))],
                    inc=cdv)
                LW[("xt4", tp)] = oadd
                LW[("utok", tp)] = oadd
                ost = p.put("scalar", lambda e, c=c, tau=tau: e.dma_start(
                    out_sh[c * 512 + tau * 128: c * 512 + (tau + 1) * 128, :],
                    ob[:, 0:2048]),
                    waits=[(cdv, oadd)], inc=s_out, inck=16)
                LW[("ob", 0)] = ost
            LW["ybf_free"] = cpe.n
            LW["ubig_free"] = cpe.n
        p.finish_dma()
        p.emit(nc)
    bstack.close()

    stack.close()
    nc.finalize()
    return nc


def _finish_debug(nc, dbg_outs, stack, qsp, qact, ccc=None):
    p = Plan()
    if ccc is not None and ccc.n:
        p.put("sync", lambda e: e.wait_ge(ccc.h, ccc.n) and None)
    n = 0
    for od, ih in dbg_outs:
        rows = ih.shape[0]
        step = max(1, min(rows, (1 << 20) // max(1, int(ih.shape[1]))))
        r = 0
        while r < rows:
            rr = min(step, rows - r)
            n = p.put("sync", lambda e, od=od, ih=ih, r=r, rr=rr: e.dma_start(
                od[r:r + rr, :], ih[r:r + rr, :]),
                waits=[(qsp, qsp.n), (qact, qact.n)], inc=qsp, inck=16)
            r += rr
    p.put("sync", lambda e: e.wait_ge(qsp.h, qsp.n) and None)
    p.put("scalar", lambda e: e.wait_ge(qact.h, qact.n) and None)
    p.emit(nc)
    stack.close()
    nc.finalize()
    return nc


_CACHE = {}


def kernel(L0_attn_x, L0_attn_A_log, L0_attn_x_proj_w, L0_attn_x_proj_b,
           L0_attn_dt_proj_w, L0_attn_dt_proj_b, L0_attn_out_proj_w,
           L0_attn_out_proj_b, L0_ffn_gate_W_packed, L0_ffn_gate_scales,
           L0_ffn_up_W_packed, L0_ffn_up_scales, L0_ffn_down_W_packed,
           L0_ffn_down_scales):
    x = np.ascontiguousarray(np.asarray(L0_attn_x, dtype=np.float32))
    if "nc" not in _CACHE:
        _CACHE["nc"] = build_program()
    nc = _CACHE["nc"]

    gwp = np.asarray(L0_ffn_gate_W_packed, dtype=np.int32)
    gsc = np.asarray(L0_ffn_gate_scales, dtype=np.float32)
    uwp = np.asarray(L0_ffn_up_W_packed, dtype=np.int32)
    usc = np.asarray(L0_ffn_up_scales, dtype=np.float32)
    dwp = np.asarray(L0_ffn_down_W_packed, dtype=np.int32)
    dsc = np.asarray(L0_ffn_down_scales, dtype=np.float32)
    wdt = np.asarray(L0_attn_dt_proj_w, dtype=np.float32)
    wout = np.asarray(L0_attn_out_proj_w, dtype=np.float32)

    in_maps = []
    for c in range(NCORE):
        b, hf = c // 2, c % 2
        sel = np.zeros((128, 8), np.float32)
        if hf == 1:
            sel[:, c - 1] = 1.0
        in_maps.append({
            "x_sh": np.ascontiguousarray(x[b, hf * TOK:(hf + 1) * TOK, :]),
            "gwp": np.ascontiguousarray(gwp[c * 2048:(c + 1) * 2048, :]),
            "gsc": np.ascontiguousarray(gsc[c * 2048:(c + 1) * 2048, :]),
            "uwp": np.ascontiguousarray(uwp[c * 2048:(c + 1) * 2048, :]),
            "usc": np.ascontiguousarray(usc[c * 2048:(c + 1) * 2048, :]),
            "dwp": np.ascontiguousarray(dwp[:, c * 1024:(c + 1) * 1024]),
            "dsc": np.ascontiguousarray(dsc[:, c * 64:(c + 1) * 64]),
            "wdt_sh": np.ascontiguousarray(wdt[c * 256:(c + 1) * 256, :]),
            "wout_sh": np.ascontiguousarray(wout[c * 256:(c + 1) * 256, :]),
            "alog": np.ascontiguousarray(np.asarray(L0_attn_A_log, np.float32)),
            "xpw": np.ascontiguousarray(np.asarray(L0_attn_x_proj_w, np.float32)),
            "xpb": np.ascontiguousarray(np.asarray(L0_attn_x_proj_b, np.float32)),
            "dtb": np.ascontiguousarray(np.asarray(L0_attn_dt_proj_b, np.float32)),
            "outb": np.ascontiguousarray(np.asarray(L0_attn_out_proj_b, np.float32)),
            "sel8": sel,
        })
    res = run_bass_kernel_spmd(nc, in_maps, core_ids=list(range(NCORE)))
    out = np.empty((B_, L_, D), np.float32)
    for c in range(NCORE):
        b, hf = c // 2, c % 2
        out[b, hf * TOK:(hf + 1) * TOK, :] = res.results[c]["out_sh"]
    return out


# version 07
# speedup vs baseline: 1.4408x; 1.0141x over previous
import numpy as np
from contextlib import ExitStack

import concourse.bass as bass
import concourse.mybir as mybir
from concourse.bass_utils import run_bass_kernel_spmd

F32 = mybir.dt.float32
BF16 = mybir.dt.bfloat16
I32 = mybir.dt.int32
AF = mybir.ActivationFunctionType
OP = mybir.AluOpType
AX = mybir.AxisListType

NCORE = 8
B_, L_, D, DS, DM = 4, 8192, 2048, 16, 16384
TOK = 4096
CH = 512
NCH = TOK // CH
QT = 1024
ND = D // 128
GELU_C = 0.7978845608028654
GELU_A2 = 0.21145925545
EPS = 1e-5

ENGS = ("sync", "scalar", "tensor", "vector", "gpsimd")
DEQ_NW = 3
DEQ_NJT = 16
B3B_MODE = 2
B3A_MODE = 2


class Ctr:
    def __init__(self, stack, nc, name):
        self.h = stack.enter_context(nc.semaphore(name))
        self.n = 0
        self.name = name


class Plan:
    def __init__(self):
        self.streams = {e: [] for e in ENGS}
        self.lastwait = {e: {} for e in ENGS}
        self.dsems = []

    def finish_dma(self, extra=()):
        for ctr in list(self.dsems) + list(extra):
            if ctr.n:
                self.put("sync", (lambda e, c=ctr, v=ctr.n: e.wait_ge(c.h, v) and None))

    def put(self, eng, fn, waits=(), inc=None, inck=1):
        lw = self.lastwait[eng]
        w2 = []
        for ctr, val in waits:
            if val is None or val <= 0:
                continue
            if lw.get(ctr.name, -1) >= val:
                continue
            lw[ctr.name] = val
            w2.append((ctr, val))
        if inc is not None:
            inc.n += inck
        self.streams[eng].append((fn, w2, inc, inck))
        return None if inc is None else inc.n

    def emit(self, nc):
        with nc.Block() as blk:
            def runner(items):
                def run(e):
                    for fn, waits, inc, inck in items:
                        for ctr, val in waits:
                            e.wait_ge(ctr.h, val)
                        inst = fn(e)
                        if inc is not None:
                            inst.then_inc(inc.h, inck)
                return run
            for eng in ENGS:
                if self.streams[eng]:
                    getattr(blk, eng)(runner(self.streams[eng]))


def build_program(upto=9, dbg=(), b1part=9):
    nc = bass.Bass("TRN2", target_bir_lowering=False)

    x_in = nc.dram_tensor("x_sh", [TOK, D], F32, kind="ExternalInput")
    wp_in = {w: nc.dram_tensor(f"{w}wp", [2048, 1024], I32, kind="ExternalInput")
             for w in ("g", "u", "d")}
    sc_in = {w: nc.dram_tensor(f"{w}sc", [2048, 64], F32, kind="ExternalInput")
             for w in ("g", "u", "d")}
    wdt_sh = nc.dram_tensor("wdt_sh", [256, 2048], F32, kind="ExternalInput")
    wout_sh = nc.dram_tensor("wout_sh", [256, 2048], F32, kind="ExternalInput")
    alog = nc.dram_tensor("alog", [D, DS], F32, kind="ExternalInput")
    xpw = nc.dram_tensor("xpw", [32, D], F32, kind="ExternalInput")
    xpb_in = nc.dram_tensor("xpb", [32], F32, kind="ExternalInput")
    dtb_in = nc.dram_tensor("dtb", [D], F32, kind="ExternalInput")
    outb_in = nc.dram_tensor("outb", [D], F32, kind="ExternalInput")
    sel8_in = nc.dram_tensor("sel8", [128, 8], F32, kind="ExternalInput")
    out_sh = nc.dram_tensor("out_sh", [TOK, D], F32, kind="ExternalOutput")

    xn_s = nc.dram_tensor("xn_s", [D, TOK], BF16)
    dt_s = nc.dram_tensor("dt_s", [D, TOK], F32)
    ssum_s = nc.dram_tensor("ssum_s", [D, TOK], F32)
    bc_s = nc.dram_tensor("bc_s", [32, TOK], BF16)
    y_s = nc.dram_tensor("y_s", [D, TOK], F32)
    inter_s = nc.dram_tensor("inter_s", [DM, TOK], BF16)
    g_s = nc.dram_tensor("g_s", [D, TOK], F32)
    c1 = nc.dram_tensor("c1", [4096, 256], BF16)
    g1 = nc.dram_tensor("g1", [8 * 4096, 256], BF16)
    cW = {w: nc.dram_tensor(f"c{w}", [2048, 2048], BF16) for w in ("g", "u", "d")}
    gW = {w: nc.dram_tensor(f"gw{w}", [8 * 2048, 2048], BF16) for w in ("g", "u", "d")}
    st_c = nc.dram_tensor("st_c", [128, 256], F32)
    st_g = nc.dram_tensor("st_g", [8 * 128, 256], F32)
    scr = dict(xn_s=xn_s, dt_s=dt_s, ssum_s=ssum_s, bc_s=bc_s, y_s=y_s,
               inter_s=inter_s, g_s=g_s, c1=c1, g1=g1, st_c=st_c, st_g=st_g)
    scr.update({f"c{w}": cW[w] for w in cW})
    scr.update({f"gw{w}": gW[w] for w in gW})
    dbg_outs = []
    for nm in dbg:
        h = scr[nm]
        dbg_outs.append((nc.dram_tensor(f"dbg_{nm}", list(h.shape), h.dtype,
                                        kind="ExternalOutput"), h))

    stack = ExitStack()
    sb = lambda nm, shape, dt: stack.enter_context(nc.sbuf_tensor(nm, shape, dt))

    aneg = sb("aneg", [128, ND * DS], F32)
    dtb_sb = sb("dtb_sb", [128, ND], F32)
    outb_sb = sb("outb_sb", [128, ND], F32)
    xpb_sb = sb("xpb_sb", [32, 1], F32)
    sel8_sb = sb("sel8_sb", [128, 8], F32)
    state = sb("state", [128, ND * DS], F32)
    sstate = sb("sstate", [128, ND], F32)
    hin = sb("hin", [128, ND * DS], F32)
    hin_st = sb("hin_st", [128, ND * DS], F32)
    idbf = sb("idbf", [128, 128], BF16)
    idf32 = sb("idf32", [128, 128], F32)
    zeros = sb("zeros", [128, CH], F32)
    wxpT = sb("wxpT", [128, ND * 128], BF16)
    csts = sb("csts", [128, 4], F32)

    qsp = Ctr(stack, nc, "qsp")
    qact = Ctr(stack, nc, "qact")

    def dsem(p, name):
        c = Ctr(stack, nc, name)
        p.dsems.append(c)
        return c

    cpe = Ctr(stack, nc, "cpe")
    cdv = Ctr(stack, nc, "cdv")
    cac = Ctr(stack, nc, "cac")
    cpl = Ctr(stack, nc, "cpl")
    ccc = Ctr(stack, nc, "ccc")

    p = Plan()
    s_c0 = dsem(p, "s_c0")
    for i in range(ND):
        p.put("sync", lambda e, i=i: e.dma_start(
            aneg[:, i * DS:(i + 1) * DS], alog[i * 128:(i + 1) * 128, :]), inc=s_c0, inck=16)
        p.put("sync", lambda e, i=i: e.dma_start(
            dtb_sb[:, i:i + 1], bass.AP(dtb_in, i * 128, [[1, 128], [1, 1]])), inc=s_c0, inck=16)
        p.put("sync", lambda e, i=i: e.dma_start(
            outb_sb[:, i:i + 1], bass.AP(outb_in, i * 128, [[1, 128], [1, 1]])), inc=s_c0, inck=16)
    p.put("sync", lambda e: e.dma_start(xpb_sb[:, :], bass.AP(xpb_in, 0, [[1, 32], [1, 1]])),
          inc=s_c0, inck=16)
    p.put("sync", lambda e: e.dma_start(sel8_sb[:, :], sel8_in[:, :]), inc=s_c0, inck=16)
    ld_done = s_c0.n
    p.put("gpsimd", lambda e: e.memset(csts[:, 0:1], -4.0))
    p.put("gpsimd", lambda e: e.memset(csts[:, 1:2], -6.0))
    p.put("gpsimd", lambda e: e.memset(csts[:, 2:3], EPS))
    p.put("gpsimd", lambda e: e.memset(csts[:, 3:4], 1.0))
    p.put("gpsimd", lambda e: e.memset(idbf[:], 0.0))
    p.put("gpsimd", lambda e: e.affine_select(out=idbf[:], in_=idbf[:], compare_op=OP.not_equal,
                                              fill=1.0, base=0, pattern=[[-1, 128]],
                                              channel_multiplier=1))
    p.put("gpsimd", lambda e: e.memset(idf32[:], 0.0))
    p.put("gpsimd", lambda e: e.affine_select(out=idf32[:], in_=idf32[:], compare_op=OP.not_equal,
                                              fill=1.0, base=0, pattern=[[-1, 128]],
                                              channel_multiplier=1))
    p.put("gpsimd", lambda e: e.memset(zeros[:], 0.0))
    p.put("gpsimd", lambda e: e.memset(state[:], 0.0))
    p.put("gpsimd", lambda e: e.memset(sstate[:], 0.0))
    p.put("gpsimd", lambda e: e.memset(hin[:], 0.0))
    p.put("gpsimd", lambda e: e.memset(wxpT[:], 0.0))
    a_exp = p.put("scalar", lambda e: e.activation(aneg[:], aneg[:], AF.Exp),
                  waits=[(s_c0, ld_done)], inc=cac)
    p.put("vector", lambda e: e.tensor_scalar_mul(aneg[:], aneg[:], -1.0),
          waits=[(cac, a_exp)], inc=cdv)
    p.finish_dma()
    p.emit(nc)

    wstack = ExitStack()
    wdtT = wstack.enter_context(nc.sbuf_tensor("wdtT", [128, ND * 2048], BF16))
    if upto < 1:
        return _finish_debug(nc, dbg_outs, stack, qsp, qact, ccc)
    p = Plan()
    LW = {}
    s_xpw = dsem(p, "s_xpw"); s_w4 = dsem(p, "s_w4"); s_stg1 = dsem(p, "s_stg1")
    s_wp = [dsem(p, "s_wp0"), dsem(p, "s_wp1")]
    s_sc = [dsem(p, "s_sc0"), dsem(p, "s_sc1")]
    s_sg = [dsem(p, "s_sg0"), dsem(p, "s_sg1")]
    s_wdt = dsem(p, "s_wdt")
    bstack = ExitStack()
    _a = lambda nm, shape, dt: bstack.enter_context(nc.sbuf_tensor(nm, shape, dt))
    w4 = _a("w4", [128, 2 * 2048], F32)
    wb4 = _a("wb4", [128, 2 * 2048], BF16)
    stg1 = _a("stg1", [128, 256], BF16)
    xpw_f = _a("xpw_f", [32, 2048], F32)
    xpw_b = _a("xpw_b", [32, 2048], BF16)
    wpt = _a("wpt", [128, 2 * 1024], I32)
    sct = _a("sct", [128, 2 * 64], F32)
    loi = _a("loi", [128, 1024], I32)
    hii = _a("hii", [128, 1024], I32)
    pf = _a("pf", [128, 2 * 1024], F32)
    ge8 = _a("ge8", [128, 2048], F32)
    cc_ = _a("cc_", [128, 2048], F32)
    r1_ = _a("r1_", [128, 2048], F32)
    r2_ = _a("r2_", [128, 2048], F32)
    t1_ = _a("t1_", [128, 2048], F32)
    mm_ = cc_
    sg_ = ge8
    vv_ = r1_
    deq = _a("deq", [128, 2 * 2048], BF16)
    stg2 = _a("stg2", [128, 2 * 2048], BF16)
    pstrA = bstack.enter_context(nc.psum_tensor("pstrA", [128, 128], BF16))
    pstrB = bstack.enter_context(nc.psum_tensor("pstrB", [128, 128], BF16))
    pstrC = bstack.enter_context(nc.psum_tensor("pstrC", [128, 128], BF16))
    pstrD = bstack.enter_context(nc.psum_tensor("pstrD", [128, 128], BF16))
    pstrP = [pstrA, pstrB, pstrC, pstrD]
    if True:
        v = p.put("sync", lambda e: e.dma_start(xpw_f[:, :], xpw[:, :]), inc=s_xpw, inck=16)
        cast = p.put("vector", lambda e: e.tensor_copy(xpw_b[:], xpw_f[:]),
                     waits=[(s_xpw, v)], inc=cdv)
        for i0 in range(0, ND, 2):
            for j in range(2):
                i = i0 + j
                p.put("tensor", lambda e, i=i, sl=j: e.transpose(
                    pstrP[sl][:, 0:32],
                    xpw_b[:, i * 128:(i + 1) * 128], idbf[0:32, 0:32]),
                    waits=[(cdv, cast), (cac, LW.get(("pstr", j), 0))], inc=None)
            dr = p.put("tensor", lambda e: e.drain(), inc=cpe)
            for j in range(2):
                i = i0 + j
                ev = p.put("scalar", lambda e, i=i, sl=j: e.copy(
                    wxpT[:, i * 128:i * 128 + 32], pstrP[sl][:, 0:32]),
                    waits=[(cpe, dr)], inc=cac)
                LW[("pstr", j)] = ev

        for wsrc, c1row in (((wdt_sh, 0), (wout_sh, 2048)) if b1part >= 1 else ()):
            lds = []
            for j in range(2):
                lds.append(p.put("sync", lambda e, j=j, wsrc=wsrc: e.dma_start(
                    w4[:, j * 2048:(j + 1) * 2048], wsrc[j * 128:(j + 1) * 128, :]),
                    waits=[(cpe, LW.get("w4_free", 0))], inc=s_w4, inck=16))
            cast = p.put("vector", lambda e: e.tensor_copy(wb4[:], w4[:]),
                         waits=[(s_w4, lds[-1]), (cpe, LW.get("w4_free", 0))], inc=cdv)
            for i in range(ND):
                for j in range(2):
                    p.put("tensor", lambda e, i=i, j=j: e.transpose(
                        pstrP[j][:, :],
                        wb4[:, j * 2048 + i * 128: j * 2048 + (i + 1) * 128], idbf[:]),
                        waits=[(cdv, cast), (cac, LW.get(("pstr", j), 0))], inc=None)
                dr = p.put("tensor", lambda e: e.drain(), inc=cpe)
                ev = None
                for j in range(2):
                    ev = p.put("scalar", lambda e, j=j: e.copy(
                        stg1[:, j * 128:(j + 1) * 128], pstrP[j][:, :]),
                        waits=[(cpe, dr), (s_stg1, LW.get("stg1_free", 0))], inc=cac)
                    LW[("pstr", j)] = ev
                std = p.put("scalar", lambda e, i=i, c1row=c1row: e.dma_start(
                    c1[c1row + i * 128: c1row + (i + 1) * 128, :], stg1[:]),
                    waits=[(cac, ev)], inc=s_stg1, inck=16)
                LW["stg1_free"] = std
            LW["w4_free"] = cpe.n
        if b1part >= 2:
            p.put("gpsimd", lambda e: e.collective_compute(
                "AllGather", OP.bypass, replica_groups=[list(range(8))],
                ins=[c1.ap()], outs=[g1.ap()]),
                waits=[(s_stg1, s_stg1.n)], inc=ccc)

        for sl in range(4):
            LW[("pstr", sl)] = cac.n
        _wl = ("g", "u", "d") if b1part >= 3 else ()
        _wl = _wl[:DEQ_NW] if b1part in (3, 4, 5) else _wl
        for w in _wl:
            for jt in range(DEQ_NJT if b1part in (3, 4, 5) else 16):
                par = jt % 2
                ldw = p.put("sync", lambda e, w=w, jt=jt, par=par: e.dma_start(
                    wpt[:, par * 1024:(par + 1) * 1024],
                    wp_in[w][jt * 128:(jt + 1) * 128, :]),
                    waits=[(cdv, LW.get(("wpt", par), 0))], inc=s_wp[par], inck=16)
                lds = p.put("sync", lambda e, w=w, jt=jt, par=par: e.dma_start(
                    sct[:, par * 64:(par + 1) * 64],
                    sc_in[w][jt * 128:(jt + 1) * 128, :]),
                    waits=[(cdv, LW.get(("sct", par), 0))], inc=s_sc[par], inck=16)
                lo = p.put("vector", lambda e, par=par: e.tensor_scalar(
                    loi[:], wpt[:, par * 1024:(par + 1) * 1024], 15, None, OP.bitwise_and),
                    waits=[(s_wp[par], ldw), (cdv, LW.get("loi_free", 0))], inc=cdv)
                hi = p.put("vector", lambda e, par=par: e.tensor_scalar(
                    hii[:], wpt[:, par * 1024:(par + 1) * 1024], 4, None,
                    OP.logical_shift_right), inc=cdv)
                LW[("wpt", par)] = hi
                p.put("vector", lambda e: e.tensor_copy(pf[:, 0:1024], loi[:]), inc=cdv)
                cast2 = p.put("vector", lambda e: e.tensor_copy(pf[:, 1024:2048], hii[:]),
                              inc=cdv)
                LW["loi_free"] = cast2
                g8 = p.put("vector", lambda e: e.tensor_scalar(
                    ge8[:], pf[:, 0:2048], 8.0, None, OP.is_ge),
                    waits=[(cpl, LW.get("ge8_free", 0))], inc=cdv)
                cc1 = p.put("vector", lambda e: e.scalar_tensor_tensor(
                    cc_[:], ge8[:], -8.0, pf[:, 0:2048],
                    OP.mult, OP.add),
                    waits=[(cpl, LW.get("cc_free", 0)), (cac, LW.get("cc_free_a", 0))],
                    inc=cdv)
                ra = p.put("scalar", lambda e: e.activation(
                    r1_[:], cc_[:], AF.Relu, bias=csts[:, 0:1]),
                    waits=[(cdv, cc1), (cdv, LW.get("r1_free", 0))], inc=cac)
                rb = p.put("scalar", lambda e: e.activation(
                    r2_[:], cc_[:], AF.Relu, bias=csts[:, 1:2]),
                    waits=[(cdv, LW.get("r2_free", 0))], inc=cac)
                tt1 = p.put("gpsimd", lambda e: e.tensor_add(t1_[:], cc_[:], r1_[:]),
                            waits=[(cac, ra), (cdv, LW.get("t1_free", 0))], inc=cpl)
                LW["cc_free_a"] = rb
                mmv = p.put("vector", lambda e: e.scalar_tensor_tensor(
                    mm_[:], t1_[:], 0.5, r2_[:], OP.mult, OP.add),
                    waits=[(cpl, tt1), (cac, rb)], inc=cdv)
                LW["t1_free"] = mmv
                LW["r2_free"] = mmv
                sgv = p.put("gpsimd", lambda e: e.tensor_scalar(
                    sg_[:], ge8[:], -2.0, 1.0, OP.mult, OP.add),
                    waits=[(cdv, cc1)], inc=cpl)
                vvv = p.put("gpsimd", lambda e: e.tensor_mul(vv_[:], mm_[:], sg_[:]),
                            waits=[(cdv, mmv)], inc=cpl)
                LW["ge8_free"] = vvv
                LW["cc_free"] = vvv
                last_scale = 0
                for pi in range(2):
                    last_scale = p.put("vector", lambda e, pi=pi, par=par: e.tensor_tensor(
                        bass.AP(deq, par * 2048 + pi, [[4096, 128], [2, 1024]]),
                        vv_[:, pi * 1024:(pi + 1) * 1024],
                        bass.AP(sct, par * 64, [[128, 128], [1, 64], [0, 16]]),
                        OP.mult),
                        waits=[(cpl, vvv), (s_sc[par], lds), (cpe, LW.get(("deq", par), 0))],
                        inc=cdv)
                LW["r1_free"] = last_scale
                LW[("sct", par)] = last_scale
                evs = 0
                for i0 in (range(0, ND, 4) if b1part >= 4 else ()):
                    for j in range(4):
                        i = i0 + j
                        p.put("tensor", lambda e, i=i, par=par, sl=j: e.transpose(
                            pstrP[sl][:, :],
                            bass.AP(deq, par * 2048 + i * 128, [[4096, 128], [1, 128]]),
                            idbf[:]),
                            waits=[(cdv, last_scale), (cac, LW.get(("pstr", j), 0))],
                            inc=None)
                    dr = p.put("tensor", lambda e: e.drain(), inc=cpe)
                    evs = 0
                    for j in range(4):
                        i = i0 + j
                        evs = p.put("scalar", lambda e, i=i, par=par, sl=j: e.copy(
                            stg2[:, par * 2048 + i * 128: par * 2048 + (i + 1) * 128],
                            pstrP[sl][:, :]),
                            waits=[(cpe, dr), (s_sg[par], LW.get(("stg2", par), 0))], inc=cac)
                        LW[("pstr", j)] = evs
                LW[("deq", par)] = cpe.n
                if b1part < 4:
                    LW[("deq", par)] = 0
                    continue
                std = 0
                if b1part < 5:
                    LW[("stg2", par)] = 0
                    continue
                for q4 in range(4):
                    std = p.put("scalar", lambda e, w=w, jt=jt, par=par, q4=q4: e.dma_start(
                        bass.AP(cW[w], jt * 128 + q4 * 4 * 128 * 2048,
                                [[2048, 128], [128 * 2048, 4], [1, 128]]),
                        bass.AP(stg2, par * 2048 + q4 * 4 * 128,
                                [[4096, 128], [128, 4], [1, 128]])),
                        waits=[(cac, evs)], inc=s_sg[par], inck=16)
                LW[("stg2", par)] = std
            if b1part >= 6:
                p.put("gpsimd", lambda e, w=w: e.collective_compute(
                    "AllGather", OP.bypass, replica_groups=[list(range(8))],
                    ins=[cW[w].ap()], outs=[gW[w].ap()]),
                    waits=[(s_sg[0], s_sg[0].n), (s_sg[1], s_sg[1].n)], inc=ccc)
        wdt_lds = 0
        for r in range(8):
            wdt_lds = p.put("sync", lambda e, r=r: e.dma_start(
                bass.AP(wdtT, r * 256, [[ND * 2048, 128], [2048, ND], [1, 256]]),
                bass.AP(g1, (r * 4096) * 256, [[256, 128], [128 * 256, ND], [1, 256]])),
                waits=[(ccc, 1)], inc=s_wdt, inck=16)
        WDT_READY = (s_wdt, wdt_lds)
        p.finish_dma()
        p.emit(nc)
    bstack.close()
    AG1_DONE, AGG_DONE, AGU_DONE, AGD_DONE, AGS_DONE = 1, 2, 3, 4, 6

    if upto < 2:
        return _finish_debug(nc, dbg_outs, stack, qsp, qact, ccc)
    p = Plan()
    LW = {}
    s_xt = [dsem(p, "s_xt0"), dsem(p, "s_xt1")]
    s_xnsp = dsem(p, "s_xnsp")
    s_dtv = [dsem(p, "s_dtv0"), dsem(p, "s_dtv1")]
    s_bcv = dsem(p, "s_bcv")
    bstack = ExitStack()
    _a = lambda nm, shape, dt: bstack.enter_context(nc.sbuf_tensor(nm, shape, dt))
    xtok = _a("xtok", [128, 2 * 2048], F32)
    sqb = _a("sqb", [128, 2048], F32)
    lnst = _a("lnst", [128, 8], F32)
    xnt = _a("xnt", [128, 2 * 2048], BF16)
    xnc = _a("xnc", [128, ND * 512], BF16)
    spb = _a("spb", [128, 4 * 512], F32)
    dtv = _a("dtv", [128, 2 * 512], F32)
    bcv = _a("bcv", [32, 512], BF16)
    ps_trA = bstack.enter_context(nc.psum_tensor("ps_trA", [128, 128], BF16))
    ps_trB = bstack.enter_context(nc.psum_tensor("ps_trB", [128, 128], BF16))
    ps_trP = [ps_trA, ps_trB]
    ps_dt = bstack.enter_context(nc.psum_tensor("ps_dt", [128, 4 * 512], F32))
    ps_xp = bstack.enter_context(nc.psum_tensor("ps_xp", [128, 512], F32))
    if True:
        s_wdtS, wdt_ready = WDT_READY

        for c in range(NCH):
            xn_evs = 0
            for tau in range(4):
                par = tau % 2
                xl = p.put("sync", lambda e, c=c, tau=tau, par=par: e.dma_start(
                    xtok[:, par * 2048:(par + 1) * 2048],
                    x_in[c * 512 + tau * 128: c * 512 + (tau + 1) * 128, :]),
                    waits=[(cdv, LW.get(("xtok", par), 0))], inc=s_xt[par], inck=16)
                sq = p.put("scalar", lambda e, par=par: e.activation(
                    sqb[:], xtok[:, par * 2048:(par + 1) * 2048], AF.Square),
                    waits=[(s_xt[par], xl), (cdv, LW.get("sqb_free", 0))], inc=cac)
                s0v = p.put("vector", lambda e, par=par: e.reduce_sum(
                    lnst[:, 0:1], xtok[:, par * 2048:(par + 1) * 2048], AX.X),
                    waits=[(s_xt[par], xl)], inc=cdv)
                s1 = p.put("vector", lambda e: e.reduce_sum(lnst[:, 1:2], sqb[:], AX.X),
                           waits=[(cac, sq)], inc=cdv)
                LW["sqb_free"] = s1
                mean = p.put("vector", lambda e: e.tensor_scalar_mul(
                    lnst[:, 2:3], lnst[:, 0:1], 1.0 / 2048),
                    waits=[(cdv, s0v)], inc=cdv)
                tch1 = p.put("vector", lambda e: e.tensor_copy(lnst[:, 7:8], lnst[:, 2:3]),
                             waits=[(cdv, mean)], inc=cdv)
                m2 = p.put("scalar", lambda e: e.activation(
                    lnst[:, 3:4], lnst[:, 2:3], AF.Square), waits=[(cdv, tch1)], inc=cac)
                t2 = p.put("scalar", lambda e: e.copy(lnst[:, 7:8], lnst[:, 3:4]),
                           waits=[(cac, m2)], inc=cac)
                vr = p.put("vector", lambda e: e.scalar_tensor_tensor(
                    lnst[:, 4:5], lnst[:, 1:2], 1.0 / 2048, lnst[:, 3:4],
                    OP.mult, OP.subtract), waits=[(cac, t2), (cdv, s1)], inc=cdv)
                tch3 = p.put("vector", lambda e: e.tensor_copy(lnst[:, 7:8], lnst[:, 4:5]),
                             waits=[(cdv, vr)], inc=cdv)
                lnv = p.put("scalar", lambda e: e.activation(
                    lnst[:, 5:6], lnst[:, 4:5], AF.Ln, bias=csts[:, 2:3]),
                    waits=[(cdv, tch3)], inc=cac)
                lt = p.put("scalar", lambda e: e.copy(lnst[:, 7:8], lnst[:, 5:6]),
                           waits=[(cac, lnv)], inc=cac)
                rs = p.put("scalar", lambda e: e.activation(
                    lnst[:, 6:7], lnst[:, 5:6], AF.Exp, scale=-0.5),
                    waits=[(cac, lt)], inc=cac)
                t4 = p.put("scalar", lambda e: e.copy(lnst[:, 7:8], lnst[:, 6:7]),
                           waits=[(cac, rs)], inc=cac)
                xn1 = p.put("vector", lambda e, par=par: e.tensor_scalar(
                    xnt[:, par * 2048:(par + 1) * 2048],
                    xtok[:, par * 2048:(par + 1) * 2048],
                    lnst[:, 2:3], lnst[:, 6:7], OP.subtract, OP.mult),
                    waits=[(cac, t4)], inc=cdv)
                LW[("xtok", par)] = xn1
                for i0 in range(0, ND, 2):
                    for j in range(2):
                        i = i0 + j
                        p.put("tensor", lambda e, i=i, sl=j, par=par: e.transpose(
                            ps_trP[sl][:, :],
                            xnt[:, par * 2048 + i * 128: par * 2048 + (i + 1) * 128],
                            idbf[:]),
                            waits=[(cdv, xn1), (cac, LW.get(("ps_tr", j), 0)),
                                   (s_xnsp, LW.get("xnc_freeq", 0))], inc=None)
                    dr = p.put("tensor", lambda e: e.drain(), inc=cpe)
                    for j in range(2):
                        i = i0 + j
                        xn_evs = p.put("scalar", lambda e, i=i, sl=j, tau=tau: e.copy(
                            xnc[:, i * 512 + tau * 128: i * 512 + (tau + 1) * 128],
                            ps_trP[sl][:, :]),
                            waits=[(cpe, dr)], inc=cac)
                        LW[("ps_tr", j)] = xn_evs
            xn_sp = 0
            for part in range(4):
                xn_sp = p.put("scalar", lambda e, c=c, part=part: e.dma_start(
                    bass.AP(xn_s, (part * 4 * 128) * TOK + c * 512,
                            [[TOK, 128], [128 * TOK, 4], [1, 512]]),
                    bass.AP(xnc, part * 4 * 512, [[ND * 512, 128], [512, 4], [1, 512]])),
                    waits=[(cac, xn_evs)], inc=s_xnsp, inck=16)
            LW["xnc_freeq"] = xn_sp

            last_mm = 0
            for k in range(ND):
                wl = [(cac, xn_evs)]
                if k == 0:
                    wl.append((cac, LW.get("ps_xp", 0)))
                p.put("tensor", lambda e, k=k: e.matmul(
                    ps_xp[:], wxpT[:, k * 128:(k + 1) * 128], xnc[:, k * 512:(k + 1) * 512],
                    start=(k == 0), stop=(k == ND - 1)), waits=wl, inc=None)
            last_mm = p.put("tensor", lambda e: e.drain(), inc=cpe)
            LW["xnc_freep"] = last_mm
            bce = p.put("scalar", lambda e: e.activation(
                bcv[:, :], ps_xp[0:32, :], AF.Identity, bias=xpb_sb[:, :]),
                waits=[(cpe, last_mm), (s_bcv, LW.get("bcv_free", 0))], inc=cac)
            LW["ps_xp"] = bce
            bcd = p.put("scalar", lambda e, c=c: e.dma_start(
                bass.AP(bc_s, c * 512, [[TOK, 32], [1, 512]]), bcv[:, :]),
                waits=[(cac, bce)], inc=s_bcv, inck=16)
            LW["bcv_free"] = bcd
            for et in range(ND):
                bank = et % 4
                last_mm = 0
                for k in range(ND):
                    wl = [(s_wdtS, wdt_ready), (cac, xn_evs)]
                    if k == 0:
                        wl.append((cdv, LW.get(("ps_dt", bank), 0)))
                        wl.append((cac, LW.get(("ps_dt_a", bank), 0)))
                    p.put("tensor", lambda e, et=et, k=k, bank=bank: e.matmul(
                        ps_dt[:, bank * 512:(bank + 1) * 512],
                        wdtT[:, k * 2048 + et * 128: k * 2048 + (et + 1) * 128],
                        xnc[:, k * 512:(k + 1) * 512],
                        start=(k == 0), stop=(k == ND - 1)),
                        waits=wl, inc=None)
                last_mm = p.put("tensor", lambda e: e.drain(), inc=cpe)
                sp = bank % 2
                a1 = p.put("scalar", lambda e, et=et, bank=bank, sp=sp: e.activation(
                    spb[:, sp * 1024:sp * 1024 + 512],
                    ps_dt[:, bank * 512:(bank + 1) * 512], AF.Abs,
                    bias=dtb_sb[:, et:et + 1]),
                    waits=[(cpe, last_mm), (cdv, LW.get(("spb", sp), 0))], inc=cac)
                p.put("scalar", lambda e, sp=sp: e.activation(
                    spb[:, sp * 1024 + 512:sp * 1024 + 1024],
                    spb[:, sp * 1024:sp * 1024 + 512], AF.Exp, scale=-1.0), inc=cac)
                p.put("scalar", lambda e, sp=sp: e.activation(
                    spb[:, sp * 1024:sp * 1024 + 512],
                    spb[:, sp * 1024 + 512:sp * 1024 + 1024], AF.Ln, bias=csts[:, 3:4]), inc=cac)
                r1v = p.put("scalar", lambda e, et=et, bank=bank, sp=sp: e.activation(
                    spb[:, sp * 1024 + 512:sp * 1024 + 1024],
                    ps_dt[:, bank * 512:(bank + 1) * 512], AF.Relu,
                    bias=dtb_sb[:, et:et + 1]), inc=cac)
                LW[("ps_dt_a", bank)] = r1v
                dtc = p.put("vector", lambda e, et=et, sp=sp: e.tensor_add(
                    dtv[:, (et % 2) * 512:(et % 2) * 512 + 512],
                    spb[:, sp * 1024:sp * 1024 + 512],
                    spb[:, sp * 1024 + 512:sp * 1024 + 1024]),
                    waits=[(cac, r1v), (s_dtv[et % 2], LW.get(("dtv", et % 2), 0))], inc=cdv)
                LW[("ps_dt", bank)] = dtc
                LW[("spb", sp)] = dtc
                dsp = p.put("scalar", lambda e, c=c, et=et: e.dma_start(
                    bass.AP(dt_s, (et * 128) * TOK + c * 512, [[TOK, 128], [1, 512]]),
                    dtv[:, (et % 2) * 512:(et % 2) * 512 + 512]),
                    waits=[(cdv, dtc)], inc=s_dtv[et % 2], inck=16)
                LW[("dtv", et % 2)] = dsp
        p.emit(nc)
    bstack.close()
    wstack.close()
    QACT_B2 = qact.n

    if upto < 3:
        return _finish_debug(nc, dbg_outs, stack, qsp, qact, ccc)
    p = Plan()
    LW = {}
    s_xnh = dsem(p, "s_xnh")
    s_gs = [dsem(p, "s_gs0"), dsem(p, "s_gs1")]
    s_us = [dsem(p, "s_us0"), dsem(p, "s_us1")]
    s_dtp = [dsem(p, "s_dtp0"), dsem(p, "s_dtp1")]
    s_xnp = [dsem(p, "s_xnp0"), dsem(p, "s_xnp1")]
    s_bcb = dsem(p, "s_bcb"); s_ccb = dsem(p, "s_ccb")
    s_sb = [dsem(p, "s_sb0"), dsem(p, "s_sb1")]
    s_yb = [dsem(p, "s_yb0"), dsem(p, "s_yb1")]
    s_insp = [dsem(p, "s_insp0"), dsem(p, "s_insp1")]
    s_stc = dsem(p, "s_stc")
    bstack = ExitStack()
    _a = lambda nm, shape, dt: bstack.enter_context(nc.sbuf_tensor(nm, shape, dt))
    xnh = _a("xnh", [128, ND * 1024], BF16)
    gsl = _a("gsl", [128, 2 * ND * 256], BF16)
    usl = _a("usl", [128, 2 * ND * 256], BF16)
    qb = _a("qb", [128, 1024], F32)
    pb = _a("pb", [128, 1024], F32)
    tb = qb
    rb = pb
    inters = _a("inters", [128, 2 * 1024], BF16)
    dtc_s = _a("dtc_s", [128, ND * 512], F32)
    xncs = _a("xncs", [128, ND * 512], BF16)
    dtxb = _a("dtxb", [128, 2 * 512], F32)
    Sb = _a("Sb", [128, 2 * 512], F32)
    dab = _a("dab", [128, 2 * 512], F32)
    dbxb = _a("dbxb", [128, 2 * 512], F32)
    strip = _a("strip", [128, 4 * 512], F32)
    Bcb = _a("Bcb", [128, DS * 512], BF16)
    Ccb = _a("Ccb", [128, DS * 512], BF16)
    ytmp = _a("ytmp", [128, 2 * 1024], F32)
    ybuf = _a("ybuf", [128, 2 * 512], F32)
    ps8 = bstack.enter_context(nc.psum_tensor("ps8", [128, 8 * 512], F32))
    if True:
        def emit_xnh_load(qt):
            last = 0
            for i in range(ND):
                last = p.put("sync", lambda e, qt=qt, i=i: e.dma_start(
                    xnh[:, i * 1024:(i + 1) * 1024],
                    bass.AP(xn_s, (i * 128) * TOK + qt * 1024, [[TOK, 128], [1, 1024]])),
                    waits=[(cpe, LW.get("xnh_free", 0))],
                    inc=s_xnh, inck=16)
            return last

        mt_count = [0]

        def emit_ffn_unit(qt, u, xnh_ready, filler=None):
            r, q8 = u >> 3, u & 7
            par = u % 2
            g_ld = p.put("sync", lambda e, r=r, q8=q8, par=par: e.dma_start(
                bass.AP(gsl, par * ND * 256, [[2 * ND * 256, 128], [256, ND], [1, 256]]),
                bass.AP(gW["g"], (r * 2048) * 2048 + q8 * 256,
                        [[2048, 128], [128 * 2048, ND], [1, 256]])),
                waits=[(ccc, AGG_DONE), (cpe, LW.get(("gsl", par), 0))],
                inc=s_gs[par], inck=16)
            u_ld = p.put("sync", lambda e, r=r, q8=q8, par=par: e.dma_start(
                bass.AP(usl, par * ND * 256, [[2 * ND * 256, 128], [256, ND], [1, 256]]),
                bass.AP(gW["u"], (r * 2048) * 2048 + q8 * 256,
                        [[2048, 128], [128 * 2048, ND], [1, 256]])),
                waits=[(ccc, AGU_DONE), (cpe, LW.get(("usl", par), 0))],
                inc=s_us[par], inck=16)
            for s in range(2):
                m128 = r * 16 + q8 * 2 + s
                mpar = mt_count[0] % 2
                mt_count[0] += 1
                gbank, ubank = mpar * 4, mpar * 4 + 2
                last_g = last_u = 0
                for k in range(ND):
                    for tch in range(2):
                        wl = [(s_gs[par], g_ld), (s_xnh, xnh_ready)]
                        if k == 0 and tch == 0:
                            wl.append((cdv, LW.get(("geluA", mpar), 0)))
                        last_g = p.put("tensor", lambda e, k=k, tch=tch, s=s, par=par,
                                       gbank=gbank: e.matmul(
                            ps8[:, (gbank + tch) * 512:(gbank + tch + 1) * 512],
                            gsl[:, par * ND * 256 + k * 256 + s * 128:
                                par * ND * 256 + k * 256 + s * 128 + 128],
                            xnh[:, k * 1024 + tch * 512: k * 1024 + (tch + 1) * 512],
                            start=(k == 0), stop=(k == ND - 1)),
                            waits=wl, inc=None)
                for k in range(ND):
                    for tch in range(2):
                        wl = [(s_us[par], u_ld)]
                        if k == 0 and tch == 0:
                            wl.append((cdv, LW.get(("geluB", mpar), 0)))
                        p.put("tensor", lambda e, k=k, tch=tch, s=s, par=par,
                              ubank=ubank: e.matmul(
                            ps8[:, (ubank + tch) * 512:(ubank + tch + 1) * 512],
                            usl[:, par * ND * 256 + k * 256 + s * 128:
                                par * ND * 256 + k * 256 + s * 128 + 128],
                            xnh[:, k * 1024 + tch * 512: k * 1024 + (tch + 1) * 512],
                            start=(k == 0), stop=(k == ND - 1)),
                            waits=wl, inc=None)
                last_u = p.put("tensor", lambda e: e.drain(), inc=cpe)
                if True:
                    qv = p.put("scalar", lambda e, gbank=gbank: e.activation(
                        qb[:], ps8[:, gbank * 512:(gbank + 2) * 512],
                        AF.Square, scale=GELU_A2),
                        waits=[(cpe, last_u), (cdv, LW.get("qb_free", 0))], inc=cac)
                    pv = p.put("vector", lambda e, gbank=gbank: e.scalar_tensor_tensor(
                        pb[:], qb[:], 1.0,
                        ps8[:, gbank * 512:(gbank + 2) * 512],
                        OP.add, OP.mult),
                        waits=[(cac, qv)], inc=cdv)
                    tv = p.put("scalar", lambda e: e.activation(
                        tb[:], pb[:], AF.Tanh, scale=GELU_C),
                        waits=[(cdv, pv)], inc=cac)
                    rv = p.put("vector", lambda e, gbank=gbank: e.scalar_tensor_tensor(
                        rb[:], tb[:], 1.0,
                        ps8[:, gbank * 512:(gbank + 2) * 512],
                        OP.add, OP.mult),
                        waits=[(cac, tv)], inc=cdv)
                    LW["qb_free"] = rv
                    iv = p.put("vector", lambda e, ubank=ubank, s=s: e.scalar_tensor_tensor(
                        inters[:, (s % 2) * 1024:(s % 2) * 1024 + 1024],
                        rb[:], 0.5,
                        ps8[:, ubank * 512:(ubank + 2) * 512],
                        OP.mult, OP.mult),
                        waits=[(s_insp[s % 2], LW.get(("inters", s % 2), 0))], inc=cdv)
                LW[("geluA", mpar)] = cdv.n
                LW[("geluB", mpar)] = cdv.n
                isp = p.put("scalar", lambda e, m128=m128, qt=qt, s=s: e.dma_start(
                    bass.AP(inter_s, (m128 * 128) * TOK + qt * 1024, [[TOK, 128], [1, 1024]]),
                    inters[:, (s % 2) * 1024:(s % 2) * 1024 + 1024]),
                    waits=[(cdv, cdv.n)], inc=s_insp[s % 2], inck=16)
                LW[("inters", s % 2)] = isp
                if filler is not None:
                    filler()
            LW[("gsl", par)] = cpe.n
            LW[("usl", par)] = cpe.n

        def emit_scan_half_loads(c, h):
            for part in (2 * h, 2 * h + 1):
                p.put("scalar", lambda e, c=c, part=part: e.dma_start(
                    bass.AP(dtc_s, part * 4 * 512, [[ND * 512, 128], [512, 4], [1, 512]]),
                    bass.AP(dt_s, (part * 4 * 128) * TOK + c * 512,
                            [[TOK, 128], [128 * TOK, 4], [1, 512]])),
                    waits=[(cdv, LW.get(("dtcF", h), 0)),
                           (cac, LW.get(("dtcFa", h), 0))], inc=s_dtp[h], inck=16)
                p.put("scalar", lambda e, c=c, part=part: e.dma_start(
                    bass.AP(xncs, part * 4 * 512, [[ND * 512, 128], [512, 4], [1, 512]]),
                    bass.AP(xn_s, (part * 4 * 128) * TOK + c * 512,
                            [[TOK, 128], [128 * TOK, 4], [1, 512]])),
                    waits=[(cdv, LW.get(("dtcF", h), 0))], inc=s_xnp[h], inck=16)

        def emit_bc_loads(c):
            for n in range(DS):
                p.put("gpsimd", lambda e, c=c, n=n: e.dma_start(
                    Bcb[:, n * 512:(n + 1) * 512],
                    bass.AP(bc_s, n * TOK + c * 512, [[0, 128], [1, 512]])),
                    waits=[(cpl, LW.get("Bcb_free", 0))], inc=s_bcb, inck=16)
                p.put("gpsimd", lambda e, c=c, n=n: e.dma_start(
                    Ccb[:, n * 512:(n + 1) * 512],
                    bass.AP(bc_s, (16 + n) * TOK + c * 512, [[0, 128], [1, 512]])),
                    waits=[(cdv, LW.get("Ccb_free", 0))], inc=s_ccb, inck=16)

        def emit_scan_dtile_head(c, i):
            h = i // 8
            ld_done = (32 * (c + 1), 32 * (c + 1))
            ip = i % 2
            dtx = p.put("vector", lambda e, i=i, ip=ip: e.tensor_mul(
                dtxb[:, ip * 512:ip * 512 + 512],
                dtc_s[:, i * 512:(i + 1) * 512], xncs[:, i * 512:(i + 1) * 512]),
                waits=[(s_dtp[h], ld_done[0]), (s_xnp[h], ld_done[1]),
                       (cpl, LW.get(("dtxb", ip), 0))], inc=cdv)
            ssc = p.put("vector", lambda e, i=i, ip=ip: e.tensor_tensor_scan(
                Sb[:, ip * 512:ip * 512 + 512],
                dtc_s[:, i * 512:(i + 1) * 512], zeros[:, 0:512],
                sstate[:, i:i + 1], OP.add, OP.add),
                waits=[(s_sb[ip], LW.get(("Sb", ip), 0))], inc=cdv)
            p.put("vector", lambda e, i=i, ip=ip: e.tensor_copy(
                sstate[:, i:i + 1], Sb[:, ip * 512 + 511:ip * 512 + 512]),
                waits=[(cdv, ssc)], inc=cdv)
            ssp = p.put("gpsimd", lambda e, c=c, i=i, ip=ip: e.dma_start(
                bass.AP(ssum_s, (i * 128) * TOK + c * 512, [[TOK, 128], [1, 512]]),
                Sb[:, ip * 512:ip * 512 + 512]),
                waits=[(cdv, ssc)], inc=s_sb[ip], inck=16)
            LW[("Sb", ip)] = ssp
            if i == 7 or i == 15:
                LW[("dtcF", h)] = cdv.n
                LW[("dtcFa", h)] = cac.n
            sc_state["cur"] = {"dtx": dtx, "ip": ip}

        def emit_scan_dtile_g(c, i, g):
            h = i // 8
            ld_done = (32 * (c + 1), 32 * (c + 1))
            bc_done = (256 * (c + 1), 256 * (c + 1))
            dtx = sc_state["cur"]["dtx"]
            ip = sc_state["cur"]["ip"]
            for nn in range(4):
                n = g * 4 + nn
                ab = n % 2
                da = p.put("scalar", lambda e, i=i, n=n, ab=ab: e.activation(
                    dab[:, ab * 512:(ab + 1) * 512],
                    dtc_s[:, i * 512:(i + 1) * 512], AF.Exp,
                    scale=aneg[:, i * DS + n:i * DS + n + 1]),
                    waits=[(s_dtp[h], ld_done[0]), (cdv, LW.get(("dab", ab), 0))],
                    inc=cac)
                dbx = p.put("gpsimd", lambda e, i=i, n=n, ab=ab, ip=ip: e.tensor_mul(
                    dbxb[:, ab * 512:(ab + 1) * 512],
                    dtxb[:, ip * 512:ip * 512 + 512],
                    Bcb[:, n * 512:(n + 1) * 512]),
                    waits=[(cdv, dtx), (s_bcb, bc_done[0]),
                           (cdv, LW.get(("dbxb", ab), 0))], inc=cpl)
                sc = p.put("vector", lambda e, i=i, n=n, nn=nn, ab=ab: e.tensor_tensor_scan(
                    strip[:, nn * 512:(nn + 1) * 512],
                    dab[:, ab * 512:(ab + 1) * 512],
                    dbxb[:, ab * 512:(ab + 1) * 512],
                    state[:, i * DS + n:i * DS + n + 1], OP.mult, OP.add),
                    waits=[(cac, da), (cpl, dbx)], inc=cdv)
                LW[("dab", ab)] = sc
                LW[("dbxb", ab)] = sc
                last_sc = sc
            p.put("vector", lambda e, i=i, g=g: e.tensor_copy(
                state[:, i * DS + g * 4: i * DS + g * 4 + 4],
                bass.AP(strip, 511, [[2048, 128], [512, 4]])),
                waits=[(cdv, last_sc)], inc=cdv)
            for np_ in range(2):
                nn0 = 2 * np_
                n0 = g * 4 + nn0
                yp = np_ % 2
                yt = p.put("vector", lambda e, n0=n0, nn0=nn0, yp=yp: e.tensor_mul(
                    ytmp[:, yp * 1024:(yp + 1) * 1024],
                    strip[:, nn0 * 512:(nn0 + 2) * 512],
                    Ccb[:, n0 * 512:(n0 + 2) * 512]),
                    waits=[(s_ccb, bc_done[1]),
                           (cdv, LW.get(("ytmp", yp), 0))], inc=cdv)
                for hh_ in range(2):
                    n = n0 + hh_
                    if n == 0:
                        ya = p.put("vector", lambda e, ip=ip: e.tensor_copy(
                            ybuf[:, ip * 512:ip * 512 + 512], ytmp[:, 0:512]),
                            waits=[(s_yb[ip], LW.get(("ybuf", ip), 0))],
                            inc=cdv)
                    else:
                        ya = p.put("vector", lambda e, ip=ip, yp=yp, hh_=hh_: e.tensor_add(
                            ybuf[:, ip * 512:ip * 512 + 512],
                            ybuf[:, ip * 512:ip * 512 + 512],
                            ytmp[:, yp * 1024 + hh_ * 512:yp * 1024 + (hh_ + 1) * 512]),
                            inc=cdv)
                LW[("ytmp", yp)] = ya
            if g == 3:
                ysp = p.put("gpsimd", lambda e, c=c, i=i, ip=ip: e.dma_start(
                    bass.AP(y_s, (i * 128) * TOK + c * 512, [[TOK, 128], [1, 512]]),
                    ybuf[:, ip * 512:ip * 512 + 512]),
                    waits=[(cdv, LW.get(("ytmp", 0), 0)), (cdv, LW.get(("ytmp", 1), 0))],
                    inc=s_yb[ip], inck=16)
                LW[("ybuf", ip)] = ysp
                LW[("dtxb", ip)] = cpl.n
                if i == 15:
                    LW["Bcb_free"] = cpl.n
                    LW["Ccb_free"] = cdv.n

        sc_state = {"c": 0, "i": 0, "ph": 0, "h0": False, "cur": {}}

        def emit_scan_heads0():
            emit_scan_half_loads(0, 0)
            emit_scan_half_loads(0, 1)
            emit_bc_loads(0)
            sc_state["h0"] = True

        def emit_scan_micro():
            c, i, ph = sc_state["c"], sc_state["i"], sc_state["ph"]
            if c >= NCH:
                return
            if not sc_state["h0"]:
                emit_scan_heads0()
            if ph == 0:
                emit_scan_dtile_head(c, i)
                sc_state["ph"] = 1
                return
            emit_scan_dtile_g(c, i, ph - 1)
            if ph < 4:
                sc_state["ph"] = ph + 1
                return
            sc_state["ph"] = 0
            if i == 7 and c + 1 < NCH:
                emit_scan_half_loads(c + 1, 0)
            if i == 15:
                if c + 1 < NCH:
                    emit_scan_half_loads(c + 1, 1)
                    emit_bc_loads(c + 1)
                sc_state["c"], sc_state["i"] = c + 1, 0
            else:
                sc_state["i"] = i + 1

        if B3A_MODE != 0:
            filler = emit_scan_micro if B3A_MODE == 2 else None
            if B3A_MODE == 2:
                emit_scan_heads0()
            for qt in range(4):
                xnh_rdy = emit_xnh_load(qt)
                for u in range(64):
                    emit_ffn_unit(qt, u, xnh_rdy, filler)
                    if B3A_MODE == 2:
                        emit_scan_micro()
                LW["xnh_free"] = cpe.n
        while sc_state["c"] < NCH:
            emit_scan_micro()
        stp = p.put("scalar", lambda e: e.dma_start(st_c.ap(), state[:]),
                    waits=[(cdv, cdv.n), (cpl, cpl.n)], inc=s_stc, inck=16)
        p.put("gpsimd", lambda e: e.collective_compute(
            "AllGather", OP.bypass, replica_groups=[list(range(8))],
            ins=[st_c.ap()], outs=[st_g.ap()]),
            waits=[(s_stc, stp)], inc=ccc)
        p.put("gpsimd", lambda e: e.collective_compute(
            "AllGather", OP.bypass, replica_groups=[list(range(8))],
            ins=[st_c.ap()], outs=[st_g.ap()]),
            inc=ccc)
        p.finish_dma()
        p.emit(nc)
    bstack.close()
    QACT_B3A = qact.n

    if upto < 4:
        return _finish_debug(nc, dbg_outs, stack, qsp, qact, ccc)
    p = Plan()
    LW = {}
    s_hin = dsem(p, "s_hin")
    s_yl = [dsem(p, "s_yl0"), dsem(p, "s_yl1")]
    s_sc2 = [dsem(p, "s_sc2a"), dsem(p, "s_sc2b")]
    s_cc2 = [dsem(p, "s_cc2a"), dsem(p, "s_cc2b")]
    s_yw = [dsem(p, "s_yw0"), dsem(p, "s_yw1")]
    s_int = [dsem(p, "s_int0"), dsem(p, "s_int1")]
    s_rb = [dsem(p, f"s_rb{i}") for i in range(4)]
    s_gw = dsem(p, "s_gw")
    bstack = ExitStack()
    _a = lambda nm, shape, dt: bstack.enter_context(nc.sbuf_tensor(nm, shape, dt))
    interc = _a("interc", [128, 128 * 512], BF16)
    Ccb2 = _a("Ccb2", [128, 2 * DS * 512], BF16)
    rbk = _a("rbk", [128, 4 * 1024], BF16)
    tnb = _a("tnb", [128, 512], F32)
    gvb = _a("gvb", [128, 512], F32)
    Sc2 = _a("Sc2", [128, 2 * 512], F32)
    Eb = _a("Eb", [128, 2 * 512], F32)
    ctmp = _a("ctmp", [128, 2 * 512], F32)
    yacc = _a("yacc", [128, 2 * 512], F32)
    yld = _a("yld", [128, 2 * 512], F32)
    psD = bstack.enter_context(nc.psum_tensor("psD", [128, 8 * 512], F32))
    if True:
        hin_ready = 0
        for r in range(8):
            hl = p.put("gpsimd", lambda e, r=r: e.dma_start(
                hin_st[:, 0:256],
                st_g[r * 128:(r + 1) * 128, :]),
                waits=[(ccc, AGS_DONE), (cdv, hin_ready)], inc=s_hin, inck=16)
            hin_ready = p.put("vector", lambda e, r=r: e.scalar_tensor_tensor(
                hin[:], hin_st[:, 0:256], sel8_sb[:, r:r + 1], hin[:], OP.mult, OP.add),
                waits=[(s_hin, hl)], inc=cdv)

        prev_add = {"eng": None, "val": 0}

        def emit_sc2_load(ct):
            c, i = divmod(ct, 16)
            if c >= NCH:
                return
            sl = ct % 2
            p.put("scalar", lambda e, c=c, i=i, sl=sl: e.dma_start(
                Sc2[:, sl * 512:(sl + 1) * 512],
                bass.AP(ssum_s, (i * 128) * TOK + c * 512, [[TOK, 128], [1, 512]])),
                inc=s_sc2[sl], inck=16)

        def emit_yld_load(ct):
            c, i = divmod(ct, 16)
            if c >= NCH:
                return
            sl = ct % 2
            p.put("scalar", lambda e, c=c, i=i, sl=sl: e.dma_start(
                yld[:, sl * 512:(sl + 1) * 512],
                bass.AP(y_s, (i * 128) * TOK + c * 512, [[TOK, 128], [1, 512]])),
                waits=[(cpl, LW.get(("yldF", sl), 0))], inc=s_yl[sl], inck=16)

        def emit_corr_heads(c):
            cp = c % 2
            for n in range(DS):
                p.put("gpsimd", lambda e, c=c, cp=cp, n=n: e.dma_start(
                    Ccb2[:, cp * DS * 512 + n * 512:cp * DS * 512 + (n + 1) * 512],
                    bass.AP(bc_s, (16 + n) * TOK + c * 512, [[0, 128], [1, 512]])),
                    waits=[(cdv, LW.get(("cc2F", cp), 0))], inc=s_cc2[cp], inck=16)

        cr_state = {"c": 0, "i": 0}

        def emit_corr_piece():
            c, i = cr_state["c"], cr_state["i"]
            if c >= NCH:
                return
            cp = c % 2
            ct = c * 16 + i
            sl = ct % 2
            ya_off = sl * 512
            for n in range(DS):
                eb = n % 2
                ev = p.put("scalar", lambda e, i=i, n=n, sl=sl: e.activation(
                    Eb[:, (n % 2) * 512:(n % 2) * 512 + 512],
                    Sc2[:, sl * 512:(sl + 1) * 512], AF.Exp,
                    scale=aneg[:, i * DS + n:i * DS + n + 1]),
                    waits=[(s_sc2[sl], 16 * (ct // 2 + 1)),
                           (cdv, LW.get(("Eb2", eb), 0))], inc=cac)
                wl = [(cac, ev), (s_cc2[cp], 256 * (c // 2 + 1)), (cdv, hin_ready)]
                if n == 0:
                    wl.append((s_yw[sl], 16 * (ct // 2)))
                    tv = p.put("vector", lambda e, i=i, n=n, cp=cp, ya_off=ya_off:
                               e.scalar_tensor_tensor(
                        yacc[:, ya_off:ya_off + 512],
                        Eb[:, 0:512],
                        hin[:, i * DS + n:i * DS + n + 1],
                        Ccb2[:, cp * DS * 512 + n * 512:cp * DS * 512 + (n + 1) * 512],
                        OP.mult, OP.mult),
                        waits=wl, inc=cdv)
                    LW[("Eb2", eb)] = tv
                    prev_add["eng"], prev_add["val"] = "vector", tv
                    continue
                wctmp = LW.get(("ctmp2", eb), None)
                if wctmp is not None:
                    wl.append(wctmp)
                tv = p.put("vector", lambda e, i=i, n=n, eb=eb, cp=cp: e.scalar_tensor_tensor(
                    ctmp[:, eb * 512:(eb + 1) * 512],
                    Eb[:, eb * 512:(eb + 1) * 512],
                    hin[:, i * DS + n:i * DS + n + 1],
                    Ccb2[:, cp * DS * 512 + n * 512:cp * DS * 512 + (n + 1) * 512],
                    OP.mult, OP.mult),
                    waits=wl, inc=cdv)
                LW[("Eb2", eb)] = tv
                if n % 2 == 1:
                    wl2 = [(cdv, tv)]
                    if prev_add["eng"] == "vector":
                        wl2.append((cdv, prev_add["val"]))
                    ya = p.put("gpsimd", lambda e, ya_off=ya_off, eb=eb: e.tensor_add(
                        yacc[:, ya_off:ya_off + 512], yacc[:, ya_off:ya_off + 512],
                        ctmp[:, eb * 512:(eb + 1) * 512]),
                        waits=wl2, inc=cpl)
                    prev_add["eng"], prev_add["val"] = "gpsimd", ya
                    LW[("ctmp2", eb)] = (cpl, ya)
                else:
                    wl2 = [(cdv, tv)]
                    if prev_add["eng"] == "gpsimd":
                        wl2.append((cpl, prev_add["val"]))
                    ya = p.put("vector", lambda e, ya_off=ya_off, eb=eb: e.tensor_add(
                        yacc[:, ya_off:ya_off + 512], yacc[:, ya_off:ya_off + 512],
                        ctmp[:, eb * 512:(eb + 1) * 512]),
                        waits=wl2, inc=cdv)
                    prev_add["eng"], prev_add["val"] = "vector", ya
                    LW[("ctmp2", eb)] = (cdv, ya)
            wl3 = [(s_yl[sl], 16 * (ct // 2 + 1))]
            if prev_add["eng"] == "vector":
                wl3.append((cdv, prev_add["val"]))
            fa = p.put("gpsimd", lambda e, ya_off=ya_off, sl=sl: e.tensor_add(
                yacc[:, ya_off:ya_off + 512], yacc[:, ya_off:ya_off + 512],
                yld[:, sl * 512:(sl + 1) * 512]),
                waits=wl3, inc=cpl)
            prev_add["eng"], prev_add["val"] = "gpsimd", fa
            LW[("yldF", sl)] = fa
            p.put("gpsimd", lambda e, c=c, i=i, ya_off=ya_off: e.dma_start(
                bass.AP(y_s, (i * 128) * TOK + c * 512, [[TOK, 128], [1, 512]]),
                yacc[:, ya_off:ya_off + 512]),
                inc=s_yw[sl], inck=16)
            emit_sc2_load(ct + 1)
            emit_yld_load(ct + 1)
            if i == 8 and c + 1 < NCH:
                emit_corr_heads(c + 1)
            if i == 15:
                LW[("cc2F", cp)] = cdv.n
                cr_state["c"], cr_state["i"] = c + 1, 0
            else:
                cr_state["i"] = i + 1

        def emit_interc_half(c2, h):
            if c2 >= NCH:
                return
            for part in range(8 * h, 8 * h + 8):
                p.put("sync", lambda e, c2=c2, part=part: e.dma_start(
                    bass.AP(interc, part * 8 * 512, [[128 * 512, 128], [512, 8], [1, 512]]),
                    bass.AP(inter_s, (part * 8 * 128) * TOK + c2 * 512,
                            [[TOK, 128], [128 * TOK, 8], [1, 512]])),
                    waits=[(cpe, LW.get(("intF", h), 0))],
                    inc=s_int[h], inck=16)

        def emit_down_tchunk(c2, filler):
            for hh in range(2):
                last_mm = 0
                for nt in range(128):
                    rpar = nt % 4
                    rld = p.put("sync", lambda e, nt=nt, hh=hh, rpar=rpar: e.dma_start(
                        rbk[:, rpar * 1024:(rpar + 1) * 1024],
                        bass.AP(gW["d"], (nt * 128) * 2048 + hh * 1024,
                                [[2048, 128], [1, 1024]])),
                        waits=[(ccc, AGD_DONE), (cpe, LW.get(("rbk", rpar), 0))],
                        inc=s_rb[rpar], inck=16)
                    for h8 in range(8):
                        wl = [(s_rb[rpar], rld), (s_int[nt // 64], 128 * (c2 + 1))]
                        if nt == 0 and h8 == 0:
                            wl.append((cdv, LW.get("psD_free", 0)))
                        last_mm = p.put("tensor", lambda e, nt=nt, h8=h8, rpar=rpar: e.matmul(
                            psD[:, h8 * 512:(h8 + 1) * 512],
                            rbk[:, rpar * 1024 + h8 * 128: rpar * 1024 + (h8 + 1) * 128],
                            interc[:, nt * 512:(nt + 1) * 512],
                            start=(nt == 0), stop=(nt == 127)),
                            waits=wl, inc=(cpe if h8 == 7 else None))
                    LW[("rbk", rpar)] = last_mm
                    if nt % 16 == 15 and filler is not None:
                        filler()
                    if hh == 1 and nt == 63:
                        LW[("intF", 0)] = cpe.n
                        emit_interc_half(c2 + 1, 0)
                for h8 in range(8):
                    tnv = p.put("scalar", lambda e, h8=h8: e.activation(
                        tnb[:, 0:512], psD[:, h8 * 512:(h8 + 1) * 512],
                        AF.Tanh, scale=0.5),
                        waits=[(cpe, last_mm), (cdv, LW.get(("tnb", 0), 0))], inc=cac)
                    gv = p.put("vector", lambda e, h8=h8: e.scalar_tensor_tensor(
                        gvb[:, 0:512], tnb[:, 0:512],
                        1.0, psD[:, h8 * 512:(h8 + 1) * 512], OP.add, OP.mult),
                        waits=[(cac, tnv), (s_gw, LW.get(("gvb", 0), 0))], inc=cdv)
                    LW[("tnb", 0)] = gv
                    gsp = p.put("scalar", lambda e, c2=c2, hh=hh, h8=h8: e.dma_start(
                        bass.AP(g_s, ((hh * 8 + h8) * 128) * TOK + c2 * 512,
                                [[TOK, 128], [1, 512]]),
                        gvb[:, 0:512]),
                        waits=[(cdv, gv)], inc=s_gw, inck=16)
                    LW[("gvb", 0)] = gsp
                LW["psD_free"] = cdv.n
            LW[("intF", 1)] = cpe.n
            emit_interc_half(c2 + 1, 1)

        filler = emit_corr_piece if B3B_MODE not in (0,) else None
        if B3B_MODE != 0:
            emit_corr_heads(0)
            emit_sc2_load(0)
            emit_yld_load(0)
        if B3B_MODE != 1:
            emit_interc_half(0, 0)
            emit_interc_half(0, 1)
            for c2 in range(NCH):
                emit_down_tchunk(c2, filler)
        while B3B_MODE != 0 and cr_state["c"] < NCH:
            emit_corr_piece()
        p.finish_dma()
        p.emit(nc)
    bstack.close()
    QACT_B3B = qact.n

    if upto < 5:
        return _finish_debug(nc, dbg_outs, stack, qsp, qact, ccc)
    p = Plan()
    LW = {}
    s_wo = dsem(p, "s_wo")
    s_ym = [dsem(p, "s_ym0"), dsem(p, "s_ym1")]
    s_gb = dsem(p, "s_gb")
    s_x4 = [dsem(p, "s_x40"), dsem(p, "s_x41")]
    s_out = dsem(p, "s_out")
    bstack = ExitStack()
    _a = lambda nm, shape, dt: bstack.enter_context(nc.sbuf_tensor(nm, shape, dt))
    woutT = _a("woutT", [128, ND * 2048], BF16)
    ysm = _a("ysm", [128, 2 * 512], F32)
    ybf = _a("ybf", [128, ND * 512], BF16)
    gbig = _a("gbig", [128, ND * 512], F32)
    sA = _a("sA", [128, 2 * 512], F32)
    ubig = _a("ubig", [128, ND * 512], F32)
    utok = _a("utok", [128, 2 * 2048], F32)
    xt4 = _a("xt4", [128, 2 * 2048], F32)
    ob = _a("ob", [128, 2048], F32)
    psS = bstack.enter_context(nc.psum_tensor("psS", [128, 4 * 512], F32))
    psTa = bstack.enter_context(nc.psum_tensor("psTa", [128, 128], F32))
    psTb = bstack.enter_context(nc.psum_tensor("psTb", [128, 128], F32))
    psTP = [psTa, psTb]
    if True:
        wo_ready = 0
        for r in range(8):
            wo_ready = p.put("sync", lambda e, r=r: e.dma_start(
                bass.AP(woutT, r * 256, [[ND * 2048, 128], [2048, ND], [1, 256]]),
                bass.AP(g1, (2048 + r * 4096) * 256, [[256, 128], [128 * 256, ND], [1, 256]])),
                inc=s_wo, inck=16)
        for c in range(NCH):
            g_ld4 = 0
            for part in range(4):
                g_ld4 = p.put("scalar", lambda e, c=c, part=part: e.dma_start(
                    bass.AP(gbig, part * 4 * 512, [[ND * 512, 128], [512, 4], [1, 512]]),
                    bass.AP(g_s, (part * 4 * 128) * TOK + c * 512,
                            [[TOK, 128], [128 * TOK, 4], [1, 512]])),
                    waits=[(cdv, LW.get("gbig_free", 0))],
                    inc=s_gb, inck=16)
            cast_done = 0
            for i in range(ND):
                ipp = i % 2
                yld1 = p.put("scalar", lambda e, c=c, i=i, ipp=ipp: e.dma_start(
                    ysm[:, ipp * 512:(ipp + 1) * 512],
                    bass.AP(y_s, (i * 128) * TOK + c * 512, [[TOK, 128], [1, 512]])),
                    waits=[(cac, LW.get(("ysm", ipp), 0))],
                    inc=s_ym[ipp], inck=16)
                cast_done = p.put("scalar", lambda e, i=i, ipp=ipp: e.activation(
                    ybf[:, i * 512:(i + 1) * 512], ysm[:, ipp * 512:(ipp + 1) * 512],
                    AF.Copy),
                    waits=[(s_ym[ipp], yld1), (cpe, LW.get("ybf_free", 0))], inc=cac)
                LW[("ysm", ipp)] = cast_done
            for mt in range(ND):
                bank = mt % 4
                last_mm = 0
                for k in range(ND):
                    wl = [(s_wo, wo_ready), (cac, cast_done)]
                    if k == 0:
                        wl.append((cdv, LW.get(("psS", bank), 0)))
                    p.put("tensor", lambda e, mt=mt, k=k, bank=bank: e.matmul(
                        psS[:, bank * 512:(bank + 1) * 512],
                        woutT[:, k * 2048 + mt * 128: k * 2048 + (mt + 1) * 128],
                        ybf[:, k * 512:(k + 1) * 512],
                        start=(k == 0), stop=(k == ND - 1)),
                        waits=wl, inc=None)
                last_mm = p.put("tensor", lambda e: e.drain(), inc=cpe)
                sev = p.put("scalar", lambda e, mt=mt, bank=bank: e.activation(
                    sA[:, (mt % 2) * 512:(mt % 2) * 512 + 512],
                    psS[:, bank * 512:(bank + 1) * 512], AF.Identity,
                    bias=outb_sb[:, mt:mt + 1]),
                    waits=[(cpe, last_mm), (cdv, LW.get(("sA", mt % 2), 0))], inc=cac)
                uv = p.put("vector", lambda e, mt=mt: e.scalar_tensor_tensor(
                    ubig[:, mt * 512:(mt + 1) * 512],
                    sA[:, (mt % 2) * 512:(mt % 2) * 512 + 512], 0.5,
                    gbig[:, mt * 512:(mt + 1) * 512], OP.mult, OP.mult),
                    waits=[(cac, sev), (s_gb, g_ld4), (cpe, LW.get("ubig_free", 0))],
                    inc=cdv)
                LW[("sA", mt % 2)] = uv
                LW[("psS", bank)] = uv
            u_done = cdv.n
            LW["gbig_free"] = u_done
            for tau in range(4):
                tp = tau % 2
                xld = p.put("sync", lambda e, c=c, tau=tau, tp=tp: e.dma_start(
                    xt4[:, tp * 2048:(tp + 1) * 2048],
                    x_in[c * 512 + tau * 128: c * 512 + (tau + 1) * 128, :]),
                    waits=[(cdv, LW.get(("xt4", tp), 0))], inc=s_x4[tp], inck=16)
                ev_last = 0
                for i0 in range(0, ND, 2):
                    for j in range(2):
                        i = i0 + j
                        p.put("tensor", lambda e, i=i, sl=j, tau=tau: e.transpose(
                            psTP[sl][:, :],
                            bass.AP(ubig, i * 512 + tau * 128, [[ND * 512, 128], [1, 128]]),
                            idf32[:]),
                            waits=[(cdv, u_done), (cac, LW.get(("psT", j), 0))], inc=None)
                    dr = p.put("tensor", lambda e: e.drain(), inc=cpe)
                    for j in range(2):
                        i = i0 + j
                        ev_last = p.put("scalar", lambda e, i=i, sl=j, tp=tp: e.copy(
                            utok[:, tp * 2048 + i * 128: tp * 2048 + (i + 1) * 128],
                            psTP[sl][:, :]),
                            waits=[(cpe, dr), (cdv, LW.get(("utok", tp), 0))], inc=cac)
                        LW[("psT", j)] = ev_last
                oadd = p.put("vector", lambda e, tp=tp: e.tensor_add(
                    ob[:, 0:2048], xt4[:, tp * 2048:(tp + 1) * 2048],
                    utok[:, tp * 2048:(tp + 1) * 2048]),
                    waits=[(cac, ev_last), (s_x4[tp], xld), (s_out, LW.get(("ob", 0), 0))],
                    inc=cdv)
                LW[("xt4", tp)] = oadd
                LW[("utok", tp)] = oadd
                ost = p.put("scalar", lambda e, c=c, tau=tau: e.dma_start(
                    out_sh[c * 512 + tau * 128: c * 512 + (tau + 1) * 128, :],
                    ob[:, 0:2048]),
                    waits=[(cdv, oadd)], inc=s_out, inck=16)
                LW[("ob", 0)] = ost
            LW["ybf_free"] = cpe.n
            LW["ubig_free"] = cpe.n
        p.finish_dma()
        p.emit(nc)
    bstack.close()

    stack.close()
    nc.finalize()
    return nc


def _finish_debug(nc, dbg_outs, stack, qsp, qact, ccc=None):
    p = Plan()
    if ccc is not None and ccc.n:
        p.put("sync", lambda e: e.wait_ge(ccc.h, ccc.n) and None)
    n = 0
    for od, ih in dbg_outs:
        rows = ih.shape[0]
        step = max(1, min(rows, (1 << 20) // max(1, int(ih.shape[1]))))
        r = 0
        while r < rows:
            rr = min(step, rows - r)
            n = p.put("sync", lambda e, od=od, ih=ih, r=r, rr=rr: e.dma_start(
                od[r:r + rr, :], ih[r:r + rr, :]),
                waits=[(qsp, qsp.n), (qact, qact.n)], inc=qsp, inck=16)
            r += rr
    p.put("sync", lambda e: e.wait_ge(qsp.h, qsp.n) and None)
    p.put("scalar", lambda e: e.wait_ge(qact.h, qact.n) and None)
    p.emit(nc)
    stack.close()
    nc.finalize()
    return nc


_CACHE = {}


def kernel(L0_attn_x, L0_attn_A_log, L0_attn_x_proj_w, L0_attn_x_proj_b,
           L0_attn_dt_proj_w, L0_attn_dt_proj_b, L0_attn_out_proj_w,
           L0_attn_out_proj_b, L0_ffn_gate_W_packed, L0_ffn_gate_scales,
           L0_ffn_up_W_packed, L0_ffn_up_scales, L0_ffn_down_W_packed,
           L0_ffn_down_scales):
    x = np.ascontiguousarray(np.asarray(L0_attn_x, dtype=np.float32))
    if "nc" not in _CACHE:
        _CACHE["nc"] = build_program()
    nc = _CACHE["nc"]

    gwp = np.asarray(L0_ffn_gate_W_packed, dtype=np.int32)
    gsc = np.asarray(L0_ffn_gate_scales, dtype=np.float32)
    uwp = np.asarray(L0_ffn_up_W_packed, dtype=np.int32)
    usc = np.asarray(L0_ffn_up_scales, dtype=np.float32)
    dwp = np.asarray(L0_ffn_down_W_packed, dtype=np.int32)
    dsc = np.asarray(L0_ffn_down_scales, dtype=np.float32)
    wdt = np.asarray(L0_attn_dt_proj_w, dtype=np.float32)
    wout = np.asarray(L0_attn_out_proj_w, dtype=np.float32)

    in_maps = []
    for c in range(NCORE):
        b, hf = c // 2, c % 2
        sel = np.zeros((128, 8), np.float32)
        if hf == 1:
            sel[:, c - 1] = 1.0
        in_maps.append({
            "x_sh": np.ascontiguousarray(x[b, hf * TOK:(hf + 1) * TOK, :]),
            "gwp": np.ascontiguousarray(gwp[c * 2048:(c + 1) * 2048, :]),
            "gsc": np.ascontiguousarray(gsc[c * 2048:(c + 1) * 2048, :]),
            "uwp": np.ascontiguousarray(uwp[c * 2048:(c + 1) * 2048, :]),
            "usc": np.ascontiguousarray(usc[c * 2048:(c + 1) * 2048, :]),
            "dwp": np.ascontiguousarray(dwp[:, c * 1024:(c + 1) * 1024]),
            "dsc": np.ascontiguousarray(dsc[:, c * 64:(c + 1) * 64]),
            "wdt_sh": np.ascontiguousarray(wdt[c * 256:(c + 1) * 256, :]),
            "wout_sh": np.ascontiguousarray(wout[c * 256:(c + 1) * 256, :]),
            "alog": np.ascontiguousarray(np.asarray(L0_attn_A_log, np.float32)),
            "xpw": np.ascontiguousarray(np.asarray(L0_attn_x_proj_w, np.float32)),
            "xpb": np.ascontiguousarray(np.asarray(L0_attn_x_proj_b, np.float32)),
            "dtb": np.ascontiguousarray(np.asarray(L0_attn_dt_proj_b, np.float32)),
            "outb": np.ascontiguousarray(np.asarray(L0_attn_out_proj_b, np.float32)),
            "sel8": sel,
        })
    res = run_bass_kernel_spmd(nc, in_maps, core_ids=list(range(NCORE)))
    out = np.empty((B_, L_, D), np.float32)
    for c in range(NCORE):
        b, hf = c // 2, c % 2
        out[b, hf * TOK:(hf + 1) * TOK, :] = res.results[c]["out_sh"]
    return out
